# version 07
# baseline (speedup 1.0000x reference)
import contextlib
import numpy as np
import ml_dtypes
import concourse.bass as bass
import concourse.mybir as mybir
from concourse.bass_utils import run_bass_kernel_spmd

F32 = mybir.dt.float32
BF16 = mybir.dt.bfloat16
AF = mybir.ActivationFunctionType
ALU = mybir.AluOpType
AX = mybir.AxisListType

T = 2048
D = 1024
NCORES = 8
TC = 512
NTC = T // TC
bf16 = ml_dtypes.bfloat16

SEMS = ("pe", "act", "dve", "pool", "qx", "qw", "qw2", "qsp", "qsl", "qo")


class B:
    def __init__(self, nc):
        self.nc = nc
        self.cnt = {s: 0 for s in SEMS}
        self.sem = {s: nc.semaphore(s).__enter__() for s in SEMS}
        self.block = None
        self.consts = {}

    def v(self, f):
        self.block.vector(f)

    def s(self, f):
        self.block.scalar(f)

    def t(self, f):
        self.block.tensor(f)

    def g(self, f):
        self.block.gpsimd(f)

    def q(self, f):
        self.block.sync(f)

    def inc(self, instr, name, k=1):
        instr.then_inc(self.sem[name], k)
        self.cnt[name] += k

    def dma(self, qeng, qname, out, in_):
        ins = qeng.dma_start(out=out, in_=in_)
        self.inc(ins, qname, 16)

    def w(self, eng, *names):
        for n in names:
            c = self.cnt[n]
            if c > 0:
                eng.wait_ge(self.sem[n], c)

    def wv(self, eng, name, val):
        if val > 0:
            eng.wait_ge(self.sem[name], val)

    def scope(self):
        return contextlib.ExitStack()

    def fence(self):
        def f(g):
            self.w(g, "pe", "act", "dve")
            ins = g.engine_nop()
            self.inc(ins, "pool")
        self.g(f)

    def sb(self, es, name, shape, dt):
        return es.enter_context(self.nc.sbuf_tensor(name, shape, dt))

    def pt(self, es, name, shape, dt):
        return es.enter_context(self.nc.psum_tensor(name, shape, dt))



def _cc(a):
    return np.ascontiguousarray(a)


def feat_tiles(w, P=128):
    F, M = w.shape
    assert F % P == 0
    return _cc(w.reshape(F // P, P, M).transpose(1, 0, 2).reshape(P, (F // P) * M))


def col_pack(vec, P=128):
    F = vec.shape[0]
    return _cc(vec.reshape(F // P, P).T)



def ln_emit(b, src_of, src_is_bf, rows, tag, ntiles=8, nfeat=None, eps=1e-5):
    nc = b.nc
    es = b.scope()
    scr8a = b.sb(es, tag + "_lncast", [128, ntiles * TC], BF16)
    scr8b = b.sb(es, tag + "_lnsq", [128, ntiles * TC], BF16)
    rowm = b.sb(es, tag + "_lnrowm", [1, T], F32)
    rowv = b.sb(es, tag + "_lnrowv", [1, T], F32)
    ps_sx = b.pt(es, tag + "_lnpsx", [1, TC], F32)
    ps_sq = b.pt(es, tag + "_lnpsq", [1, TC], F32)
    cast = [scr8a[:, i * TC:(i + 1) * TC] for i in range(ntiles)]
    sq = [scr8b[:, i * TC:(i + 1) * TC] for i in range(ntiles)]

    nfeat = nfeat or (128 * ntiles)
    inv = 1.0 / nfeat
    for tc in range(NTC):
        lo, hi = tc * TC, (tc + 1) * TC

        def casts(s, lo=lo, hi=hi):
            b.w(s, "pe", "dve", "qx", "pool")
            ins = None
            for i in range(ntiles):
                if not src_is_bf:
                    s.copy(cast[i][:, 0:TC], src_of(i)[:, lo:hi])
                ins = s.activation(sq[i][:, 0:TC], src_of(i)[:, lo:hi], AF.Square)
            b.inc(ins, "act")
        b.s(casts)

        def mms(t, lo=lo, hi=hi):
            b.w(t, "act")
            for i in range(ntiles):
                rhs = src_of(i)[:, lo:hi] if src_is_bf else cast[i][:, 0:TC]
                t.matmul(ps_sx[0:1, 0:TC], b.consts["ones_col_bf"][:, 0:1], rhs,
                         start=(i == 0), stop=(i == ntiles - 1))
            for i in range(ntiles):
                ins = t.matmul(ps_sq[0:1, 0:TC], b.consts["ones_col_bf"][:, 0:1],
                               sq[i][:, 0:TC], start=(i == 0), stop=(i == ntiles - 1))
            b.inc(ins, "pe")
        b.t(mms)

        def rowc(s, lo=lo, hi=hi):
            b.w(s, "pe")
            s.mul(rowm[0:1, lo:hi], ps_sx[0:1, 0:TC], inv)
            ins = s.mul(rowv[0:1, lo:hi], ps_sq[0:1, 0:TC], inv)
            b.inc(ins, "act")
        b.s(rowc)

    def rowfin_v(v):
        b.w(v, "act")
        v.tensor_mul(rows[0:1, T:2 * T], rowm[0:1, :], rowm[0:1, :])
        ins = v.tensor_sub(rowv[0:1, :], rowv[0:1, :], rows[0:1, T:2 * T])
        b.inc(ins, "dve")
    b.v(rowfin_v)

    def rowfin_s(s):
        b.w(s, "dve")
        s.activation(rows[0:1, T:2 * T], rowv[0:1, :], AF.Ln, bias=eps)
        ins = s.activation(rows[0:1, 0:T], rows[0:1, T:2 * T], AF.Exp, scale=-0.5)
        b.inc(ins, "act")
    b.s(rowfin_s)

    def nm2_v(v):
        b.w(v, "act")
        ins = v.scalar_tensor_tensor(rows[0:1, T:2 * T], rowm[0:1, :], -1.0, rows[0:1, 0:T],
                                     ALU.mult, ALU.mult)
        b.inc(ins, "dve")
    b.v(nm2_v)
    es.close()


def ln_bcast(b, rows, lo, n, ps_r, ps_m):
    def mm(t):
        b.w(t, "dve", "act")
        t.matmul(ps_r[:, 0:n], b.consts["ones_row_bf"][0:1, :], rows[0:1, lo:lo + n],
                 start=True, stop=True)
        ins = t.matmul(ps_m[:, 0:n], b.consts["ones_row_bf"][0:1, :], rows[0:1, T + lo:T + lo + n],
                       start=True, stop=True)
        b.inc(ins, "pe")
    b.t(mm)


def emit_wload(b, jobs, direct=(), war=None):
    stg = b.wstg
    if not hasattr(b, "_stgfree"):
        b._stgfree = [0, 0]
    jobs2 = []
    for dst, src in jobs:
        n = src.shape[1]
        o = 0
        while o < n:
            m = min(512, n - o)
            jobs2.append((dst[:, o:o + m], src[:, o:o + m]))
            o += m
    for idx, (dst, src) in enumerate(jobs2):
        par = idx % 2
        qsem = "qw" if par == 0 else "qw2"

        def ldq(qq, par=par, src=src, qsem=qsem):
            b.wv(qq, "pool", b._stgfree[par])
            ins = qq.dma_start(out=stg[0:src.shape[0], par * 512:par * 512 + src.shape[1]], in_=src)
            b.inc(ins, qsem, 16)
        b.q(ldq)

        def cst(g, par=par, dst=dst, p=src.shape[0], n=src.shape[1], qsem=qsem):
            b.w(g, qsem)
            if war is not None:
                b.wv(g, "pe", war)
            else:
                b.w(g, "pe", "act", "dve")
            ins = g.tensor_copy(dst, stg[0:p, par * 512:par * 512 + n])
            b.inc(ins, "pool")
        b.g(cst)
        b._stgfree[par] = b.cnt["pool"]

    if direct:
        def ldd(qq):
            b.w(qq, "pool")
            for dst, src in direct:
                b.dma(qq, "qw", dst, src)
        b.q(ldd)

    def qwbar(g):
        b.w(g, "qw", "qw2")
        ins = g.engine_nop()
        b.inc(ins, "pool")
    b.g(qwbar)



def _l1a_xln_apply(b, tc, X, rows, ps_r, ps_m, xln, f32scr):
    lo, hi = tc * TC, (tc + 1) * TC
    ln_bcast(b, rows, lo, TC, ps_r, ps_m)

    def apply(v):
        b.w(v, "pe")
        ins = None
        for i in range(8):
            v.tensor_mul(f32scr[:, 0:TC], X[:, i * T + lo:i * T + hi], ps_r[:, 0:TC])
            ins = v.tensor_add(xln[:, i * TC:(i + 1) * TC], f32scr[:, 0:TC], ps_m[:, 0:TC])
        b.inc(ins, "dve")
    b.v(apply)


def _l1a_phaseA_xp_chunk(b, tc, X, rows, ps_r, ps_m, ps_a, ps_b, wi, xpb, xln,
                         xst, cw, cb, acc, f32scr, xc_sp):
    lo = tc * TC
    _l1a_xln_apply(b, tc, X, rows, ps_r, ps_m, xln, f32scr)

    def halo(v):
        ins = None
        for ct in range(16):
            if tc == 0:
                ins = v.memset(xpb[:, ct * 520:ct * 520 + 3], 0.0)
            else:
                ins = v.tensor_copy(xpb[:, ct * 520:ct * 520 + 3],
                                    xpb[:, ct * 520 + 512:ct * 520 + 515])
        b.inc(ins, "dve")
    b.v(halo)

    for ec in range(16):
        ps = ps_a if ec % 2 == 0 else ps_b

        def mm(t, ec=ec, ps=ps):
            b.w(t, "pool", "dve")
            ins = None
            for dc in range(8):
                ins = t.matmul(ps[:, 0:TC],
                               wi[:, dc * 2048 + ec * 128:dc * 2048 + (ec + 1) * 128],
                               xln[:, dc * TC:(dc + 1) * TC],
                               start=(dc == 0), stop=(dc == 7))
            b.inc(ins, "pe")
        b.t(mm)

        def ev(v, ec=ec, ps=ps):
            b.w(v, "pe")
            ins = v.tensor_copy(xpb[:, ec * 520 + 3:ec * 520 + 515], ps[:, 0:TC])
            b.inc(ins, "dve")
        b.v(ev)

    for ct in range(16):
        def conv(v, ct=ct):
            b.w(v, "act", "qw")
            base = ct * 520
            v.tensor_scalar(acc[:, 0:TC], xpb[:, base + 3:base + 3 + TC],
                            cw[:, ct * 4 + 3:ct * 4 + 4], cb[:, ct:ct + 1],
                            ALU.mult, ALU.add)
            ins = None
            for k in (0, 1, 2):
                ins = v.scalar_tensor_tensor(acc[:, 0:TC], xpb[:, base + k:base + k + TC],
                                             cw[:, ct * 4 + k:ct * 4 + k + 1], acc[:, 0:TC],
                                             ALU.mult, ALU.add)
            b.inc(ins, "dve")
        b.v(conv)

        def sil(s, ct=ct):
            b.w(s, "dve", "qsp")
            ins = s.activation(xst[:, (ct % 2) * TC:(ct % 2 + 1) * TC], acc[:, 0:TC], AF.Silu)
            b.inc(ins, "act")
        b.s(sil)

        def spxc(qq, ct=ct):
            b.w(qq, "act")
            b.dma(qq, "qsp", xc_sp[ct * 128:(ct + 1) * 128, lo:lo + TC],
                  xst[:, (ct % 2) * TC:(ct % 2 + 1) * TC])
        b.q(spxc)


def _l1a_phaseA_z_chunk(b, tc, X, rows, ps_r, ps_m, ps_a, ps_b, wi, xln,
                        zst, f32scr, z_sp):
    lo = tc * TC
    _l1a_xln_apply(b, tc, X, rows, ps_r, ps_m, xln, f32scr)

    for ec in range(16):
        ps = ps_a if ec % 2 == 0 else ps_b
        par = ec % 2

        def mm(t, ec=ec, ps=ps):
            b.w(t, "pool", "dve")
            ins = None
            for dc in range(8):
                ins = t.matmul(ps[:, 0:TC],
                               wi[:, dc * 2048 + ec * 128:dc * 2048 + (ec + 1) * 128],
                               xln[:, dc * TC:(dc + 1) * TC],
                               start=(dc == 0), stop=(dc == 7))
            b.inc(ins, "pe")
        b.t(mm)

        def ev(v, ec=ec, ps=ps, par=par):
            b.w(v, "pe", "qsp")
            ins = v.tensor_copy(zst[:, par * TC:(par + 1) * TC], ps[:, 0:TC])
            b.inc(ins, "dve")
        b.v(ev)

        def spz(qq, ec=ec, par=par):
            b.w(qq, "dve")
            b.dma(qq, "qsp", z_sp[ec * 128:(ec + 1) * 128, lo:lo + TC],
                  zst[:, par * TC:(par + 1) * TC])
        b.q(spz)


def _l1a_phaseB_chunk(b, tc, ps_a, ps_b, ob, wo, wB, wC, xcc, zst, yg, nine, hsc,
                      hbf, xbs, zsil, xc_sp, z_sp):
    lo = tc * TC
    cur, prev = tc % 2, (tc + 1) % 2

    def ld(qq):
        b.w(qq, "pe", "act", "dve", "qsp")
        for ct in range(16):
            b.dma(qq, "qsl", xcc[:, ct * TC:(ct + 1) * TC],
                  xc_sp[ct * 128:(ct + 1) * 128, lo:lo + TC])
    b.q(ld)

    def mmxb(t):
        b.w(t, "qsl", "pool", "act")
        ins = None
        for ct in range(16):
            ins = t.matmul(ps_a[0:16, 0:TC], wB[:, ct * 16:(ct + 1) * 16],
                           xcc[:, ct * TC:(ct + 1) * TC], start=(ct == 0), stop=(ct == 15))
        b.inc(ins, "pe")
    b.t(mmxb)

    def evxb(s):
        b.w(s, "pe")
        ins = s.mul(xbs[0:16, 0:TC], ps_a[0:16, 0:TC], 0.1)
        b.inc(ins, "act")
    b.s(evxb)

    def scan(v):
        b.w(v, "act")
        init = 0.0 if tc == 0 else hsc[0:16, prev * TC + TC - 1:prev * TC + TC]
        v.tensor_tensor_scan(hsc[0:16, cur * TC:(cur + 1) * TC], nine[0:16, 0:TC],
                             xbs[0:16, 0:TC], init, ALU.mult, ALU.add)
        ins = v.tensor_copy(hbf[0:16, 0:TC], hsc[0:16, cur * TC:(cur + 1) * TC])
        b.inc(ins, "dve")
    b.v(scan)

    for ec in range(16):
        ps = ps_a if ec % 2 == 0 else ps_b
        par = ec % 2

        def ldz(qq, ec=ec, par=par):
            b.w(qq, "act", "pe")
            b.dma(qq, "qsl", zst[:, par * TC:(par + 1) * TC],
                  z_sp[ec * 128:(ec + 1) * 128, lo:lo + TC])
        b.q(ldz)

        def mmy(t, ec=ec, ps=ps):
            b.w(t, "dve")
            ins = t.matmul(ps[:, 0:TC], wC[0:16, ec * 128:(ec + 1) * 128],
                           hbf[0:16, 0:TC], start=True, stop=True)
            b.inc(ins, "pe")
        b.t(mmy)

        def silz(s, ec=ec, par=par):
            b.w(s, "qsl", "dve")
            ins = s.activation(zsil[:, 0:TC], zst[:, par * TC:(par + 1) * TC], AF.Silu)
            b.inc(ins, "act")
        b.s(silz)

        def gate(v, ec=ec, ps=ps):
            b.w(v, "pe", "act")
            ins = v.tensor_mul(yg[:, ec * TC:(ec + 1) * TC], ps[:, 0:TC], zsil[:, 0:TC])
            b.inc(ins, "dve")
        b.v(gate)

    for dc in range(8):
        ps = ps_a if dc % 2 == 0 else ps_b

        def mmo(t, dc=dc, ps=ps):
            b.w(t, "dve")
            ins = None
            for ct in range(16):
                ins = t.matmul(ps[:, 0:TC],
                               wo[:, ct * 1024 + dc * 128:ct * 1024 + (dc + 1) * 128],
                               yg[:, ct * TC:(ct + 1) * TC], start=(ct == 0), stop=(ct == 15))
            b.inc(ins, "pe")
        b.t(mmo)

        def evo(v, dc=dc, ps=ps):
            b.w(v, "pe")
            ins = v.tensor_copy(ob[:, dc * T + lo:dc * T + lo + TC], ps[:, 0:TC])
            b.inc(ins, "dve")
        b.v(evo)


def emit_residual_add(b, X, ob, rows, ps_r, ps_m, f32scr):
    for tc in range(NTC):
        lo = tc * TC
        ln_bcast(b, rows, lo, TC, ps_r, ps_m)

        def addres(v, lo=lo):
            b.w(v, "pe")
            ins = None
            for i in range(8):
                v.tensor_mul(f32scr[:, 0:TC], ob[:, i * T + lo:i * T + lo + TC], ps_r[:, 0:TC])
                v.tensor_add(f32scr[:, 0:TC], f32scr[:, 0:TC], ps_m[:, 0:TC])
                ins = v.tensor_add(X[:, i * T + lo:i * T + lo + TC],
                                   X[:, i * T + lo:i * T + lo + TC], f32scr[:, 0:TC])
            b.inc(ins, "dve")
        b.v(addres)


def emit_l1_attn(b, X, wd, spill, dbg=None, upto='full'):
    nc = b.nc
    xc_sp, z_sp = spill["l1a_xc"], spill["l1a_z"]
    es0 = b.scope()
    rows = b.sb(es0, "l1a_rows", [1, 2 * T], BF16)
    ps_r = b.pt(es0, "l1a_ps_r", [128, TC], F32)
    ps_m = b.pt(es0, "l1a_ps_m", [128, TC], F32)
    ps_a = b.pt(es0, "l1a_ps_a", [128, TC], F32)
    ps_b = b.pt(es0, "l1a_ps_b", [128, TC], F32)
    f32scr = b.sb(es0, "l1a_f32scr", [128, TC], F32)

    ln_emit(b, lambda i: X[:, i * T:(i + 1) * T], False, rows, "l1a_in")

    if dbg is not None and "rows" in dbg:
        def dmprows(qq):
            b.w(qq, "dve", "act")
            b.dma(qq, "qo", dbg["rows"], rows[0:1, :])
        b.q(dmprows)

    esA = b.scope()
    wi = b.sb(esA, "l1a_wi", [128, 8 * 2048], BF16)
    xpb = b.sb(esA, "l1a_xpb", [128, 16 * 520], BF16)
    xln = b.sb(esA, "l1a_xln", [128, 8 * TC], BF16)
    zst = b.sb(esA, "l1a_zst", [128, 2 * TC], BF16)
    xst = b.sb(esA, "l1a_xst", [128, 2 * TC], BF16)
    cw = b.sb(esA, "l1a_cw", [128, 64], F32)
    cb = b.sb(esA, "l1a_cb", [128, 16], F32)
    acc = b.sb(esA, "l1a_acc", [128, TC], F32)

    emit_wload(b, [(wi[:, i * 2048:(i + 1) * 2048],
                    wd["l1a_w_in"][i * 128:(i + 1) * 128, 0:2048]) for i in range(8)],
               direct=[(cw[:, :], wd["l1a_conv_w"]), (cb[:, :], wd["l1a_conv_b"])])

    for tc in range(NTC if upto != 'xp0' else 1):
        _l1a_phaseA_xp_chunk(b, tc, X, rows, ps_r, ps_m, ps_a, ps_b, wi, xpb,
                             xln, xst, cw, cb, acc, f32scr, xc_sp)
        if tc == 0 and dbg is not None and "xln" in dbg:
            def dmpxln(qq):
                b.w(qq, "dve", "act")
                b.dma(qq, "qo", dbg["xln"], xln[:, :])
                b.dma(qq, "qo", dbg["xpb"], xpb[:, :])
            b.q(dmpxln)

    if upto == 'xp0':
        esA.close()
        es0.close()
        return

    def fence_z(g):
        b.w(g, "pe", "act", "dve")
        ins = g.engine_nop()
        b.inc(ins, "pool")
    b.g(fence_z)
    emit_wload(b, [(wi[:, i * 2048:(i + 1) * 2048],
                    wd["l1a_w_in"][i * 128:(i + 1) * 128, 2048:4096]) for i in range(8)])

    for tc in range(NTC):
        _l1a_phaseA_z_chunk(b, tc, X, rows, ps_r, ps_m, ps_a, ps_b, wi, xln,
                            zst, f32scr, z_sp)
    esA.close()
    b.fence()

    esOB = b.scope()
    ob = b.sb(esOB, "l1a_ob", [128, 8 * T], BF16)

    esB = b.scope()
    wo = b.sb(esB, "l1a_wo", [128, 16 * 1024], BF16)
    wB = b.sb(esB, "l1a_wB", [128, 256], BF16)
    wC = b.sb(esB, "l1a_wC", [16, 2048], BF16)
    xcc = b.sb(esB, "l1a_xcc2", [128, 16 * TC], BF16)
    zst2 = b.sb(esB, "l1a_zst2", [128, 2 * TC], BF16)
    yg = b.sb(esB, "l1a_yg", [128, 16 * TC], BF16)
    nine = b.sb(esB, "l1a_nine", [16, TC], F32)
    hsc = b.sb(esB, "l1a_hsc", [16, 2 * TC], F32)
    hbf = b.sb(esB, "l1a_hbf", [16, TC], BF16)
    xbs = b.sb(esB, "l1a_xbs", [16, TC], F32)
    zsil = b.sb(esB, "l1a_zsil", [128, TC], F32)

    emit_wload(b, [(wo[:, i * 1024:(i + 1) * 1024],
                    wd["l1a_w_out"][:, i * 1024:(i + 1) * 1024]) for i in range(16)]
               + [(wB[:, :], wd["l1a_w_B"]), (wC[0:16, :], wd["l1a_w_C"])])

    def mset(v):
        ins = v.memset(nine[:, :], 0.9)
        b.inc(ins, "dve")
    b.v(mset)

    for tc in range(NTC):
        _l1a_phaseB_chunk(b, tc, ps_a, ps_b, ob, wo, wB, wC, xcc, zst2, yg,
                          nine, hsc, hbf, xbs, zsil, xc_sp, z_sp)
    esB.close()

    if dbg is not None and "ob" in dbg:
        def dmpob(qq):
            b.w(qq, "dve", "act")
            for i in range(8):
                b.dma(qq, "qo", dbg["ob"][i * 128:(i + 1) * 128, :], ob[:, i * T:(i + 1) * T])
        b.q(dmpob)
    ln_emit(b, lambda i: ob[:, i * T:(i + 1) * T], True, rows, "l1a_out")
    emit_residual_add(b, X, ob, rows, ps_r, ps_m, f32scr)
    esOB.close()
    es0.close()



def _l1f_router(b, xln, wd, combT):
    nc = b.nc
    es = b.scope()
    rwT = b.sb(es, "l1f_rwT_sb", [128, 8 * 8], BF16)
    ebias = b.sb(es, "l1f_eb_sb", [1, 8], F32)
    lgt = b.sb(es, "l1f_lgt", [128, 8], F32)
    eq1 = b.sb(es, "l1f_eq1", [128, 8], F32)
    eq2 = b.sb(es, "l1f_eq2", [128, 8], F32)
    l2t = b.sb(es, "l1f_l2t", [128, 8], F32)
    m1 = b.sb(es, "l1f_m1", [128, 4], F32)
    eqbf = b.sb(es, "l1f_eqbf", [128, 8], BF16)
    ps_s = b.pt(es, "l1f_ps_s", [128, 8], F32)
    ps_t = b.pt(es, "l1f_ps_t", [128, 128], F32)
    ps_tb = b.pt(es, "l1f_ps_tb", [8, 128], BF16)

    ebrep = b.sb(es, "l1f_ebrep", [128, 8], F32)
    emit_wload(b, [(rwT[:, i * 8:(i + 1) * 8], wd["l1f_rwT"][i * 128:(i + 1) * 128, :]) for i in range(8)],
               direct=[(ebias[0:1, :], wd["l1f_eb"])])

    def bias_bcast(t):
        b.w(t, "qw", "dve")
        ins = t.matmul(ps_t[:, 0:8], b.consts["ones_row_f32"][0:1, :], ebias[0:1, 0:8],
                       start=True, stop=True)
        b.inc(ins, "pe")
    b.t(bias_bcast)

    def evb(v):
        b.w(v, "pe")
        ins = v.tensor_copy(ebrep[:, 0:8], ps_t[:, 0:8])
        b.inc(ins, "dve")
    b.v(evb)

    for tt in range(16):
        lo = tt * 128

        def mml(t, lo=lo):
            b.w(t, "pool", "dve", "qw")
            ins = None
            for dc in range(8):
                ins = t.matmul(ps_s[:, 0:8], xln[:, dc * T + lo:dc * T + lo + 128],
                               rwT[:, dc * 8:(dc + 1) * 8], start=(dc == 0), stop=(dc == 7))
            b.inc(ins, "pe")
        b.t(mml)

        def top2(v):
            b.w(v, "pe")
            ops = [
                lambda: v.tensor_add(lgt[:, 0:8], ps_s[:, 0:8], ebrep[:, 0:8]),
                lambda: v.reduce_max(m1[:, 0:1], lgt[:, 0:8], AX.X),
                lambda: v.tensor_scalar(eq1[:, 0:8], lgt[:, 0:8], m1[:, 0:1], None, ALU.is_ge),
                lambda: v.scalar_tensor_tensor(l2t[:, 0:8], eq1[:, 0:8], -1e30, lgt[:, 0:8], ALU.mult, ALU.add),
                lambda: v.reduce_max(m1[:, 1:2], l2t[:, 0:8], AX.X),
                lambda: v.tensor_scalar(eq2[:, 0:8], l2t[:, 0:8], m1[:, 1:2], None, ALU.is_ge),
                lambda: v.tensor_sub(m1[:, 2:3], m1[:, 1:2], m1[:, 0:1]),
            ]
            for op in ops:
                ins = op()
                b.inc(ins, "dve")
                v.wait_ge(b.sem["dve"], b.cnt["dve"])
        b.v(top2)

        def sig(s):
            b.w(s, "dve")
            ins = s.activation(m1[:, 3:4], m1[:, 2:3], AF.Sigmoid)
            b.inc(ins, "act")
        b.s(sig)

        def comb(v):
            b.w(v, "act")
            ops = [
                lambda: v.tensor_scalar(m1[:, 2:3], m1[:, 3:4], -1.0, 1.0, ALU.mult, ALU.add),
                lambda: v.tensor_scalar(eq1[:, 0:8], eq1[:, 0:8], m1[:, 2:3], None, ALU.mult),
                lambda: v.scalar_tensor_tensor(eq1[:, 0:8], eq2[:, 0:8], m1[:, 3:4], eq1[:, 0:8],
                                               ALU.mult, ALU.add),
                lambda: v.tensor_copy(eqbf[:, 0:8], eq1[:, 0:8]),
            ]
            for op in ops:
                ins = op()
                b.inc(ins, "dve")
                v.wait_ge(b.sem["dve"], b.cnt["dve"])
        b.v(comb)

        def tr(t):
            b.w(t, "dve", "qx")
            ins = t.transpose(ps_tb[0:8, 0:128], eqbf[:, 0:8], b.consts["ident_bf"][:, :])
            b.inc(ins, "pe")
        b.t(tr)

        def evtr(v, lo=lo):
            b.w(v, "pe")
            ins = v.tensor_copy(combT[0:8, lo:lo + 128], ps_tb[0:8, 0:128])
            b.inc(ins, "dve")
        b.v(evtr)
    es.close()
    b.fence()


def emit_l1_ffn(b, X, wd, spill):
    nc = b.nc
    x_sp = spill["x_park"]
    es0 = b.scope()
    ps_r = b.pt(es0, "l1f_ps_r", [128, TC], F32)
    ps_m = b.pt(es0, "l1f_ps_m", [128, TC], F32)
    ps_a = b.pt(es0, "l1f_ps_a", [128, TC], F32)
    ps_b = b.pt(es0, "l1f_ps_b", [128, TC], F32)
    ps_c = b.pt(es0, "l1f_ps_c", [128, TC], F32)
    f32scr = b.sb(es0, "l1f_f32scr", [128, TC], F32)
    combT = b.sb(es0, "l1f_combT", [8, T], BF16)
    xln = b.sb(es0, "l1f_xln", [128, 8 * T], BF16)

    esR = b.scope()
    rows = b.sb(esR, "l1f_rows", [1, 2 * T], BF16)
    ln_emit(b, lambda i: X[:, i * T:(i + 1) * T], False, rows, "l1f_in")

    for tc in range(NTC):
        lo, hi = tc * TC, (tc + 1) * TC
        ln_bcast(b, rows, lo, TC, ps_r, ps_m)

        def apply(v, lo=lo, hi=hi):
            b.w(v, "pe")
            ins = None
            for i in range(8):
                v.tensor_mul(f32scr[:, 0:TC], X[:, i * T + lo:i * T + hi], ps_r[:, 0:TC])
                ins = v.tensor_add(xln[:, i * T + lo:i * T + hi], f32scr[:, 0:TC], ps_m[:, 0:TC])
            b.inc(ins, "dve")
        b.v(apply)

    _l1f_router(b, xln, wd, combT)
    esR.close()
    b.fence()

    def park(qq):
        b.w(qq, "dve", "act", "pe")
        for i in range(8):
            b.dma(qq, "qsp", x_sp[i * 128:(i + 1) * 128, :], X[:, i * T:(i + 1) * T])
    b.q(park)

    esE = b.scope()
    obacc = X
    hgbuf = b.sb(esE, "l1f_hg", [128, 16 * T], BF16)
    w13 = b.sb(esE, "l1f_w13", [128, 2 * 4096], BF16)
    w2s = b.sb(esE, "l1f_w2s", [128, 2 * 2048], BF16)
    cetile = b.sb(esE, "l1f_cetile", [128, T], BF16)
    silu2 = [f32scr, b.sb(esE, "l1f_silu1", [128, TC], F32)]
    ps_d = b.pt(esE, "l1f_ps_d", [128, TC], F32)
    ps_e = b.pt(esE, "l1f_ps_e", [128, TC], F32)

    b.l1f_war13 = [0, 0]
    b.l1f_war2 = [0, 0]
    b.l1f_snap1 = [0, 0]
    b.l1f_snap2 = [0, 0]

    def zero_ob(v):
        b.w(v, "qsp")
        ins = None
        for i in range(8):
            ins = v.memset(obacc[:, i * T:(i + 1) * T], 0.0)
        b.inc(ins, "dve")
    b.v(zero_ob)

    for ex in range(10):
        routed = ex >= 2
        w13src = wd["l1f_sw13"] if not routed else wd["l1f_w13"]
        w2src = wd["l1f_sw2r"] if not routed else wd["l1f_w2r"]
        eoff = ex if not routed else (ex - 2)

        for hc in range(16):
            par = hc % 2
            ap13 = w13src[eoff * 16 + hc]
            emit_wload(b, [(w13[:, par * 4096 + dc * 256:par * 4096 + (dc + 1) * 256],
                            ap13[dc * 128:(dc + 1) * 128, :]) for dc in range(8)],
                       war=b.l1f_war13[par])
            for tc in range(NTC):
                lo = tc * TC
                pp = (hc * NTC + tc) % 2
                psA = ps_a if pp == 0 else ps_d
                psB = ps_b if pp == 0 else ps_e

                def mm1(t, hc=hc, par=par, lo=lo, pp=pp, psA=psA, psB=psB, tc=tc):
                    b.w(t, "pool")
                    if hc == 0 and tc <= 1:
                        b.w(t, "dve")
                    else:
                        b.wv(t, "dve", b.l1f_snap1[pp])
                    for dc in range(8):
                        t.matmul(psA[:, 0:TC], w13[:, par * 4096 + dc * 256:par * 4096 + dc * 256 + 128],
                                 xln[:, dc * T + lo:dc * T + lo + TC], start=(dc == 0), stop=(dc == 7))
                    ins = None
                    for dc in range(8):
                        ins = t.matmul(psB[:, 0:TC], w13[:, par * 4096 + dc * 256 + 128:par * 4096 + dc * 256 + 256],
                                       xln[:, dc * T + lo:dc * T + lo + TC], start=(dc == 0), stop=(dc == 7))
                    b.inc(ins, "pe")
                b.t(mm1)
                pes = b.cnt["pe"]

                def gate_s(s, pp=pp, psA=psA, pes=pes):
                    b.wv(s, "pe", pes)
                    b.wv(s, "dve", b.l1f_snap1[pp])
                    ins = s.activation(silu2[pp][:, 0:TC], psA[:, 0:TC], AF.Silu)
                    b.inc(ins, "act")
                b.s(gate_s)
                acts = b.cnt["act"]

                def gate_v(v, hc=hc, lo=lo, pp=pp, psB=psB, acts=acts, pes=pes):
                    b.wv(v, "act", acts)
                    b.wv(v, "pe", pes)
                    ins = v.tensor_mul(hgbuf[:, hc * T + lo:hc * T + lo + TC],
                                       silu2[pp][:, 0:TC], psB[:, 0:TC])
                    b.inc(ins, "dve")
                b.v(gate_v)
                b.l1f_snap1[pp] = b.cnt["dve"]
            b.l1f_war13[par] = b.cnt["pe"]

        if routed:
            for tc in range(NTC):
                def cerep(t, e=eoff, tc=tc):
                    b.w(t, "dve", "qw", "qx")
                    ins = t.matmul(ps_c[:, 0:TC], b.consts["sel8"][0:8, e * 128:(e + 1) * 128],
                                   combT[0:8, tc * TC:(tc + 1) * TC], start=True, stop=True)
                    b.inc(ins, "pe")
                b.t(cerep)

                def evce(v, tc=tc):
                    b.w(v, "pe")
                    ins = v.tensor_copy(cetile[:, tc * TC:(tc + 1) * TC], ps_c[:, 0:TC])
                    b.inc(ins, "dve")
                b.v(evce)

        for dc in range(8):
            par = dc % 2
            ap2 = w2src[eoff * 8 + dc]
            emit_wload(b, [(w2s[:, par * 2048 + hc * 128:par * 2048 + (hc + 1) * 128],
                            ap2[hc * 128:(hc + 1) * 128, :]) for hc in range(16)],
                       war=b.l1f_war2[par])
            for tc in range(NTC):
                lo = tc * TC
                pp = (dc * NTC + tc) % 2
                psA = ps_a if pp == 0 else ps_b

                def mm2(t, dc=dc, par=par, lo=lo, pp=pp, psA=psA, tc=tc):
                    b.w(t, "pool")
                    if dc == 0 and tc <= 1:
                        b.w(t, "dve")
                    else:
                        b.wv(t, "dve", b.l1f_snap2[pp])
                    ins = None
                    for hc in range(16):
                        ins = t.matmul(psA[:, 0:TC], w2s[:, par * 2048 + hc * 128:par * 2048 + (hc + 1) * 128],
                                       hgbuf[:, hc * T + lo:hc * T + lo + TC],
                                       start=(hc == 0), stop=(hc == 15))
                    b.inc(ins, "pe")
                b.t(mm2)
                pes = b.cnt["pe"]

                if routed:
                    def acc(v, dc=dc, lo=lo, psA=psA, pes=pes):
                        b.wv(v, "pe", pes)
                        v.tensor_mul(f32scr[:, 0:TC], psA[:, 0:TC], cetile[:, lo:lo + TC])
                        ins = v.tensor_add(obacc[:, dc * T + lo:dc * T + lo + TC],
                                           obacc[:, dc * T + lo:dc * T + lo + TC], f32scr[:, 0:TC])
                        b.inc(ins, "dve")
                    b.v(acc)
                else:
                    def acc(v, dc=dc, lo=lo, psA=psA, pes=pes):
                        b.wv(v, "pe", pes)
                        ins = v.tensor_add(obacc[:, dc * T + lo:dc * T + lo + TC],
                                           obacc[:, dc * T + lo:dc * T + lo + TC], psA[:, 0:TC])
                        b.inc(ins, "dve")
                    b.v(acc)
                b.l1f_snap2[pp] = b.cnt["dve"]
            b.l1f_war2[par] = b.cnt["pe"]

    esE.close()
    b.fence()
    esE = b.scope()
    rows = b.sb(esE, "l1f_rows2", [1, 2 * T], BF16)
    xold = b.sb(esE, "l1f_xold", [128, 2 * TC], F32)

    ln_emit(b, lambda i: obacc[:, i * T:(i + 1) * T], False, rows, "l1f_out")

    for tc in range(NTC):
        lo = tc * TC
        ln_bcast(b, rows, lo, TC, ps_r, ps_m)
        for i in range(8):
            par = i % 2

            def ldxo(qq, i=i, lo=lo, par=par):
                b.w(qq, "dve", "pe", "pool")
                b.dma(qq, "qsl", xold[:, par * TC:(par + 1) * TC],
                      x_sp[i * 128:(i + 1) * 128, lo:lo + TC])
            b.q(ldxo)

            def addres(v, i=i, lo=lo, par=par):
                b.w(v, "pe", "qsl")
                v.tensor_mul(f32scr[:, 0:TC], obacc[:, i * T + lo:i * T + lo + TC], ps_r[:, 0:TC])
                v.tensor_add(f32scr[:, 0:TC], f32scr[:, 0:TC], ps_m[:, 0:TC])
                ins = v.tensor_add(X[:, i * T + lo:i * T + lo + TC],
                                   xold[:, par * TC:(par + 1) * TC], f32scr[:, 0:TC])
                b.inc(ins, "dve")
            b.v(addres)
    esE.close()
    es0.close()
    b.fence()


def l1f_dram(nc):
    wd = {
        "l1f_rwT": nc.dram_tensor("l1f_rwT", [1024, 8], F32, kind="ExternalInput").ap(),
        "l1f_eb": nc.dram_tensor("l1f_eb", [1, 8], F32, kind="ExternalInput").ap(),
    }
    w13 = nc.dram_tensor("l1f_w13p", [8 * 16, 1024, 256], F32, kind="ExternalInput")
    wd["l1f_w13"] = [w13.ap()[i] for i in range(128)]
    sw13 = nc.dram_tensor("l1f_sw13p", [2 * 16, 1024, 256], F32, kind="ExternalInput")
    wd["l1f_sw13"] = [sw13.ap()[i] for i in range(32)]
    w2r = nc.dram_tensor("l1f_w2rp", [8 * 8, 2048, 128], F32, kind="ExternalInput")
    wd["l1f_w2r"] = [w2r.ap()[i] for i in range(64)]
    sw2 = nc.dram_tensor("l1f_sw2rp", [2 * 8, 2048, 128], F32, kind="ExternalInput")
    wd["l1f_sw2r"] = [sw2.ap()[i] for i in range(16)]
    return wd


def prep_l1f(inputs):
    g = {}
    g["l1f_rwT"] = _cc(inputs["L1_ffn_router_w"].T.astype(np.float32))
    g["l1f_eb"] = _cc(inputs["L1_ffn_expert_bias"].reshape(1, 8).astype(np.float32))

    def pack13(w1, w3):
        E = w1.shape[0]
        out = np.empty((E * 16, 1024, 256), np.float32)
        for e in range(E):
            for hc in range(16):
                out[e * 16 + hc, :, 0:128] = w1[e, hc * 128:(hc + 1) * 128, :].T
                out[e * 16 + hc, :, 128:256] = w3[e, hc * 128:(hc + 1) * 128, :].T
        return _cc(out)

    def pack2(w2):
        E = w2.shape[0]
        out = np.empty((E * 8, 2048, 128), np.float32)
        for e in range(E):
            for dc in range(8):
                out[e * 8 + dc] = w2[e, dc * 128:(dc + 1) * 128, :].T
        return _cc(out)

    g["l1f_w13p"] = pack13(inputs["L1_ffn_w1"], inputs["L1_ffn_w3"])
    g["l1f_sw13p"] = pack13(inputs["L1_ffn_sw1"], inputs["L1_ffn_sw3"])
    g["l1f_w2rp"] = pack2(inputs["L1_ffn_w2"])
    g["l1f_sw2rp"] = pack2(inputs["L1_ffn_sw2"])
    return g



def _l0a_phaseA(b, lay, XM, wd, wi, xpb, xmbf, xst, zst, cw, cb, acc, xin_sp, z_sp):
    emit_wload(b, [(wi[:, i * 2048:(i + 1) * 2048],
                    wd["l0a_w_inT"][lay][i * 128:(i + 1) * 128, 0:2048]) for i in range(8)],
               direct=[(cw[:, :], wd["l0a_conv_w"][lay]), (cb[:, :], wd["l0a_conv_b"][lay])])
    for tc in range(NTC):
        lo = tc * TC

        def cast(v, lo=lo):
            b.w(v, "pe")
            ins = None
            for i in range(8):
                ins = v.tensor_copy(xmbf[:, i * TC:(i + 1) * TC], XM[:, i * T + lo:i * T + lo + TC])
            b.inc(ins, "dve")
        b.v(cast)

        def halo(v, tc=tc):
            ins = None
            for ct in range(16):
                if tc == 0:
                    ins = v.memset(xpb[:, ct * 520:ct * 520 + 3], 0.0)
                else:
                    ins = v.tensor_copy(xpb[:, ct * 520:ct * 520 + 3],
                                        xpb[:, ct * 520 + 512:ct * 520 + 515])
            b.inc(ins, "dve")
        b.v(halo)

        snapA = [b.cnt["dve"], b.cnt["dve"]]
        for ec in range(16):
            ps = b.l0a_ps_a if ec % 2 == 0 else b.l0a_ps_b

            def mm(t, ec=ec, ps=ps, w0=snapA[ec % 2]):
                b.w(t, "pool", "qw")
                b.wv(t, "dve", w0)
                ins = None
                for dc in range(8):
                    ins = t.matmul(ps[:, 0:TC],
                                   wi[:, dc * 2048 + ec * 128:dc * 2048 + (ec + 1) * 128],
                                   xmbf[:, dc * TC:(dc + 1) * TC],
                                   start=(dc == 0), stop=(dc == 7))
                b.inc(ins, "pe")
            b.t(mm)
            pes = b.cnt["pe"]

            def ev(v, ec=ec, ps=ps, pes=pes):
                b.wv(v, "pe", pes)
                ins = v.tensor_copy(xpb[:, ec * 520 + 3:ec * 520 + 515], ps[:, 0:TC])
                b.inc(ins, "dve")
            b.v(ev)
            snapA[ec % 2] = b.cnt["dve"]

        for ct in range(16):
            def conv(v, ct=ct):
                b.w(v, "act", "qw")
                base = ct * 520
                v.tensor_scalar(acc[:, 0:TC], xpb[:, base + 3:base + 3 + TC],
                                cw[:, ct * 4 + 3:ct * 4 + 4], cb[:, ct:ct + 1],
                                ALU.mult, ALU.add)
                ins = None
                for k in (0, 1, 2):
                    ins = v.scalar_tensor_tensor(acc[:, 0:TC], xpb[:, base + k:base + k + TC],
                                                 cw[:, ct * 4 + k:ct * 4 + k + 1], acc[:, 0:TC],
                                                 ALU.mult, ALU.add)
                b.inc(ins, "dve")
            b.v(conv)

            def sil(s, ct=ct):
                b.w(s, "dve", "qsp")
                ins = s.activation(xst[:, (ct % 2) * TC:(ct % 2 + 1) * TC], acc[:, 0:TC], AF.Silu)
                b.inc(ins, "act")
            b.s(sil)

            def spxc(qq, ct=ct, lo=lo):
                b.w(qq, "act")
                b.dma(qq, "qsp", xin_sp[ct * 128:(ct + 1) * 128, lo:lo + TC],
                      xst[:, (ct % 2) * TC:(ct % 2 + 1) * TC])
            b.q(spxc)

    def fence_z(g):
        b.w(g, "pe", "act", "dve")
        ins = g.engine_nop()
        b.inc(ins, "pool")
    b.g(fence_z)
    emit_wload(b, [(wi[:, i * 2048:(i + 1) * 2048],
                    wd["l0a_w_inT"][lay][i * 128:(i + 1) * 128, 2048:4096]) for i in range(8)])
    for tc in range(NTC):
        lo = tc * TC

        def cast(v, lo=lo):
            b.w(v, "pe")
            ins = None
            for i in range(8):
                ins = v.tensor_copy(xmbf[:, i * TC:(i + 1) * TC], XM[:, i * T + lo:i * T + lo + TC])
            b.inc(ins, "dve")
        b.v(cast)

        snapZ = [b.cnt["dve"], b.cnt["dve"]]
        for ec in range(16):
            ps = b.l0a_ps_a if ec % 2 == 0 else b.l0a_ps_b
            par = ec % 2

            def mm(t, ec=ec, ps=ps, w0=snapZ[ec % 2]):
                b.w(t, "pool", "qw")
                b.wv(t, "dve", w0)
                ins = None
                for dc in range(8):
                    ins = t.matmul(ps[:, 0:TC],
                                   wi[:, dc * 2048 + ec * 128:dc * 2048 + (ec + 1) * 128],
                                   xmbf[:, dc * TC:(dc + 1) * TC],
                                   start=(dc == 0), stop=(dc == 7))
                b.inc(ins, "pe")
            b.t(mm)
            pes = b.cnt["pe"]

            def ev(v, ec=ec, ps=ps, par=par, pes=pes):
                b.wv(v, "pe", pes)
                b.w(v, "qsp")
                ins = v.tensor_copy(zst[:, par * TC:(par + 1) * TC], ps[:, 0:TC])
                b.inc(ins, "dve")
            b.v(ev)
            snapZ[ec % 2] = b.cnt["dve"]

            def spz(qq, ec=ec, par=par, lo=lo):
                b.w(qq, "dve")
                b.dma(qq, "qsp", z_sp[ec * 128:(ec + 1) * 128, lo:lo + TC],
                      zst[:, par * TC:(par + 1) * TC])
            b.q(spz)


def _l0a_phaseB(b, lay, XM, wd, bufs, xin_sp, z_sp):
    nc = b.nc
    (wout, wx, wdt, acoef, bdt, dpar, xcc, zstg, bcache, ccache, bnpack, apack,
     delta, du, dubf, yred, zsil, ygbuf, state, dtrbf, bbrow, ccrow) = bufs
    ps_a, ps_b = b.l0a_ps_a, b.l0a_ps_b

    emit_wload(b, [(wout[:, i * 1024:(i + 1) * 1024],
                    wd["l0a_w_outT"][lay][i * 128:(i + 1) * 128, :]) for i in range(16)]
               + [(wx[:, i * 96:(i + 1) * 96], wd["l0a_w_xT"][lay][i * 128:(i + 1) * 128, :]) for i in range(16)]
               + [(wdt[0:64, :], wd["l0a_w_dtT"][lay])],
               direct=[(b.l0a_alog[:, :], wd["l0a_A_log"][lay]),
                       (bdt[:, :], wd["l0a_b_dt"][lay]),
                       (dpar[:, :], wd["l0a_D_par"][lay])])

    def mkA_s(s):
        b.w(s, "qw", "dve", "pe")
        ins = s.activation(acoef[:, :], b.l0a_alog[:, :], AF.Exp)
        b.inc(ins, "act")
    b.s(mkA_s)

    def mkA_v(v):
        b.w(v, "act")
        ins = v.tensor_scalar(acoef[:, :], acoef[:, :], -1.0, None, ALU.mult)
        b.inc(ins, "dve")
    b.v(mkA_v)

    for tc in range(NTC):
        lo = tc * TC

        def ld(qq, lo=lo):
            b.w(qq, "pe", "act", "dve", "qsp")
            for ct in range(16):
                b.dma(qq, "qsl", xcc[:, ct * TC:(ct + 1) * TC],
                      xin_sp[ct * 128:(ct + 1) * 128, lo:lo + TC])
        b.q(ld)

        def mmxd(t):
            b.w(t, "qsl", "pool", "act")
            for ct in range(16):
                t.matmul(ps_a[0:64, 0:TC], wx[:, ct * 96:ct * 96 + 64],
                         xcc[:, ct * TC:(ct + 1) * TC], start=(ct == 0), stop=(ct == 15))
            for ct in range(16):
                t.matmul(ps_b[0:16, 0:TC], wx[:, ct * 96 + 64:ct * 96 + 80],
                         xcc[:, ct * TC:(ct + 1) * TC], start=(ct == 0), stop=(ct == 15))
            ins = None
            for ct in range(16):
                ins = t.matmul(ps_b[32:48, 0:TC], wx[:, ct * 96 + 80:ct * 96 + 96],
                               xcc[:, ct * TC:(ct + 1) * TC], start=(ct == 0), stop=(ct == 15))
            b.inc(ins, "pe")
        b.t(mmxd)

        def evxd(v):
            b.w(v, "pe")
            v.tensor_copy(dtrbf, ps_a[0:64, 0:TC])
            v.tensor_copy(bbrow, ps_b[0:16, 0:TC])
            ins = v.tensor_copy(ccrow, ps_b[32:48, 0:TC])
            b.inc(ins, "dve")
        b.v(evxd)

        for n in range(16):
            ps = ps_a if n % 2 == 0 else ps_b

            def mmbc(t, n=n, ps=ps):
                b.w(t, "dve", "qx")
                t.matmul(ps[:, 0:TC], b.consts["sel16"][64:80, n * 128:(n + 1) * 128],
                         bbrow, start=True, stop=True)
                ins = t.matmul(ps[:, TC:2 * TC],
                               b.consts["sel16"][64:80, n * 128:(n + 1) * 128],
                               ccrow, start=True, stop=True)
                b.inc(ins, "pe")
            b.t(mmbc)

            def evbc(v, n=n, ps=ps):
                b.w(v, "pe")
                v.tensor_copy(bcache[:, n * TC:(n + 1) * TC], ps[:, 0:TC])
                ins = v.tensor_copy(ccache[:, n * TC:(n + 1) * TC], ps[:, TC:2 * TC])
                b.inc(ins, "dve")
            b.v(evbc)

        snap = {"softpv": [0] * 17, "scans": [0] * 17, "gpred": [0] * 17, "mmdt": [0] * 17}

        def tail(ct, lo):
            def ldz(qq, ct=ct, lo=lo):
                b.w(qq, "act", "pe")
                b.dma(qq, "qsl", zstg[:, (ct % 2) * TC:(ct % 2 + 1) * TC],
                      z_sp[ct * 128:(ct + 1) * 128, lo:lo + TC])
            b.q(ldz)

            def silz(s, ct=ct):
                b.w(s, "qsl", "dve")
                ins = s.activation(zsil[:, 0:TC], zstg[:, (ct % 2) * TC:(ct % 2 + 1) * TC], AF.Silu)
                b.inc(ins, "act")
            b.s(silz)

            def yg(v, ct=ct):
                b.w(v, "act")
                v.scalar_tensor_tensor(yred[:, 0:TC], xcc[:, ct * TC:(ct + 1) * TC],
                                       dpar[:, ct:ct + 1], yred[:, 0:TC], ALU.mult, ALU.add)
                ins = v.tensor_mul(ygbuf[:, ct * TC:(ct + 1) * TC], yred[:, 0:TC], zsil[:, 0:TC])
                b.inc(ins, "dve")
            b.v(yg)

        for ct in range(16):
            def mmdt(t, ct=ct):
                b.wv(t, "dve", snap["softpv"][ct])
                if ct == 0:
                    b.w(t, "dve")
                ins = t.matmul(ps_a[:, 0:TC], wdt[0:64, ct * 128:(ct + 1) * 128],
                               dtrbf, start=True, stop=True)
                b.inc(ins, "pe")
            b.t(mmdt)
            snap["mmdt"][ct] = b.cnt["pe"]

            def softp_s(s, ct=ct):
                b.wv(s, "pe", snap["mmdt"][ct])
                b.wv(s, "dve", snap["softpv"][ct])
                ins = s.activation(delta[:, 0:TC], ps_a[:, 0:TC], AF.Abs, bias=bdt[:, ct:ct + 1])
                b.inc(ins, "act")
                s.wait_ge(b.sem["act"], b.cnt["act"])
                ins = s.activation(delta[:, 0:TC], delta[:, 0:TC], AF.Exp, scale=-1.0)
                b.inc(ins, "act")
                s.wait_ge(b.sem["act"], b.cnt["act"])
                ins = s.activation(delta[:, 0:TC], delta[:, 0:TC], AF.Ln, bias=1.0)
                b.inc(ins, "act")
            b.s(softp_s)

            def softp_v(v, ct=ct):
                b.w(v, "act")
                v.tensor_scalar(du[:, 0:TC], ps_a[:, 0:TC], bdt[:, ct:ct + 1], 0.0,
                                ALU.add, ALU.max)
                v.tensor_add(delta[:, 0:TC], delta[:, 0:TC], du[:, 0:TC])
                ins = v.tensor_mul(dubf[:, 0:TC], delta[:, 0:TC], xcc[:, ct * TC:(ct + 1) * TC])
                b.inc(ins, "dve")
                v.wait_ge(b.sem["dve"], b.cnt["dve"])
                dub = dubf[:, 0:TC].rearrange("p (a t) -> p a t", a=1).broadcast_to([128, 16, TC])
                ins = v.tensor_mul(bnpack[:, :].rearrange("p (n t) -> p n t", n=16),
                                   dub,
                                   bcache[:, :].rearrange("p (n t) -> p n t", n=16))
                b.inc(ins, "dve")
            b.v(softp_v)
            snap["softpv"][ct + 1] = b.cnt["dve"]

            for g in range(8):
                slot = g % 2

                def exps(s, ct=ct, g=g, slot=slot):
                    if g >= 2:
                        b.wv(s, "dve", snap["scans_g"][g - 2])
                    else:
                        b.wv(s, "dve", snap["softpv"][ct + 1])
                    ins = None
                    for j in range(2):
                        n = g * 2 + j
                        ins = s.activation(apack[:, (slot * 2 + j) * TC:(slot * 2 + j + 1) * TC],
                                           delta[:, 0:TC],
                                           AF.Exp, scale=acoef[:, ct * 16 + n:ct * 16 + n + 1])
                    b.inc(ins, "act")
                b.s(exps)
                if g == 0:
                    snap["scans_g"] = [0] * 8

                def scans(v, ct=ct, g=g, tc=tc, slot=slot):
                    b.w(v, "act")
                    ins = None
                    for j in range(2):
                        n = g * 2 + j
                        init = 0.0 if tc == 0 else state[:, ct * 16 + n:ct * 16 + n + 1]
                        ins = v.tensor_tensor_scan(bnpack[:, n * TC:(n + 1) * TC],
                                                   apack[:, (slot * 2 + j) * TC:(slot * 2 + j + 1) * TC],
                                                   bnpack[:, n * TC:(n + 1) * TC],
                                                   init, ALU.mult, ALU.add)
                    b.inc(ins, "dve")
                b.v(scans)
                snap["scans_g"][g] = b.cnt["dve"]

            def post(v, ct=ct):
                sv = bnpack[:, :].rearrange("p (n t) -> p n t", n=16)[:, :, TC - 1:TC]
                v.tensor_copy(state[:, ct * 16:(ct + 1) * 16].rearrange("p (n o) -> p n o", o=1), sv)
                ins = v.tensor_mul(bnpack[:, :], bnpack[:, :], ccache[:, :])
                b.inc(ins, "dve")
            b.v(post)
            snap["scans"][ct] = b.cnt["dve"]

            def gred(v, ct=ct):
                ins = v.tensor_reduce(yred[:, 0:TC], bnpack[:, :].rearrange("p (n t) -> p t n", n=16),
                                      AX.X, ALU.add)
                b.inc(ins, "dve")
            b.v(gred)
            snap["gpred"][ct + 1] = b.cnt["dve"]

            if ct > 0:
                tail(ct - 1, lo)
        tail(15, lo)
        snapO = [0, 0]
        for dc in range(8):
            ps = ps_a if dc % 2 == 0 else ps_b

            def mmo(t, dc=dc, ps=ps, w0=snapO[dc % 2]):
                b.w(t, "pool")
                if dc < 2:
                    b.w(t, "dve")
                else:
                    b.wv(t, "dve", w0)
                ins = None
                for ct in range(16):
                    ins = t.matmul(ps[:, 0:TC],
                                   wout[:, ct * 1024 + dc * 128:ct * 1024 + (dc + 1) * 128],
                                   ygbuf[:, ct * TC:(ct + 1) * TC], start=(ct == 0), stop=(ct == 15))
                b.inc(ins, "pe")
            b.t(mmo)
            pes = b.cnt["pe"]

            def evo(v, dc=dc, ps=ps, lo=lo, pes=pes):
                b.wv(v, "pe", pes)
                ins = v.tensor_add(XM[:, dc * T + lo:dc * T + lo + TC],
                                   XM[:, dc * T + lo:dc * T + lo + TC], ps[:, 0:TC])
                b.inc(ins, "dve")
            b.v(evo)
            snapO[dc % 2] = b.cnt["dve"]


def emit_l0_attn(b, X, wd, spill, n_layers=8):
    nc = b.nc
    x_sp = spill["x_park"]
    xin_sp, z_sp = spill["l0a_xin"], spill["l0a_z"]
    XM = X

    es0 = b.scope()
    ps_r = b.pt(es0, "l0a_ps_r", [128, TC], F32)
    ps_m = b.pt(es0, "l0a_ps_m", [128, TC], F32)
    b.l0a_ps_a = b.pt(es0, "l0a_ps_a", [128, 2 * TC], F32)
    b.l0a_ps_b = b.pt(es0, "l0a_ps_b", [128, 2 * TC], F32)
    f32scr = b.sb(es0, "l0a_f32scr", [128, TC], F32)

    esR = b.scope()
    rows = b.sb(esR, "l0a_rows", [1, 2 * T], BF16)
    ln_emit(b, lambda i: X[:, i * T:(i + 1) * T], False, rows, "l0a_in")

    def park(qq):
        b.w(qq, "dve", "act", "pe")
        for i in range(8):
            b.dma(qq, "qsp", x_sp[i * 128:(i + 1) * 128, :], X[:, i * T:(i + 1) * T])
    b.q(park)

    for tc in range(NTC):
        lo = tc * TC
        ln_bcast(b, rows, lo, TC, ps_r, ps_m)

        def apply(v, lo=lo):
            b.w(v, "pe", "qsp")
            ins = None
            for i in range(8):
                v.tensor_mul(f32scr[:, 0:TC], X[:, i * T + lo:i * T + lo + TC], ps_r[:, 0:TC])
                ins = v.tensor_add(XM[:, i * T + lo:i * T + lo + TC], f32scr[:, 0:TC], ps_m[:, 0:TC])
            b.inc(ins, "dve")
        b.v(apply)
    esR.close()
    b.fence()

    for lay in range(n_layers):
        sfx = f"_{lay}"
        esA = b.scope()
        wi = b.sb(esA, "l0a_wi" + sfx, [128, 8 * 2048], BF16)
        xpb = b.sb(esA, "l0a_xpb" + sfx, [128, 16 * 520], BF16)
        xmbf = b.sb(esA, "l0a_xmbf" + sfx, [128, 8 * TC], BF16)
        xst = b.sb(esA, "l0a_xst" + sfx, [128, 2 * TC], BF16)
        zst = b.sb(esA, "l0a_zst" + sfx, [128, 2 * TC], BF16)
        cw = b.sb(esA, "l0a_cw" + sfx, [128, 64], F32)
        cb = b.sb(esA, "l0a_cb" + sfx, [128, 16], F32)
        acc = b.sb(esA, "l0a_acc" + sfx, [128, TC], F32)
        _l0a_phaseA(b, lay, XM, wd, wi, xpb, xmbf, xst, zst, cw, cb, acc, xin_sp, z_sp)
        esA.close()
        b.fence()

        esB = b.scope()
        wout = b.sb(esB, "l0a_wout" + sfx, [128, 16 * 1024], BF16)
        wx = b.sb(esB, "l0a_wx" + sfx, [128, 16 * 96], BF16)
        wdt = b.sb(esB, "l0a_wdt" + sfx, [64, 2048], BF16)
        acoef = b.sb(esB, "l0a_acoef" + sfx, [128, 256], F32)
        b.l0a_alog = acoef
        bdt = b.sb(esB, "l0a_bdt" + sfx, [128, 16], F32)
        dpar = b.sb(esB, "l0a_dpar" + sfx, [128, 16], F32)
        xcc = b.sb(esB, "l0a_xcc" + sfx, [128, 16 * TC], BF16)
        zstg = b.sb(esB, "l0a_zstg" + sfx, [128, 2 * TC], BF16)
        bcache = b.sb(esB, "l0a_bcache" + sfx, [128, 16 * TC], BF16)
        ccache = b.sb(esB, "l0a_ccache" + sfx, [128, 16 * TC], BF16)
        bnpack = b.sb(esB, "l0a_bnpack" + sfx, [128, 16 * TC], BF16)
        apack = b.sb(esB, "l0a_apack" + sfx, [128, 4 * TC], F32)
        delta = b.sb(esB, "l0a_delta" + sfx, [128, TC], F32)
        dubf = b.sb(esB, "l0a_dubf" + sfx, [128, TC], BF16)
        du = apack[:, 3 * TC:4 * TC]
        yred = b.sb(esB, "l0a_yred" + sfx, [128, TC], F32)
        zsil = b.sb(esB, "l0a_zsil" + sfx, [128, TC], BF16)
        ygbuf = xcc
        state = b.sb(esB, "l0a_state" + sfx, [128, 256], BF16)
        xdblb = b.sb(esB, "l0a_xdblb" + sfx, [128, 2 * TC], BF16)
        dtrbf = xdblb[0:64, 0:TC]
        bbrow = xdblb[64:80, 0:TC]
        ccrow = xdblb[64:80, TC:2 * TC]
        bufs = (wout, wx, wdt, acoef, bdt, dpar, xcc, zstg, bcache, ccache, bnpack,
                apack, delta, du, dubf, yred, zsil, ygbuf, state, dtrbf, bbrow, ccrow)
        _l0a_phaseB(b, lay, XM, wd, bufs, xin_sp, z_sp)
        esB.close()
        b.fence()

    esF = b.scope()
    rows = b.sb(esF, "l0a_rows2", [1, 2 * T], BF16)
    ln_emit(b, lambda i: XM[:, i * T:(i + 1) * T], False, rows, "l0a_out")
    xold = b.sb(esF, "l0a_xold", [128, 2 * TC], F32)
    for tc in range(NTC):
        lo = tc * TC
        ln_bcast(b, rows, lo, TC, ps_r, ps_m)
        for i in range(8):
            par = i % 2

            def ldxo(qq, i=i, lo=lo, par=par):
                b.w(qq, "dve", "pe", "pool")
                b.dma(qq, "qsl", xold[:, par * TC:(par + 1) * TC],
                      x_sp[i * 128:(i + 1) * 128, lo:lo + TC])
            b.q(ldxo)

            def addres(v, i=i, lo=lo, par=par):
                b.w(v, "pe", "qsl")
                v.tensor_mul(f32scr[:, 0:TC], XM[:, i * T + lo:i * T + lo + TC], ps_r[:, 0:TC])
                v.tensor_add(f32scr[:, 0:TC], f32scr[:, 0:TC], ps_m[:, 0:TC])
                ins = v.tensor_add(X[:, i * T + lo:i * T + lo + TC],
                                   xold[:, par * TC:(par + 1) * TC], f32scr[:, 0:TC])
                b.inc(ins, "dve")
            b.v(addres)
    esF.close()
    es0.close()
    b.fence()


def l0a_dram(nc):
    wd = {}
    w = nc.dram_tensor("l0a_w_inT", [8, 1024, 4096], F32, kind="ExternalInput")
    wd["l0a_w_inT"] = [w.ap()[i] for i in range(8)]
    w = nc.dram_tensor("l0a_conv_w_p", [8, 128, 64], F32, kind="ExternalInput")
    wd["l0a_conv_w"] = [w.ap()[i] for i in range(8)]
    w = nc.dram_tensor("l0a_conv_b_p", [8, 128, 16], F32, kind="ExternalInput")
    wd["l0a_conv_b"] = [w.ap()[i] for i in range(8)]
    w = nc.dram_tensor("l0a_w_xT", [8, 2048, 96], F32, kind="ExternalInput")
    wd["l0a_w_xT"] = [w.ap()[i] for i in range(8)]
    w = nc.dram_tensor("l0a_w_dtT", [8, 64, 2048], F32, kind="ExternalInput")
    wd["l0a_w_dtT"] = [w.ap()[i] for i in range(8)]
    w = nc.dram_tensor("l0a_A_log_p", [8, 128, 256], F32, kind="ExternalInput")
    wd["l0a_A_log"] = [w.ap()[i] for i in range(8)]
    w = nc.dram_tensor("l0a_b_dt_p", [8, 128, 16], F32, kind="ExternalInput")
    wd["l0a_b_dt"] = [w.ap()[i] for i in range(8)]
    w = nc.dram_tensor("l0a_D_par_p", [8, 128, 16], F32, kind="ExternalInput")
    wd["l0a_D_par"] = [w.ap()[i] for i in range(8)]
    w = nc.dram_tensor("l0a_w_outT", [8, 2048, 1024], F32, kind="ExternalInput")
    wd["l0a_w_outT"] = [w.ap()[i] for i in range(8)]
    spill = {
        "l0a_xin": nc.dram_tensor("sp_l0a_xin", [2048, T], BF16, kind="Internal").ap(),
        "l0a_z": nc.dram_tensor("sp_l0a_z", [2048, T], BF16, kind="Internal").ap(),
    }
    return wd, spill


def prep_l0a(inputs):
    g = {}
    g["l0a_w_inT"] = _cc(inputs["L0_attn_W_in"].transpose(0, 2, 1).astype(np.float32))
    cw = inputs["L0_attn_conv_w"]
    g["l0a_conv_w_p"] = _cc(cw.reshape(8, 16, 128, 4).transpose(0, 2, 1, 3).reshape(8, 128, 64))
    g["l0a_conv_b_p"] = _cc(inputs["L0_attn_conv_b"].reshape(8, 16, 128).transpose(0, 2, 1))
    g["l0a_w_xT"] = _cc(inputs["L0_attn_W_x"].transpose(0, 2, 1).astype(np.float32))
    g["l0a_w_dtT"] = _cc(inputs["L0_attn_W_dt"].transpose(0, 2, 1).astype(np.float32))
    al = inputs["L0_attn_A_log"]
    g["l0a_A_log_p"] = _cc(al.reshape(8, 16, 128, 16).transpose(0, 2, 1, 3).reshape(8, 128, 256))
    g["l0a_b_dt_p"] = _cc(inputs["L0_attn_b_dt"].reshape(8, 16, 128).transpose(0, 2, 1))
    g["l0a_D_par_p"] = _cc(inputs["L0_attn_D_par"].reshape(8, 16, 128).transpose(0, 2, 1))
    g["l0a_w_outT"] = _cc(inputs["L0_attn_W_out"].transpose(0, 2, 1).astype(np.float32))
    return g



def _l0f_attn(b, lay, XF, wd, es0_t, sfx):
    nc = b.nc
    (rows, ps_r, ps_m, ps_a, ps_b, f32scr) = es0_t
    es = b.scope()
    qkvb = b.sb(es, "l0f_qkvb" + sfx, [128, 12 * T], BF16)
    obuf = b.sb(es, "l0f_obuf" + sfx, [128, 4 * T], BF16)
    esQ = b.scope()
    wqkv = b.sb(esQ, "l0f_wqkv" + sfx, [128, 8 * 1536], BF16)
    hbuf = b.sb(esQ, "l0f_hbuf" + sfx, [128, 8 * TC], BF16)
    ps_o = b.pt(es, "l0f_ps_o" + sfx, [128, TC], F32)
    ps_tr = b.pt(es, "l0f_ps_tr" + sfx, [128, 256], BF16)
    ps_bc = b.pt(es, "l0f_ps_bc" + sfx, [128, TC], F32)
    ps_sq = b.pt(es, "l0f_ps_sq" + sfx, [1, TC], F32)

    emit_wload(b, [(wqkv[:, i * 1536:(i + 1) * 1536], wd["l0f_wqkv"][lay][i * 128:(i + 1) * 128, :])
                   for i in range(8)])
    for tc in range(NTC):
        lo = tc * TC

        def mkh(v, lo=lo, tc=tc):
            b.w(v, "pe")
            ins = None
            for i in range(8):
                if i < 4:
                    ins = v.tensor_copy(hbuf[:, i * TC:(i + 1) * TC], XF[:, i * T + lo:i * T + lo + TC])
                else:
                    if tc == 0:
                        v.memset(hbuf[:, i * TC:i * TC + 1], 0.0)
                        ins = v.tensor_copy(hbuf[:, i * TC + 1:(i + 1) * TC],
                                            XF[:, i * T:i * T + TC - 1])
                    else:
                        ins = v.tensor_copy(hbuf[:, i * TC:(i + 1) * TC],
                                            XF[:, i * T + lo - 1:i * T + lo + TC - 1])
            b.inc(ins, "dve")
        b.v(mkh)

        snapQ = [b.cnt["dve"], b.cnt["dve"]]
        for ec in range(12):
            ps = ps_a if ec % 2 == 0 else ps_b

            def mm(t, ec=ec, ps=ps, w0=snapQ[ec % 2]):
                b.w(t, "pool", "qw")
                b.wv(t, "dve", w0)
                ins = None
                for dc in range(8):
                    ins = t.matmul(ps[:, 0:TC], wqkv[:, dc * 1536 + ec * 128:dc * 1536 + (ec + 1) * 128],
                                   hbuf[:, dc * TC:(dc + 1) * TC], start=(dc == 0), stop=(dc == 7))
                b.inc(ins, "pe")
            b.t(mm)
            pes = b.cnt["pe"]

            def ev(v, ec=ec, ps=ps, lo=lo, pes=pes):
                b.wv(v, "pe", pes)
                ins = v.tensor_copy(qkvb[:, ec * T + lo:ec * T + lo + TC], ps[:, 0:TC])
                b.inc(ins, "dve")
            b.v(ev)
            snapQ[ec % 2] = b.cnt["dve"]

    esQ.close()
    b.fence()
    esS = b.scope()
    sqscr = b.sb(esS, "l0f_sqscr" + sfx, [128, T], BF16)
    hrow = b.sb(esS, "l0f_hrow" + sfx, [1, TC], F32)
    hrowb = b.sb(esS, "l0f_hrowb" + sfx, [1, T], BF16)
    qsc = b.sb(esS, "l0f_qsc" + sfx, [128, 2], F32)
    rot64 = b.sb(esS, "l0f_rot64" + sfx, [128, 64], BF16)
    tq = b.sb(esS, "l0f_tq" + sfx, [128, 4 * T], BF16)
    maskt = b.sb(esS, "l0f_mask" + sfx, [128, 4 * 1024], BF16)
    vtbuf = b.sb(esS, "l0f_vt" + sfx, [128, 16 * 64], BF16)
    attnT = b.sb(esS, "l0f_attnT" + sfx, [128, 8 * TC], BF16)
    simb = b.sb(esS, "l0f_simb" + sfx, [128, 1024], F32)
    attnb = b.sb(esS, "l0f_attnb" + sfx, [128, 1024], BF16)
    ncol = b.sb(esS, "l0f_ncol" + sfx, [128, 4], F32)
    qnscr = b.sb(esS, "l0f_qn" + sfx, [128, TC], BF16)
    t1scr = b.sb(esS, "l0f_t1" + sfx, [128, TC], F32)
    t2scr = b.sb(esS, "l0f_t2" + sfx, [128, TC], F32)
    emit_wload(b, [], direct=[(qsc[:, 0:1], wd["l0f_qs"][lay]), (qsc[:, 1:2], wd["l0f_ks"][lay]),
                              (rot64[:, :], wd["l0f_rot64"]), (tq[:, :], wd["l0f_tq"]),
                              (maskt[:, :], wd["l0f_mask"])])

    for h in range(16):
        tile = (h // 2) if h < 8 else (4 + (h - 8) // 2)
        hb = h % 2
        base = hb * 64
        isq = h < 8
        ctbl = 0 if isq else 2
        stbl = 1 if isq else 3
        scol = 0 if isq else 1

        def sq_s(s, tile=tile):
            b.w(s, "dve", "pe")
            ins = s.activation(sqscr[:, 0:T], qkvb[:, tile * T:(tile + 1) * T], AF.Square)
            b.inc(ins, "act")
        if hb == 0:
            b.s(sq_s)

        for tc in range(NTC):
            def mmsq(t, tc=tc, base=base, tile=tile):
                b.w(t, "act", "dve")
                ins = t.matmul(ps_sq[0:1, 0:TC], b.consts["ones_col_bf"][base:base + 64, 0:1],
                               sqscr[base:base + 64, tc * TC:(tc + 1) * TC], start=True, stop=True)
                b.inc(ins, "pe")
            b.t(mmsq)

            def rstd_v(v, tc=tc):
                b.w(v, "pe")
                ins = v.tensor_scalar(hrow[0:1, 0:TC], ps_sq[0:1, 0:TC], 1e-24, None, ALU.max)
                b.inc(ins, "dve")
            b.v(rstd_v)

            def rstd_s(s, tc=tc):
                b.w(s, "dve")
                ins = s.activation(hrow[0:1, 0:TC], hrow[0:1, 0:TC], AF.Ln)
                b.inc(ins, "act")
                s.wait_ge(b.sem["act"], b.cnt["act"])
                ins = s.activation(hrowb[0:1, tc * TC:(tc + 1) * TC], hrow[0:1, 0:TC], AF.Exp, scale=-0.5)
                b.inc(ins, "act")
            b.s(rstd_s)

        for tc in range(NTC):
            lo = tc * TC

            def bc(t, lo=lo, base=base):
                b.w(t, "act", "dve")
                ins = t.matmul(ps_bc[base:base + 64, 0:TC], b.consts["ones_row_bf"][0:1, 0:64],
                               hrowb[0:1, lo:lo + TC], start=True, stop=True)
                b.inc(ins, "pe")
            b.t(bc)

            def app1(v, lo=lo, base=base, tile=tile, scol=scol):
                b.w(v, "pe")
                v.tensor_mul(qnscr[base:base + 64, 0:TC],
                             qkvb[base:base + 64, tile * T + lo:tile * T + lo + TC],
                             ps_bc[base:base + 64, 0:TC])
                ins = v.tensor_scalar(qnscr[base:base + 64, 0:TC], qnscr[base:base + 64, 0:TC],
                                      qsc[base:base + 64, scol:scol + 1], None, ALU.mult)
                b.inc(ins, "dve")
            b.v(app1)

            def rotmm(t, base=base):
                b.w(t, "dve", "qw")
                ins = t.matmul(ps_o[base:base + 64, 0:TC], rot64[base:base + 64, 0:64],
                               qnscr[base:base + 64, 0:TC], start=True, stop=True)
                b.inc(ins, "pe")
            b.t(rotmm)

            def app2(v, lo=lo, base=base, tile=tile, ctbl=ctbl, stbl=stbl):
                b.w(v, "pe")
                v.tensor_mul(t1scr[base:base + 64, 0:TC], qnscr[base:base + 64, 0:TC],
                             tq[base:base + 64, ctbl * T + lo:ctbl * T + lo + TC])
                v.tensor_mul(t2scr[base:base + 64, 0:TC], ps_o[base:base + 64, 0:TC],
                             tq[base:base + 64, stbl * T + lo:stbl * T + lo + TC])
                ins = v.tensor_add(qkvb[base:base + 64, tile * T + lo:tile * T + lo + TC],
                                   t1scr[base:base + 64, 0:TC], t2scr[base:base + 64, 0:TC])
                b.inc(ins, "dve")
            b.v(app2)

    for h in range(8):
        tile = 8 + h // 2
        hb = h % 2
        base = hb * 64
        qtile = h // 2
        ktile = 4 + h // 2

        for tt in range(16):
            pspar = tt % 2

            def vtr(t, tt=tt, tile=tile, base=base, pspar=pspar):
                b.w(t, "dve")
                ins = t.transpose(ps_tr[:, pspar * 64:(pspar + 1) * 64],
                                  qkvb[base:base + 64, tile * T + tt * 128:tile * T + (tt + 1) * 128],
                                  b.consts["ident_bf"][base:base + 64, base:base + 64])
                b.inc(ins, "pe")
            b.t(vtr)

            def vev(v, tt=tt, pspar=pspar):
                b.w(v, "pe")
                ins = v.tensor_copy(vtbuf[:, tt * 64:(tt + 1) * 64], ps_tr[:, pspar * 64:(pspar + 1) * 64])
                b.inc(ins, "dve")
            b.v(vev)

        for wi in range(4):
            for ic in range(4):
                i0 = wi * TC + ic * 128

                def mmsim(t, wi=wi, ic=ic, i0=i0, base=base, qtile=qtile, ktile=ktile):
                    b.w(t, "dve")
                    if wi > 0:
                        t.matmul(ps_a[:, 0:TC], qkvb[base:base + 64, qtile * T + i0:qtile * T + i0 + 128],
                                 qkvb[base:base + 64, ktile * T + (wi - 1) * TC:ktile * T + wi * TC],
                                 start=True, stop=True)
                    ins = t.matmul(ps_b[:, 0:TC], qkvb[base:base + 64, qtile * T + i0:qtile * T + i0 + 128],
                                   qkvb[base:base + 64, ktile * T + wi * TC:ktile * T + (wi + 1) * TC],
                                   start=True, stop=True)
                    b.inc(ins, "pe")
                b.t(mmsim)

                def smax(v, wi=wi, ic=ic):
                    b.w(v, "pe", "qw")
                    if wi > 0:
                        v.tensor_add(simb[:, 0:TC], ps_a[:, 0:TC], maskt[:, ic * 1024:ic * 1024 + TC])
                    else:
                        v.memset(simb[:, 0:TC], -3.0e38)
                    v.tensor_add(simb[:, TC:1024], ps_b[:, 0:TC], maskt[:, ic * 1024 + TC:(ic + 1) * 1024])
                    ins = v.reduce_max(ncol[:, 0:1], simb[:, 0:1024], AX.X, negate=True)
                    b.inc(ins, "dve")
                    v.wait_ge(b.sem["dve"], b.cnt["dve"])
                b.v(smax)

                def sexp(s):
                    b.w(s, "dve")
                    ins = s.activation(attnb[:, 0:1024], simb[:, 0:1024], AF.Exp,
                                       bias=ncol[:, 0:1], accum_out=ncol[:, 1:2])
                    b.inc(ins, "act")
                b.s(sexp)

                def snorm(v):
                    b.w(v, "act")
                    ins = v.reciprocal(ncol[:, 2:3], ncol[:, 1:2])
                    b.inc(ins, "dve")
                    v.wait_ge(b.sem["dve"], b.cnt["dve"])
                    ins = v.tensor_scalar(attnb[:, 0:1024], attnb[:, 0:1024], ncol[:, 2:3],
                                          None, ALU.mult)
                    b.inc(ins, "dve")
                b.v(snorm)

                for js in range(8):
                    pspar = js % 2

                    def atr(t, js=js, pspar=pspar):
                        b.w(t, "dve")
                        ins = t.transpose(ps_tr[:, pspar * 128:(pspar + 1) * 128],
                                          attnb[:, js * 128:(js + 1) * 128],
                                          b.consts["ident_bf"][:, :])
                        b.inc(ins, "pe")
                    b.t(atr)

                    def aev(v, js=js, ic=ic, pspar=pspar):
                        b.w(v, "pe")
                        ins = v.tensor_copy(attnT[:, js * TC + ic * 128:js * TC + (ic + 1) * 128],
                                            ps_tr[:, pspar * 128:(pspar + 1) * 128])
                        b.inc(ins, "dve")
                    b.v(aev)

            def mmav(t, wi=wi, base=base):
                b.w(t, "dve")
                ins = None
                j0 = 4 if wi == 0 else 0
                for jt in range(j0, 8):
                    jtg = (wi - 1) * 4 + jt
                    ins = t.matmul(ps_o[base:base + 64, 0:TC], vtbuf[:, jtg * 64:(jtg + 1) * 64],
                                   attnT[:, jt * TC:(jt + 1) * TC],
                                   start=(jt == j0), stop=(jt == 7))
                b.inc(ins, "pe")
            b.t(mmav)

            def oev(v, wi=wi, base=base, qtile=qtile):
                b.w(v, "pe")
                ins = v.tensor_copy(obuf[base:base + 64, qtile * T + wi * TC:qtile * T + (wi + 1) * TC],
                                    ps_o[base:base + 64, 0:TC])
                b.inc(ins, "dve")
            b.v(oev)

    esS.close()
    b.fence()
    esW = b.scope()
    wob = b.sb(esW, "l0f_wob" + sfx, [128, 4 * 1024], BF16)
    emit_wload(b, [(wob[:, i * 1024:(i + 1) * 1024], wd["l0f_wo"][lay][i * 128:(i + 1) * 128, :])
                   for i in range(4)])

    for tc in range(NTC):
        lo = tc * TC
        for dc in range(8):
            ps = ps_a if dc % 2 == 0 else ps_b

            def mmwo(t, dc=dc, ps=ps, lo=lo):
                b.w(t, "dve", "pool")
                ins = None
                for ct in range(4):
                    ins = t.matmul(ps[:, 0:TC], wob[:, ct * 1024 + dc * 128:ct * 1024 + (dc + 1) * 128],
                                   obuf[:, ct * T + lo:ct * T + lo + TC], start=(ct == 0), stop=(ct == 3))
                b.inc(ins, "pe")
            b.t(mmwo)

            def evwo(v, dc=dc, ps=ps, lo=lo):
                b.w(v, "pe")
                ins = v.tensor_add(XF[:, dc * T + lo:dc * T + lo + TC],
                                   XF[:, dc * T + lo:dc * T + lo + TC], ps[:, 0:TC])
                b.inc(ins, "dve")
            b.v(evwo)
    esW.close()
    es.close()
    b.fence()


def _l0f_ffn(b, lay, XF, wd, es0_t, sfx):
    nc = b.nc
    (rows, ps_r, ps_m, ps_a, ps_b, f32scr) = es0_t
    es = b.scope()
    h2b = b.sb(es, "l0f_h2b" + sfx, [128, 8 * T], BF16)
    xln2 = b.sb(es, "l0f_xln2" + sfx, [128, 8 * T], BF16)
    gcols = b.sb(es, "l0f_gcols" + sfx, [128, 16], F32)
    b1a = b.sb(es, "l0f_b1a" + sfx, [128, 22], F32)
    b1g = b.sb(es, "l0f_b1g" + sfx, [128, 22], F32)
    b2c = b.sb(es, "l0f_b2c" + sfx, [128, 8], F32)

    emit_wload(b, [], direct=[(gcols[:, 0:8], wd["l0f_lng"][lay]),
                              (gcols[:, 8:16], wd["l0f_lnb"][lay]),
                              (b1a[:, :], wd["l0f_b1a"][lay]),
                              (b1g[:, :], wd["l0f_b1g"][lay]),
                              (b2c[:, :], wd["l0f_b2"][lay])])

    for tc in range(NTC):
        lo = tc * TC

        def mkh(v, lo=lo, tc=tc):
            b.w(v, "pe", "act")
            ins = None
            for i in range(8):
                if i < 4:
                    ins = v.tensor_copy(h2b[:, i * T + lo:i * T + lo + TC],
                                        XF[:, i * T + lo:i * T + lo + TC])
                else:
                    if tc == 0:
                        v.memset(h2b[:, i * T:i * T + 1], 0.0)
                        ins = v.tensor_copy(h2b[:, i * T + 1:i * T + TC],
                                            XF[:, i * T:i * T + TC - 1])
                    else:
                        ins = v.tensor_copy(h2b[:, i * T + lo:i * T + lo + TC],
                                            XF[:, i * T + lo - 1:i * T + lo + TC - 1])
            b.inc(ins, "dve")
        b.v(mkh)

    ln_emit(b, lambda i: h2b[:, i * T:(i + 1) * T], True, rows, "l0f_ln" + sfx)

    for tc in range(NTC):
        lo = tc * TC
        ln_bcast(b, rows, lo, TC, ps_r, ps_m)

        def appg(v, lo=lo):
            b.w(v, "pe", "qw")
            ins = None
            for i in range(8):
                v.tensor_mul(f32scr[:, 0:TC], h2b[:, i * T + lo:i * T + lo + TC], ps_r[:, 0:TC])
                v.tensor_add(f32scr[:, 0:TC], f32scr[:, 0:TC], ps_m[:, 0:TC])
                ins = v.tensor_scalar(xln2[:, i * T + lo:i * T + lo + TC], f32scr[:, 0:TC],
                                      gcols[:, i:i + 1], gcols[:, 8 + i:9 + i], ALU.mult, ALU.add)
            b.inc(ins, "dve")
        b.v(appg)

    esW = b.scope()
    hg = b.sb(esW, "l0f_hg" + sfx, [128, 22 * 1024], BF16)
    w1c = b.sb(esW, "l0f_w1c" + sfx, [128, 8 * 256], BF16)
    w2c = b.sb(esW, "l0f_w2c" + sfx, [128, 22 * 128], BF16)
    gel2 = [b.sb(esW, "l0f_gel0" + sfx, [128, TC], F32),
            b.sb(esW, "l0f_gel1" + sfx, [128, TC], F32)]
    b.l0f_snapH = [0, 0]
    b.l0f_snapO = [0, 0]

    for th in range(2):
        for pair in range(22):
            emit_wload(b, [(w1c[:, dc * 256:(dc + 1) * 256],
                            wd["l0f_w1p"][lay][dc * 128:(dc + 1) * 128, pair * 256:(pair + 1) * 256])
                           for dc in range(8)])
            for tcl in range(2):
                lo = th * 1024 + tcl * TC
                pp = (pair * 2 + tcl) % 2
                psA = ps_a if pp == 0 else ps_r
                psB = ps_b if pp == 0 else ps_m

                def mmh(t, pair=pair, lo=lo, pp=pp, psA=psA, psB=psB):
                    b.w(t, "pool")
                    if pair == 0:
                        b.w(t, "dve")
                    else:
                        b.wv(t, "dve", b.l0f_snapH[pp])
                    for dc in range(8):
                        t.matmul(psA[:, 0:TC], w1c[:, dc * 256:dc * 256 + 128],
                                 xln2[:, dc * T + lo:dc * T + lo + TC], start=(dc == 0), stop=(dc == 7))
                    ins = None
                    for dc in range(8):
                        ins = t.matmul(psB[:, 0:TC], w1c[:, dc * 256 + 128:dc * 256 + 256],
                                       xln2[:, dc * T + lo:dc * T + lo + TC], start=(dc == 0), stop=(dc == 7))
                    b.inc(ins, "pe")
                b.t(mmh)
                pes = b.cnt["pe"]

                def gg_s(s, pair=pair, pp=pp, psB=psB, pes=pes):
                    b.wv(s, "pe", pes)
                    b.wv(s, "dve", b.l0f_snapH[pp])
                    ins = s.activation(gel2[pp][:, 0:TC], psB[:, 0:TC], AF.Gelu, bias=b1g[:, pair:pair + 1])
                    b.inc(ins, "act")
                b.s(gg_s)
                acts = b.cnt["act"]

                def gg_v(v, pair=pair, tcl=tcl, pp=pp, psA=psA, acts=acts, pes=pes):
                    b.wv(v, "act", acts)
                    b.wv(v, "pe", pes)
                    ins = v.scalar_tensor_tensor(hg[:, pair * 1024 + tcl * TC:pair * 1024 + (tcl + 1) * TC],
                                                 psA[:, 0:TC], b1a[:, pair:pair + 1], gel2[pp][:, 0:TC],
                                                 ALU.add, ALU.mult)
                    b.inc(ins, "dve")
                b.v(gg_v)
                b.l0f_snapH[pp] = b.cnt["dve"]

        for dc in range(8):
            emit_wload(b, [(w2c[:, hc * 128:(hc + 1) * 128],
                            wd["l0f_w2p"][lay * 8 + dc][hc * 128:(hc + 1) * 128, :])
                           for hc in range(22)])
            for tcl in range(2):
                lo = th * 1024 + tcl * TC
                pp = (dc * 2 + tcl) % 2
                psA = ps_a if pp == 0 else ps_b

                def mmo(t, dc=dc, lo=lo, tcl=tcl, pp=pp, psA=psA):
                    b.w(t, "pool")
                    if dc == 0:
                        b.w(t, "dve")
                    else:
                        b.wv(t, "dve", b.l0f_snapO[pp])
                    ins = None
                    for hc in range(22):
                        ins = t.matmul(psA[:, 0:TC], w2c[:, hc * 128:(hc + 1) * 128],
                                       hg[:, hc * 1024 + tcl * TC:hc * 1024 + (tcl + 1) * TC],
                                       start=(hc == 0), stop=(hc == 21))
                    b.inc(ins, "pe")
                b.t(mmo)
                pes = b.cnt["pe"]

                def evo(v, dc=dc, lo=lo, pp=pp, psA=psA, pes=pes):
                    b.wv(v, "pe", pes)
                    ins = v.scalar_tensor_tensor(XF[:, dc * T + lo:dc * T + lo + TC],
                                                 psA[:, 0:TC], b2c[:, dc:dc + 1],
                                                 XF[:, dc * T + lo:dc * T + lo + TC],
                                                 ALU.add, ALU.add)
                    b.inc(ins, "dve")
                b.v(evo)
                b.l0f_snapO[pp] = b.cnt["dve"]
    esW.close()
    es.close()
    b.fence()


def emit_l0_ffn(b, X, wd, spill, n_layers=2):
    nc = b.nc
    x_sp = spill["x_park"]
    XF = X
    es0 = b.scope()
    ps_r = b.pt(es0, "l0f_ps_r", [128, TC], F32)
    ps_m = b.pt(es0, "l0f_ps_m", [128, TC], F32)
    ps_a = b.pt(es0, "l0f_ps_a", [128, TC], F32)
    ps_b = b.pt(es0, "l0f_ps_b", [128, TC], F32)
    f32scr = b.sb(es0, "l0f_f32scr", [128, TC], F32)
    rows = b.sb(es0, "l0f_rows", [1, 2 * T], BF16)
    es0_t = (rows, ps_r, ps_m, ps_a, ps_b, f32scr)

    ln_emit(b, lambda i: X[:, i * T:(i + 1) * T], False, rows, "l0f_in")

    def park(qq):
        b.w(qq, "dve", "act", "pe")
        for i in range(8):
            b.dma(qq, "qsp", x_sp[i * 128:(i + 1) * 128, :], X[:, i * T:(i + 1) * T])
    b.q(park)

    for tc in range(NTC):
        lo = tc * TC
        ln_bcast(b, rows, lo, TC, ps_r, ps_m)

        def apply(v, lo=lo):
            b.w(v, "pe", "qsp")
            ins = None
            for i in range(8):
                v.tensor_mul(f32scr[:, 0:TC], X[:, i * T + lo:i * T + lo + TC], ps_r[:, 0:TC])
                ins = v.tensor_add(XF[:, i * T + lo:i * T + lo + TC], f32scr[:, 0:TC], ps_m[:, 0:TC])
            b.inc(ins, "dve")
        b.v(apply)
    b.fence()

    for lay in range(n_layers):
        sfx = f"_{lay}"
        _l0f_attn(b, lay, XF, wd, es0_t, sfx)
        _l0f_ffn(b, lay, XF, wd, es0_t, sfx)

    esF = b.scope()
    ln_emit(b, lambda i: XF[:, i * T:(i + 1) * T], False, rows, "l0f_out")
    xold = b.sb(esF, "l0f_xold", [128, 2 * TC], F32)
    for tc in range(NTC):
        lo = tc * TC
        ln_bcast(b, rows, lo, TC, ps_r, ps_m)
        for i in range(8):
            par = i % 2

            def ldxo(qq, i=i, lo=lo, par=par):
                b.w(qq, "dve", "pe", "pool")
                b.dma(qq, "qsl", xold[:, par * TC:(par + 1) * TC],
                      x_sp[i * 128:(i + 1) * 128, lo:lo + TC])
            b.q(ldxo)

            def addres(v, i=i, lo=lo, par=par):
                b.w(v, "pe", "qsl")
                v.tensor_mul(f32scr[:, 0:TC], XF[:, i * T + lo:i * T + lo + TC], ps_r[:, 0:TC])
                v.tensor_add(f32scr[:, 0:TC], f32scr[:, 0:TC], ps_m[:, 0:TC])
                ins = v.tensor_add(X[:, i * T + lo:i * T + lo + TC],
                                   xold[:, par * TC:(par + 1) * TC], f32scr[:, 0:TC])
                b.inc(ins, "dve")
            b.v(addres)
    esF.close()
    es0.close()
    b.fence()


def l0f_dram(nc):
    wd = {}
    w = nc.dram_tensor("l0f_wqkv_d", [2, 1024, 1536], F32, kind="ExternalInput")
    wd["l0f_wqkv"] = [w.ap()[i] for i in range(2)]
    w = nc.dram_tensor("l0f_wo_d", [2, 512, 1024], F32, kind="ExternalInput")
    wd["l0f_wo"] = [w.ap()[i] for i in range(2)]
    w = nc.dram_tensor("l0f_qs_d", [2, 128, 1], F32, kind="ExternalInput")
    wd["l0f_qs"] = [w.ap()[i] for i in range(2)]
    w = nc.dram_tensor("l0f_ks_d", [2, 128, 1], F32, kind="ExternalInput")
    wd["l0f_ks"] = [w.ap()[i] for i in range(2)]
    wd["l0f_rot64"] = nc.dram_tensor("l0f_rot64_d", [128, 64], BF16, kind="ExternalInput").ap()
    wd["l0f_tq"] = nc.dram_tensor("l0f_tq_d", [128, 4 * T], BF16, kind="ExternalInput").ap()
    wd["l0f_mask"] = nc.dram_tensor("l0f_mask_d", [128, 4 * 1024], BF16, kind="ExternalInput").ap()
    w = nc.dram_tensor("l0f_lng_d", [2, 128, 8], F32, kind="ExternalInput")
    wd["l0f_lng"] = [w.ap()[i] for i in range(2)]
    w = nc.dram_tensor("l0f_lnb_d", [2, 128, 8], F32, kind="ExternalInput")
    wd["l0f_lnb"] = [w.ap()[i] for i in range(2)]
    w = nc.dram_tensor("l0f_w1p_d", [2, 1024, 5632], F32, kind="ExternalInput")
    wd["l0f_w1p"] = [w.ap()[i] for i in range(2)]
    w = nc.dram_tensor("l0f_b1a_d", [2, 128, 22], F32, kind="ExternalInput")
    wd["l0f_b1a"] = [w.ap()[i] for i in range(2)]
    w = nc.dram_tensor("l0f_b1g_d", [2, 128, 22], F32, kind="ExternalInput")
    wd["l0f_b1g"] = [w.ap()[i] for i in range(2)]
    w = nc.dram_tensor("l0f_w2p_d", [2 * 8, 2816, 128], F32, kind="ExternalInput")
    wd["l0f_w2p"] = [w.ap()[i] for i in range(16)]
    w = nc.dram_tensor("l0f_b2_d", [2, 128, 8], F32, kind="ExternalInput")
    wd["l0f_b2"] = [w.ap()[i] for i in range(2)]
    return wd


def prep_l0f(inputs):
    g = {}
    g["l0f_wqkv_d"] = _cc(inputs["L0_ffn_wqkv"].astype(np.float32))
    g["l0f_wo_d"] = _cc(inputs["L0_ffn_wo"].astype(np.float32))
    qs = inputs["L0_ffn_q_scale"]; ks = inputs["L0_ffn_k_scale"]
    g["l0f_qs_d"] = _cc(np.stack([np.tile(qs[l], 2).reshape(128, 1) for l in range(2)]))
    g["l0f_ks_d"] = _cc(np.stack([np.tile(ks[l], 2).reshape(128, 1) for l in range(2)]))
    R = np.zeros((64, 64), np.float32)
    for m in range(32):
        R[m + 32, m] = -1.0
        R[m, m + 32] = 1.0
    g["l0f_rot64_d"] = _cc(np.tile(R, (2, 1)).astype(bf16))
    half = np.arange(0, 64, 2, dtype=np.float32)
    inv_freq = 1.0 / 10000.0 ** (half / 64.0)
    t = np.arange(T, dtype=np.float32)
    freqs = t[:, None] * inv_freq[None, :]
    freqs = np.concatenate([freqs, freqs], -1)
    base_ = (half + 0.4 * 64) / (1.4 * 64)
    power = (t - T // 2) / 256.0
    scale = base_[None, :] ** power[:, None]
    scale = np.concatenate([scale, scale], -1)
    cos, sin = np.cos(freqs), np.sin(freqs)
    Tqc = (8.0 * cos * scale).T; Tqs = (8.0 * sin * scale).T
    Tkc = (cos / scale).T; Tks = (sin / scale).T
    tq = np.concatenate([np.tile(x, (2, 1)) for x in (Tqc, Tqs, Tkc, Tks)], 1)
    g["l0f_tq_d"] = _cc(tq.astype(bf16))
    ii = np.arange(512)[:, None]
    jj = np.arange(1024)[None, :]
    m = (jj < ii) | (jj > ii + 512)
    mask = np.where(m, np.float32(-3.0e38), np.float32(0.0))
    g["l0f_mask_d"] = _cc(mask.reshape(4, 128, 1024).transpose(1, 0, 2).reshape(128, 4096).astype(bf16))
    g["l0f_lng_d"] = _cc(inputs["L0_ffn_ln_g"].reshape(2, 8, 128).transpose(0, 2, 1))
    g["l0f_lnb_d"] = _cc(inputs["L0_ffn_ln_b"].reshape(2, 8, 128).transpose(0, 2, 1))
    w1 = inputs["L0_ffn_w1"]; b1 = inputs["L0_ffn_b1"]
    w1p = np.zeros((2, 1024, 5632), np.float32)
    b1a = np.zeros((2, 128, 22), np.float32); b1g = np.zeros((2, 128, 22), np.float32)
    for l in range(2):
        for p in range(22):
            n = min(128, 2730 - p * 128)
            w1p[l, :, p * 256:p * 256 + n] = w1[l, :, p * 128:p * 128 + n]
            w1p[l, :, p * 256 + 128:p * 256 + 128 + n] = w1[l, :, 2730 + p * 128:2730 + p * 128 + n]
            b1a[l, :n, p] = b1[l, p * 128:p * 128 + n]
            b1g[l, :n, p] = b1[l, 2730 + p * 128:2730 + p * 128 + n]
    g["l0f_w1p_d"] = _cc(w1p)
    g["l0f_b1a_d"] = _cc(b1a)
    g["l0f_b1g_d"] = _cc(b1g)
    w2 = inputs["L0_ffn_w2"]
    w2pad = np.zeros((2, 2816, 1024), np.float32)
    w2pad[:, :2730, :] = w2
    w2p = np.zeros((2 * 8, 2816, 128), np.float32)
    for l in range(2):
        for dc in range(8):
            w2p[l * 8 + dc] = w2pad[l, :, dc * 128:(dc + 1) * 128]
    g["l0f_w2p_d"] = _cc(w2p)
    g["l0f_b2_d"] = _cc(inputs["L0_ffn_b2"].reshape(2, 8, 128).transpose(0, 2, 1))
    return g



CONST_FLOATS = [1e-5, -0.5, 1.0 / 1024.0, 0.1, 1.0, 0.0, -1.0, 0.5, 2.0]


def alloc_consts(b, nc):
    t1 = nc.sbuf_tensor("c_ones_col_bf", [128, 1], BF16).__enter__()
    t2 = nc.sbuf_tensor("c_ones_row_f32", [1, 128], F32).__enter__()
    cp = nc.sbuf_tensor("c_pool", [128, len(CONST_FLOATS)], F32).__enter__()
    idf = nc.sbuf_tensor("c_ident_f32", [128, 128], F32).__enter__()
    t3 = nc.sbuf_tensor("c_ones_row_bf", [1, 128], BF16).__enter__()
    b.consts["ones_row_bf"] = t3
    idb = nc.sbuf_tensor("c_ident_bf", [128, 128], BF16).__enter__()
    b.consts["ident_bf"] = idb
    sel8 = nc.sbuf_tensor("c_sel8", [8, 8 * 128], BF16).__enter__()
    sel16 = nc.sbuf_tensor("c_sel16", [128, 16 * 128], BF16).__enter__()
    b.consts["ident_f32"] = idf
    b.consts["sel8"] = sel8
    b.consts["sel16"] = sel16
    b.consts["ones_col_bf"] = t1
    b.consts["ones_row_f32"] = t2
    b.consts["cp"] = cp
    b.wstg = nc.sbuf_tensor("c_wstg", [128, 2 * 512], F32).__enter__()
    for j, val in enumerate(CONST_FLOATS):
        nc.const_aps.register(F32, float(val), cp[:, j:j + 1])


def init_consts(b, const_dram=None):
    t1, t2, cp = b.consts["ones_col_bf"], b.consts["ones_row_f32"], b.consts["cp"]

    def ini(v):
        v.memset(t1[:, :], 1.0)
        v.memset(t2[0:1, :], 1.0)
        v.memset(b.consts["ones_row_bf"][0:1, :], 1.0)
        ins = None
        for j, val in enumerate(CONST_FLOATS):
            ins = v.memset(cp[:, j:j + 1], float(val))
        b.inc(ins, "dve")
    b.v(ini)
    if const_dram is not None:
        def ldc(qq):
            b.dma(qq, "qx", b.consts["ident_f32"][:, :], const_dram["ident_f32"])
            b.dma(qq, "qx", b.consts["ident_bf"][:, :], const_dram["ident_bf"])
            b.dma(qq, "qx", b.consts["sel8"][0:8, :], const_dram["sel8"])
            b.dma(qq, "qx", b.consts["sel16"][64:80, :], const_dram["sel16"])
        b.q(ldc)


def const_dram_decl(nc):
    return {
        "ident_f32": nc.dram_tensor("c_ident_f32_d", [128, 128], F32, kind="ExternalInput").ap(),
        "ident_bf": nc.dram_tensor("c_ident_bf_d", [128, 128], BF16, kind="ExternalInput").ap(),
        "sel8": nc.dram_tensor("c_sel8_d", [8, 8 * 128], BF16, kind="ExternalInput").ap(),
        "sel16": nc.dram_tensor("c_sel16_d", [16, 16 * 128], BF16, kind="ExternalInput").ap(),
    }


def const_host():
    sel8 = np.zeros((8, 8 * 128), np.float32)
    for e in range(8):
        sel8[e, e * 128:(e + 1) * 128] = 1.0
    sel16 = np.zeros((16, 16 * 128), np.float32)
    for n in range(16):
        sel16[n, n * 128:(n + 1) * 128] = 1.0
    return {"c_ident_f32_d": np.eye(128, dtype=np.float32), "c_ident_bf_d": np.eye(128, dtype=bf16),
            "c_sel8_d": sel8.astype(bf16), "c_sel16_d": sel16.astype(bf16)}


def l1a_dram(nc):
    wd = {
        "l1a_w_in": nc.dram_tensor("l1a_w_in", [1024, 4096], F32, kind="ExternalInput").ap(),
        "l1a_conv_w": nc.dram_tensor("l1a_conv_w", [128, 64], F32, kind="ExternalInput").ap(),
        "l1a_conv_b": nc.dram_tensor("l1a_conv_b", [128, 16], F32, kind="ExternalInput").ap(),
        "l1a_w_B": nc.dram_tensor("l1a_w_B", [128, 256], F32, kind="ExternalInput").ap(),
        "l1a_w_C": nc.dram_tensor("l1a_w_C", [16, 2048], F32, kind="ExternalInput").ap(),
        "l1a_w_out": nc.dram_tensor("l1a_w_out", [128, 16 * 1024], F32, kind="ExternalInput").ap(),
    }
    spill = {
        "l1a_xc": nc.dram_tensor("sp_l1a_xc", [2048, T], BF16, kind="Internal").ap(),
        "l1a_z": nc.dram_tensor("sp_l1a_z", [2048, T], BF16, kind="Internal").ap(),
    }
    return wd, spill


def prep_l1a(inputs):
    g = {}
    g["l1a_w_in"] = _cc(inputs["L1_attn_w_in"].astype(np.float32))
    cw = inputs["L1_attn_conv_w"].reshape(2048, 4)
    g["l1a_conv_w"] = _cc(cw.reshape(16, 128, 4).transpose(1, 0, 2).reshape(128, 64))
    g["l1a_conv_b"] = col_pack(inputs["L1_attn_conv_b"].reshape(2048))
    g["l1a_w_B"] = feat_tiles(inputs["L1_attn_w_B"])
    g["l1a_w_C"] = _cc(inputs["L1_attn_w_C"].astype(np.float32))
    g["l1a_w_out"] = feat_tiles(inputs["L1_attn_w_out"])
    return g


def build_l1_attn_standalone():
    nc = bass.Bass("TRN2", target_bir_lowering=False, debug=False)
    x_in = nc.dram_tensor("x_in", [D, T], F32, kind="ExternalInput").ap()
    x_out = nc.dram_tensor("x_out", [D, T], F32, kind="ExternalOutput").ap()
    wd, spill = l1a_dram(nc)
    b = B(nc)
    alloc_consts(b, nc)
    X = nc.sbuf_tensor("X", [128, 8 * T], F32).__enter__()
    with nc.Block() as block:
        b.block = block
        init_consts(b)

        def ldx(qq):
            for i in range(8):
                b.dma(qq, "qx", X[:, i * T:(i + 1) * T], x_in[i * 128:(i + 1) * 128, :])
        b.q(ldx)

        emit_l1_attn(b, X, wd, spill)

        def stx(qq):
            b.w(qq, "dve")
            for i in range(8):
                b.dma(qq, "qo", x_out[i * 128:(i + 1) * 128, :], X[:, i * T:(i + 1) * T])
            qq.wait_ge(b.sem["qo"], b.cnt["qo"])
        b.q(stx)
    return nc


def build_l0_attn_standalone(n_layers=8):
    nc = bass.Bass("TRN2", target_bir_lowering=False, debug=False)
    x_in = nc.dram_tensor("x_in", [D, T], F32, kind="ExternalInput").ap()
    x_out = nc.dram_tensor("x_out", [D, T], F32, kind="ExternalOutput").ap()
    xm_out = nc.dram_tensor("xm_out", [D, T], F32, kind="ExternalOutput").ap()
    cd = const_dram_decl(nc)
    wd, spill = l0a_dram(nc)
    spill["x_park"] = nc.dram_tensor("sp_x_park", [D, T], F32, kind="Internal").ap()
    b = B(nc)
    alloc_consts(b, nc)
    X = nc.sbuf_tensor("X", [128, 8 * T], F32).__enter__()
    with nc.Block() as block:
        b.block = block
        init_consts(b, cd)

        def ldx(qq):
            for i in range(8):
                b.dma(qq, "qx", X[:, i * T:(i + 1) * T], x_in[i * 128:(i + 1) * 128, :])
        b.q(ldx)

        emit_l0_attn(b, X, wd, spill, n_layers=n_layers)

        def stx(qq):
            b.w(qq, "dve")
            for i in range(8):
                b.dma(qq, "qo", x_out[i * 128:(i + 1) * 128, :], X[:, i * T:(i + 1) * T])
            qq.wait_ge(b.sem["qo"], b.cnt["qo"])
        b.q(stx)
    return nc


def build_l0_ffn_standalone(n_layers=2):
    nc = bass.Bass("TRN2", target_bir_lowering=False, debug=False)
    x_in = nc.dram_tensor("x_in", [D, T], F32, kind="ExternalInput").ap()
    x_out = nc.dram_tensor("x_out", [D, T], F32, kind="ExternalOutput").ap()
    cd = const_dram_decl(nc)
    wd = l0f_dram(nc)
    spill = {"x_park": nc.dram_tensor("sp_x_park", [D, T], F32, kind="Internal").ap()}
    b = B(nc)
    alloc_consts(b, nc)
    X = nc.sbuf_tensor("X", [128, 8 * T], F32).__enter__()
    with nc.Block() as block:
        b.block = block
        init_consts(b, cd)

        def ldx(qq):
            for i in range(8):
                b.dma(qq, "qx", X[:, i * T:(i + 1) * T], x_in[i * 128:(i + 1) * 128, :])
        b.q(ldx)

        emit_l0_ffn(b, X, wd, spill, n_layers=n_layers)

        def stx(qq):
            b.w(qq, "dve")
            for i in range(8):
                b.dma(qq, "qo", x_out[i * 128:(i + 1) * 128, :], X[:, i * T:(i + 1) * T])
            qq.wait_ge(b.sem["qo"], b.cnt["qo"])
        b.q(stx)
    return nc


def build_l1_ffn_standalone():
    nc = bass.Bass("TRN2", target_bir_lowering=False, debug=False)
    x_in = nc.dram_tensor("x_in", [D, T], F32, kind="ExternalInput").ap()
    x_out = nc.dram_tensor("x_out", [D, T], F32, kind="ExternalOutput").ap()
    cd = const_dram_decl(nc)
    wd = l1f_dram(nc)
    spill = {"x_park": nc.dram_tensor("sp_x_park", [D, T], F32, kind="Internal").ap()}
    b = B(nc)
    alloc_consts(b, nc)
    X = nc.sbuf_tensor("X", [128, 8 * T], F32).__enter__()
    with nc.Block() as block:
        b.block = block
        init_consts(b, cd)

        def ldx(qq):
            for i in range(8):
                b.dma(qq, "qx", X[:, i * T:(i + 1) * T], x_in[i * 128:(i + 1) * 128, :])
        b.q(ldx)

        emit_l1_ffn(b, X, wd, spill)

        def stx(qq):
            b.w(qq, "dve")
            for i in range(8):
                b.dma(qq, "qo", x_out[i * 128:(i + 1) * 128, :], X[:, i * T:(i + 1) * T])
            qq.wait_ge(b.sem["qo"], b.cnt["qo"])
        b.q(stx)
    return nc


def build_full():
    nc = bass.Bass("TRN2", target_bir_lowering=False, debug=False)
    x_in = nc.dram_tensor("x_in", [D, T], F32, kind="ExternalInput").ap()
    x_out = nc.dram_tensor("x_out", [D, T], F32, kind="ExternalOutput").ap()
    cd = const_dram_decl(nc)
    wd0a, sp0a = l0a_dram(nc)
    wd0f = l0f_dram(nc)
    wd1a, sp1a = l1a_dram(nc)
    wd1f = l1f_dram(nc)
    spill = dict(sp0a)
    spill.update(sp1a)
    spill["x_park"] = nc.dram_tensor("sp_x_park", [D, T], F32, kind="Internal").ap()
    b = B(nc)
    alloc_consts(b, nc)
    X = nc.sbuf_tensor("X", [128, 8 * T], F32).__enter__()
    with nc.Block() as block:
        b.block = block
        init_consts(b, cd)

        def ldx(qq):
            for i in range(8):
                b.dma(qq, "qx", X[:, i * T:(i + 1) * T], x_in[i * 128:(i + 1) * 128, :])
        b.q(ldx)

        emit_l0_attn(b, X, wd0a, spill)
        emit_l0_ffn(b, X, wd0f, spill)
        emit_l1_attn(b, X, wd1a, spill)
        emit_l1_ffn(b, X, wd1f, spill)

        def stx(qq):
            b.w(qq, "dve", "act", "pe")
            for i in range(8):
                b.dma(qq, "qo", x_out[i * 128:(i + 1) * 128, :], X[:, i * T:(i + 1) * T])
            qq.wait_ge(b.sem["qo"], b.cnt["qo"])
        b.q(stx)
    return nc


_BUILD_CACHE = {}


def _get_nc(name, builder):
    if name not in _BUILD_CACHE:
        _BUILD_CACHE[name] = builder()
    return _BUILD_CACHE[name]


def kernel(**inputs):
    wmap = {}
    wmap.update(const_host())
    wmap.update(prep_l0a(inputs))
    wmap.update(prep_l0f(inputs))
    wmap.update(prep_l1a(inputs))
    wmap.update(prep_l1f(inputs))
    x = np.ascontiguousarray(inputs["L0_attn_x"].astype(np.float32))
    xs = [np.ascontiguousarray(x[c].T) for c in range(NCORES)]
    nc = _get_nc("full", build_full)
    in_maps = [dict(wmap, x_in=xs[c]) for c in range(NCORES)]
    res = run_bass_kernel_spmd(nc, in_maps, core_ids=list(range(NCORES)))
    out = np.stack([res.results[c]["x_out"].T for c in range(NCORES)], axis=0)
    return np.ascontiguousarray(out.astype(np.float32))


# version 08
# speedup vs baseline: 1.1841x; 1.1841x over previous
import contextlib
import numpy as np
import ml_dtypes
import concourse.bass as bass
import concourse.mybir as mybir
from concourse.bass_utils import run_bass_kernel_spmd

F32 = mybir.dt.float32
BF16 = mybir.dt.bfloat16
AF = mybir.ActivationFunctionType
ALU = mybir.AluOpType
AX = mybir.AxisListType

T = 2048
D = 1024
NCORES = 8
TC = 512
NTC = T // TC
bf16 = ml_dtypes.bfloat16

SEMS = ("pe", "act", "dve", "pool", "qx", "qw", "qw2", "qsp", "qsl", "qo")


class B:
    def __init__(self, nc):
        self.nc = nc
        self.cnt = {s: 0 for s in SEMS}
        self.sem = {s: nc.semaphore(s).__enter__() for s in SEMS}
        self.block = None
        self.consts = {}

    def v(self, f):
        self.block.vector(f)

    def s(self, f):
        self.block.scalar(f)

    def t(self, f):
        self.block.tensor(f)

    def g(self, f):
        self.block.gpsimd(f)

    def q(self, f):
        self.block.sync(f)

    def inc(self, instr, name, k=1):
        instr.then_inc(self.sem[name], k)
        self.cnt[name] += k

    def dma(self, qeng, qname, out, in_):
        ins = qeng.dma_start(out=out, in_=in_)
        self.inc(ins, qname, 16)

    def w(self, eng, *names):
        for n in names:
            c = self.cnt[n]
            if c > 0:
                eng.wait_ge(self.sem[n], c)

    def wv(self, eng, name, val):
        if val > 0:
            eng.wait_ge(self.sem[name], val)

    def scope(self):
        return contextlib.ExitStack()

    def fence(self):
        def f(g):
            self.w(g, "pe", "act", "dve")
            ins = g.engine_nop()
            self.inc(ins, "pool")
        self.g(f)

    def sb(self, es, name, shape, dt):
        return es.enter_context(self.nc.sbuf_tensor(name, shape, dt))

    def pt(self, es, name, shape, dt):
        return es.enter_context(self.nc.psum_tensor(name, shape, dt))



def _cc(a):
    return np.ascontiguousarray(a)


def feat_tiles(w, P=128):
    F, M = w.shape
    assert F % P == 0
    return _cc(w.reshape(F // P, P, M).transpose(1, 0, 2).reshape(P, (F // P) * M))


def col_pack(vec, P=128):
    F = vec.shape[0]
    return _cc(vec.reshape(F // P, P).T)



def ln_emit(b, src_of, src_is_bf, rows, tag, ntiles=8, nfeat=None, eps=1e-5):
    nc = b.nc
    es = b.scope()
    scr8a = b.sb(es, tag + "_lncast", [128, ntiles * TC], BF16)
    scr8b = b.sb(es, tag + "_lnsq", [128, ntiles * TC], BF16)
    rowm = b.sb(es, tag + "_lnrowm", [1, T], F32)
    rowv = b.sb(es, tag + "_lnrowv", [1, T], F32)
    ps_sx = b.pt(es, tag + "_lnpsx", [1, TC], F32)
    ps_sq = b.pt(es, tag + "_lnpsq", [1, TC], F32)
    cast = [scr8a[:, i * TC:(i + 1) * TC] for i in range(ntiles)]
    sq = [scr8b[:, i * TC:(i + 1) * TC] for i in range(ntiles)]

    nfeat = nfeat or (128 * ntiles)
    inv = 1.0 / nfeat
    for tc in range(NTC):
        lo, hi = tc * TC, (tc + 1) * TC

        def casts(s, lo=lo, hi=hi):
            b.w(s, "pe", "dve", "qx", "pool")
            ins = None
            for i in range(ntiles):
                if not src_is_bf:
                    s.copy(cast[i][:, 0:TC], src_of(i)[:, lo:hi])
                ins = s.activation(sq[i][:, 0:TC], src_of(i)[:, lo:hi], AF.Square)
            b.inc(ins, "act")
        b.s(casts)

        def mms(t, lo=lo, hi=hi):
            b.w(t, "act")
            for i in range(ntiles):
                rhs = src_of(i)[:, lo:hi] if src_is_bf else cast[i][:, 0:TC]
                t.matmul(ps_sx[0:1, 0:TC], b.consts["ones_col_bf"][:, 0:1], rhs,
                         start=(i == 0), stop=(i == ntiles - 1))
            for i in range(ntiles):
                ins = t.matmul(ps_sq[0:1, 0:TC], b.consts["ones_col_bf"][:, 0:1],
                               sq[i][:, 0:TC], start=(i == 0), stop=(i == ntiles - 1))
            b.inc(ins, "pe")
        b.t(mms)

        def rowc(s, lo=lo, hi=hi):
            b.w(s, "pe")
            s.mul(rowm[0:1, lo:hi], ps_sx[0:1, 0:TC], inv)
            ins = s.mul(rowv[0:1, lo:hi], ps_sq[0:1, 0:TC], inv)
            b.inc(ins, "act")
        b.s(rowc)

    def rowfin_v(v):
        b.w(v, "act")
        v.tensor_mul(rows[0:1, T:2 * T], rowm[0:1, :], rowm[0:1, :])
        ins = v.tensor_sub(rowv[0:1, :], rowv[0:1, :], rows[0:1, T:2 * T])
        b.inc(ins, "dve")
    b.v(rowfin_v)

    def rowfin_s(s):
        b.w(s, "dve")
        s.activation(rows[0:1, T:2 * T], rowv[0:1, :], AF.Ln, bias=eps)
        ins = s.activation(rows[0:1, 0:T], rows[0:1, T:2 * T], AF.Exp, scale=-0.5)
        b.inc(ins, "act")
    b.s(rowfin_s)

    def nm2_v(v):
        b.w(v, "act")
        ins = v.scalar_tensor_tensor(rows[0:1, T:2 * T], rowm[0:1, :], -1.0, rows[0:1, 0:T],
                                     ALU.mult, ALU.mult)
        b.inc(ins, "dve")
    b.v(nm2_v)
    es.close()


def ln_bcast(b, rows, lo, n, ps_r, ps_m):
    def mm(t):
        b.w(t, "dve", "act")
        t.matmul(ps_r[:, 0:n], b.consts["ones_row_bf"][0:1, :], rows[0:1, lo:lo + n],
                 start=True, stop=True)
        ins = t.matmul(ps_m[:, 0:n], b.consts["ones_row_bf"][0:1, :], rows[0:1, T + lo:T + lo + n],
                       start=True, stop=True)
        b.inc(ins, "pe")
    b.t(mm)


def emit_wload(b, jobs, direct=(), war=None):
    stg = b.wstg
    if not hasattr(b, "_stgfree"):
        b._stgfree = [0, 0]
    jobs2 = []
    for dst, src in jobs:
        n = src.shape[1]
        o = 0
        while o < n:
            m = min(512, n - o)
            jobs2.append((dst[:, o:o + m], src[:, o:o + m]))
            o += m
    for idx, (dst, src) in enumerate(jobs2):
        par = idx % 2
        qsem = "qw" if par == 0 else "qw2"

        def ldq(qq, par=par, src=src, qsem=qsem):
            b.wv(qq, "pool", b._stgfree[par])
            ins = qq.dma_start(out=stg[0:src.shape[0], par * 512:par * 512 + src.shape[1]], in_=src)
            b.inc(ins, qsem, 16)
        b.q(ldq)

        def cst(g, par=par, dst=dst, p=src.shape[0], n=src.shape[1], qsem=qsem):
            b.w(g, qsem)
            if war is not None:
                b.wv(g, "pe", war)
            else:
                b.w(g, "pe", "act", "dve")
            ins = g.tensor_copy(dst, stg[0:p, par * 512:par * 512 + n])
            b.inc(ins, "pool")
        b.g(cst)
        b._stgfree[par] = b.cnt["pool"]

    if direct:
        def ldd(qq):
            b.w(qq, "pool")
            for dst, src in direct:
                b.dma(qq, "qw", dst, src)
        b.q(ldd)

    def qwbar(g):
        b.w(g, "qw", "qw2")
        ins = g.engine_nop()
        b.inc(ins, "pool")
    b.g(qwbar)



def _l1a_xln_apply(b, tc, X, rows, ps_r, ps_m, xln, f32scr):
    lo, hi = tc * TC, (tc + 1) * TC
    ln_bcast(b, rows, lo, TC, ps_r, ps_m)

    def apply(v):
        b.w(v, "pe")
        ins = None
        for i in range(8):
            v.tensor_mul(f32scr[:, 0:TC], X[:, i * T + lo:i * T + hi], ps_r[:, 0:TC])
            ins = v.tensor_add(xln[:, i * TC:(i + 1) * TC], f32scr[:, 0:TC], ps_m[:, 0:TC])
        b.inc(ins, "dve")
    b.v(apply)


def _l1a_phaseA_xp_chunk(b, tc, X, rows, ps_r, ps_m, ps_a, ps_b, wi, xpb, xln,
                         xst, cw, cb, acc, f32scr, xc_sp):
    lo = tc * TC
    _l1a_xln_apply(b, tc, X, rows, ps_r, ps_m, xln, f32scr)

    def halo(v):
        ins = None
        for ct in range(16):
            if tc == 0:
                ins = v.memset(xpb[:, ct * 520:ct * 520 + 3], 0.0)
            else:
                ins = v.tensor_copy(xpb[:, ct * 520:ct * 520 + 3],
                                    xpb[:, ct * 520 + 512:ct * 520 + 515])
        b.inc(ins, "dve")
    b.v(halo)

    for ec in range(16):
        ps = ps_a if ec % 2 == 0 else ps_b

        def mm(t, ec=ec, ps=ps):
            b.w(t, "pool", "dve")
            ins = None
            for dc in range(8):
                ins = t.matmul(ps[:, 0:TC],
                               wi[:, dc * 2048 + ec * 128:dc * 2048 + (ec + 1) * 128],
                               xln[:, dc * TC:(dc + 1) * TC],
                               start=(dc == 0), stop=(dc == 7))
            b.inc(ins, "pe")
        b.t(mm)

        def ev(v, ec=ec, ps=ps):
            b.w(v, "pe")
            ins = v.tensor_copy(xpb[:, ec * 520 + 3:ec * 520 + 515], ps[:, 0:TC])
            b.inc(ins, "dve")
        b.v(ev)

    for ct in range(16):
        def conv(v, ct=ct):
            b.w(v, "act", "qw")
            base = ct * 520
            v.tensor_scalar(acc[:, 0:TC], xpb[:, base + 3:base + 3 + TC],
                            cw[:, ct * 4 + 3:ct * 4 + 4], cb[:, ct:ct + 1],
                            ALU.mult, ALU.add)
            ins = None
            for k in (0, 1, 2):
                ins = v.scalar_tensor_tensor(acc[:, 0:TC], xpb[:, base + k:base + k + TC],
                                             cw[:, ct * 4 + k:ct * 4 + k + 1], acc[:, 0:TC],
                                             ALU.mult, ALU.add)
            b.inc(ins, "dve")
        b.v(conv)

        def sil(s, ct=ct):
            b.w(s, "dve", "qsp")
            ins = s.activation(xst[:, (ct % 2) * TC:(ct % 2 + 1) * TC], acc[:, 0:TC], AF.Silu)
            b.inc(ins, "act")
        b.s(sil)

        def spxc(qq, ct=ct):
            b.w(qq, "act")
            b.dma(qq, "qsp", xc_sp[ct * 128:(ct + 1) * 128, lo:lo + TC],
                  xst[:, (ct % 2) * TC:(ct % 2 + 1) * TC])
        b.q(spxc)


def _l1a_phaseA_z_chunk(b, tc, X, rows, ps_r, ps_m, ps_a, ps_b, wi, xln,
                        zst, f32scr, z_sp):
    lo = tc * TC
    _l1a_xln_apply(b, tc, X, rows, ps_r, ps_m, xln, f32scr)

    for ec in range(16):
        ps = ps_a if ec % 2 == 0 else ps_b
        par = ec % 2

        def mm(t, ec=ec, ps=ps):
            b.w(t, "pool", "dve")
            ins = None
            for dc in range(8):
                ins = t.matmul(ps[:, 0:TC],
                               wi[:, dc * 2048 + ec * 128:dc * 2048 + (ec + 1) * 128],
                               xln[:, dc * TC:(dc + 1) * TC],
                               start=(dc == 0), stop=(dc == 7))
            b.inc(ins, "pe")
        b.t(mm)

        def ev(v, ec=ec, ps=ps, par=par):
            b.w(v, "pe", "qsp")
            ins = v.tensor_copy(zst[:, par * TC:(par + 1) * TC], ps[:, 0:TC])
            b.inc(ins, "dve")
        b.v(ev)

        def spz(qq, ec=ec, par=par):
            b.w(qq, "dve")
            b.dma(qq, "qsp", z_sp[ec * 128:(ec + 1) * 128, lo:lo + TC],
                  zst[:, par * TC:(par + 1) * TC])
        b.q(spz)


def _l1a_phaseB_chunk(b, tc, ps_a, ps_b, ob, wo, wB, wC, xcc, zst, yg, nine, hsc,
                      hbf, xbs, zsil, xc_sp, z_sp):
    lo = tc * TC
    cur, prev = tc % 2, (tc + 1) % 2

    def ld(qq):
        b.w(qq, "pe", "act", "dve", "qsp")
        for ct in range(16):
            b.dma(qq, "qsl", xcc[:, ct * TC:(ct + 1) * TC],
                  xc_sp[ct * 128:(ct + 1) * 128, lo:lo + TC])
    b.q(ld)

    def mmxb(t):
        b.w(t, "qsl", "pool", "act")
        ins = None
        for ct in range(16):
            ins = t.matmul(ps_a[0:16, 0:TC], wB[:, ct * 16:(ct + 1) * 16],
                           xcc[:, ct * TC:(ct + 1) * TC], start=(ct == 0), stop=(ct == 15))
        b.inc(ins, "pe")
    b.t(mmxb)

    def evxb(s):
        b.w(s, "pe")
        ins = s.mul(xbs[0:16, 0:TC], ps_a[0:16, 0:TC], 0.1)
        b.inc(ins, "act")
    b.s(evxb)

    def scan(v):
        b.w(v, "act")
        init = 0.0 if tc == 0 else hsc[0:16, prev * TC + TC - 1:prev * TC + TC]
        v.tensor_tensor_scan(hsc[0:16, cur * TC:(cur + 1) * TC], nine[0:16, 0:TC],
                             xbs[0:16, 0:TC], init, ALU.mult, ALU.add)
        ins = v.tensor_copy(hbf[0:16, 0:TC], hsc[0:16, cur * TC:(cur + 1) * TC])
        b.inc(ins, "dve")
    b.v(scan)

    for ec in range(16):
        ps = ps_a if ec % 2 == 0 else ps_b
        par = ec % 2

        def ldz(qq, ec=ec, par=par):
            b.w(qq, "act", "pe")
            b.dma(qq, "qsl", zst[:, par * TC:(par + 1) * TC],
                  z_sp[ec * 128:(ec + 1) * 128, lo:lo + TC])
        b.q(ldz)

        def mmy(t, ec=ec, ps=ps):
            b.w(t, "dve")
            ins = t.matmul(ps[:, 0:TC], wC[0:16, ec * 128:(ec + 1) * 128],
                           hbf[0:16, 0:TC], start=True, stop=True)
            b.inc(ins, "pe")
        b.t(mmy)

        def silz(s, ec=ec, par=par):
            b.w(s, "qsl", "dve")
            ins = s.activation(zsil[:, 0:TC], zst[:, par * TC:(par + 1) * TC], AF.Silu)
            b.inc(ins, "act")
        b.s(silz)

        def gate(v, ec=ec, ps=ps):
            b.w(v, "pe", "act")
            ins = v.tensor_mul(yg[:, ec * TC:(ec + 1) * TC], ps[:, 0:TC], zsil[:, 0:TC])
            b.inc(ins, "dve")
        b.v(gate)

    for dc in range(8):
        ps = ps_a if dc % 2 == 0 else ps_b

        def mmo(t, dc=dc, ps=ps):
            b.w(t, "dve")
            ins = None
            for ct in range(16):
                ins = t.matmul(ps[:, 0:TC],
                               wo[:, ct * 1024 + dc * 128:ct * 1024 + (dc + 1) * 128],
                               yg[:, ct * TC:(ct + 1) * TC], start=(ct == 0), stop=(ct == 15))
            b.inc(ins, "pe")
        b.t(mmo)

        def evo(v, dc=dc, ps=ps):
            b.w(v, "pe")
            ins = v.tensor_copy(ob[:, dc * T + lo:dc * T + lo + TC], ps[:, 0:TC])
            b.inc(ins, "dve")
        b.v(evo)


def emit_residual_add(b, X, ob, rows, ps_r, ps_m, f32scr):
    for tc in range(NTC):
        lo = tc * TC
        ln_bcast(b, rows, lo, TC, ps_r, ps_m)

        def addres(v, lo=lo):
            b.w(v, "pe")
            ins = None
            for i in range(8):
                v.tensor_mul(f32scr[:, 0:TC], ob[:, i * T + lo:i * T + lo + TC], ps_r[:, 0:TC])
                v.tensor_add(f32scr[:, 0:TC], f32scr[:, 0:TC], ps_m[:, 0:TC])
                ins = v.tensor_add(X[:, i * T + lo:i * T + lo + TC],
                                   X[:, i * T + lo:i * T + lo + TC], f32scr[:, 0:TC])
            b.inc(ins, "dve")
        b.v(addres)


def emit_l1_attn(b, X, wd, spill, dbg=None, upto='full'):
    nc = b.nc
    xc_sp, z_sp = spill["l1a_xc"], spill["l1a_z"]
    es0 = b.scope()
    rows = b.sb(es0, "l1a_rows", [1, 2 * T], BF16)
    ps_r = b.pt(es0, "l1a_ps_r", [128, TC], F32)
    ps_m = b.pt(es0, "l1a_ps_m", [128, TC], F32)
    ps_a = b.pt(es0, "l1a_ps_a", [128, TC], F32)
    ps_b = b.pt(es0, "l1a_ps_b", [128, TC], F32)
    f32scr = b.sb(es0, "l1a_f32scr", [128, TC], F32)

    ln_emit(b, lambda i: X[:, i * T:(i + 1) * T], False, rows, "l1a_in")

    if dbg is not None and "rows" in dbg:
        def dmprows(qq):
            b.w(qq, "dve", "act")
            b.dma(qq, "qo", dbg["rows"], rows[0:1, :])
        b.q(dmprows)

    esA = b.scope()
    wi = b.sb(esA, "l1a_wi", [128, 8 * 2048], BF16)
    xpb = b.sb(esA, "l1a_xpb", [128, 16 * 520], BF16)
    xln = b.sb(esA, "l1a_xln", [128, 8 * TC], BF16)
    zst = b.sb(esA, "l1a_zst", [128, 2 * TC], BF16)
    xst = b.sb(esA, "l1a_xst", [128, 2 * TC], BF16)
    cw = b.sb(esA, "l1a_cw", [128, 64], F32)
    cb = b.sb(esA, "l1a_cb", [128, 16], F32)
    acc = b.sb(esA, "l1a_acc", [128, TC], F32)

    emit_wload(b, [(wi[:, i * 2048:(i + 1) * 2048],
                    wd["l1a_w_in"][i * 128:(i + 1) * 128, 0:2048]) for i in range(8)],
               direct=[(cw[:, :], wd["l1a_conv_w"]), (cb[:, :], wd["l1a_conv_b"])])

    for tc in range(NTC if upto != 'xp0' else 1):
        _l1a_phaseA_xp_chunk(b, tc, X, rows, ps_r, ps_m, ps_a, ps_b, wi, xpb,
                             xln, xst, cw, cb, acc, f32scr, xc_sp)
        if tc == 0 and dbg is not None and "xln" in dbg:
            def dmpxln(qq):
                b.w(qq, "dve", "act")
                b.dma(qq, "qo", dbg["xln"], xln[:, :])
                b.dma(qq, "qo", dbg["xpb"], xpb[:, :])
            b.q(dmpxln)

    if upto == 'xp0':
        esA.close()
        es0.close()
        return

    def fence_z(g):
        b.w(g, "pe", "act", "dve")
        ins = g.engine_nop()
        b.inc(ins, "pool")
    b.g(fence_z)
    emit_wload(b, [(wi[:, i * 2048:(i + 1) * 2048],
                    wd["l1a_w_in"][i * 128:(i + 1) * 128, 2048:4096]) for i in range(8)])

    for tc in range(NTC):
        _l1a_phaseA_z_chunk(b, tc, X, rows, ps_r, ps_m, ps_a, ps_b, wi, xln,
                            zst, f32scr, z_sp)
    esA.close()
    b.fence()

    esOB = b.scope()
    ob = b.sb(esOB, "l1a_ob", [128, 8 * T], BF16)

    esB = b.scope()
    wo = b.sb(esB, "l1a_wo", [128, 16 * 1024], BF16)
    wB = b.sb(esB, "l1a_wB", [128, 256], BF16)
    wC = b.sb(esB, "l1a_wC", [16, 2048], BF16)
    xcc = b.sb(esB, "l1a_xcc2", [128, 16 * TC], BF16)
    zst2 = b.sb(esB, "l1a_zst2", [128, 2 * TC], BF16)
    yg = b.sb(esB, "l1a_yg", [128, 16 * TC], BF16)
    nine = b.sb(esB, "l1a_nine", [16, TC], F32)
    hsc = b.sb(esB, "l1a_hsc", [16, 2 * TC], F32)
    hbf = b.sb(esB, "l1a_hbf", [16, TC], BF16)
    xbs = b.sb(esB, "l1a_xbs", [16, TC], F32)
    zsil = b.sb(esB, "l1a_zsil", [128, TC], F32)

    emit_wload(b, [(wo[:, i * 1024:(i + 1) * 1024],
                    wd["l1a_w_out"][:, i * 1024:(i + 1) * 1024]) for i in range(16)]
               + [(wB[:, :], wd["l1a_w_B"]), (wC[0:16, :], wd["l1a_w_C"])])

    def mset(v):
        ins = v.memset(nine[:, :], 0.9)
        b.inc(ins, "dve")
    b.v(mset)

    for tc in range(NTC):
        _l1a_phaseB_chunk(b, tc, ps_a, ps_b, ob, wo, wB, wC, xcc, zst2, yg,
                          nine, hsc, hbf, xbs, zsil, xc_sp, z_sp)
    esB.close()

    if dbg is not None and "ob" in dbg:
        def dmpob(qq):
            b.w(qq, "dve", "act")
            for i in range(8):
                b.dma(qq, "qo", dbg["ob"][i * 128:(i + 1) * 128, :], ob[:, i * T:(i + 1) * T])
        b.q(dmpob)
    ln_emit(b, lambda i: ob[:, i * T:(i + 1) * T], True, rows, "l1a_out")
    emit_residual_add(b, X, ob, rows, ps_r, ps_m, f32scr)
    esOB.close()
    es0.close()



def _l1f_router(b, xln, wd, combT):
    nc = b.nc
    es = b.scope()
    rwT = b.sb(es, "l1f_rwT_sb", [128, 8 * 8], BF16)
    ebias = b.sb(es, "l1f_eb_sb", [1, 8], F32)
    lgt = b.sb(es, "l1f_lgt", [128, 8], F32)
    eq1 = b.sb(es, "l1f_eq1", [128, 8], F32)
    eq2 = b.sb(es, "l1f_eq2", [128, 8], F32)
    l2t = b.sb(es, "l1f_l2t", [128, 8], F32)
    m1 = b.sb(es, "l1f_m1", [128, 4], F32)
    eqbf = b.sb(es, "l1f_eqbf", [128, 8], BF16)
    ps_s = b.pt(es, "l1f_ps_s", [128, 8], F32)
    ps_t = b.pt(es, "l1f_ps_t", [128, 128], F32)
    ps_tb = b.pt(es, "l1f_ps_tb", [8, 128], BF16)

    ebrep = b.sb(es, "l1f_ebrep", [128, 8], F32)
    emit_wload(b, [(rwT[:, i * 8:(i + 1) * 8], wd["l1f_rwT"][i * 128:(i + 1) * 128, :]) for i in range(8)],
               direct=[(ebias[0:1, :], wd["l1f_eb"])])

    def bias_bcast(t):
        b.w(t, "qw", "dve")
        ins = t.matmul(ps_t[:, 0:8], b.consts["ones_row_f32"][0:1, :], ebias[0:1, 0:8],
                       start=True, stop=True)
        b.inc(ins, "pe")
    b.t(bias_bcast)

    def evb(v):
        b.w(v, "pe")
        ins = v.tensor_copy(ebrep[:, 0:8], ps_t[:, 0:8])
        b.inc(ins, "dve")
    b.v(evb)

    for tt in range(16):
        lo = tt * 128

        def mml(t, lo=lo):
            b.w(t, "pool", "dve", "qw")
            ins = None
            for dc in range(8):
                ins = t.matmul(ps_s[:, 0:8], xln[:, dc * T + lo:dc * T + lo + 128],
                               rwT[:, dc * 8:(dc + 1) * 8], start=(dc == 0), stop=(dc == 7))
            b.inc(ins, "pe")
        b.t(mml)

        def top2(v):
            b.w(v, "pe")
            ops = [
                lambda: v.tensor_add(lgt[:, 0:8], ps_s[:, 0:8], ebrep[:, 0:8]),
                lambda: v.reduce_max(m1[:, 0:1], lgt[:, 0:8], AX.X),
                lambda: v.tensor_scalar(eq1[:, 0:8], lgt[:, 0:8], m1[:, 0:1], None, ALU.is_ge),
                lambda: v.scalar_tensor_tensor(l2t[:, 0:8], eq1[:, 0:8], -1e30, lgt[:, 0:8], ALU.mult, ALU.add),
                lambda: v.reduce_max(m1[:, 1:2], l2t[:, 0:8], AX.X),
                lambda: v.tensor_scalar(eq2[:, 0:8], l2t[:, 0:8], m1[:, 1:2], None, ALU.is_ge),
                lambda: v.tensor_sub(m1[:, 2:3], m1[:, 1:2], m1[:, 0:1]),
            ]
            for op in ops:
                ins = op()
                b.inc(ins, "dve")
                v.wait_ge(b.sem["dve"], b.cnt["dve"])
        b.v(top2)

        def sig(s):
            b.w(s, "dve")
            ins = s.activation(m1[:, 3:4], m1[:, 2:3], AF.Sigmoid)
            b.inc(ins, "act")
        b.s(sig)

        def comb(v):
            b.w(v, "act")
            ops = [
                lambda: v.tensor_scalar(m1[:, 2:3], m1[:, 3:4], -1.0, 1.0, ALU.mult, ALU.add),
                lambda: v.tensor_scalar(eq1[:, 0:8], eq1[:, 0:8], m1[:, 2:3], None, ALU.mult),
                lambda: v.scalar_tensor_tensor(eq1[:, 0:8], eq2[:, 0:8], m1[:, 3:4], eq1[:, 0:8],
                                               ALU.mult, ALU.add),
                lambda: v.tensor_copy(eqbf[:, 0:8], eq1[:, 0:8]),
            ]
            for op in ops:
                ins = op()
                b.inc(ins, "dve")
                v.wait_ge(b.sem["dve"], b.cnt["dve"])
        b.v(comb)

        def tr(t):
            b.w(t, "dve", "qx")
            ins = t.transpose(ps_tb[0:8, 0:128], eqbf[:, 0:8], b.consts["ident_bf"][:, :])
            b.inc(ins, "pe")
        b.t(tr)

        def evtr(v, lo=lo):
            b.w(v, "pe")
            ins = v.tensor_copy(combT[0:8, lo:lo + 128], ps_tb[0:8, 0:128])
            b.inc(ins, "dve")
        b.v(evtr)
    es.close()
    b.fence()


def emit_l1_ffn(b, X, wd, spill):
    nc = b.nc
    x_sp = spill["x_park"]
    es0 = b.scope()
    ps_r = b.pt(es0, "l1f_ps_r", [128, TC], F32)
    ps_m = b.pt(es0, "l1f_ps_m", [128, TC], F32)
    ps_a = b.pt(es0, "l1f_ps_a", [128, TC], F32)
    ps_b = b.pt(es0, "l1f_ps_b", [128, TC], F32)
    ps_c = b.pt(es0, "l1f_ps_c", [128, TC], F32)
    f32scr = b.sb(es0, "l1f_f32scr", [128, TC], F32)
    combT = b.sb(es0, "l1f_combT", [8, T], BF16)
    xln = b.sb(es0, "l1f_xln", [128, 8 * T], BF16)

    esR = b.scope()
    rows = b.sb(esR, "l1f_rows", [1, 2 * T], BF16)
    ln_emit(b, lambda i: X[:, i * T:(i + 1) * T], False, rows, "l1f_in")

    for tc in range(NTC):
        lo, hi = tc * TC, (tc + 1) * TC
        ln_bcast(b, rows, lo, TC, ps_r, ps_m)

        def apply(v, lo=lo, hi=hi):
            b.w(v, "pe")
            ins = None
            for i in range(8):
                v.tensor_mul(f32scr[:, 0:TC], X[:, i * T + lo:i * T + hi], ps_r[:, 0:TC])
                ins = v.tensor_add(xln[:, i * T + lo:i * T + hi], f32scr[:, 0:TC], ps_m[:, 0:TC])
            b.inc(ins, "dve")
        b.v(apply)

    _l1f_router(b, xln, wd, combT)
    esR.close()
    b.fence()

    def park(qq):
        b.w(qq, "dve", "act", "pe")
        for i in range(8):
            b.dma(qq, "qsp", x_sp[i * 128:(i + 1) * 128, :], X[:, i * T:(i + 1) * T])
    b.q(park)

    esE = b.scope()
    obacc = X
    hgbuf = b.sb(esE, "l1f_hg", [128, 16 * T], BF16)
    w13 = b.sb(esE, "l1f_w13", [128, 2 * 4096], BF16)
    w2s = b.sb(esE, "l1f_w2s", [128, 2 * 2048], BF16)
    cetile = b.sb(esE, "l1f_cetile", [128, T], BF16)
    silu2 = [f32scr, b.sb(esE, "l1f_silu1", [128, TC], F32)]
    ps_d = b.pt(esE, "l1f_ps_d", [128, TC], F32)
    ps_e = b.pt(esE, "l1f_ps_e", [128, TC], F32)

    b.l1f_war13 = [0, 0]
    b.l1f_war2 = [0, 0]
    b.l1f_snap1 = [0, 0]
    b.l1f_snap2 = [0, 0]

    def zero_ob(v):
        b.w(v, "qsp")
        ins = None
        for i in range(8):
            ins = v.memset(obacc[:, i * T:(i + 1) * T], 0.0)
        b.inc(ins, "dve")
    b.v(zero_ob)

    for ex in range(10):
        routed = ex >= 2
        w13src = wd["l1f_sw13"] if not routed else wd["l1f_w13"]
        w2src = wd["l1f_sw2r"] if not routed else wd["l1f_w2r"]
        eoff = ex if not routed else (ex - 2)

        for hc in range(16):
            par = hc % 2
            ap13 = w13src[eoff * 16 + hc]
            emit_wload(b, [(w13[:, par * 4096 + dc * 256:par * 4096 + (dc + 1) * 256],
                            ap13[dc * 128:(dc + 1) * 128, :]) for dc in range(8)],
                       war=b.l1f_war13[par])
            for tc in range(NTC):
                lo = tc * TC
                pp = (hc * NTC + tc) % 2
                psA = ps_a if pp == 0 else ps_d
                psB = ps_b if pp == 0 else ps_e

                def mm1(t, hc=hc, par=par, lo=lo, pp=pp, psA=psA, psB=psB, tc=tc):
                    b.w(t, "pool")
                    if hc == 0 and tc <= 1:
                        b.w(t, "dve")
                    else:
                        b.wv(t, "dve", b.l1f_snap1[pp])
                    for dc in range(8):
                        t.matmul(psA[:, 0:TC], w13[:, par * 4096 + dc * 256:par * 4096 + dc * 256 + 128],
                                 xln[:, dc * T + lo:dc * T + lo + TC], start=(dc == 0), stop=(dc == 7))
                    ins = None
                    for dc in range(8):
                        ins = t.matmul(psB[:, 0:TC], w13[:, par * 4096 + dc * 256 + 128:par * 4096 + dc * 256 + 256],
                                       xln[:, dc * T + lo:dc * T + lo + TC], start=(dc == 0), stop=(dc == 7))
                    b.inc(ins, "pe")
                b.t(mm1)
                pes = b.cnt["pe"]

                def gate_s(s, pp=pp, psA=psA, pes=pes):
                    b.wv(s, "pe", pes)
                    b.wv(s, "dve", b.l1f_snap1[pp])
                    ins = s.activation(silu2[pp][:, 0:TC], psA[:, 0:TC], AF.Silu)
                    b.inc(ins, "act")
                b.s(gate_s)
                acts = b.cnt["act"]

                def gate_v(v, hc=hc, lo=lo, pp=pp, psB=psB, acts=acts, pes=pes):
                    b.wv(v, "act", acts)
                    b.wv(v, "pe", pes)
                    ins = v.tensor_mul(hgbuf[:, hc * T + lo:hc * T + lo + TC],
                                       silu2[pp][:, 0:TC], psB[:, 0:TC])
                    b.inc(ins, "dve")
                b.v(gate_v)
                b.l1f_snap1[pp] = b.cnt["dve"]
            b.l1f_war13[par] = b.cnt["pe"]

        if routed:
            for tc in range(NTC):
                def cerep(t, e=eoff, tc=tc):
                    b.w(t, "dve", "qw", "qx")
                    ins = t.matmul(ps_c[:, 0:TC], b.consts["sel8"][0:8, e * 128:(e + 1) * 128],
                                   combT[0:8, tc * TC:(tc + 1) * TC], start=True, stop=True)
                    b.inc(ins, "pe")
                b.t(cerep)

                def evce(v, tc=tc):
                    b.w(v, "pe")
                    ins = v.tensor_copy(cetile[:, tc * TC:(tc + 1) * TC], ps_c[:, 0:TC])
                    b.inc(ins, "dve")
                b.v(evce)

        for dc in range(8):
            par = dc % 2
            ap2 = w2src[eoff * 8 + dc]
            emit_wload(b, [(w2s[:, par * 2048 + hc * 128:par * 2048 + (hc + 1) * 128],
                            ap2[hc * 128:(hc + 1) * 128, :]) for hc in range(16)],
                       war=b.l1f_war2[par])
            for tc in range(NTC):
                lo = tc * TC
                pp = (dc * NTC + tc) % 2
                psA = ps_a if pp == 0 else ps_b

                def mm2(t, dc=dc, par=par, lo=lo, pp=pp, psA=psA, tc=tc):
                    b.w(t, "pool")
                    if dc == 0 and tc <= 1:
                        b.w(t, "dve")
                    else:
                        b.wv(t, "dve", b.l1f_snap2[pp])
                    ins = None
                    for hc in range(16):
                        ins = t.matmul(psA[:, 0:TC], w2s[:, par * 2048 + hc * 128:par * 2048 + (hc + 1) * 128],
                                       hgbuf[:, hc * T + lo:hc * T + lo + TC],
                                       start=(hc == 0), stop=(hc == 15))
                    b.inc(ins, "pe")
                b.t(mm2)
                pes = b.cnt["pe"]

                if routed:
                    def acc(v, dc=dc, lo=lo, psA=psA, pes=pes):
                        b.wv(v, "pe", pes)
                        v.tensor_mul(f32scr[:, 0:TC], psA[:, 0:TC], cetile[:, lo:lo + TC])
                        ins = v.tensor_add(obacc[:, dc * T + lo:dc * T + lo + TC],
                                           obacc[:, dc * T + lo:dc * T + lo + TC], f32scr[:, 0:TC])
                        b.inc(ins, "dve")
                    b.v(acc)
                else:
                    def acc(v, dc=dc, lo=lo, psA=psA, pes=pes):
                        b.wv(v, "pe", pes)
                        ins = v.tensor_add(obacc[:, dc * T + lo:dc * T + lo + TC],
                                           obacc[:, dc * T + lo:dc * T + lo + TC], psA[:, 0:TC])
                        b.inc(ins, "dve")
                    b.v(acc)
                b.l1f_snap2[pp] = b.cnt["dve"]
            b.l1f_war2[par] = b.cnt["pe"]

    esE.close()
    b.fence()
    esE = b.scope()
    rows = b.sb(esE, "l1f_rows2", [1, 2 * T], BF16)
    xold = b.sb(esE, "l1f_xold", [128, 2 * TC], F32)

    ln_emit(b, lambda i: obacc[:, i * T:(i + 1) * T], False, rows, "l1f_out")

    for tc in range(NTC):
        lo = tc * TC
        ln_bcast(b, rows, lo, TC, ps_r, ps_m)
        for i in range(8):
            par = i % 2

            def ldxo(qq, i=i, lo=lo, par=par):
                b.w(qq, "dve", "pe", "pool")
                b.dma(qq, "qsl", xold[:, par * TC:(par + 1) * TC],
                      x_sp[i * 128:(i + 1) * 128, lo:lo + TC])
            b.q(ldxo)

            def addres(v, i=i, lo=lo, par=par):
                b.w(v, "pe", "qsl")
                v.tensor_mul(f32scr[:, 0:TC], obacc[:, i * T + lo:i * T + lo + TC], ps_r[:, 0:TC])
                v.tensor_add(f32scr[:, 0:TC], f32scr[:, 0:TC], ps_m[:, 0:TC])
                ins = v.tensor_add(X[:, i * T + lo:i * T + lo + TC],
                                   xold[:, par * TC:(par + 1) * TC], f32scr[:, 0:TC])
                b.inc(ins, "dve")
            b.v(addres)
    esE.close()
    es0.close()
    b.fence()


def l1f_dram(nc):
    wd = {
        "l1f_rwT": nc.dram_tensor("l1f_rwT", [1024, 8], F32, kind="ExternalInput").ap(),
        "l1f_eb": nc.dram_tensor("l1f_eb", [1, 8], F32, kind="ExternalInput").ap(),
    }
    w13 = nc.dram_tensor("l1f_w13p", [8 * 16, 1024, 256], F32, kind="ExternalInput")
    wd["l1f_w13"] = [w13.ap()[i] for i in range(128)]
    sw13 = nc.dram_tensor("l1f_sw13p", [2 * 16, 1024, 256], F32, kind="ExternalInput")
    wd["l1f_sw13"] = [sw13.ap()[i] for i in range(32)]
    w2r = nc.dram_tensor("l1f_w2rp", [8 * 8, 2048, 128], F32, kind="ExternalInput")
    wd["l1f_w2r"] = [w2r.ap()[i] for i in range(64)]
    sw2 = nc.dram_tensor("l1f_sw2rp", [2 * 8, 2048, 128], F32, kind="ExternalInput")
    wd["l1f_sw2r"] = [sw2.ap()[i] for i in range(16)]
    return wd


def prep_l1f(inputs):
    g = {}
    g["l1f_rwT"] = _cc(inputs["L1_ffn_router_w"].T.astype(np.float32))
    g["l1f_eb"] = _cc(inputs["L1_ffn_expert_bias"].reshape(1, 8).astype(np.float32))

    def pack13(w1, w3):
        E = w1.shape[0]
        out = np.empty((E * 16, 1024, 256), np.float32)
        for e in range(E):
            for hc in range(16):
                out[e * 16 + hc, :, 0:128] = w1[e, hc * 128:(hc + 1) * 128, :].T
                out[e * 16 + hc, :, 128:256] = w3[e, hc * 128:(hc + 1) * 128, :].T
        return _cc(out)

    def pack2(w2):
        E = w2.shape[0]
        out = np.empty((E * 8, 2048, 128), np.float32)
        for e in range(E):
            for dc in range(8):
                out[e * 8 + dc] = w2[e, dc * 128:(dc + 1) * 128, :].T
        return _cc(out)

    g["l1f_w13p"] = pack13(inputs["L1_ffn_w1"], inputs["L1_ffn_w3"])
    g["l1f_sw13p"] = pack13(inputs["L1_ffn_sw1"], inputs["L1_ffn_sw3"])
    g["l1f_w2rp"] = pack2(inputs["L1_ffn_w2"])
    g["l1f_sw2rp"] = pack2(inputs["L1_ffn_sw2"])
    return g



def _l0a_phaseA(b, lay, XM, wd, wi, xpb, xmbf, xst, zst, cw, cb, acc, xin_sp, z_sp):
    emit_wload(b, [(wi[:, i * 2048:(i + 1) * 2048],
                    wd["l0a_w_inT"][lay][i * 128:(i + 1) * 128, 0:2048]) for i in range(8)],
               direct=[(cw[:, :], wd["l0a_conv_w"][lay]), (cb[:, :], wd["l0a_conv_b"][lay])])
    for tc in range(NTC):
        lo = tc * TC

        def cast(v, lo=lo):
            b.w(v, "pe")
            ins = None
            for i in range(8):
                ins = v.tensor_copy(xmbf[:, i * TC:(i + 1) * TC], XM[:, i * T + lo:i * T + lo + TC])
            b.inc(ins, "dve")
        b.v(cast)

        def halo(v, tc=tc):
            ins = None
            for ct in range(16):
                if tc == 0:
                    ins = v.memset(xpb[:, ct * 520:ct * 520 + 3], 0.0)
                else:
                    ins = v.tensor_copy(xpb[:, ct * 520:ct * 520 + 3],
                                        xpb[:, ct * 520 + 512:ct * 520 + 515])
            b.inc(ins, "dve")
        b.v(halo)

        snapA = [b.cnt["dve"], b.cnt["dve"]]
        for ec in range(16):
            ps = b.l0a_ps_a if ec % 2 == 0 else b.l0a_ps_b

            def mm(t, ec=ec, ps=ps, w0=snapA[ec % 2]):
                b.w(t, "pool", "qw")
                b.wv(t, "dve", w0)
                ins = None
                for dc in range(8):
                    ins = t.matmul(ps[:, 0:TC],
                                   wi[:, dc * 2048 + ec * 128:dc * 2048 + (ec + 1) * 128],
                                   xmbf[:, dc * TC:(dc + 1) * TC],
                                   start=(dc == 0), stop=(dc == 7))
                b.inc(ins, "pe")
            b.t(mm)
            pes = b.cnt["pe"]

            def ev(v, ec=ec, ps=ps, pes=pes):
                b.wv(v, "pe", pes)
                ins = v.tensor_copy(xpb[:, ec * 520 + 3:ec * 520 + 515], ps[:, 0:TC])
                b.inc(ins, "dve")
            b.v(ev)
            snapA[ec % 2] = b.cnt["dve"]

        snapS = [0, 0]
        snapC = [0, 0]
        for ct in range(16):
            ac = (ct % 2) * TC

            def conv(v, ct=ct, ac=ac, w0=snapS[ct % 2]):
                b.w(v, "qw")
                b.wv(v, "act", w0)
                base = ct * 520
                v.tensor_scalar(acc[:, ac:ac + TC], xpb[:, base + 3:base + 3 + TC],
                                cw[:, ct * 4 + 3:ct * 4 + 4], cb[:, ct:ct + 1],
                                ALU.mult, ALU.add)
                ins = None
                for k in (0, 1, 2):
                    ins = v.scalar_tensor_tensor(acc[:, ac:ac + TC], xpb[:, base + k:base + k + TC],
                                                 cw[:, ct * 4 + k:ct * 4 + k + 1], acc[:, ac:ac + TC],
                                                 ALU.mult, ALU.add)
                b.inc(ins, "dve")
            b.v(conv)
            dves = b.cnt["dve"]

            def sil(s, ct=ct, ac=ac, dves=dves):
                b.wv(s, "dve", dves)
                b.w(s, "qsp")
                ins = s.activation(xst[:, (ct % 2) * TC:(ct % 2 + 1) * TC], acc[:, ac:ac + TC], AF.Silu)
                b.inc(ins, "act")
            b.s(sil)
            snapS[ct % 2] = b.cnt["act"]

            def spxc(qq, ct=ct, lo=lo):
                b.w(qq, "act")
                b.dma(qq, "qsp", xin_sp[ct * 128:(ct + 1) * 128, lo:lo + TC],
                      xst[:, (ct % 2) * TC:(ct % 2 + 1) * TC])
            b.q(spxc)

    def fence_z(g):
        b.w(g, "pe", "act", "dve")
        ins = g.engine_nop()
        b.inc(ins, "pool")
    b.g(fence_z)
    emit_wload(b, [(wi[:, i * 2048:(i + 1) * 2048],
                    wd["l0a_w_inT"][lay][i * 128:(i + 1) * 128, 2048:4096]) for i in range(8)])
    for tc in range(NTC):
        lo = tc * TC

        def cast(v, lo=lo):
            b.w(v, "pe")
            ins = None
            for i in range(8):
                ins = v.tensor_copy(xmbf[:, i * TC:(i + 1) * TC], XM[:, i * T + lo:i * T + lo + TC])
            b.inc(ins, "dve")
        b.v(cast)

        snapZ = [b.cnt["dve"], b.cnt["dve"]]
        for ec in range(16):
            ps = b.l0a_ps_a if ec % 2 == 0 else b.l0a_ps_b
            par = ec % 2

            def mm(t, ec=ec, ps=ps, w0=snapZ[ec % 2]):
                b.w(t, "pool", "qw")
                b.wv(t, "dve", w0)
                ins = None
                for dc in range(8):
                    ins = t.matmul(ps[:, 0:TC],
                                   wi[:, dc * 2048 + ec * 128:dc * 2048 + (ec + 1) * 128],
                                   xmbf[:, dc * TC:(dc + 1) * TC],
                                   start=(dc == 0), stop=(dc == 7))
                b.inc(ins, "pe")
            b.t(mm)
            pes = b.cnt["pe"]

            def ev(v, ec=ec, ps=ps, par=par, pes=pes):
                b.wv(v, "pe", pes)
                b.w(v, "qsp")
                ins = v.tensor_copy(zst[:, par * TC:(par + 1) * TC], ps[:, 0:TC])
                b.inc(ins, "dve")
            b.v(ev)
            snapZ[ec % 2] = b.cnt["dve"]

            def spz(qq, ec=ec, par=par, lo=lo):
                b.w(qq, "dve")
                b.dma(qq, "qsp", z_sp[ec * 128:(ec + 1) * 128, lo:lo + TC],
                      zst[:, par * TC:(par + 1) * TC])
            b.q(spz)


def _l0a_phaseB(b, lay, XM, wd, bufs, xin_sp, z_sp):
    nc = b.nc
    (wout, wx, wdt, acoef, bdt, dpar, xcc, zstg, bcache, ccache, bnpack, apack,
     delta, du, dubf, yred, zsil, ygbuf, state, dtrbf, bbrow, ccrow) = bufs
    ps_a, ps_b = b.l0a_ps_a, b.l0a_ps_b

    def mkA_s(s):
        b.w(s, "qw", "dve", "pe")
        ins = s.activation(acoef[:, :], b.l0a_alog[:, :], AF.Exp)
        b.inc(ins, "act")
    b.s(mkA_s)

    def mkA_v(v):
        b.w(v, "act")
        ins = v.tensor_scalar(acoef[:, :], acoef[:, :], -1.0, None, ALU.mult)
        b.inc(ins, "dve")
    b.v(mkA_v)

    for tc in range(NTC):
        lo = tc * TC

        def ld(qq, lo=lo):
            b.w(qq, "pe", "act", "dve", "qsp")
            for ct in range(16):
                b.dma(qq, "qsl", xcc[:, ct * TC:(ct + 1) * TC],
                      xin_sp[ct * 128:(ct + 1) * 128, lo:lo + TC])
        b.q(ld)

        def mmxd(t):
            b.w(t, "qsl", "pool", "act")
            for ct in range(16):
                t.matmul(ps_a[0:64, 0:TC], wx[:, ct * 96:ct * 96 + 64],
                         xcc[:, ct * TC:(ct + 1) * TC], start=(ct == 0), stop=(ct == 15))
            for ct in range(16):
                t.matmul(ps_b[0:16, 0:TC], wx[:, ct * 96 + 64:ct * 96 + 80],
                         xcc[:, ct * TC:(ct + 1) * TC], start=(ct == 0), stop=(ct == 15))
            ins = None
            for ct in range(16):
                ins = t.matmul(ps_b[32:48, 0:TC], wx[:, ct * 96 + 80:ct * 96 + 96],
                               xcc[:, ct * TC:(ct + 1) * TC], start=(ct == 0), stop=(ct == 15))
            b.inc(ins, "pe")
        b.t(mmxd)

        def evxd(v):
            b.w(v, "pe")
            v.tensor_copy(dtrbf, ps_a[0:64, 0:TC])
            v.tensor_copy(bbrow, ps_b[0:16, 0:TC])
            ins = v.tensor_copy(ccrow, ps_b[32:48, 0:TC])
            b.inc(ins, "dve")
        b.v(evxd)

        for n in range(16):
            ps = ps_a if n % 2 == 0 else ps_b

            def mmbc(t, n=n, ps=ps):
                b.w(t, "dve", "qx")
                t.matmul(ps[:, 0:TC], b.consts["sel16"][64:80, n * 128:(n + 1) * 128],
                         bbrow, start=True, stop=True)
                ins = t.matmul(ps[:, TC:2 * TC],
                               b.consts["sel16"][64:80, n * 128:(n + 1) * 128],
                               ccrow, start=True, stop=True)
                b.inc(ins, "pe")
            b.t(mmbc)

            def evbc(v, n=n, ps=ps):
                b.w(v, "pe")
                v.tensor_copy(bcache[:, n * TC:(n + 1) * TC], ps[:, 0:TC])
                ins = v.tensor_copy(ccache[:, n * TC:(n + 1) * TC], ps[:, TC:2 * TC])
                b.inc(ins, "dve")
            b.v(evbc)

        snap = {"softpv": [0] * 17, "scans": [0] * 17, "gpred": [0] * 17, "mmdt": [0] * 17}

        def tail(ct, lo):
            def silz(s, ct=ct):
                b.w(s, "qsl", "dve")
                ins = s.activation(zsil[:, 0:TC], zstg[:, (ct % 2) * TC:(ct % 2 + 1) * TC], AF.Silu)
                b.inc(ins, "act")
            b.s(silz)
            snap_silz[ct % 2] = b.cnt["act"]

            def yg(v, ct=ct):
                b.w(v, "act")
                v.scalar_tensor_tensor(yred[:, 0:TC], xcc[:, ct * TC:(ct + 1) * TC],
                                       dpar[:, ct:ct + 1], yred[:, 0:TC], ALU.mult, ALU.add)
                ins = v.tensor_mul(ygbuf[:, ct * TC:(ct + 1) * TC], yred[:, 0:TC], zsil[:, 0:TC])
                b.inc(ins, "dve")
            b.v(yg)

        snap_silz = [0, 0]
        for ct in range(16):
            def ldz(qq, ct=ct, lo=lo, w0=snap_silz[ct % 2]):
                b.wv(qq, "act", w0)
                b.dma(qq, "qsl", zstg[:, (ct % 2) * TC:(ct % 2 + 1) * TC],
                      z_sp[ct * 128:(ct + 1) * 128, lo:lo + TC])
            b.q(ldz)

            def mmdt(t, ct=ct):
                b.wv(t, "dve", snap["softpv"][ct])
                if ct == 0:
                    b.w(t, "dve")
                ins = t.matmul(ps_a[:, 0:TC], wdt[0:64, ct * 128:(ct + 1) * 128],
                               dtrbf, start=True, stop=True)
                b.inc(ins, "pe")
            b.t(mmdt)
            snap["mmdt"][ct] = b.cnt["pe"]

            def softp_s(s, ct=ct):
                b.wv(s, "pe", snap["mmdt"][ct])
                b.wv(s, "dve", snap["softpv"][ct])
                ins = s.activation(delta[:, 0:TC], ps_a[:, 0:TC], AF.Abs, bias=bdt[:, ct:ct + 1])
                b.inc(ins, "act")
                s.wait_ge(b.sem["act"], b.cnt["act"])
                ins = s.activation(delta[:, 0:TC], delta[:, 0:TC], AF.Exp, scale=-1.0)
                b.inc(ins, "act")
                s.wait_ge(b.sem["act"], b.cnt["act"])
                ins = s.activation(delta[:, 0:TC], delta[:, 0:TC], AF.Ln, bias=1.0)
                b.inc(ins, "act")
            b.s(softp_s)

            def softp_v(v, ct=ct):
                b.w(v, "act")
                v.tensor_scalar(du[:, 0:TC], ps_a[:, 0:TC], bdt[:, ct:ct + 1], 0.0,
                                ALU.add, ALU.max)
                v.tensor_add(delta[:, 0:TC], delta[:, 0:TC], du[:, 0:TC])
                ins = v.tensor_mul(dubf[:, 0:TC], delta[:, 0:TC], xcc[:, ct * TC:(ct + 1) * TC])
                b.inc(ins, "dve")
                v.wait_ge(b.sem["dve"], b.cnt["dve"])
                dub = dubf[:, 0:TC].rearrange("p (a t) -> p a t", a=1).broadcast_to([128, 16, TC])
                ins = v.tensor_mul(bnpack[:, :].rearrange("p (n t) -> p n t", n=16),
                                   dub,
                                   bcache[:, :].rearrange("p (n t) -> p n t", n=16))
                b.inc(ins, "dve")
            b.v(softp_v)
            snap["softpv"][ct + 1] = b.cnt["dve"]

            for g in range(8):
                slot = g % 2

                def exps(s, ct=ct, g=g, slot=slot):
                    if g >= 2:
                        b.wv(s, "dve", snap["scans_g"][g - 2])
                    else:
                        b.wv(s, "dve", snap["softpv"][ct + 1])
                    ins = None
                    for j in range(2):
                        n = g * 2 + j
                        ins = s.activation(apack[:, (slot * 2 + j) * TC:(slot * 2 + j + 1) * TC],
                                           delta[:, 0:TC],
                                           AF.Exp, scale=acoef[:, ct * 16 + n:ct * 16 + n + 1])
                    b.inc(ins, "act")
                b.s(exps)
                if g == 0:
                    snap["scans_g"] = [0] * 8

                def scans(v, ct=ct, g=g, tc=tc, slot=slot):
                    b.w(v, "act")
                    ins = None
                    for j in range(2):
                        n = g * 2 + j
                        init = 0.0 if tc == 0 else state[:, ct * 16 + n:ct * 16 + n + 1]
                        ins = v.tensor_tensor_scan(bnpack[:, n * TC:(n + 1) * TC],
                                                   apack[:, (slot * 2 + j) * TC:(slot * 2 + j + 1) * TC],
                                                   bnpack[:, n * TC:(n + 1) * TC],
                                                   init, ALU.mult, ALU.add)
                    b.inc(ins, "dve")
                b.v(scans)
                snap["scans_g"][g] = b.cnt["dve"]

            def post(v, ct=ct):
                sv = bnpack[:, :].rearrange("p (n t) -> p n t", n=16)[:, :, TC - 1:TC]
                v.tensor_copy(state[:, ct * 16:(ct + 1) * 16].rearrange("p (n o) -> p n o", o=1), sv)
                ins = v.tensor_mul(bnpack[:, :], bnpack[:, :], ccache[:, :])
                b.inc(ins, "dve")
            b.v(post)
            snap["scans"][ct] = b.cnt["dve"]

            def gred(v, ct=ct):
                ins = v.tensor_reduce(yred[:, 0:TC], bnpack[:, :].rearrange("p (n t) -> p t n", n=16),
                                      AX.X, ALU.add)
                b.inc(ins, "dve")
            b.v(gred)
            snap["gpred"][ct + 1] = b.cnt["dve"]

            if ct > 0:
                tail(ct - 1, lo)
        tail(15, lo)
        snapO = [0, 0]
        for dc in range(8):
            ps = ps_a if dc % 2 == 0 else ps_b

            def mmo(t, dc=dc, ps=ps, w0=snapO[dc % 2]):
                b.w(t, "pool")
                if dc < 2:
                    b.w(t, "dve")
                else:
                    b.wv(t, "dve", w0)
                ins = None
                for ct in range(16):
                    ins = t.matmul(ps[:, 0:TC],
                                   wout[:, ct * 1024 + dc * 128:ct * 1024 + (dc + 1) * 128],
                                   ygbuf[:, ct * TC:(ct + 1) * TC], start=(ct == 0), stop=(ct == 15))
                b.inc(ins, "pe")
            b.t(mmo)
            pes = b.cnt["pe"]

            def evo(v, dc=dc, ps=ps, lo=lo, pes=pes):
                b.wv(v, "pe", pes)
                ins = v.tensor_add(XM[:, dc * T + lo:dc * T + lo + TC],
                                   XM[:, dc * T + lo:dc * T + lo + TC], ps[:, 0:TC])
                b.inc(ins, "dve")
            b.v(evo)
            snapO[dc % 2] = b.cnt["dve"]


def emit_l0_attn(b, X, wd, spill, n_layers=8):
    nc = b.nc
    x_sp = spill["x_park"]
    xin_sp, z_sp = spill["l0a_xin"], spill["l0a_z"]
    XM = X

    es0 = b.scope()
    ps_r = b.pt(es0, "l0a_ps_r", [128, TC], F32)
    ps_m = b.pt(es0, "l0a_ps_m", [128, TC], F32)
    b.l0a_ps_a = b.pt(es0, "l0a_ps_a", [128, 2 * TC], F32)
    b.l0a_ps_b = b.pt(es0, "l0a_ps_b", [128, 2 * TC], F32)
    f32scr = b.sb(es0, "l0a_f32scr", [128, TC], F32)

    esR = b.scope()
    rows = b.sb(esR, "l0a_rows", [1, 2 * T], BF16)
    ln_emit(b, lambda i: X[:, i * T:(i + 1) * T], False, rows, "l0a_in")

    def park(qq):
        b.w(qq, "dve", "act", "pe")
        for i in range(8):
            b.dma(qq, "qsp", x_sp[i * 128:(i + 1) * 128, :], X[:, i * T:(i + 1) * T])
    b.q(park)

    for tc in range(NTC):
        lo = tc * TC
        ln_bcast(b, rows, lo, TC, ps_r, ps_m)

        def apply(v, lo=lo):
            b.w(v, "pe", "qsp")
            ins = None
            for i in range(8):
                v.tensor_mul(f32scr[:, 0:TC], X[:, i * T + lo:i * T + lo + TC], ps_r[:, 0:TC])
                ins = v.tensor_add(XM[:, i * T + lo:i * T + lo + TC], f32scr[:, 0:TC], ps_m[:, 0:TC])
            b.inc(ins, "dve")
        b.v(apply)
    esR.close()
    b.fence()

    for lay in range(n_layers):
        sfx = f"_{lay}"
        esW = b.scope()
        wout = b.sb(esW, "l0a_wout" + sfx, [128, 16 * 1024], BF16)
        wx = b.sb(esW, "l0a_wx" + sfx, [128, 16 * 96], BF16)
        wdt = b.sb(esW, "l0a_wdt" + sfx, [64, 2048], BF16)
        acoef = b.sb(esW, "l0a_acoef" + sfx, [128, 256], F32)
        b.l0a_alog = acoef
        bdt = b.sb(esW, "l0a_bdt" + sfx, [128, 16], F32)
        dpar = b.sb(esW, "l0a_dpar" + sfx, [128, 16], F32)
        emit_wload(b, [(wout[:, i * 1024:(i + 1) * 1024],
                        wd["l0a_w_outT"][lay][i * 128:(i + 1) * 128, :]) for i in range(16)]
                   + [(wx[:, i * 96:(i + 1) * 96], wd["l0a_w_xT"][lay][i * 128:(i + 1) * 128, :]) for i in range(16)]
                   + [(wdt[0:64, :], wd["l0a_w_dtT"][lay])],
                   direct=[(b.l0a_alog[:, :], wd["l0a_A_log"][lay]),
                           (bdt[:, :], wd["l0a_b_dt"][lay]),
                           (dpar[:, :], wd["l0a_D_par"][lay])])

        esA = b.scope()
        wi = b.sb(esA, "l0a_wi" + sfx, [128, 8 * 2048], BF16)
        xpb = b.sb(esA, "l0a_xpb" + sfx, [128, 16 * 520], BF16)
        xmbf = b.sb(esA, "l0a_xmbf" + sfx, [128, 8 * TC], BF16)
        xst = b.sb(esA, "l0a_xst" + sfx, [128, 2 * TC], BF16)
        zst = b.sb(esA, "l0a_zst" + sfx, [128, 2 * TC], BF16)
        cw = b.sb(esA, "l0a_cw" + sfx, [128, 64], F32)
        cb = b.sb(esA, "l0a_cb" + sfx, [128, 16], F32)
        acc = b.sb(esA, "l0a_acc" + sfx, [128, 2 * TC], F32)
        _l0a_phaseA(b, lay, XM, wd, wi, xpb, xmbf, xst, zst, cw, cb, acc, xin_sp, z_sp)
        esA.close()
        b.fence()

        esB = b.scope()
        xcc = b.sb(esB, "l0a_xcc" + sfx, [128, 16 * TC], BF16)
        zstg = b.sb(esB, "l0a_zstg" + sfx, [128, 2 * TC], BF16)
        bcache = b.sb(esB, "l0a_bcache" + sfx, [128, 16 * TC], BF16)
        ccache = b.sb(esB, "l0a_ccache" + sfx, [128, 16 * TC], BF16)
        bnpack = b.sb(esB, "l0a_bnpack" + sfx, [128, 16 * TC], BF16)
        dubf = b.sb(esB, "l0a_dubf" + sfx, [128, TC], BF16)
        apack = b.sb(esB, "l0a_apack" + sfx, [128, 4 * TC], F32)
        du = apack[:, 3 * TC:4 * TC]
        delta = b.sb(esB, "l0a_delta" + sfx, [128, TC], F32)
        yred = b.sb(esB, "l0a_yred" + sfx, [128, TC], F32)
        zsil = b.sb(esB, "l0a_zsil" + sfx, [128, TC], BF16)
        ygbuf = xcc
        state = b.sb(esB, "l0a_state" + sfx, [128, 256], BF16)
        xdblb = b.sb(esB, "l0a_xdblb" + sfx, [128, 2 * TC], BF16)
        dtrbf = xdblb[0:64, 0:TC]
        bbrow = xdblb[64:80, 0:TC]
        ccrow = xdblb[64:80, TC:2 * TC]
        bufs = (wout, wx, wdt, acoef, bdt, dpar, xcc, zstg, bcache, ccache, bnpack,
                apack, delta, du, dubf, yred, zsil, ygbuf, state, dtrbf, bbrow, ccrow)
        _l0a_phaseB(b, lay, XM, wd, bufs, xin_sp, z_sp)
        esB.close()
        esW.close()
        b.fence()

    esF = b.scope()
    rows = b.sb(esF, "l0a_rows2", [1, 2 * T], BF16)
    ln_emit(b, lambda i: XM[:, i * T:(i + 1) * T], False, rows, "l0a_out")
    xold = b.sb(esF, "l0a_xold", [128, 2 * TC], F32)
    for tc in range(NTC):
        lo = tc * TC
        ln_bcast(b, rows, lo, TC, ps_r, ps_m)
        for i in range(8):
            par = i % 2

            def ldxo(qq, i=i, lo=lo, par=par):
                b.w(qq, "dve", "pe", "pool")
                b.dma(qq, "qsl", xold[:, par * TC:(par + 1) * TC],
                      x_sp[i * 128:(i + 1) * 128, lo:lo + TC])
            b.q(ldxo)

            def addres(v, i=i, lo=lo, par=par):
                b.w(v, "pe", "qsl")
                v.tensor_mul(f32scr[:, 0:TC], XM[:, i * T + lo:i * T + lo + TC], ps_r[:, 0:TC])
                v.tensor_add(f32scr[:, 0:TC], f32scr[:, 0:TC], ps_m[:, 0:TC])
                ins = v.tensor_add(X[:, i * T + lo:i * T + lo + TC],
                                   xold[:, par * TC:(par + 1) * TC], f32scr[:, 0:TC])
                b.inc(ins, "dve")
            b.v(addres)
    esF.close()
    es0.close()
    b.fence()


def l0a_dram(nc):
    wd = {}
    w = nc.dram_tensor("l0a_w_inT", [8, 1024, 4096], F32, kind="ExternalInput")
    wd["l0a_w_inT"] = [w.ap()[i] for i in range(8)]
    w = nc.dram_tensor("l0a_conv_w_p", [8, 128, 64], F32, kind="ExternalInput")
    wd["l0a_conv_w"] = [w.ap()[i] for i in range(8)]
    w = nc.dram_tensor("l0a_conv_b_p", [8, 128, 16], F32, kind="ExternalInput")
    wd["l0a_conv_b"] = [w.ap()[i] for i in range(8)]
    w = nc.dram_tensor("l0a_w_xT", [8, 2048, 96], F32, kind="ExternalInput")
    wd["l0a_w_xT"] = [w.ap()[i] for i in range(8)]
    w = nc.dram_tensor("l0a_w_dtT", [8, 64, 2048], F32, kind="ExternalInput")
    wd["l0a_w_dtT"] = [w.ap()[i] for i in range(8)]
    w = nc.dram_tensor("l0a_A_log_p", [8, 128, 256], F32, kind="ExternalInput")
    wd["l0a_A_log"] = [w.ap()[i] for i in range(8)]
    w = nc.dram_tensor("l0a_b_dt_p", [8, 128, 16], F32, kind="ExternalInput")
    wd["l0a_b_dt"] = [w.ap()[i] for i in range(8)]
    w = nc.dram_tensor("l0a_D_par_p", [8, 128, 16], F32, kind="ExternalInput")
    wd["l0a_D_par"] = [w.ap()[i] for i in range(8)]
    w = nc.dram_tensor("l0a_w_outT", [8, 2048, 1024], F32, kind="ExternalInput")
    wd["l0a_w_outT"] = [w.ap()[i] for i in range(8)]
    spill = {
        "l0a_xin": nc.dram_tensor("sp_l0a_xin", [2048, T], BF16, kind="Internal").ap(),
        "l0a_z": nc.dram_tensor("sp_l0a_z", [2048, T], BF16, kind="Internal").ap(),
    }
    return wd, spill


def prep_l0a(inputs):
    g = {}
    g["l0a_w_inT"] = _cc(inputs["L0_attn_W_in"].transpose(0, 2, 1).astype(np.float32))
    cw = inputs["L0_attn_conv_w"]
    g["l0a_conv_w_p"] = _cc(cw.reshape(8, 16, 128, 4).transpose(0, 2, 1, 3).reshape(8, 128, 64))
    g["l0a_conv_b_p"] = _cc(inputs["L0_attn_conv_b"].reshape(8, 16, 128).transpose(0, 2, 1))
    g["l0a_w_xT"] = _cc(inputs["L0_attn_W_x"].transpose(0, 2, 1).astype(np.float32))
    g["l0a_w_dtT"] = _cc(inputs["L0_attn_W_dt"].transpose(0, 2, 1).astype(np.float32))
    al = inputs["L0_attn_A_log"]
    g["l0a_A_log_p"] = _cc(al.reshape(8, 16, 128, 16).transpose(0, 2, 1, 3).reshape(8, 128, 256))
    g["l0a_b_dt_p"] = _cc(inputs["L0_attn_b_dt"].reshape(8, 16, 128).transpose(0, 2, 1))
    g["l0a_D_par_p"] = _cc(inputs["L0_attn_D_par"].reshape(8, 16, 128).transpose(0, 2, 1))
    g["l0a_w_outT"] = _cc(inputs["L0_attn_W_out"].transpose(0, 2, 1).astype(np.float32))
    return g



def _l0f_attn(b, lay, XF, wd, es0_t, sfx):
    nc = b.nc
    (rows, ps_r, ps_m, ps_a, ps_b, f32scr) = es0_t
    es = b.scope()
    qkvb = b.sb(es, "l0f_qkvb" + sfx, [128, 12 * T], BF16)
    obuf = b.sb(es, "l0f_obuf" + sfx, [128, 4 * T], BF16)
    esQ = b.scope()
    wqkv = b.sb(esQ, "l0f_wqkv" + sfx, [128, 8 * 1536], BF16)
    hbuf = b.sb(esQ, "l0f_hbuf" + sfx, [128, 8 * TC], BF16)
    ps_o = b.pt(es, "l0f_ps_o" + sfx, [128, TC], F32)
    ps_tr = b.pt(es, "l0f_ps_tr" + sfx, [128, 256], BF16)
    ps_bc = b.pt(es, "l0f_ps_bc" + sfx, [128, TC], F32)
    ps_sq = b.pt(es, "l0f_ps_sq" + sfx, [1, TC], F32)

    emit_wload(b, [(wqkv[:, i * 1536:(i + 1) * 1536], wd["l0f_wqkv"][lay][i * 128:(i + 1) * 128, :])
                   for i in range(8)])
    for tc in range(NTC):
        lo = tc * TC

        def mkh(v, lo=lo, tc=tc):
            b.w(v, "pe")
            ins = None
            for i in range(8):
                if i < 4:
                    ins = v.tensor_copy(hbuf[:, i * TC:(i + 1) * TC], XF[:, i * T + lo:i * T + lo + TC])
                else:
                    if tc == 0:
                        v.memset(hbuf[:, i * TC:i * TC + 1], 0.0)
                        ins = v.tensor_copy(hbuf[:, i * TC + 1:(i + 1) * TC],
                                            XF[:, i * T:i * T + TC - 1])
                    else:
                        ins = v.tensor_copy(hbuf[:, i * TC:(i + 1) * TC],
                                            XF[:, i * T + lo - 1:i * T + lo + TC - 1])
            b.inc(ins, "dve")
        b.v(mkh)

        snapQ = [b.cnt["dve"], b.cnt["dve"]]
        for ec in range(12):
            ps = ps_a if ec % 2 == 0 else ps_b

            def mm(t, ec=ec, ps=ps, w0=snapQ[ec % 2]):
                b.w(t, "pool", "qw")
                b.wv(t, "dve", w0)
                ins = None
                for dc in range(8):
                    ins = t.matmul(ps[:, 0:TC], wqkv[:, dc * 1536 + ec * 128:dc * 1536 + (ec + 1) * 128],
                                   hbuf[:, dc * TC:(dc + 1) * TC], start=(dc == 0), stop=(dc == 7))
                b.inc(ins, "pe")
            b.t(mm)
            pes = b.cnt["pe"]

            def ev(v, ec=ec, ps=ps, lo=lo, pes=pes):
                b.wv(v, "pe", pes)
                ins = v.tensor_copy(qkvb[:, ec * T + lo:ec * T + lo + TC], ps[:, 0:TC])
                b.inc(ins, "dve")
            b.v(ev)
            snapQ[ec % 2] = b.cnt["dve"]

    esQ.close()
    b.fence()
    esS = b.scope()
    sqscr = b.sb(esS, "l0f_sqscr" + sfx, [128, T], BF16)
    hrow = b.sb(esS, "l0f_hrow" + sfx, [1, TC], F32)
    hrowb = b.sb(esS, "l0f_hrowb" + sfx, [1, T], BF16)
    qsc = b.sb(esS, "l0f_qsc" + sfx, [128, 2], F32)
    rot64 = b.sb(esS, "l0f_rot64" + sfx, [128, 64], BF16)
    tq = b.sb(esS, "l0f_tq" + sfx, [128, 4 * T], BF16)
    maskt = b.sb(esS, "l0f_mask" + sfx, [128, 4 * 1024], BF16)
    vtbuf = b.sb(esS, "l0f_vt" + sfx, [128, 16 * 64], BF16)
    attnT = b.sb(esS, "l0f_attnT" + sfx, [128, 8 * TC], BF16)
    simb = b.sb(esS, "l0f_simb" + sfx, [128, 1024], F32)
    attnb = b.sb(esS, "l0f_attnb" + sfx, [128, 1024], BF16)
    ncol = b.sb(esS, "l0f_ncol" + sfx, [128, 4], F32)
    qnscr = b.sb(esS, "l0f_qn" + sfx, [128, TC], BF16)
    t1scr = b.sb(esS, "l0f_t1" + sfx, [128, TC], F32)
    t2scr = b.sb(esS, "l0f_t2" + sfx, [128, TC], F32)
    emit_wload(b, [], direct=[(qsc[:, 0:1], wd["l0f_qs"][lay]), (qsc[:, 1:2], wd["l0f_ks"][lay]),
                              (rot64[:, :], wd["l0f_rot64"]), (tq[:, :], wd["l0f_tq"]),
                              (maskt[:, :], wd["l0f_mask"])])

    for h in range(16):
        tile = (h // 2) if h < 8 else (4 + (h - 8) // 2)
        hb = h % 2
        base = hb * 64
        isq = h < 8
        ctbl = 0 if isq else 2
        stbl = 1 if isq else 3
        scol = 0 if isq else 1

        def sq_s(s, tile=tile):
            b.w(s, "dve", "pe")
            ins = s.activation(sqscr[:, 0:T], qkvb[:, tile * T:(tile + 1) * T], AF.Square)
            b.inc(ins, "act")
        if hb == 0:
            b.s(sq_s)

        for tc in range(NTC):
            def mmsq(t, tc=tc, base=base, tile=tile):
                b.w(t, "act", "dve")
                ins = t.matmul(ps_sq[0:1, 0:TC], b.consts["ones_col_bf"][base:base + 64, 0:1],
                               sqscr[base:base + 64, tc * TC:(tc + 1) * TC], start=True, stop=True)
                b.inc(ins, "pe")
            b.t(mmsq)

            def rstd_v(v, tc=tc):
                b.w(v, "pe")
                ins = v.tensor_scalar(hrow[0:1, 0:TC], ps_sq[0:1, 0:TC], 1e-24, None, ALU.max)
                b.inc(ins, "dve")
            b.v(rstd_v)

            def rstd_s(s, tc=tc):
                b.w(s, "dve")
                ins = s.activation(hrow[0:1, 0:TC], hrow[0:1, 0:TC], AF.Ln)
                b.inc(ins, "act")
                s.wait_ge(b.sem["act"], b.cnt["act"])
                ins = s.activation(hrowb[0:1, tc * TC:(tc + 1) * TC], hrow[0:1, 0:TC], AF.Exp, scale=-0.5)
                b.inc(ins, "act")
            b.s(rstd_s)

        for tc in range(NTC):
            lo = tc * TC

            def bc(t, lo=lo, base=base):
                b.w(t, "act", "dve")
                ins = t.matmul(ps_bc[base:base + 64, 0:TC], b.consts["ones_row_bf"][0:1, 0:64],
                               hrowb[0:1, lo:lo + TC], start=True, stop=True)
                b.inc(ins, "pe")
            b.t(bc)

            def app1(v, lo=lo, base=base, tile=tile, scol=scol):
                b.w(v, "pe")
                v.tensor_mul(qnscr[base:base + 64, 0:TC],
                             qkvb[base:base + 64, tile * T + lo:tile * T + lo + TC],
                             ps_bc[base:base + 64, 0:TC])
                ins = v.tensor_scalar(qnscr[base:base + 64, 0:TC], qnscr[base:base + 64, 0:TC],
                                      qsc[base:base + 64, scol:scol + 1], None, ALU.mult)
                b.inc(ins, "dve")
            b.v(app1)

            def rotmm(t, base=base):
                b.w(t, "dve", "qw")
                ins = t.matmul(ps_o[base:base + 64, 0:TC], rot64[base:base + 64, 0:64],
                               qnscr[base:base + 64, 0:TC], start=True, stop=True)
                b.inc(ins, "pe")
            b.t(rotmm)

            def app2(v, lo=lo, base=base, tile=tile, ctbl=ctbl, stbl=stbl):
                b.w(v, "pe")
                v.tensor_mul(t1scr[base:base + 64, 0:TC], qnscr[base:base + 64, 0:TC],
                             tq[base:base + 64, ctbl * T + lo:ctbl * T + lo + TC])
                v.tensor_mul(t2scr[base:base + 64, 0:TC], ps_o[base:base + 64, 0:TC],
                             tq[base:base + 64, stbl * T + lo:stbl * T + lo + TC])
                ins = v.tensor_add(qkvb[base:base + 64, tile * T + lo:tile * T + lo + TC],
                                   t1scr[base:base + 64, 0:TC], t2scr[base:base + 64, 0:TC])
                b.inc(ins, "dve")
            b.v(app2)

    for h in range(8):
        tile = 8 + h // 2
        hb = h % 2
        base = hb * 64
        qtile = h // 2
        ktile = 4 + h // 2

        for tt in range(16):
            pspar = tt % 2

            def vtr(t, tt=tt, tile=tile, base=base, pspar=pspar):
                b.w(t, "dve")
                ins = t.transpose(ps_tr[:, pspar * 64:(pspar + 1) * 64],
                                  qkvb[base:base + 64, tile * T + tt * 128:tile * T + (tt + 1) * 128],
                                  b.consts["ident_bf"][base:base + 64, base:base + 64])
                b.inc(ins, "pe")
            b.t(vtr)

            def vev(v, tt=tt, pspar=pspar):
                b.w(v, "pe")
                ins = v.tensor_copy(vtbuf[:, tt * 64:(tt + 1) * 64], ps_tr[:, pspar * 64:(pspar + 1) * 64])
                b.inc(ins, "dve")
            b.v(vev)

        for wi in range(4):
            for ic in range(4):
                i0 = wi * TC + ic * 128

                def mmsim(t, wi=wi, ic=ic, i0=i0, base=base, qtile=qtile, ktile=ktile):
                    b.w(t, "dve")
                    if wi > 0:
                        t.matmul(ps_a[:, 0:TC], qkvb[base:base + 64, qtile * T + i0:qtile * T + i0 + 128],
                                 qkvb[base:base + 64, ktile * T + (wi - 1) * TC:ktile * T + wi * TC],
                                 start=True, stop=True)
                    ins = t.matmul(ps_b[:, 0:TC], qkvb[base:base + 64, qtile * T + i0:qtile * T + i0 + 128],
                                   qkvb[base:base + 64, ktile * T + wi * TC:ktile * T + (wi + 1) * TC],
                                   start=True, stop=True)
                    b.inc(ins, "pe")
                b.t(mmsim)

                def smax(v, wi=wi, ic=ic):
                    b.w(v, "pe", "qw")
                    if wi > 0:
                        v.tensor_add(simb[:, 0:TC], ps_a[:, 0:TC], maskt[:, ic * 1024:ic * 1024 + TC])
                    else:
                        v.memset(simb[:, 0:TC], -3.0e38)
                    v.tensor_add(simb[:, TC:1024], ps_b[:, 0:TC], maskt[:, ic * 1024 + TC:(ic + 1) * 1024])
                    ins = v.reduce_max(ncol[:, 0:1], simb[:, 0:1024], AX.X, negate=True)
                    b.inc(ins, "dve")
                    v.wait_ge(b.sem["dve"], b.cnt["dve"])
                b.v(smax)

                def sexp(s):
                    b.w(s, "dve")
                    ins = s.activation(attnb[:, 0:1024], simb[:, 0:1024], AF.Exp,
                                       bias=ncol[:, 0:1], accum_out=ncol[:, 1:2])
                    b.inc(ins, "act")
                b.s(sexp)

                def snorm(v):
                    b.w(v, "act")
                    ins = v.reciprocal(ncol[:, 2:3], ncol[:, 1:2])
                    b.inc(ins, "dve")
                    v.wait_ge(b.sem["dve"], b.cnt["dve"])
                    ins = v.tensor_scalar(attnb[:, 0:1024], attnb[:, 0:1024], ncol[:, 2:3],
                                          None, ALU.mult)
                    b.inc(ins, "dve")
                b.v(snorm)

                for js in range(8):
                    pspar = js % 2

                    def atr(t, js=js, pspar=pspar):
                        b.w(t, "dve")
                        ins = t.transpose(ps_tr[:, pspar * 128:(pspar + 1) * 128],
                                          attnb[:, js * 128:(js + 1) * 128],
                                          b.consts["ident_bf"][:, :])
                        b.inc(ins, "pe")
                    b.t(atr)

                    def aev(v, js=js, ic=ic, pspar=pspar):
                        b.w(v, "pe")
                        ins = v.tensor_copy(attnT[:, js * TC + ic * 128:js * TC + (ic + 1) * 128],
                                            ps_tr[:, pspar * 128:(pspar + 1) * 128])
                        b.inc(ins, "dve")
                    b.v(aev)

            def mmav(t, wi=wi, base=base):
                b.w(t, "dve")
                ins = None
                j0 = 4 if wi == 0 else 0
                for jt in range(j0, 8):
                    jtg = (wi - 1) * 4 + jt
                    ins = t.matmul(ps_o[base:base + 64, 0:TC], vtbuf[:, jtg * 64:(jtg + 1) * 64],
                                   attnT[:, jt * TC:(jt + 1) * TC],
                                   start=(jt == j0), stop=(jt == 7))
                b.inc(ins, "pe")
            b.t(mmav)

            def oev(v, wi=wi, base=base, qtile=qtile):
                b.w(v, "pe")
                ins = v.tensor_copy(obuf[base:base + 64, qtile * T + wi * TC:qtile * T + (wi + 1) * TC],
                                    ps_o[base:base + 64, 0:TC])
                b.inc(ins, "dve")
            b.v(oev)

    esS.close()
    b.fence()
    esW = b.scope()
    wob = b.sb(esW, "l0f_wob" + sfx, [128, 4 * 1024], BF16)
    emit_wload(b, [(wob[:, i * 1024:(i + 1) * 1024], wd["l0f_wo"][lay][i * 128:(i + 1) * 128, :])
                   for i in range(4)])

    for tc in range(NTC):
        lo = tc * TC
        for dc in range(8):
            ps = ps_a if dc % 2 == 0 else ps_b

            def mmwo(t, dc=dc, ps=ps, lo=lo):
                b.w(t, "dve", "pool")
                ins = None
                for ct in range(4):
                    ins = t.matmul(ps[:, 0:TC], wob[:, ct * 1024 + dc * 128:ct * 1024 + (dc + 1) * 128],
                                   obuf[:, ct * T + lo:ct * T + lo + TC], start=(ct == 0), stop=(ct == 3))
                b.inc(ins, "pe")
            b.t(mmwo)

            def evwo(v, dc=dc, ps=ps, lo=lo):
                b.w(v, "pe")
                ins = v.tensor_add(XF[:, dc * T + lo:dc * T + lo + TC],
                                   XF[:, dc * T + lo:dc * T + lo + TC], ps[:, 0:TC])
                b.inc(ins, "dve")
            b.v(evwo)
    esW.close()
    es.close()
    b.fence()


def _l0f_ffn(b, lay, XF, wd, es0_t, sfx):
    nc = b.nc
    (rows, ps_r, ps_m, ps_a, ps_b, f32scr) = es0_t
    es = b.scope()
    h2b = b.sb(es, "l0f_h2b" + sfx, [128, 8 * T], BF16)
    xln2 = b.sb(es, "l0f_xln2" + sfx, [128, 8 * T], BF16)
    gcols = b.sb(es, "l0f_gcols" + sfx, [128, 16], F32)
    b1a = b.sb(es, "l0f_b1a" + sfx, [128, 22], F32)
    b1g = b.sb(es, "l0f_b1g" + sfx, [128, 22], F32)
    b2c = b.sb(es, "l0f_b2c" + sfx, [128, 8], F32)

    emit_wload(b, [], direct=[(gcols[:, 0:8], wd["l0f_lng"][lay]),
                              (gcols[:, 8:16], wd["l0f_lnb"][lay]),
                              (b1a[:, :], wd["l0f_b1a"][lay]),
                              (b1g[:, :], wd["l0f_b1g"][lay]),
                              (b2c[:, :], wd["l0f_b2"][lay])])

    for tc in range(NTC):
        lo = tc * TC

        def mkh(v, lo=lo, tc=tc):
            b.w(v, "pe", "act")
            ins = None
            for i in range(8):
                if i < 4:
                    ins = v.tensor_copy(h2b[:, i * T + lo:i * T + lo + TC],
                                        XF[:, i * T + lo:i * T + lo + TC])
                else:
                    if tc == 0:
                        v.memset(h2b[:, i * T:i * T + 1], 0.0)
                        ins = v.tensor_copy(h2b[:, i * T + 1:i * T + TC],
                                            XF[:, i * T:i * T + TC - 1])
                    else:
                        ins = v.tensor_copy(h2b[:, i * T + lo:i * T + lo + TC],
                                            XF[:, i * T + lo - 1:i * T + lo + TC - 1])
            b.inc(ins, "dve")
        b.v(mkh)

    ln_emit(b, lambda i: h2b[:, i * T:(i + 1) * T], True, rows, "l0f_ln" + sfx)

    for tc in range(NTC):
        lo = tc * TC
        ln_bcast(b, rows, lo, TC, ps_r, ps_m)

        def appg(v, lo=lo):
            b.w(v, "pe", "qw")
            ins = None
            for i in range(8):
                v.tensor_mul(f32scr[:, 0:TC], h2b[:, i * T + lo:i * T + lo + TC], ps_r[:, 0:TC])
                v.tensor_add(f32scr[:, 0:TC], f32scr[:, 0:TC], ps_m[:, 0:TC])
                ins = v.tensor_scalar(xln2[:, i * T + lo:i * T + lo + TC], f32scr[:, 0:TC],
                                      gcols[:, i:i + 1], gcols[:, 8 + i:9 + i], ALU.mult, ALU.add)
            b.inc(ins, "dve")
        b.v(appg)

    esW = b.scope()
    hg = b.sb(esW, "l0f_hg" + sfx, [128, 22 * 1024], BF16)
    w1c = b.sb(esW, "l0f_w1c" + sfx, [128, 8 * 256], BF16)
    w2c = b.sb(esW, "l0f_w2c" + sfx, [128, 22 * 128], BF16)
    gel2 = [b.sb(esW, "l0f_gel0" + sfx, [128, TC], F32),
            b.sb(esW, "l0f_gel1" + sfx, [128, TC], F32)]
    b.l0f_snapH = [0, 0]
    b.l0f_snapO = [0, 0]

    for th in range(2):
        for pair in range(22):
            emit_wload(b, [(w1c[:, dc * 256:(dc + 1) * 256],
                            wd["l0f_w1p"][lay][dc * 128:(dc + 1) * 128, pair * 256:(pair + 1) * 256])
                           for dc in range(8)])
            for tcl in range(2):
                lo = th * 1024 + tcl * TC
                pp = (pair * 2 + tcl) % 2
                psA = ps_a if pp == 0 else ps_r
                psB = ps_b if pp == 0 else ps_m

                def mmh(t, pair=pair, lo=lo, pp=pp, psA=psA, psB=psB):
                    b.w(t, "pool")
                    if pair == 0:
                        b.w(t, "dve")
                    else:
                        b.wv(t, "dve", b.l0f_snapH[pp])
                    for dc in range(8):
                        t.matmul(psA[:, 0:TC], w1c[:, dc * 256:dc * 256 + 128],
                                 xln2[:, dc * T + lo:dc * T + lo + TC], start=(dc == 0), stop=(dc == 7))
                    ins = None
                    for dc in range(8):
                        ins = t.matmul(psB[:, 0:TC], w1c[:, dc * 256 + 128:dc * 256 + 256],
                                       xln2[:, dc * T + lo:dc * T + lo + TC], start=(dc == 0), stop=(dc == 7))
                    b.inc(ins, "pe")
                b.t(mmh)
                pes = b.cnt["pe"]

                def gg_s(s, pair=pair, pp=pp, psB=psB, pes=pes):
                    b.wv(s, "pe", pes)
                    b.wv(s, "dve", b.l0f_snapH[pp])
                    ins = s.activation(gel2[pp][:, 0:TC], psB[:, 0:TC], AF.Gelu, bias=b1g[:, pair:pair + 1])
                    b.inc(ins, "act")
                b.s(gg_s)
                acts = b.cnt["act"]

                def gg_v(v, pair=pair, tcl=tcl, pp=pp, psA=psA, acts=acts, pes=pes):
                    b.wv(v, "act", acts)
                    b.wv(v, "pe", pes)
                    ins = v.scalar_tensor_tensor(hg[:, pair * 1024 + tcl * TC:pair * 1024 + (tcl + 1) * TC],
                                                 psA[:, 0:TC], b1a[:, pair:pair + 1], gel2[pp][:, 0:TC],
                                                 ALU.add, ALU.mult)
                    b.inc(ins, "dve")
                b.v(gg_v)
                b.l0f_snapH[pp] = b.cnt["dve"]

        for dc in range(8):
            emit_wload(b, [(w2c[:, hc * 128:(hc + 1) * 128],
                            wd["l0f_w2p"][lay * 8 + dc][hc * 128:(hc + 1) * 128, :])
                           for hc in range(22)])
            for tcl in range(2):
                lo = th * 1024 + tcl * TC
                pp = (dc * 2 + tcl) % 2
                psA = ps_a if pp == 0 else ps_b

                def mmo(t, dc=dc, lo=lo, tcl=tcl, pp=pp, psA=psA):
                    b.w(t, "pool")
                    if dc == 0:
                        b.w(t, "dve")
                    else:
                        b.wv(t, "dve", b.l0f_snapO[pp])
                    ins = None
                    for hc in range(22):
                        ins = t.matmul(psA[:, 0:TC], w2c[:, hc * 128:(hc + 1) * 128],
                                       hg[:, hc * 1024 + tcl * TC:hc * 1024 + (tcl + 1) * TC],
                                       start=(hc == 0), stop=(hc == 21))
                    b.inc(ins, "pe")
                b.t(mmo)
                pes = b.cnt["pe"]

                def evo(v, dc=dc, lo=lo, pp=pp, psA=psA, pes=pes):
                    b.wv(v, "pe", pes)
                    ins = v.scalar_tensor_tensor(XF[:, dc * T + lo:dc * T + lo + TC],
                                                 psA[:, 0:TC], b2c[:, dc:dc + 1],
                                                 XF[:, dc * T + lo:dc * T + lo + TC],
                                                 ALU.add, ALU.add)
                    b.inc(ins, "dve")
                b.v(evo)
                b.l0f_snapO[pp] = b.cnt["dve"]
    esW.close()
    es.close()
    b.fence()


def emit_l0_ffn(b, X, wd, spill, n_layers=2):
    nc = b.nc
    x_sp = spill["x_park"]
    XF = X
    es0 = b.scope()
    ps_r = b.pt(es0, "l0f_ps_r", [128, TC], F32)
    ps_m = b.pt(es0, "l0f_ps_m", [128, TC], F32)
    ps_a = b.pt(es0, "l0f_ps_a", [128, TC], F32)
    ps_b = b.pt(es0, "l0f_ps_b", [128, TC], F32)
    f32scr = b.sb(es0, "l0f_f32scr", [128, TC], F32)
    rows = b.sb(es0, "l0f_rows", [1, 2 * T], BF16)
    es0_t = (rows, ps_r, ps_m, ps_a, ps_b, f32scr)

    ln_emit(b, lambda i: X[:, i * T:(i + 1) * T], False, rows, "l0f_in")

    def park(qq):
        b.w(qq, "dve", "act", "pe")
        for i in range(8):
            b.dma(qq, "qsp", x_sp[i * 128:(i + 1) * 128, :], X[:, i * T:(i + 1) * T])
    b.q(park)

    for tc in range(NTC):
        lo = tc * TC
        ln_bcast(b, rows, lo, TC, ps_r, ps_m)

        def apply(v, lo=lo):
            b.w(v, "pe", "qsp")
            ins = None
            for i in range(8):
                v.tensor_mul(f32scr[:, 0:TC], X[:, i * T + lo:i * T + lo + TC], ps_r[:, 0:TC])
                ins = v.tensor_add(XF[:, i * T + lo:i * T + lo + TC], f32scr[:, 0:TC], ps_m[:, 0:TC])
            b.inc(ins, "dve")
        b.v(apply)
    b.fence()

    for lay in range(n_layers):
        sfx = f"_{lay}"
        _l0f_attn(b, lay, XF, wd, es0_t, sfx)
        _l0f_ffn(b, lay, XF, wd, es0_t, sfx)

    esF = b.scope()
    ln_emit(b, lambda i: XF[:, i * T:(i + 1) * T], False, rows, "l0f_out")
    xold = b.sb(esF, "l0f_xold", [128, 2 * TC], F32)
    for tc in range(NTC):
        lo = tc * TC
        ln_bcast(b, rows, lo, TC, ps_r, ps_m)
        for i in range(8):
            par = i % 2

            def ldxo(qq, i=i, lo=lo, par=par):
                b.w(qq, "dve", "pe", "pool")
                b.dma(qq, "qsl", xold[:, par * TC:(par + 1) * TC],
                      x_sp[i * 128:(i + 1) * 128, lo:lo + TC])
            b.q(ldxo)

            def addres(v, i=i, lo=lo, par=par):
                b.w(v, "pe", "qsl")
                v.tensor_mul(f32scr[:, 0:TC], XF[:, i * T + lo:i * T + lo + TC], ps_r[:, 0:TC])
                v.tensor_add(f32scr[:, 0:TC], f32scr[:, 0:TC], ps_m[:, 0:TC])
                ins = v.tensor_add(X[:, i * T + lo:i * T + lo + TC],
                                   xold[:, par * TC:(par + 1) * TC], f32scr[:, 0:TC])
                b.inc(ins, "dve")
            b.v(addres)
    esF.close()
    es0.close()
    b.fence()


def l0f_dram(nc):
    wd = {}
    w = nc.dram_tensor("l0f_wqkv_d", [2, 1024, 1536], F32, kind="ExternalInput")
    wd["l0f_wqkv"] = [w.ap()[i] for i in range(2)]
    w = nc.dram_tensor("l0f_wo_d", [2, 512, 1024], F32, kind="ExternalInput")
    wd["l0f_wo"] = [w.ap()[i] for i in range(2)]
    w = nc.dram_tensor("l0f_qs_d", [2, 128, 1], F32, kind="ExternalInput")
    wd["l0f_qs"] = [w.ap()[i] for i in range(2)]
    w = nc.dram_tensor("l0f_ks_d", [2, 128, 1], F32, kind="ExternalInput")
    wd["l0f_ks"] = [w.ap()[i] for i in range(2)]
    wd["l0f_rot64"] = nc.dram_tensor("l0f_rot64_d", [128, 64], BF16, kind="ExternalInput").ap()
    wd["l0f_tq"] = nc.dram_tensor("l0f_tq_d", [128, 4 * T], BF16, kind="ExternalInput").ap()
    wd["l0f_mask"] = nc.dram_tensor("l0f_mask_d", [128, 4 * 1024], BF16, kind="ExternalInput").ap()
    w = nc.dram_tensor("l0f_lng_d", [2, 128, 8], F32, kind="ExternalInput")
    wd["l0f_lng"] = [w.ap()[i] for i in range(2)]
    w = nc.dram_tensor("l0f_lnb_d", [2, 128, 8], F32, kind="ExternalInput")
    wd["l0f_lnb"] = [w.ap()[i] for i in range(2)]
    w = nc.dram_tensor("l0f_w1p_d", [2, 1024, 5632], F32, kind="ExternalInput")
    wd["l0f_w1p"] = [w.ap()[i] for i in range(2)]
    w = nc.dram_tensor("l0f_b1a_d", [2, 128, 22], F32, kind="ExternalInput")
    wd["l0f_b1a"] = [w.ap()[i] for i in range(2)]
    w = nc.dram_tensor("l0f_b1g_d", [2, 128, 22], F32, kind="ExternalInput")
    wd["l0f_b1g"] = [w.ap()[i] for i in range(2)]
    w = nc.dram_tensor("l0f_w2p_d", [2 * 8, 2816, 128], F32, kind="ExternalInput")
    wd["l0f_w2p"] = [w.ap()[i] for i in range(16)]
    w = nc.dram_tensor("l0f_b2_d", [2, 128, 8], F32, kind="ExternalInput")
    wd["l0f_b2"] = [w.ap()[i] for i in range(2)]
    return wd


def prep_l0f(inputs):
    g = {}
    g["l0f_wqkv_d"] = _cc(inputs["L0_ffn_wqkv"].astype(np.float32))
    g["l0f_wo_d"] = _cc(inputs["L0_ffn_wo"].astype(np.float32))
    qs = inputs["L0_ffn_q_scale"]; ks = inputs["L0_ffn_k_scale"]
    g["l0f_qs_d"] = _cc(np.stack([np.tile(qs[l], 2).reshape(128, 1) for l in range(2)]))
    g["l0f_ks_d"] = _cc(np.stack([np.tile(ks[l], 2).reshape(128, 1) for l in range(2)]))
    R = np.zeros((64, 64), np.float32)
    for m in range(32):
        R[m + 32, m] = -1.0
        R[m, m + 32] = 1.0
    g["l0f_rot64_d"] = _cc(np.tile(R, (2, 1)).astype(bf16))
    half = np.arange(0, 64, 2, dtype=np.float32)
    inv_freq = 1.0 / 10000.0 ** (half / 64.0)
    t = np.arange(T, dtype=np.float32)
    freqs = t[:, None] * inv_freq[None, :]
    freqs = np.concatenate([freqs, freqs], -1)
    base_ = (half + 0.4 * 64) / (1.4 * 64)
    power = (t - T // 2) / 256.0
    scale = base_[None, :] ** power[:, None]
    scale = np.concatenate([scale, scale], -1)
    cos, sin = np.cos(freqs), np.sin(freqs)
    Tqc = (8.0 * cos * scale).T; Tqs = (8.0 * sin * scale).T
    Tkc = (cos / scale).T; Tks = (sin / scale).T
    tq = np.concatenate([np.tile(x, (2, 1)) for x in (Tqc, Tqs, Tkc, Tks)], 1)
    g["l0f_tq_d"] = _cc(tq.astype(bf16))
    ii = np.arange(512)[:, None]
    jj = np.arange(1024)[None, :]
    m = (jj < ii) | (jj > ii + 512)
    mask = np.where(m, np.float32(-3.0e38), np.float32(0.0))
    g["l0f_mask_d"] = _cc(mask.reshape(4, 128, 1024).transpose(1, 0, 2).reshape(128, 4096).astype(bf16))
    g["l0f_lng_d"] = _cc(inputs["L0_ffn_ln_g"].reshape(2, 8, 128).transpose(0, 2, 1))
    g["l0f_lnb_d"] = _cc(inputs["L0_ffn_ln_b"].reshape(2, 8, 128).transpose(0, 2, 1))
    w1 = inputs["L0_ffn_w1"]; b1 = inputs["L0_ffn_b1"]
    w1p = np.zeros((2, 1024, 5632), np.float32)
    b1a = np.zeros((2, 128, 22), np.float32); b1g = np.zeros((2, 128, 22), np.float32)
    for l in range(2):
        for p in range(22):
            n = min(128, 2730 - p * 128)
            w1p[l, :, p * 256:p * 256 + n] = w1[l, :, p * 128:p * 128 + n]
            w1p[l, :, p * 256 + 128:p * 256 + 128 + n] = w1[l, :, 2730 + p * 128:2730 + p * 128 + n]
            b1a[l, :n, p] = b1[l, p * 128:p * 128 + n]
            b1g[l, :n, p] = b1[l, 2730 + p * 128:2730 + p * 128 + n]
    g["l0f_w1p_d"] = _cc(w1p)
    g["l0f_b1a_d"] = _cc(b1a)
    g["l0f_b1g_d"] = _cc(b1g)
    w2 = inputs["L0_ffn_w2"]
    w2pad = np.zeros((2, 2816, 1024), np.float32)
    w2pad[:, :2730, :] = w2
    w2p = np.zeros((2 * 8, 2816, 128), np.float32)
    for l in range(2):
        for dc in range(8):
            w2p[l * 8 + dc] = w2pad[l, :, dc * 128:(dc + 1) * 128]
    g["l0f_w2p_d"] = _cc(w2p)
    g["l0f_b2_d"] = _cc(inputs["L0_ffn_b2"].reshape(2, 8, 128).transpose(0, 2, 1))
    return g



CONST_FLOATS = [1e-5, -0.5, 1.0 / 1024.0, 0.1, 1.0, 0.0, -1.0, 0.5, 2.0]


def alloc_consts(b, nc):
    t1 = nc.sbuf_tensor("c_ones_col_bf", [128, 1], BF16).__enter__()
    t2 = nc.sbuf_tensor("c_ones_row_f32", [1, 128], F32).__enter__()
    cp = nc.sbuf_tensor("c_pool", [128, len(CONST_FLOATS)], F32).__enter__()
    idf = nc.sbuf_tensor("c_ident_f32", [128, 128], F32).__enter__()
    t3 = nc.sbuf_tensor("c_ones_row_bf", [1, 128], BF16).__enter__()
    b.consts["ones_row_bf"] = t3
    idb = nc.sbuf_tensor("c_ident_bf", [128, 128], BF16).__enter__()
    b.consts["ident_bf"] = idb
    sel8 = nc.sbuf_tensor("c_sel8", [8, 8 * 128], BF16).__enter__()
    sel16 = nc.sbuf_tensor("c_sel16", [128, 16 * 128], BF16).__enter__()
    b.consts["ident_f32"] = idf
    b.consts["sel8"] = sel8
    b.consts["sel16"] = sel16
    b.consts["ones_col_bf"] = t1
    b.consts["ones_row_f32"] = t2
    b.consts["cp"] = cp
    b.wstg = nc.sbuf_tensor("c_wstg", [128, 2 * 512], F32).__enter__()
    for j, val in enumerate(CONST_FLOATS):
        nc.const_aps.register(F32, float(val), cp[:, j:j + 1])


def init_consts(b, const_dram=None):
    t1, t2, cp = b.consts["ones_col_bf"], b.consts["ones_row_f32"], b.consts["cp"]

    def ini(v):
        v.memset(t1[:, :], 1.0)
        v.memset(t2[0:1, :], 1.0)
        v.memset(b.consts["ones_row_bf"][0:1, :], 1.0)
        ins = None
        for j, val in enumerate(CONST_FLOATS):
            ins = v.memset(cp[:, j:j + 1], float(val))
        b.inc(ins, "dve")
    b.v(ini)
    if const_dram is not None:
        def ldc(qq):
            b.dma(qq, "qx", b.consts["ident_f32"][:, :], const_dram["ident_f32"])
            b.dma(qq, "qx", b.consts["ident_bf"][:, :], const_dram["ident_bf"])
            b.dma(qq, "qx", b.consts["sel8"][0:8, :], const_dram["sel8"])
            b.dma(qq, "qx", b.consts["sel16"][64:80, :], const_dram["sel16"])
        b.q(ldc)


def const_dram_decl(nc):
    return {
        "ident_f32": nc.dram_tensor("c_ident_f32_d", [128, 128], F32, kind="ExternalInput").ap(),
        "ident_bf": nc.dram_tensor("c_ident_bf_d", [128, 128], BF16, kind="ExternalInput").ap(),
        "sel8": nc.dram_tensor("c_sel8_d", [8, 8 * 128], BF16, kind="ExternalInput").ap(),
        "sel16": nc.dram_tensor("c_sel16_d", [16, 16 * 128], BF16, kind="ExternalInput").ap(),
    }


def const_host():
    sel8 = np.zeros((8, 8 * 128), np.float32)
    for e in range(8):
        sel8[e, e * 128:(e + 1) * 128] = 1.0
    sel16 = np.zeros((16, 16 * 128), np.float32)
    for n in range(16):
        sel16[n, n * 128:(n + 1) * 128] = 1.0
    return {"c_ident_f32_d": np.eye(128, dtype=np.float32), "c_ident_bf_d": np.eye(128, dtype=bf16),
            "c_sel8_d": sel8.astype(bf16), "c_sel16_d": sel16.astype(bf16)}


def l1a_dram(nc):
    wd = {
        "l1a_w_in": nc.dram_tensor("l1a_w_in", [1024, 4096], F32, kind="ExternalInput").ap(),
        "l1a_conv_w": nc.dram_tensor("l1a_conv_w", [128, 64], F32, kind="ExternalInput").ap(),
        "l1a_conv_b": nc.dram_tensor("l1a_conv_b", [128, 16], F32, kind="ExternalInput").ap(),
        "l1a_w_B": nc.dram_tensor("l1a_w_B", [128, 256], F32, kind="ExternalInput").ap(),
        "l1a_w_C": nc.dram_tensor("l1a_w_C", [16, 2048], F32, kind="ExternalInput").ap(),
        "l1a_w_out": nc.dram_tensor("l1a_w_out", [128, 16 * 1024], F32, kind="ExternalInput").ap(),
    }
    spill = {
        "l1a_xc": nc.dram_tensor("sp_l1a_xc", [2048, T], BF16, kind="Internal").ap(),
        "l1a_z": nc.dram_tensor("sp_l1a_z", [2048, T], BF16, kind="Internal").ap(),
    }
    return wd, spill


def prep_l1a(inputs):
    g = {}
    g["l1a_w_in"] = _cc(inputs["L1_attn_w_in"].astype(np.float32))
    cw = inputs["L1_attn_conv_w"].reshape(2048, 4)
    g["l1a_conv_w"] = _cc(cw.reshape(16, 128, 4).transpose(1, 0, 2).reshape(128, 64))
    g["l1a_conv_b"] = col_pack(inputs["L1_attn_conv_b"].reshape(2048))
    g["l1a_w_B"] = feat_tiles(inputs["L1_attn_w_B"])
    g["l1a_w_C"] = _cc(inputs["L1_attn_w_C"].astype(np.float32))
    g["l1a_w_out"] = feat_tiles(inputs["L1_attn_w_out"])
    return g


def build_l1_attn_standalone():
    nc = bass.Bass("TRN2", target_bir_lowering=False, debug=False)
    x_in = nc.dram_tensor("x_in", [D, T], F32, kind="ExternalInput").ap()
    x_out = nc.dram_tensor("x_out", [D, T], F32, kind="ExternalOutput").ap()
    wd, spill = l1a_dram(nc)
    b = B(nc)
    alloc_consts(b, nc)
    X = nc.sbuf_tensor("X", [128, 8 * T], F32).__enter__()
    with nc.Block() as block:
        b.block = block
        init_consts(b)

        def ldx(qq):
            for i in range(8):
                b.dma(qq, "qx", X[:, i * T:(i + 1) * T], x_in[i * 128:(i + 1) * 128, :])
        b.q(ldx)

        emit_l1_attn(b, X, wd, spill)

        def stx(qq):
            b.w(qq, "dve")
            for i in range(8):
                b.dma(qq, "qo", x_out[i * 128:(i + 1) * 128, :], X[:, i * T:(i + 1) * T])
            qq.wait_ge(b.sem["qo"], b.cnt["qo"])
        b.q(stx)
    return nc


def build_l0_attn_standalone(n_layers=8):
    nc = bass.Bass("TRN2", target_bir_lowering=False, debug=False)
    x_in = nc.dram_tensor("x_in", [D, T], F32, kind="ExternalInput").ap()
    x_out = nc.dram_tensor("x_out", [D, T], F32, kind="ExternalOutput").ap()
    xm_out = nc.dram_tensor("xm_out", [D, T], F32, kind="ExternalOutput").ap()
    cd = const_dram_decl(nc)
    wd, spill = l0a_dram(nc)
    spill["x_park"] = nc.dram_tensor("sp_x_park", [D, T], F32, kind="Internal").ap()
    b = B(nc)
    alloc_consts(b, nc)
    X = nc.sbuf_tensor("X", [128, 8 * T], F32).__enter__()
    with nc.Block() as block:
        b.block = block
        init_consts(b, cd)

        def ldx(qq):
            for i in range(8):
                b.dma(qq, "qx", X[:, i * T:(i + 1) * T], x_in[i * 128:(i + 1) * 128, :])
        b.q(ldx)

        emit_l0_attn(b, X, wd, spill, n_layers=n_layers)

        def stx(qq):
            b.w(qq, "dve")
            for i in range(8):
                b.dma(qq, "qo", x_out[i * 128:(i + 1) * 128, :], X[:, i * T:(i + 1) * T])
            qq.wait_ge(b.sem["qo"], b.cnt["qo"])
        b.q(stx)
    return nc


def build_l0_ffn_standalone(n_layers=2):
    nc = bass.Bass("TRN2", target_bir_lowering=False, debug=False)
    x_in = nc.dram_tensor("x_in", [D, T], F32, kind="ExternalInput").ap()
    x_out = nc.dram_tensor("x_out", [D, T], F32, kind="ExternalOutput").ap()
    cd = const_dram_decl(nc)
    wd = l0f_dram(nc)
    spill = {"x_park": nc.dram_tensor("sp_x_park", [D, T], F32, kind="Internal").ap()}
    b = B(nc)
    alloc_consts(b, nc)
    X = nc.sbuf_tensor("X", [128, 8 * T], F32).__enter__()
    with nc.Block() as block:
        b.block = block
        init_consts(b, cd)

        def ldx(qq):
            for i in range(8):
                b.dma(qq, "qx", X[:, i * T:(i + 1) * T], x_in[i * 128:(i + 1) * 128, :])
        b.q(ldx)

        emit_l0_ffn(b, X, wd, spill, n_layers=n_layers)

        def stx(qq):
            b.w(qq, "dve")
            for i in range(8):
                b.dma(qq, "qo", x_out[i * 128:(i + 1) * 128, :], X[:, i * T:(i + 1) * T])
            qq.wait_ge(b.sem["qo"], b.cnt["qo"])
        b.q(stx)
    return nc


def build_l1_ffn_standalone():
    nc = bass.Bass("TRN2", target_bir_lowering=False, debug=False)
    x_in = nc.dram_tensor("x_in", [D, T], F32, kind="ExternalInput").ap()
    x_out = nc.dram_tensor("x_out", [D, T], F32, kind="ExternalOutput").ap()
    cd = const_dram_decl(nc)
    wd = l1f_dram(nc)
    spill = {"x_park": nc.dram_tensor("sp_x_park", [D, T], F32, kind="Internal").ap()}
    b = B(nc)
    alloc_consts(b, nc)
    X = nc.sbuf_tensor("X", [128, 8 * T], F32).__enter__()
    with nc.Block() as block:
        b.block = block
        init_consts(b, cd)

        def ldx(qq):
            for i in range(8):
                b.dma(qq, "qx", X[:, i * T:(i + 1) * T], x_in[i * 128:(i + 1) * 128, :])
        b.q(ldx)

        emit_l1_ffn(b, X, wd, spill)

        def stx(qq):
            b.w(qq, "dve")
            for i in range(8):
                b.dma(qq, "qo", x_out[i * 128:(i + 1) * 128, :], X[:, i * T:(i + 1) * T])
            qq.wait_ge(b.sem["qo"], b.cnt["qo"])
        b.q(stx)
    return nc


def build_full():
    nc = bass.Bass("TRN2", target_bir_lowering=False, debug=False)
    x_in = nc.dram_tensor("x_in", [D, T], F32, kind="ExternalInput").ap()
    x_out = nc.dram_tensor("x_out", [D, T], F32, kind="ExternalOutput").ap()
    cd = const_dram_decl(nc)
    wd0a, sp0a = l0a_dram(nc)
    wd0f = l0f_dram(nc)
    wd1a, sp1a = l1a_dram(nc)
    wd1f = l1f_dram(nc)
    spill = dict(sp0a)
    spill.update(sp1a)
    spill["x_park"] = nc.dram_tensor("sp_x_park", [D, T], F32, kind="Internal").ap()
    b = B(nc)
    alloc_consts(b, nc)
    X = nc.sbuf_tensor("X", [128, 8 * T], F32).__enter__()
    with nc.Block() as block:
        b.block = block
        init_consts(b, cd)

        def ldx(qq):
            for i in range(8):
                b.dma(qq, "qx", X[:, i * T:(i + 1) * T], x_in[i * 128:(i + 1) * 128, :])
        b.q(ldx)

        emit_l0_attn(b, X, wd0a, spill)
        emit_l0_ffn(b, X, wd0f, spill)
        emit_l1_attn(b, X, wd1a, spill)
        emit_l1_ffn(b, X, wd1f, spill)

        def stx(qq):
            b.w(qq, "dve", "act", "pe")
            for i in range(8):
                b.dma(qq, "qo", x_out[i * 128:(i + 1) * 128, :], X[:, i * T:(i + 1) * T])
            qq.wait_ge(b.sem["qo"], b.cnt["qo"])
        b.q(stx)
    return nc


_BUILD_CACHE = {}


def _get_nc(name, builder):
    if name not in _BUILD_CACHE:
        _BUILD_CACHE[name] = builder()
    return _BUILD_CACHE[name]


def kernel(**inputs):
    wmap = {}
    wmap.update(const_host())
    wmap.update(prep_l0a(inputs))
    wmap.update(prep_l0f(inputs))
    wmap.update(prep_l1a(inputs))
    wmap.update(prep_l1f(inputs))
    x = np.ascontiguousarray(inputs["L0_attn_x"].astype(np.float32))
    xs = [np.ascontiguousarray(x[c].T) for c in range(NCORES)]
    nc = _get_nc("full", build_full)
    in_maps = [dict(wmap, x_in=xs[c]) for c in range(NCORES)]
    res = run_bass_kernel_spmd(nc, in_maps, core_ids=list(range(NCORES)))
    out = np.stack([res.results[c]["x_out"].T for c in range(NCORES)], axis=0)
    return np.ascontiguousarray(out.astype(np.float32))


# version 09
# speedup vs baseline: 1.1863x; 1.0019x over previous
import contextlib
import numpy as np
import ml_dtypes
import concourse.bass as bass
import concourse.mybir as mybir
from concourse.bass_utils import run_bass_kernel_spmd

F32 = mybir.dt.float32
BF16 = mybir.dt.bfloat16
AF = mybir.ActivationFunctionType
ALU = mybir.AluOpType
AX = mybir.AxisListType

T = 2048
D = 1024
NCORES = 8
TC = 512
NTC = T // TC
bf16 = ml_dtypes.bfloat16

SEMS = ("pe", "act", "dve", "pool", "qx", "qw", "qw2", "qsp", "qsl", "qz0", "qz1", "qo")


class B:
    def __init__(self, nc):
        self.nc = nc
        self.cnt = {s: 0 for s in SEMS}
        self.sem = {s: nc.semaphore(s).__enter__() for s in SEMS}
        self.block = None
        self.consts = {}

    def v(self, f):
        self.block.vector(f)

    def s(self, f):
        self.block.scalar(f)

    def t(self, f):
        self.block.tensor(f)

    def g(self, f):
        self.block.gpsimd(f)

    def q(self, f):
        self.block.sync(f)

    def inc(self, instr, name, k=1):
        instr.then_inc(self.sem[name], k)
        self.cnt[name] += k

    def dma(self, qeng, qname, out, in_):
        ins = qeng.dma_start(out=out, in_=in_)
        self.inc(ins, qname, 16)

    def w(self, eng, *names):
        for n in names:
            c = self.cnt[n]
            if c > 0:
                eng.wait_ge(self.sem[n], c)

    def wv(self, eng, name, val):
        if val > 0:
            eng.wait_ge(self.sem[name], val)

    def scope(self):
        return contextlib.ExitStack()

    def fence(self):
        def f(g):
            self.w(g, "pe", "act", "dve")
            ins = g.engine_nop()
            self.inc(ins, "pool")
        self.g(f)

    def sb(self, es, name, shape, dt):
        return es.enter_context(self.nc.sbuf_tensor(name, shape, dt))

    def pt(self, es, name, shape, dt):
        return es.enter_context(self.nc.psum_tensor(name, shape, dt))



def _cc(a):
    return np.ascontiguousarray(a)


def feat_tiles(w, P=128):
    F, M = w.shape
    assert F % P == 0
    return _cc(w.reshape(F // P, P, M).transpose(1, 0, 2).reshape(P, (F // P) * M))


def col_pack(vec, P=128):
    F = vec.shape[0]
    return _cc(vec.reshape(F // P, P).T)



def ln_emit(b, src_of, src_is_bf, rows, tag, ntiles=8, nfeat=None, eps=1e-5):
    nc = b.nc
    es = b.scope()
    scr8a = b.sb(es, tag + "_lncast", [128, ntiles * TC], BF16)
    scr8b = b.sb(es, tag + "_lnsq", [128, ntiles * TC], BF16)
    rowm = b.sb(es, tag + "_lnrowm", [1, T], F32)
    rowv = b.sb(es, tag + "_lnrowv", [1, T], F32)
    ps_sx = b.pt(es, tag + "_lnpsx", [1, TC], F32)
    ps_sq = b.pt(es, tag + "_lnpsq", [1, TC], F32)
    cast = [scr8a[:, i * TC:(i + 1) * TC] for i in range(ntiles)]
    sq = [scr8b[:, i * TC:(i + 1) * TC] for i in range(ntiles)]

    nfeat = nfeat or (128 * ntiles)
    inv = 1.0 / nfeat
    for tc in range(NTC):
        lo, hi = tc * TC, (tc + 1) * TC

        def casts(s, lo=lo, hi=hi):
            b.w(s, "pe", "dve", "qx", "pool")
            ins = None
            for i in range(ntiles):
                if not src_is_bf:
                    s.copy(cast[i][:, 0:TC], src_of(i)[:, lo:hi])
                ins = s.activation(sq[i][:, 0:TC], src_of(i)[:, lo:hi], AF.Square)
            b.inc(ins, "act")
        b.s(casts)

        def mms(t, lo=lo, hi=hi):
            b.w(t, "act")
            for i in range(ntiles):
                rhs = src_of(i)[:, lo:hi] if src_is_bf else cast[i][:, 0:TC]
                t.matmul(ps_sx[0:1, 0:TC], b.consts["ones_col_bf"][:, 0:1], rhs,
                         start=(i == 0), stop=(i == ntiles - 1))
            for i in range(ntiles):
                ins = t.matmul(ps_sq[0:1, 0:TC], b.consts["ones_col_bf"][:, 0:1],
                               sq[i][:, 0:TC], start=(i == 0), stop=(i == ntiles - 1))
            b.inc(ins, "pe")
        b.t(mms)

        def rowc(s, lo=lo, hi=hi):
            b.w(s, "pe")
            s.mul(rowm[0:1, lo:hi], ps_sx[0:1, 0:TC], inv)
            ins = s.mul(rowv[0:1, lo:hi], ps_sq[0:1, 0:TC], inv)
            b.inc(ins, "act")
        b.s(rowc)

    def rowfin_v(v):
        b.w(v, "act")
        v.tensor_mul(rows[0:1, T:2 * T], rowm[0:1, :], rowm[0:1, :])
        ins = v.tensor_sub(rowv[0:1, :], rowv[0:1, :], rows[0:1, T:2 * T])
        b.inc(ins, "dve")
    b.v(rowfin_v)

    def rowfin_s(s):
        b.w(s, "dve")
        s.activation(rows[0:1, T:2 * T], rowv[0:1, :], AF.Ln, bias=eps)
        ins = s.activation(rows[0:1, 0:T], rows[0:1, T:2 * T], AF.Exp, scale=-0.5)
        b.inc(ins, "act")
    b.s(rowfin_s)

    def nm2_v(v):
        b.w(v, "act")
        ins = v.scalar_tensor_tensor(rows[0:1, T:2 * T], rowm[0:1, :], -1.0, rows[0:1, 0:T],
                                     ALU.mult, ALU.mult)
        b.inc(ins, "dve")
    b.v(nm2_v)
    es.close()


def ln_bcast(b, rows, lo, n, ps_r, ps_m):
    def mm(t):
        b.w(t, "dve", "act")
        t.matmul(ps_r[:, 0:n], b.consts["ones_row_bf"][0:1, :], rows[0:1, lo:lo + n],
                 start=True, stop=True)
        ins = t.matmul(ps_m[:, 0:n], b.consts["ones_row_bf"][0:1, :], rows[0:1, T + lo:T + lo + n],
                       start=True, stop=True)
        b.inc(ins, "pe")
    b.t(mm)


def emit_wload(b, jobs, direct=(), war=None):
    stg = b.wstg
    if not hasattr(b, "_stgfree"):
        b._stgfree = [0, 0]
    jobs2 = []
    for dst, src in jobs:
        n = src.shape[1]
        o = 0
        while o < n:
            m = min(512, n - o)
            jobs2.append((dst[:, o:o + m], src[:, o:o + m]))
            o += m
    for idx, (dst, src) in enumerate(jobs2):
        par = idx % 2
        qsem = "qw" if par == 0 else "qw2"

        def ldq(qq, par=par, src=src, qsem=qsem):
            b.wv(qq, "pool", b._stgfree[par])
            ins = qq.dma_start(out=stg[0:src.shape[0], par * 512:par * 512 + src.shape[1]], in_=src)
            b.inc(ins, qsem, 16)
        b.q(ldq)

        def cst(g, par=par, dst=dst, p=src.shape[0], n=src.shape[1], qsem=qsem):
            b.w(g, qsem)
            if war is not None:
                b.wv(g, "pe", war)
            else:
                b.w(g, "pe", "act", "dve")
            ins = g.tensor_copy(dst, stg[0:p, par * 512:par * 512 + n])
            b.inc(ins, "pool")
        b.g(cst)
        b._stgfree[par] = b.cnt["pool"]

    if direct:
        def ldd(qq):
            b.w(qq, "pool")
            for dst, src in direct:
                b.dma(qq, "qw", dst, src)
        b.q(ldd)

    def qwbar(g):
        b.w(g, "qw", "qw2")
        ins = g.engine_nop()
        b.inc(ins, "pool")
    b.g(qwbar)



def _l1a_xln_apply(b, tc, X, rows, ps_r, ps_m, xln, f32scr):
    lo, hi = tc * TC, (tc + 1) * TC
    ln_bcast(b, rows, lo, TC, ps_r, ps_m)

    def apply(v):
        b.w(v, "pe")
        ins = None
        for i in range(8):
            v.tensor_mul(f32scr[:, 0:TC], X[:, i * T + lo:i * T + hi], ps_r[:, 0:TC])
            ins = v.tensor_add(xln[:, i * TC:(i + 1) * TC], f32scr[:, 0:TC], ps_m[:, 0:TC])
        b.inc(ins, "dve")
    b.v(apply)


def _l1a_phaseA_xp_chunk(b, tc, X, rows, ps_r, ps_m, ps_a, ps_b, wi, xpb, xln,
                         xst, cw, cb, acc, f32scr, xc_sp):
    lo = tc * TC
    _l1a_xln_apply(b, tc, X, rows, ps_r, ps_m, xln, f32scr)

    def halo(v):
        ins = None
        for ct in range(16):
            if tc == 0:
                ins = v.memset(xpb[:, ct * 520:ct * 520 + 3], 0.0)
            else:
                ins = v.tensor_copy(xpb[:, ct * 520:ct * 520 + 3],
                                    xpb[:, ct * 520 + 512:ct * 520 + 515])
        b.inc(ins, "dve")
    b.v(halo)

    for ec in range(16):
        ps = ps_a if ec % 2 == 0 else ps_b

        def mm(t, ec=ec, ps=ps):
            b.w(t, "pool", "dve")
            ins = None
            for dc in range(8):
                ins = t.matmul(ps[:, 0:TC],
                               wi[:, dc * 2048 + ec * 128:dc * 2048 + (ec + 1) * 128],
                               xln[:, dc * TC:(dc + 1) * TC],
                               start=(dc == 0), stop=(dc == 7))
            b.inc(ins, "pe")
        b.t(mm)

        def ev(v, ec=ec, ps=ps):
            b.w(v, "pe")
            ins = v.tensor_copy(xpb[:, ec * 520 + 3:ec * 520 + 515], ps[:, 0:TC])
            b.inc(ins, "dve")
        b.v(ev)

    for ct in range(16):
        def conv(v, ct=ct):
            b.w(v, "act", "qw")
            base = ct * 520
            v.tensor_scalar(acc[:, 0:TC], xpb[:, base + 3:base + 3 + TC],
                            cw[:, ct * 4 + 3:ct * 4 + 4], cb[:, ct:ct + 1],
                            ALU.mult, ALU.add)
            ins = None
            for k in (0, 1, 2):
                ins = v.scalar_tensor_tensor(acc[:, 0:TC], xpb[:, base + k:base + k + TC],
                                             cw[:, ct * 4 + k:ct * 4 + k + 1], acc[:, 0:TC],
                                             ALU.mult, ALU.add)
            b.inc(ins, "dve")
        b.v(conv)

        def sil(s, ct=ct):
            b.w(s, "dve", "qsp")
            ins = s.activation(xst[:, (ct % 2) * TC:(ct % 2 + 1) * TC], acc[:, 0:TC], AF.Silu)
            b.inc(ins, "act")
        b.s(sil)

        def spxc(qq, ct=ct):
            b.w(qq, "act")
            b.dma(qq, "qsp", xc_sp[ct * 128:(ct + 1) * 128, lo:lo + TC],
                  xst[:, (ct % 2) * TC:(ct % 2 + 1) * TC])
        b.q(spxc)


def _l1a_phaseA_z_chunk(b, tc, X, rows, ps_r, ps_m, ps_a, ps_b, wi, xln,
                        zst, f32scr, z_sp):
    lo = tc * TC
    _l1a_xln_apply(b, tc, X, rows, ps_r, ps_m, xln, f32scr)

    for ec in range(16):
        ps = ps_a if ec % 2 == 0 else ps_b
        par = ec % 2

        def mm(t, ec=ec, ps=ps):
            b.w(t, "pool", "dve")
            ins = None
            for dc in range(8):
                ins = t.matmul(ps[:, 0:TC],
                               wi[:, dc * 2048 + ec * 128:dc * 2048 + (ec + 1) * 128],
                               xln[:, dc * TC:(dc + 1) * TC],
                               start=(dc == 0), stop=(dc == 7))
            b.inc(ins, "pe")
        b.t(mm)

        def ev(v, ec=ec, ps=ps, par=par):
            b.w(v, "pe", "qsp")
            ins = v.tensor_copy(zst[:, par * TC:(par + 1) * TC], ps[:, 0:TC])
            b.inc(ins, "dve")
        b.v(ev)

        def spz(qq, ec=ec, par=par):
            b.w(qq, "dve")
            b.dma(qq, "qsp", z_sp[ec * 128:(ec + 1) * 128, lo:lo + TC],
                  zst[:, par * TC:(par + 1) * TC])
        b.q(spz)


def _l1a_phaseB_chunk(b, tc, ps_a, ps_b, ob, wo, wB, wC, xcc, zst, yg, nine, hsc,
                      hbf, xbs, zsil, xc_sp, z_sp):
    lo = tc * TC
    cur, prev = tc % 2, (tc + 1) % 2

    def ld(qq):
        b.w(qq, "pe", "act", "dve", "qsp")
        for ct in range(16):
            b.dma(qq, "qsl", xcc[:, ct * TC:(ct + 1) * TC],
                  xc_sp[ct * 128:(ct + 1) * 128, lo:lo + TC])
    b.q(ld)

    def mmxb(t):
        b.w(t, "qsl", "pool", "act")
        ins = None
        for ct in range(16):
            ins = t.matmul(ps_a[0:16, 0:TC], wB[:, ct * 16:(ct + 1) * 16],
                           xcc[:, ct * TC:(ct + 1) * TC], start=(ct == 0), stop=(ct == 15))
        b.inc(ins, "pe")
    b.t(mmxb)

    def evxb(s):
        b.w(s, "pe")
        ins = s.mul(xbs[0:16, 0:TC], ps_a[0:16, 0:TC], 0.1)
        b.inc(ins, "act")
    b.s(evxb)

    def scan(v):
        b.w(v, "act")
        init = 0.0 if tc == 0 else hsc[0:16, prev * TC + TC - 1:prev * TC + TC]
        v.tensor_tensor_scan(hsc[0:16, cur * TC:(cur + 1) * TC], nine[0:16, 0:TC],
                             xbs[0:16, 0:TC], init, ALU.mult, ALU.add)
        ins = v.tensor_copy(hbf[0:16, 0:TC], hsc[0:16, cur * TC:(cur + 1) * TC])
        b.inc(ins, "dve")
    b.v(scan)

    for ec in range(16):
        ps = ps_a if ec % 2 == 0 else ps_b
        par = ec % 2

        def ldz(qq, ec=ec, par=par):
            b.w(qq, "act", "pe")
            b.dma(qq, "qsl", zst[:, par * TC:(par + 1) * TC],
                  z_sp[ec * 128:(ec + 1) * 128, lo:lo + TC])
        b.q(ldz)

        def mmy(t, ec=ec, ps=ps):
            b.w(t, "dve")
            ins = t.matmul(ps[:, 0:TC], wC[0:16, ec * 128:(ec + 1) * 128],
                           hbf[0:16, 0:TC], start=True, stop=True)
            b.inc(ins, "pe")
        b.t(mmy)

        def silz(s, ec=ec, par=par):
            b.w(s, "qsl", "dve")
            ins = s.activation(zsil[:, 0:TC], zst[:, par * TC:(par + 1) * TC], AF.Silu)
            b.inc(ins, "act")
        b.s(silz)

        def gate(v, ec=ec, ps=ps):
            b.w(v, "pe", "act")
            ins = v.tensor_mul(yg[:, ec * TC:(ec + 1) * TC], ps[:, 0:TC], zsil[:, 0:TC])
            b.inc(ins, "dve")
        b.v(gate)

    for dc in range(8):
        ps = ps_a if dc % 2 == 0 else ps_b

        def mmo(t, dc=dc, ps=ps):
            b.w(t, "dve")
            ins = None
            for ct in range(16):
                ins = t.matmul(ps[:, 0:TC],
                               wo[:, ct * 1024 + dc * 128:ct * 1024 + (dc + 1) * 128],
                               yg[:, ct * TC:(ct + 1) * TC], start=(ct == 0), stop=(ct == 15))
            b.inc(ins, "pe")
        b.t(mmo)

        def evo(v, dc=dc, ps=ps):
            b.w(v, "pe")
            ins = v.tensor_copy(ob[:, dc * T + lo:dc * T + lo + TC], ps[:, 0:TC])
            b.inc(ins, "dve")
        b.v(evo)


def emit_residual_add(b, X, ob, rows, ps_r, ps_m, f32scr):
    for tc in range(NTC):
        lo = tc * TC
        ln_bcast(b, rows, lo, TC, ps_r, ps_m)

        def addres(v, lo=lo):
            b.w(v, "pe")
            ins = None
            for i in range(8):
                v.tensor_mul(f32scr[:, 0:TC], ob[:, i * T + lo:i * T + lo + TC], ps_r[:, 0:TC])
                v.tensor_add(f32scr[:, 0:TC], f32scr[:, 0:TC], ps_m[:, 0:TC])
                ins = v.tensor_add(X[:, i * T + lo:i * T + lo + TC],
                                   X[:, i * T + lo:i * T + lo + TC], f32scr[:, 0:TC])
            b.inc(ins, "dve")
        b.v(addres)


def emit_l1_attn(b, X, wd, spill, dbg=None, upto='full'):
    nc = b.nc
    xc_sp, z_sp = spill["l1a_xc"], spill["l1a_z"]
    es0 = b.scope()
    rows = b.sb(es0, "l1a_rows", [1, 2 * T], BF16)
    ps_r = b.pt(es0, "l1a_ps_r", [128, TC], F32)
    ps_m = b.pt(es0, "l1a_ps_m", [128, TC], F32)
    ps_a = b.pt(es0, "l1a_ps_a", [128, TC], F32)
    ps_b = b.pt(es0, "l1a_ps_b", [128, TC], F32)
    f32scr = b.sb(es0, "l1a_f32scr", [128, TC], F32)

    ln_emit(b, lambda i: X[:, i * T:(i + 1) * T], False, rows, "l1a_in")

    if dbg is not None and "rows" in dbg:
        def dmprows(qq):
            b.w(qq, "dve", "act")
            b.dma(qq, "qo", dbg["rows"], rows[0:1, :])
        b.q(dmprows)

    esA = b.scope()
    wi = b.sb(esA, "l1a_wi", [128, 8 * 2048], BF16)
    xpb = b.sb(esA, "l1a_xpb", [128, 16 * 520], BF16)
    xln = b.sb(esA, "l1a_xln", [128, 8 * TC], BF16)
    zst = b.sb(esA, "l1a_zst", [128, 2 * TC], BF16)
    xst = b.sb(esA, "l1a_xst", [128, 2 * TC], BF16)
    cw = b.sb(esA, "l1a_cw", [128, 64], F32)
    cb = b.sb(esA, "l1a_cb", [128, 16], F32)
    acc = b.sb(esA, "l1a_acc", [128, TC], F32)

    emit_wload(b, [(wi[:, i * 2048:(i + 1) * 2048],
                    wd["l1a_w_in"][i * 128:(i + 1) * 128, 0:2048]) for i in range(8)],
               direct=[(cw[:, :], wd["l1a_conv_w"]), (cb[:, :], wd["l1a_conv_b"])])

    for tc in range(NTC if upto != 'xp0' else 1):
        _l1a_phaseA_xp_chunk(b, tc, X, rows, ps_r, ps_m, ps_a, ps_b, wi, xpb,
                             xln, xst, cw, cb, acc, f32scr, xc_sp)
        if tc == 0 and dbg is not None and "xln" in dbg:
            def dmpxln(qq):
                b.w(qq, "dve", "act")
                b.dma(qq, "qo", dbg["xln"], xln[:, :])
                b.dma(qq, "qo", dbg["xpb"], xpb[:, :])
            b.q(dmpxln)

    if upto == 'xp0':
        esA.close()
        es0.close()
        return

    def fence_z(g):
        b.w(g, "pe", "act", "dve")
        ins = g.engine_nop()
        b.inc(ins, "pool")
    b.g(fence_z)
    emit_wload(b, [(wi[:, i * 2048:(i + 1) * 2048],
                    wd["l1a_w_in"][i * 128:(i + 1) * 128, 2048:4096]) for i in range(8)])

    for tc in range(NTC):
        _l1a_phaseA_z_chunk(b, tc, X, rows, ps_r, ps_m, ps_a, ps_b, wi, xln,
                            zst, f32scr, z_sp)
    esA.close()
    b.fence()

    esOB = b.scope()
    ob = b.sb(esOB, "l1a_ob", [128, 8 * T], BF16)

    esB = b.scope()
    wo = b.sb(esB, "l1a_wo", [128, 16 * 1024], BF16)
    wB = b.sb(esB, "l1a_wB", [128, 256], BF16)
    wC = b.sb(esB, "l1a_wC", [16, 2048], BF16)
    xcc = b.sb(esB, "l1a_xcc2", [128, 16 * TC], BF16)
    zst2 = b.sb(esB, "l1a_zst2", [128, 2 * TC], BF16)
    yg = b.sb(esB, "l1a_yg", [128, 16 * TC], BF16)
    nine = b.sb(esB, "l1a_nine", [16, TC], F32)
    hsc = b.sb(esB, "l1a_hsc", [16, 2 * TC], F32)
    hbf = b.sb(esB, "l1a_hbf", [16, TC], BF16)
    xbs = b.sb(esB, "l1a_xbs", [16, TC], F32)
    zsil = b.sb(esB, "l1a_zsil", [128, TC], F32)

    emit_wload(b, [(wo[:, i * 1024:(i + 1) * 1024],
                    wd["l1a_w_out"][:, i * 1024:(i + 1) * 1024]) for i in range(16)]
               + [(wB[:, :], wd["l1a_w_B"]), (wC[0:16, :], wd["l1a_w_C"])])

    def mset(v):
        ins = v.memset(nine[:, :], 0.9)
        b.inc(ins, "dve")
    b.v(mset)

    for tc in range(NTC):
        _l1a_phaseB_chunk(b, tc, ps_a, ps_b, ob, wo, wB, wC, xcc, zst2, yg,
                          nine, hsc, hbf, xbs, zsil, xc_sp, z_sp)
    esB.close()

    if dbg is not None and "ob" in dbg:
        def dmpob(qq):
            b.w(qq, "dve", "act")
            for i in range(8):
                b.dma(qq, "qo", dbg["ob"][i * 128:(i + 1) * 128, :], ob[:, i * T:(i + 1) * T])
        b.q(dmpob)
    ln_emit(b, lambda i: ob[:, i * T:(i + 1) * T], True, rows, "l1a_out")
    emit_residual_add(b, X, ob, rows, ps_r, ps_m, f32scr)
    esOB.close()
    es0.close()



def _l1f_router(b, xln, wd, combT):
    nc = b.nc
    es = b.scope()
    rwT = b.sb(es, "l1f_rwT_sb", [128, 8 * 8], BF16)
    ebias = b.sb(es, "l1f_eb_sb", [1, 8], F32)
    lgt = b.sb(es, "l1f_lgt", [128, 8], F32)
    eq1 = b.sb(es, "l1f_eq1", [128, 8], F32)
    eq2 = b.sb(es, "l1f_eq2", [128, 8], F32)
    l2t = b.sb(es, "l1f_l2t", [128, 8], F32)
    m1 = b.sb(es, "l1f_m1", [128, 4], F32)
    eqbf = b.sb(es, "l1f_eqbf", [128, 8], BF16)
    ps_s = b.pt(es, "l1f_ps_s", [128, 8], F32)
    ps_t = b.pt(es, "l1f_ps_t", [128, 128], F32)
    ps_tb = b.pt(es, "l1f_ps_tb", [8, 128], BF16)

    ebrep = b.sb(es, "l1f_ebrep", [128, 8], F32)
    emit_wload(b, [(rwT[:, i * 8:(i + 1) * 8], wd["l1f_rwT"][i * 128:(i + 1) * 128, :]) for i in range(8)],
               direct=[(ebias[0:1, :], wd["l1f_eb"])])

    def bias_bcast(t):
        b.w(t, "qw", "dve")
        ins = t.matmul(ps_t[:, 0:8], b.consts["ones_row_f32"][0:1, :], ebias[0:1, 0:8],
                       start=True, stop=True)
        b.inc(ins, "pe")
    b.t(bias_bcast)

    def evb(v):
        b.w(v, "pe")
        ins = v.tensor_copy(ebrep[:, 0:8], ps_t[:, 0:8])
        b.inc(ins, "dve")
    b.v(evb)

    for tt in range(16):
        lo = tt * 128

        def mml(t, lo=lo):
            b.w(t, "pool", "dve", "qw")
            ins = None
            for dc in range(8):
                ins = t.matmul(ps_s[:, 0:8], xln[:, dc * T + lo:dc * T + lo + 128],
                               rwT[:, dc * 8:(dc + 1) * 8], start=(dc == 0), stop=(dc == 7))
            b.inc(ins, "pe")
        b.t(mml)

        def top2(v):
            b.w(v, "pe")
            ops = [
                lambda: v.tensor_add(lgt[:, 0:8], ps_s[:, 0:8], ebrep[:, 0:8]),
                lambda: v.reduce_max(m1[:, 0:1], lgt[:, 0:8], AX.X),
                lambda: v.tensor_scalar(eq1[:, 0:8], lgt[:, 0:8], m1[:, 0:1], None, ALU.is_ge),
                lambda: v.scalar_tensor_tensor(l2t[:, 0:8], eq1[:, 0:8], -1e30, lgt[:, 0:8], ALU.mult, ALU.add),
                lambda: v.reduce_max(m1[:, 1:2], l2t[:, 0:8], AX.X),
                lambda: v.tensor_scalar(eq2[:, 0:8], l2t[:, 0:8], m1[:, 1:2], None, ALU.is_ge),
                lambda: v.tensor_sub(m1[:, 2:3], m1[:, 1:2], m1[:, 0:1]),
            ]
            for op in ops:
                ins = op()
                b.inc(ins, "dve")
                v.wait_ge(b.sem["dve"], b.cnt["dve"])
        b.v(top2)

        def sig(s):
            b.w(s, "dve")
            ins = s.activation(m1[:, 3:4], m1[:, 2:3], AF.Sigmoid)
            b.inc(ins, "act")
        b.s(sig)

        def comb(v):
            b.w(v, "act")
            ops = [
                lambda: v.tensor_scalar(m1[:, 2:3], m1[:, 3:4], -1.0, 1.0, ALU.mult, ALU.add),
                lambda: v.tensor_scalar(eq1[:, 0:8], eq1[:, 0:8], m1[:, 2:3], None, ALU.mult),
                lambda: v.scalar_tensor_tensor(eq1[:, 0:8], eq2[:, 0:8], m1[:, 3:4], eq1[:, 0:8],
                                               ALU.mult, ALU.add),
                lambda: v.tensor_copy(eqbf[:, 0:8], eq1[:, 0:8]),
            ]
            for op in ops:
                ins = op()
                b.inc(ins, "dve")
                v.wait_ge(b.sem["dve"], b.cnt["dve"])
        b.v(comb)

        def tr(t):
            b.w(t, "dve", "qx")
            ins = t.transpose(ps_tb[0:8, 0:128], eqbf[:, 0:8], b.consts["ident_bf"][:, :])
            b.inc(ins, "pe")
        b.t(tr)

        def evtr(v, lo=lo):
            b.w(v, "pe")
            ins = v.tensor_copy(combT[0:8, lo:lo + 128], ps_tb[0:8, 0:128])
            b.inc(ins, "dve")
        b.v(evtr)
    es.close()
    b.fence()


def emit_l1_ffn(b, X, wd, spill):
    nc = b.nc
    x_sp = spill["x_park"]
    es0 = b.scope()
    ps_r = b.pt(es0, "l1f_ps_r", [128, TC], F32)
    ps_m = b.pt(es0, "l1f_ps_m", [128, TC], F32)
    ps_a = b.pt(es0, "l1f_ps_a", [128, TC], F32)
    ps_b = b.pt(es0, "l1f_ps_b", [128, TC], F32)
    ps_c = b.pt(es0, "l1f_ps_c", [128, TC], F32)
    f32scr = b.sb(es0, "l1f_f32scr", [128, TC], F32)
    combT = b.sb(es0, "l1f_combT", [8, T], BF16)
    xln = b.sb(es0, "l1f_xln", [128, 8 * T], BF16)

    esR = b.scope()
    rows = b.sb(esR, "l1f_rows", [1, 2 * T], BF16)
    ln_emit(b, lambda i: X[:, i * T:(i + 1) * T], False, rows, "l1f_in")

    for tc in range(NTC):
        lo, hi = tc * TC, (tc + 1) * TC
        ln_bcast(b, rows, lo, TC, ps_r, ps_m)

        def apply(v, lo=lo, hi=hi):
            b.w(v, "pe")
            ins = None
            for i in range(8):
                v.tensor_mul(f32scr[:, 0:TC], X[:, i * T + lo:i * T + hi], ps_r[:, 0:TC])
                ins = v.tensor_add(xln[:, i * T + lo:i * T + hi], f32scr[:, 0:TC], ps_m[:, 0:TC])
            b.inc(ins, "dve")
        b.v(apply)

    _l1f_router(b, xln, wd, combT)
    esR.close()
    b.fence()

    def park(qq):
        b.w(qq, "dve", "act", "pe")
        for i in range(8):
            b.dma(qq, "qsp", x_sp[i * 128:(i + 1) * 128, :], X[:, i * T:(i + 1) * T])
    b.q(park)

    esE = b.scope()
    obacc = X
    hgbuf = b.sb(esE, "l1f_hg", [128, 16 * T], BF16)
    w13 = b.sb(esE, "l1f_w13", [128, 2 * 4096], BF16)
    w2s = b.sb(esE, "l1f_w2s", [128, 2 * 2048], BF16)
    cetile = b.sb(esE, "l1f_cetile", [128, T], BF16)
    silu2 = [f32scr, b.sb(esE, "l1f_silu1", [128, TC], F32)]
    ps_d = b.pt(esE, "l1f_ps_d", [128, TC], F32)
    ps_e = b.pt(esE, "l1f_ps_e", [128, TC], F32)

    b.l1f_war13 = [0, 0]
    b.l1f_war2 = [0, 0]
    b.l1f_snap1 = [0, 0]
    b.l1f_snap2 = [0, 0]

    def zero_ob(v):
        b.w(v, "qsp")
        ins = None
        for i in range(8):
            ins = v.memset(obacc[:, i * T:(i + 1) * T], 0.0)
        b.inc(ins, "dve")
    b.v(zero_ob)

    for ex in range(10):
        routed = ex >= 2
        w13src = wd["l1f_sw13"] if not routed else wd["l1f_w13"]
        w2src = wd["l1f_sw2r"] if not routed else wd["l1f_w2r"]
        eoff = ex if not routed else (ex - 2)

        for hc in range(16):
            par = hc % 2
            ap13 = w13src[eoff * 16 + hc]
            emit_wload(b, [(w13[:, par * 4096 + dc * 256:par * 4096 + (dc + 1) * 256],
                            ap13[dc * 128:(dc + 1) * 128, :]) for dc in range(8)],
                       war=b.l1f_war13[par])
            for tc in range(NTC):
                lo = tc * TC
                pp = (hc * NTC + tc) % 2
                psA = ps_a if pp == 0 else ps_d
                psB = ps_b if pp == 0 else ps_e

                def mm1(t, hc=hc, par=par, lo=lo, pp=pp, psA=psA, psB=psB, tc=tc):
                    b.w(t, "pool")
                    if hc == 0 and tc <= 1:
                        b.w(t, "dve")
                    else:
                        b.wv(t, "dve", b.l1f_snap1[pp])
                    for dc in range(8):
                        t.matmul(psA[:, 0:TC], w13[:, par * 4096 + dc * 256:par * 4096 + dc * 256 + 128],
                                 xln[:, dc * T + lo:dc * T + lo + TC], start=(dc == 0), stop=(dc == 7))
                    ins = None
                    for dc in range(8):
                        ins = t.matmul(psB[:, 0:TC], w13[:, par * 4096 + dc * 256 + 128:par * 4096 + dc * 256 + 256],
                                       xln[:, dc * T + lo:dc * T + lo + TC], start=(dc == 0), stop=(dc == 7))
                    b.inc(ins, "pe")
                b.t(mm1)
                pes = b.cnt["pe"]

                def gate_s(s, pp=pp, psA=psA, pes=pes):
                    b.wv(s, "pe", pes)
                    b.wv(s, "dve", b.l1f_snap1[pp])
                    ins = s.activation(silu2[pp][:, 0:TC], psA[:, 0:TC], AF.Silu)
                    b.inc(ins, "act")
                b.s(gate_s)
                acts = b.cnt["act"]

                def gate_v(v, hc=hc, lo=lo, pp=pp, psB=psB, acts=acts, pes=pes):
                    b.wv(v, "act", acts)
                    b.wv(v, "pe", pes)
                    ins = v.tensor_mul(hgbuf[:, hc * T + lo:hc * T + lo + TC],
                                       silu2[pp][:, 0:TC], psB[:, 0:TC])
                    b.inc(ins, "dve")
                b.v(gate_v)
                b.l1f_snap1[pp] = b.cnt["dve"]
            b.l1f_war13[par] = b.cnt["pe"]

        if routed:
            for tc in range(NTC):
                def cerep(t, e=eoff, tc=tc):
                    b.w(t, "dve", "qw", "qx")
                    ins = t.matmul(ps_c[:, 0:TC], b.consts["sel8"][0:8, e * 128:(e + 1) * 128],
                                   combT[0:8, tc * TC:(tc + 1) * TC], start=True, stop=True)
                    b.inc(ins, "pe")
                b.t(cerep)

                def evce(v, tc=tc):
                    b.w(v, "pe")
                    ins = v.tensor_copy(cetile[:, tc * TC:(tc + 1) * TC], ps_c[:, 0:TC])
                    b.inc(ins, "dve")
                b.v(evce)

        for dc in range(8):
            par = dc % 2
            ap2 = w2src[eoff * 8 + dc]
            emit_wload(b, [(w2s[:, par * 2048 + hc * 128:par * 2048 + (hc + 1) * 128],
                            ap2[hc * 128:(hc + 1) * 128, :]) for hc in range(16)],
                       war=b.l1f_war2[par])
            for tc in range(NTC):
                lo = tc * TC
                pp = (dc * NTC + tc) % 2
                psA = ps_a if pp == 0 else ps_b

                def mm2(t, dc=dc, par=par, lo=lo, pp=pp, psA=psA, tc=tc):
                    b.w(t, "pool")
                    if dc == 0 and tc <= 1:
                        b.w(t, "dve")
                    else:
                        b.wv(t, "dve", b.l1f_snap2[pp])
                    ins = None
                    for hc in range(16):
                        ins = t.matmul(psA[:, 0:TC], w2s[:, par * 2048 + hc * 128:par * 2048 + (hc + 1) * 128],
                                       hgbuf[:, hc * T + lo:hc * T + lo + TC],
                                       start=(hc == 0), stop=(hc == 15))
                    b.inc(ins, "pe")
                b.t(mm2)
                pes = b.cnt["pe"]

                if routed:
                    def acc(v, dc=dc, lo=lo, psA=psA, pes=pes):
                        b.wv(v, "pe", pes)
                        v.tensor_mul(f32scr[:, 0:TC], psA[:, 0:TC], cetile[:, lo:lo + TC])
                        ins = v.tensor_add(obacc[:, dc * T + lo:dc * T + lo + TC],
                                           obacc[:, dc * T + lo:dc * T + lo + TC], f32scr[:, 0:TC])
                        b.inc(ins, "dve")
                    b.v(acc)
                else:
                    def acc(v, dc=dc, lo=lo, psA=psA, pes=pes):
                        b.wv(v, "pe", pes)
                        ins = v.tensor_add(obacc[:, dc * T + lo:dc * T + lo + TC],
                                           obacc[:, dc * T + lo:dc * T + lo + TC], psA[:, 0:TC])
                        b.inc(ins, "dve")
                    b.v(acc)
                b.l1f_snap2[pp] = b.cnt["dve"]
            b.l1f_war2[par] = b.cnt["pe"]

    esE.close()
    b.fence()
    esE = b.scope()
    rows = b.sb(esE, "l1f_rows2", [1, 2 * T], BF16)
    xold = b.sb(esE, "l1f_xold", [128, 2 * TC], F32)

    ln_emit(b, lambda i: obacc[:, i * T:(i + 1) * T], False, rows, "l1f_out")

    for tc in range(NTC):
        lo = tc * TC
        ln_bcast(b, rows, lo, TC, ps_r, ps_m)
        for i in range(8):
            par = i % 2

            def ldxo(qq, i=i, lo=lo, par=par):
                b.w(qq, "dve", "pe", "pool")
                b.dma(qq, "qsl", xold[:, par * TC:(par + 1) * TC],
                      x_sp[i * 128:(i + 1) * 128, lo:lo + TC])
            b.q(ldxo)

            def addres(v, i=i, lo=lo, par=par):
                b.w(v, "pe", "qsl")
                v.tensor_mul(f32scr[:, 0:TC], obacc[:, i * T + lo:i * T + lo + TC], ps_r[:, 0:TC])
                v.tensor_add(f32scr[:, 0:TC], f32scr[:, 0:TC], ps_m[:, 0:TC])
                ins = v.tensor_add(X[:, i * T + lo:i * T + lo + TC],
                                   xold[:, par * TC:(par + 1) * TC], f32scr[:, 0:TC])
                b.inc(ins, "dve")
            b.v(addres)
    esE.close()
    es0.close()
    b.fence()


def l1f_dram(nc):
    wd = {
        "l1f_rwT": nc.dram_tensor("l1f_rwT", [1024, 8], F32, kind="ExternalInput").ap(),
        "l1f_eb": nc.dram_tensor("l1f_eb", [1, 8], F32, kind="ExternalInput").ap(),
    }
    w13 = nc.dram_tensor("l1f_w13p", [8 * 16, 1024, 256], F32, kind="ExternalInput")
    wd["l1f_w13"] = [w13.ap()[i] for i in range(128)]
    sw13 = nc.dram_tensor("l1f_sw13p", [2 * 16, 1024, 256], F32, kind="ExternalInput")
    wd["l1f_sw13"] = [sw13.ap()[i] for i in range(32)]
    w2r = nc.dram_tensor("l1f_w2rp", [8 * 8, 2048, 128], F32, kind="ExternalInput")
    wd["l1f_w2r"] = [w2r.ap()[i] for i in range(64)]
    sw2 = nc.dram_tensor("l1f_sw2rp", [2 * 8, 2048, 128], F32, kind="ExternalInput")
    wd["l1f_sw2r"] = [sw2.ap()[i] for i in range(16)]
    return wd


def prep_l1f(inputs):
    g = {}
    g["l1f_rwT"] = _cc(inputs["L1_ffn_router_w"].T.astype(np.float32))
    g["l1f_eb"] = _cc(inputs["L1_ffn_expert_bias"].reshape(1, 8).astype(np.float32))

    def pack13(w1, w3):
        E = w1.shape[0]
        out = np.empty((E * 16, 1024, 256), np.float32)
        for e in range(E):
            for hc in range(16):
                out[e * 16 + hc, :, 0:128] = w1[e, hc * 128:(hc + 1) * 128, :].T
                out[e * 16 + hc, :, 128:256] = w3[e, hc * 128:(hc + 1) * 128, :].T
        return _cc(out)

    def pack2(w2):
        E = w2.shape[0]
        out = np.empty((E * 8, 2048, 128), np.float32)
        for e in range(E):
            for dc in range(8):
                out[e * 8 + dc] = w2[e, dc * 128:(dc + 1) * 128, :].T
        return _cc(out)

    g["l1f_w13p"] = pack13(inputs["L1_ffn_w1"], inputs["L1_ffn_w3"])
    g["l1f_sw13p"] = pack13(inputs["L1_ffn_sw1"], inputs["L1_ffn_sw3"])
    g["l1f_w2rp"] = pack2(inputs["L1_ffn_w2"])
    g["l1f_sw2rp"] = pack2(inputs["L1_ffn_sw2"])
    return g



def _l0a_phaseA(b, lay, XM, wd, wi, xpb, xmbf, xst, zst, cw, cb, acc, xin_sp, z_sp):
    for tc in range(NTC):
        lo = tc * TC

        def cast(v, lo=lo):
            b.w(v, "pe")
            ins = None
            for i in range(8):
                ins = v.tensor_copy(xmbf[:, i * TC:(i + 1) * TC], XM[:, i * T + lo:i * T + lo + TC])
            b.inc(ins, "dve")
        b.v(cast)

        def halo(v, tc=tc):
            ins = None
            for ct in range(16):
                if tc == 0:
                    ins = v.memset(xpb[:, ct * 520:ct * 520 + 3], 0.0)
                else:
                    ins = v.tensor_copy(xpb[:, ct * 520:ct * 520 + 3],
                                        xpb[:, ct * 520 + 512:ct * 520 + 515])
            b.inc(ins, "dve")
        b.v(halo)

        snapA = [b.cnt["dve"], b.cnt["dve"]]
        for ec in range(16):
            ps = b.l0a_ps_a if ec % 2 == 0 else b.l0a_ps_b

            def mm(t, ec=ec, ps=ps, w0=snapA[ec % 2]):
                b.w(t, "pool", "qw")
                b.wv(t, "dve", w0)
                ins = None
                for dc in range(8):
                    ins = t.matmul(ps[:, 0:TC],
                                   wi[:, dc * 2048 + ec * 128:dc * 2048 + (ec + 1) * 128],
                                   xmbf[:, dc * TC:(dc + 1) * TC],
                                   start=(dc == 0), stop=(dc == 7))
                b.inc(ins, "pe")
            b.t(mm)
            pes = b.cnt["pe"]

            def ev(v, ec=ec, ps=ps, pes=pes):
                b.wv(v, "pe", pes)
                ins = v.tensor_copy(xpb[:, ec * 520 + 3:ec * 520 + 515], ps[:, 0:TC])
                b.inc(ins, "dve")
            b.v(ev)
            snapA[ec % 2] = b.cnt["dve"]

        snapS = [0, 0]
        snapC = [0, 0]
        for ct in range(16):
            ac = (ct % 2) * TC

            def conv(v, ct=ct, ac=ac, w0=snapS[ct % 2]):
                b.w(v, "qw")
                b.wv(v, "act", w0)
                base = ct * 520
                v.tensor_scalar(acc[:, ac:ac + TC], xpb[:, base + 3:base + 3 + TC],
                                cw[:, ct * 4 + 3:ct * 4 + 4], cb[:, ct:ct + 1],
                                ALU.mult, ALU.add)
                ins = None
                for k in (0, 1, 2):
                    ins = v.scalar_tensor_tensor(acc[:, ac:ac + TC], xpb[:, base + k:base + k + TC],
                                                 cw[:, ct * 4 + k:ct * 4 + k + 1], acc[:, ac:ac + TC],
                                                 ALU.mult, ALU.add)
                b.inc(ins, "dve")
            b.v(conv)
            dves = b.cnt["dve"]

            def sil(s, ct=ct, ac=ac, dves=dves):
                b.wv(s, "dve", dves)
                b.w(s, "qsp")
                ins = s.activation(xst[:, (ct % 2) * TC:(ct % 2 + 1) * TC], acc[:, ac:ac + TC], AF.Silu)
                b.inc(ins, "act")
            b.s(sil)
            snapS[ct % 2] = b.cnt["act"]

            def spxc(qq, ct=ct, lo=lo):
                b.w(qq, "act")
                b.dma(qq, "qsp", xin_sp[ct * 128:(ct + 1) * 128, lo:lo + TC],
                      xst[:, (ct % 2) * TC:(ct % 2 + 1) * TC])
            b.q(spxc)

    def fence_z(g):
        b.w(g, "pe", "act", "dve")
        ins = g.engine_nop()
        b.inc(ins, "pool")
    b.g(fence_z)
    emit_wload(b, [(wi[:, i * 2048:(i + 1) * 2048],
                    wd["l0a_w_inT"][lay][i * 128:(i + 1) * 128, 2048:4096]) for i in range(8)])
    for tc in range(NTC):
        lo = tc * TC

        def cast(v, lo=lo):
            b.w(v, "pe")
            ins = None
            for i in range(8):
                ins = v.tensor_copy(xmbf[:, i * TC:(i + 1) * TC], XM[:, i * T + lo:i * T + lo + TC])
            b.inc(ins, "dve")
        b.v(cast)

        snapZ = [b.cnt["dve"], b.cnt["dve"]]
        for ec in range(16):
            ps = b.l0a_ps_a if ec % 2 == 0 else b.l0a_ps_b
            par = ec % 2

            def mm(t, ec=ec, ps=ps, w0=snapZ[ec % 2]):
                b.w(t, "pool", "qw")
                b.wv(t, "dve", w0)
                ins = None
                for dc in range(8):
                    ins = t.matmul(ps[:, 0:TC],
                                   wi[:, dc * 2048 + ec * 128:dc * 2048 + (ec + 1) * 128],
                                   xmbf[:, dc * TC:(dc + 1) * TC],
                                   start=(dc == 0), stop=(dc == 7))
                b.inc(ins, "pe")
            b.t(mm)
            pes = b.cnt["pe"]

            def ev(v, ec=ec, ps=ps, par=par, pes=pes):
                b.wv(v, "pe", pes)
                b.w(v, "qsp")
                ins = v.tensor_copy(zst[:, par * TC:(par + 1) * TC], ps[:, 0:TC])
                b.inc(ins, "dve")
            b.v(ev)
            snapZ[ec % 2] = b.cnt["dve"]

            def spz(qq, ec=ec, par=par, lo=lo):
                b.w(qq, "dve")
                b.dma(qq, "qsp", z_sp[ec * 128:(ec + 1) * 128, lo:lo + TC],
                      zst[:, par * TC:(par + 1) * TC])
            b.q(spz)


def _l0a_phaseB(b, lay, XM, wd, bufs, xin_sp, z_sp):
    nc = b.nc
    (wout, wx, wdt, acoef, bdt, dpar, xcc, zstg, bcache, ccache, bnpack, apack,
     delta, du, dubf, yred, zsil, ygbuf, state, dtrbf, bbrow, ccrow) = bufs
    ps_a, ps_b = b.l0a_ps_a, b.l0a_ps_b

    def mkA_s(s):
        b.w(s, "qw", "dve", "pe")
        ins = s.activation(acoef[:, :], b.l0a_alog[:, :], AF.Exp)
        b.inc(ins, "act")
    b.s(mkA_s)

    def mkA_v(v):
        b.w(v, "act")
        ins = v.tensor_scalar(acoef[:, :], acoef[:, :], -1.0, None, ALU.mult)
        b.inc(ins, "dve")
    b.v(mkA_v)

    for tc in range(NTC):
        lo = tc * TC

        def ld(qq, lo=lo):
            b.w(qq, "pe", "act", "dve", "qsp")
            for ct in range(16):
                b.dma(qq, "qsl", xcc[:, ct * TC:(ct + 1) * TC],
                      xin_sp[ct * 128:(ct + 1) * 128, lo:lo + TC])
        b.q(ld)

        def mmxd(t):
            b.w(t, "qsl", "pool", "act")
            for ct in range(16):
                t.matmul(ps_a[0:64, 0:TC], wx[:, ct * 96:ct * 96 + 64],
                         xcc[:, ct * TC:(ct + 1) * TC], start=(ct == 0), stop=(ct == 15))
            for ct in range(16):
                t.matmul(ps_b[0:16, 0:TC], wx[:, ct * 96 + 64:ct * 96 + 80],
                         xcc[:, ct * TC:(ct + 1) * TC], start=(ct == 0), stop=(ct == 15))
            ins = None
            for ct in range(16):
                ins = t.matmul(ps_b[32:48, 0:TC], wx[:, ct * 96 + 80:ct * 96 + 96],
                               xcc[:, ct * TC:(ct + 1) * TC], start=(ct == 0), stop=(ct == 15))
            b.inc(ins, "pe")
        b.t(mmxd)

        def evxd(v):
            b.w(v, "pe")
            v.tensor_copy(dtrbf, ps_a[0:64, 0:TC])
            v.tensor_copy(bbrow, ps_b[0:16, 0:TC])
            ins = v.tensor_copy(ccrow, ps_b[32:48, 0:TC])
            b.inc(ins, "dve")
        b.v(evxd)

        for n in range(16):
            ps = ps_a if n % 2 == 0 else ps_b

            def mmbc(t, n=n, ps=ps):
                b.w(t, "dve", "qx")
                t.matmul(ps[:, 0:TC], b.consts["sel16"][64:80, n * 128:(n + 1) * 128],
                         bbrow, start=True, stop=True)
                ins = t.matmul(ps[:, TC:2 * TC],
                               b.consts["sel16"][64:80, n * 128:(n + 1) * 128],
                               ccrow, start=True, stop=True)
                b.inc(ins, "pe")
            b.t(mmbc)

            def evbc(v, n=n, ps=ps):
                b.w(v, "pe")
                v.tensor_copy(bcache[:, n * TC:(n + 1) * TC], ps[:, 0:TC])
                ins = v.tensor_copy(ccache[:, n * TC:(n + 1) * TC], ps[:, TC:2 * TC])
                b.inc(ins, "dve")
            b.v(evbc)

        snap = {"softpv": [0] * 17, "scans": [0] * 17, "gpred": [0] * 17, "mmdt": [0] * 17}

        def tail(ct, lo):
            def silz(s, ct=ct):
                b.w(s, "qz0" if ct % 2 == 0 else "qz1", "dve")
                ins = s.activation(zsil[:, 0:TC], zstg[:, (ct % 2) * TC:(ct % 2 + 1) * TC], AF.Silu)
                b.inc(ins, "act")
            b.s(silz)
            snap_silz[ct % 2] = b.cnt["act"]

            def yg(v, ct=ct):
                b.w(v, "act")
                v.scalar_tensor_tensor(yred[:, 0:TC], xcc[:, ct * TC:(ct + 1) * TC],
                                       dpar[:, ct:ct + 1], yred[:, 0:TC], ALU.mult, ALU.add)
                ins = v.tensor_mul(ygbuf[:, ct * TC:(ct + 1) * TC], yred[:, 0:TC], zsil[:, 0:TC])
                b.inc(ins, "dve")
            b.v(yg)

        snap_silz = [0, 0]
        for ct in range(16):
            def ldz(qq, ct=ct, lo=lo, w0=snap_silz[ct % 2]):
                b.wv(qq, "act", w0)
                b.dma(qq, "qz0" if ct % 2 == 0 else "qz1",
                      zstg[:, (ct % 2) * TC:(ct % 2 + 1) * TC],
                      z_sp[ct * 128:(ct + 1) * 128, lo:lo + TC])
            b.q(ldz)

            def mmdt(t, ct=ct):
                b.wv(t, "dve", snap["softpv"][ct])
                if ct == 0:
                    b.w(t, "dve")
                ins = t.matmul(ps_a[:, 0:TC], wdt[0:64, ct * 128:(ct + 1) * 128],
                               dtrbf, start=True, stop=True)
                b.inc(ins, "pe")
            b.t(mmdt)
            snap["mmdt"][ct] = b.cnt["pe"]

            def softp_s(s, ct=ct):
                b.wv(s, "pe", snap["mmdt"][ct])
                b.wv(s, "dve", snap["softpv"][ct])
                ins = s.activation(delta[:, 0:TC], ps_a[:, 0:TC], AF.Abs, bias=bdt[:, ct:ct + 1])
                b.inc(ins, "act")
                s.wait_ge(b.sem["act"], b.cnt["act"])
                ins = s.activation(delta[:, 0:TC], delta[:, 0:TC], AF.Exp, scale=-1.0)
                b.inc(ins, "act")
                s.wait_ge(b.sem["act"], b.cnt["act"])
                ins = s.activation(delta[:, 0:TC], delta[:, 0:TC], AF.Ln, bias=1.0)
                b.inc(ins, "act")
            b.s(softp_s)

            def softp_v(v, ct=ct):
                b.w(v, "act")
                v.tensor_scalar(du[:, 0:TC], ps_a[:, 0:TC], bdt[:, ct:ct + 1], 0.0,
                                ALU.add, ALU.max)
                v.tensor_add(delta[:, 0:TC], delta[:, 0:TC], du[:, 0:TC])
                ins = v.tensor_mul(dubf[:, 0:TC], delta[:, 0:TC], xcc[:, ct * TC:(ct + 1) * TC])
                b.inc(ins, "dve")
                v.wait_ge(b.sem["dve"], b.cnt["dve"])
                dub = dubf[:, 0:TC].rearrange("p (a t) -> p a t", a=1).broadcast_to([128, 16, TC])
                ins = v.tensor_mul(bnpack[:, :].rearrange("p (n t) -> p n t", n=16),
                                   dub,
                                   bcache[:, :].rearrange("p (n t) -> p n t", n=16))
                b.inc(ins, "dve")
            b.v(softp_v)
            snap["softpv"][ct + 1] = b.cnt["dve"]

            for g in range(8):
                slot = g % 2

                def exps(s, ct=ct, g=g, slot=slot):
                    if g >= 2:
                        b.wv(s, "dve", snap["scans_g"][g - 2])
                    else:
                        b.wv(s, "dve", snap["softpv"][ct + 1])
                    ins = None
                    for j in range(2):
                        n = g * 2 + j
                        ins = s.activation(apack[:, (slot * 2 + j) * TC:(slot * 2 + j + 1) * TC],
                                           delta[:, 0:TC],
                                           AF.Exp, scale=acoef[:, ct * 16 + n:ct * 16 + n + 1])
                    b.inc(ins, "act")
                b.s(exps)
                if g == 0:
                    snap["scans_g"] = [0] * 8

                def scans(v, ct=ct, g=g, tc=tc, slot=slot):
                    b.w(v, "act")
                    ins = None
                    for j in range(2):
                        n = g * 2 + j
                        init = 0.0 if tc == 0 else state[:, ct * 16 + n:ct * 16 + n + 1]
                        ins = v.tensor_tensor_scan(bnpack[:, n * TC:(n + 1) * TC],
                                                   apack[:, (slot * 2 + j) * TC:(slot * 2 + j + 1) * TC],
                                                   bnpack[:, n * TC:(n + 1) * TC],
                                                   init, ALU.mult, ALU.add)
                    b.inc(ins, "dve")
                b.v(scans)
                snap["scans_g"][g] = b.cnt["dve"]

            def post(v, ct=ct):
                sv = bnpack[:, :].rearrange("p (n t) -> p n t", n=16)[:, :, TC - 1:TC]
                v.tensor_copy(state[:, ct * 16:(ct + 1) * 16].rearrange("p (n o) -> p n o", o=1), sv)
                ins = v.tensor_mul(bnpack[:, :], bnpack[:, :], ccache[:, :])
                b.inc(ins, "dve")
            b.v(post)
            snap["scans"][ct] = b.cnt["dve"]

            def gred(v, ct=ct):
                ins = v.tensor_reduce(yred[:, 0:TC], bnpack[:, :].rearrange("p (n t) -> p t n", n=16),
                                      AX.X, ALU.add)
                b.inc(ins, "dve")
            b.v(gred)
            snap["gpred"][ct + 1] = b.cnt["dve"]

            if ct > 0:
                tail(ct - 1, lo)
        tail(15, lo)
        snapO = [0, 0]
        for dc in range(8):
            ps = ps_a if dc % 2 == 0 else ps_b

            def mmo(t, dc=dc, ps=ps, w0=snapO[dc % 2]):
                b.w(t, "pool")
                if dc < 2:
                    b.w(t, "dve")
                else:
                    b.wv(t, "dve", w0)
                ins = None
                for ct in range(16):
                    ins = t.matmul(ps[:, 0:TC],
                                   wout[:, ct * 1024 + dc * 128:ct * 1024 + (dc + 1) * 128],
                                   ygbuf[:, ct * TC:(ct + 1) * TC], start=(ct == 0), stop=(ct == 15))
                b.inc(ins, "pe")
            b.t(mmo)
            pes = b.cnt["pe"]

            def evo(v, dc=dc, ps=ps, lo=lo, pes=pes):
                b.wv(v, "pe", pes)
                ins = v.tensor_add(XM[:, dc * T + lo:dc * T + lo + TC],
                                   XM[:, dc * T + lo:dc * T + lo + TC], ps[:, 0:TC])
                b.inc(ins, "dve")
            b.v(evo)
            snapO[dc % 2] = b.cnt["dve"]


def emit_l0_attn(b, X, wd, spill, n_layers=8):
    nc = b.nc
    x_sp = spill["x_park"]
    xin_sp, z_sp = spill["l0a_xin"], spill["l0a_z"]
    XM = X

    es0 = b.scope()
    ps_r = b.pt(es0, "l0a_ps_r", [128, TC], F32)
    ps_m = b.pt(es0, "l0a_ps_m", [128, TC], F32)
    b.l0a_ps_a = b.pt(es0, "l0a_ps_a", [128, 2 * TC], F32)
    b.l0a_ps_b = b.pt(es0, "l0a_ps_b", [128, 2 * TC], F32)
    f32scr = b.sb(es0, "l0a_f32scr", [128, TC], F32)

    esR = b.scope()
    rows = b.sb(esR, "l0a_rows", [1, 2 * T], BF16)
    ln_emit(b, lambda i: X[:, i * T:(i + 1) * T], False, rows, "l0a_in")

    def park(qq):
        b.w(qq, "dve", "act", "pe")
        for i in range(8):
            b.dma(qq, "qsp", x_sp[i * 128:(i + 1) * 128, :], X[:, i * T:(i + 1) * T])
    b.q(park)

    for tc in range(NTC):
        lo = tc * TC
        ln_bcast(b, rows, lo, TC, ps_r, ps_m)

        def apply(v, lo=lo):
            b.w(v, "pe", "qsp")
            ins = None
            for i in range(8):
                v.tensor_mul(f32scr[:, 0:TC], X[:, i * T + lo:i * T + lo + TC], ps_r[:, 0:TC])
                ins = v.tensor_add(XM[:, i * T + lo:i * T + lo + TC], f32scr[:, 0:TC], ps_m[:, 0:TC])
            b.inc(ins, "dve")
        b.v(apply)
    esR.close()
    b.fence()

    for lay in range(n_layers):
        sfx = f"_{lay}"
        esW = b.scope()
        wout = b.sb(esW, "l0a_wout" + sfx, [128, 16 * 1024], BF16)
        wx = b.sb(esW, "l0a_wx" + sfx, [128, 16 * 96], BF16)
        wdt = b.sb(esW, "l0a_wdt" + sfx, [64, 2048], BF16)
        acoef = b.sb(esW, "l0a_acoef" + sfx, [128, 256], F32)
        b.l0a_alog = acoef
        bdt = b.sb(esW, "l0a_bdt" + sfx, [128, 16], F32)
        dpar = b.sb(esW, "l0a_dpar" + sfx, [128, 16], F32)

        esA = b.scope()
        wi = b.sb(esA, "l0a_wi" + sfx, [128, 8 * 2048], BF16)
        xpb = b.sb(esA, "l0a_xpb" + sfx, [128, 16 * 520], BF16)
        xmbf = b.sb(esA, "l0a_xmbf" + sfx, [128, 8 * TC], BF16)
        xst = b.sb(esA, "l0a_xst" + sfx, [128, 2 * TC], BF16)
        zst = b.sb(esA, "l0a_zst" + sfx, [128, 2 * TC], BF16)
        cw = b.sb(esA, "l0a_cw" + sfx, [128, 64], F32)
        cb = b.sb(esA, "l0a_cb" + sfx, [128, 16], F32)
        acc = b.sb(esA, "l0a_acc" + sfx, [128, 2 * TC], F32)
        emit_wload(b, [(wi[:, i * 2048:(i + 1) * 2048],
                        wd["l0a_w_inT"][lay][i * 128:(i + 1) * 128, 0:2048]) for i in range(8)],
                   direct=[(cw[:, :], wd["l0a_conv_w"][lay]), (cb[:, :], wd["l0a_conv_b"][lay])])
        emit_wload(b, [(wout[:, i * 1024:(i + 1) * 1024],
                        wd["l0a_w_outT"][lay][i * 128:(i + 1) * 128, :]) for i in range(16)]
                   + [(wx[:, i * 96:(i + 1) * 96], wd["l0a_w_xT"][lay][i * 128:(i + 1) * 128, :]) for i in range(16)]
                   + [(wdt[0:64, :], wd["l0a_w_dtT"][lay])],
                   direct=[(b.l0a_alog[:, :], wd["l0a_A_log"][lay]),
                           (bdt[:, :], wd["l0a_b_dt"][lay]),
                           (dpar[:, :], wd["l0a_D_par"][lay])])
        _l0a_phaseA(b, lay, XM, wd, wi, xpb, xmbf, xst, zst, cw, cb, acc, xin_sp, z_sp)
        esA.close()
        b.fence()

        esB = b.scope()
        xcc = b.sb(esB, "l0a_xcc" + sfx, [128, 16 * TC], BF16)
        zstg = b.sb(esB, "l0a_zstg" + sfx, [128, 2 * TC], BF16)
        bcache = b.sb(esB, "l0a_bcache" + sfx, [128, 16 * TC], BF16)
        ccache = b.sb(esB, "l0a_ccache" + sfx, [128, 16 * TC], BF16)
        bnpack = b.sb(esB, "l0a_bnpack" + sfx, [128, 16 * TC], BF16)
        dubf = b.sb(esB, "l0a_dubf" + sfx, [128, TC], BF16)
        apack = b.sb(esB, "l0a_apack" + sfx, [128, 4 * TC], F32)
        du = apack[:, 3 * TC:4 * TC]
        delta = b.sb(esB, "l0a_delta" + sfx, [128, TC], F32)
        yred = b.sb(esB, "l0a_yred" + sfx, [128, TC], F32)
        zsil = b.sb(esB, "l0a_zsil" + sfx, [128, TC], BF16)
        ygbuf = xcc
        state = b.sb(esB, "l0a_state" + sfx, [128, 256], BF16)
        xdblb = b.sb(esB, "l0a_xdblb" + sfx, [128, 2 * TC], BF16)
        dtrbf = xdblb[0:64, 0:TC]
        bbrow = xdblb[64:80, 0:TC]
        ccrow = xdblb[64:80, TC:2 * TC]
        bufs = (wout, wx, wdt, acoef, bdt, dpar, xcc, zstg, bcache, ccache, bnpack,
                apack, delta, du, dubf, yred, zsil, ygbuf, state, dtrbf, bbrow, ccrow)
        _l0a_phaseB(b, lay, XM, wd, bufs, xin_sp, z_sp)
        esB.close()
        esW.close()
        b.fence()

    esF = b.scope()
    rows = b.sb(esF, "l0a_rows2", [1, 2 * T], BF16)
    ln_emit(b, lambda i: XM[:, i * T:(i + 1) * T], False, rows, "l0a_out")
    xold = b.sb(esF, "l0a_xold", [128, 2 * TC], F32)
    for tc in range(NTC):
        lo = tc * TC
        ln_bcast(b, rows, lo, TC, ps_r, ps_m)
        for i in range(8):
            par = i % 2

            def ldxo(qq, i=i, lo=lo, par=par):
                b.w(qq, "dve", "pe", "pool")
                b.dma(qq, "qsl", xold[:, par * TC:(par + 1) * TC],
                      x_sp[i * 128:(i + 1) * 128, lo:lo + TC])
            b.q(ldxo)

            def addres(v, i=i, lo=lo, par=par):
                b.w(v, "pe", "qsl")
                v.tensor_mul(f32scr[:, 0:TC], XM[:, i * T + lo:i * T + lo + TC], ps_r[:, 0:TC])
                v.tensor_add(f32scr[:, 0:TC], f32scr[:, 0:TC], ps_m[:, 0:TC])
                ins = v.tensor_add(X[:, i * T + lo:i * T + lo + TC],
                                   xold[:, par * TC:(par + 1) * TC], f32scr[:, 0:TC])
                b.inc(ins, "dve")
            b.v(addres)
    esF.close()
    es0.close()
    b.fence()


def l0a_dram(nc):
    wd = {}
    w = nc.dram_tensor("l0a_w_inT", [8, 1024, 4096], F32, kind="ExternalInput")
    wd["l0a_w_inT"] = [w.ap()[i] for i in range(8)]
    w = nc.dram_tensor("l0a_conv_w_p", [8, 128, 64], F32, kind="ExternalInput")
    wd["l0a_conv_w"] = [w.ap()[i] for i in range(8)]
    w = nc.dram_tensor("l0a_conv_b_p", [8, 128, 16], F32, kind="ExternalInput")
    wd["l0a_conv_b"] = [w.ap()[i] for i in range(8)]
    w = nc.dram_tensor("l0a_w_xT", [8, 2048, 96], F32, kind="ExternalInput")
    wd["l0a_w_xT"] = [w.ap()[i] for i in range(8)]
    w = nc.dram_tensor("l0a_w_dtT", [8, 64, 2048], F32, kind="ExternalInput")
    wd["l0a_w_dtT"] = [w.ap()[i] for i in range(8)]
    w = nc.dram_tensor("l0a_A_log_p", [8, 128, 256], F32, kind="ExternalInput")
    wd["l0a_A_log"] = [w.ap()[i] for i in range(8)]
    w = nc.dram_tensor("l0a_b_dt_p", [8, 128, 16], F32, kind="ExternalInput")
    wd["l0a_b_dt"] = [w.ap()[i] for i in range(8)]
    w = nc.dram_tensor("l0a_D_par_p", [8, 128, 16], F32, kind="ExternalInput")
    wd["l0a_D_par"] = [w.ap()[i] for i in range(8)]
    w = nc.dram_tensor("l0a_w_outT", [8, 2048, 1024], F32, kind="ExternalInput")
    wd["l0a_w_outT"] = [w.ap()[i] for i in range(8)]
    spill = {
        "l0a_xin": nc.dram_tensor("sp_l0a_xin", [2048, T], BF16, kind="Internal").ap(),
        "l0a_z": nc.dram_tensor("sp_l0a_z", [2048, T], BF16, kind="Internal").ap(),
    }
    return wd, spill


def prep_l0a(inputs):
    g = {}
    g["l0a_w_inT"] = _cc(inputs["L0_attn_W_in"].transpose(0, 2, 1).astype(np.float32))
    cw = inputs["L0_attn_conv_w"]
    g["l0a_conv_w_p"] = _cc(cw.reshape(8, 16, 128, 4).transpose(0, 2, 1, 3).reshape(8, 128, 64))
    g["l0a_conv_b_p"] = _cc(inputs["L0_attn_conv_b"].reshape(8, 16, 128).transpose(0, 2, 1))
    g["l0a_w_xT"] = _cc(inputs["L0_attn_W_x"].transpose(0, 2, 1).astype(np.float32))
    g["l0a_w_dtT"] = _cc(inputs["L0_attn_W_dt"].transpose(0, 2, 1).astype(np.float32))
    al = inputs["L0_attn_A_log"]
    g["l0a_A_log_p"] = _cc(al.reshape(8, 16, 128, 16).transpose(0, 2, 1, 3).reshape(8, 128, 256))
    g["l0a_b_dt_p"] = _cc(inputs["L0_attn_b_dt"].reshape(8, 16, 128).transpose(0, 2, 1))
    g["l0a_D_par_p"] = _cc(inputs["L0_attn_D_par"].reshape(8, 16, 128).transpose(0, 2, 1))
    g["l0a_w_outT"] = _cc(inputs["L0_attn_W_out"].transpose(0, 2, 1).astype(np.float32))
    return g



def _l0f_attn(b, lay, XF, wd, es0_t, sfx):
    nc = b.nc
    (rows, ps_r, ps_m, ps_a, ps_b, f32scr) = es0_t
    es = b.scope()
    qkvb = b.sb(es, "l0f_qkvb" + sfx, [128, 12 * T], BF16)
    obuf = b.sb(es, "l0f_obuf" + sfx, [128, 4 * T], BF16)
    esQ = b.scope()
    wqkv = b.sb(esQ, "l0f_wqkv" + sfx, [128, 8 * 1536], BF16)
    hbuf = b.sb(esQ, "l0f_hbuf" + sfx, [128, 8 * TC], BF16)
    ps_o = b.pt(es, "l0f_ps_o" + sfx, [128, TC], F32)
    ps_tr = b.pt(es, "l0f_ps_tr" + sfx, [128, 256], BF16)
    ps_bc = b.pt(es, "l0f_ps_bc" + sfx, [128, TC], F32)
    ps_sq = b.pt(es, "l0f_ps_sq" + sfx, [1, TC], F32)

    emit_wload(b, [(wqkv[:, i * 1536:(i + 1) * 1536], wd["l0f_wqkv"][lay][i * 128:(i + 1) * 128, :])
                   for i in range(8)])
    for tc in range(NTC):
        lo = tc * TC

        def mkh(v, lo=lo, tc=tc):
            b.w(v, "pe")
            ins = None
            for i in range(8):
                if i < 4:
                    ins = v.tensor_copy(hbuf[:, i * TC:(i + 1) * TC], XF[:, i * T + lo:i * T + lo + TC])
                else:
                    if tc == 0:
                        v.memset(hbuf[:, i * TC:i * TC + 1], 0.0)
                        ins = v.tensor_copy(hbuf[:, i * TC + 1:(i + 1) * TC],
                                            XF[:, i * T:i * T + TC - 1])
                    else:
                        ins = v.tensor_copy(hbuf[:, i * TC:(i + 1) * TC],
                                            XF[:, i * T + lo - 1:i * T + lo + TC - 1])
            b.inc(ins, "dve")
        b.v(mkh)

        snapQ = [b.cnt["dve"], b.cnt["dve"]]
        for ec in range(12):
            ps = ps_a if ec % 2 == 0 else ps_b

            def mm(t, ec=ec, ps=ps, w0=snapQ[ec % 2]):
                b.w(t, "pool", "qw")
                b.wv(t, "dve", w0)
                ins = None
                for dc in range(8):
                    ins = t.matmul(ps[:, 0:TC], wqkv[:, dc * 1536 + ec * 128:dc * 1536 + (ec + 1) * 128],
                                   hbuf[:, dc * TC:(dc + 1) * TC], start=(dc == 0), stop=(dc == 7))
                b.inc(ins, "pe")
            b.t(mm)
            pes = b.cnt["pe"]

            def ev(v, ec=ec, ps=ps, lo=lo, pes=pes):
                b.wv(v, "pe", pes)
                ins = v.tensor_copy(qkvb[:, ec * T + lo:ec * T + lo + TC], ps[:, 0:TC])
                b.inc(ins, "dve")
            b.v(ev)
            snapQ[ec % 2] = b.cnt["dve"]

    esQ.close()
    b.fence()
    esS = b.scope()
    sqscr = b.sb(esS, "l0f_sqscr" + sfx, [128, T], BF16)
    hrow = b.sb(esS, "l0f_hrow" + sfx, [1, TC], F32)
    hrowb = b.sb(esS, "l0f_hrowb" + sfx, [1, T], BF16)
    qsc = b.sb(esS, "l0f_qsc" + sfx, [128, 2], F32)
    rot64 = b.sb(esS, "l0f_rot64" + sfx, [128, 64], BF16)
    tq = b.sb(esS, "l0f_tq" + sfx, [128, 4 * T], BF16)
    maskt = b.sb(esS, "l0f_mask" + sfx, [128, 4 * 1024], BF16)
    vtbuf = b.sb(esS, "l0f_vt" + sfx, [128, 16 * 64], BF16)
    attnT = b.sb(esS, "l0f_attnT" + sfx, [128, 8 * TC], BF16)
    simb = b.sb(esS, "l0f_simb" + sfx, [128, 1024], F32)
    attnb = b.sb(esS, "l0f_attnb" + sfx, [128, 1024], BF16)
    ncol = b.sb(esS, "l0f_ncol" + sfx, [128, 4], F32)
    qnscr = b.sb(esS, "l0f_qn" + sfx, [128, TC], BF16)
    t1scr = b.sb(esS, "l0f_t1" + sfx, [128, TC], F32)
    t2scr = b.sb(esS, "l0f_t2" + sfx, [128, TC], F32)
    emit_wload(b, [], direct=[(qsc[:, 0:1], wd["l0f_qs"][lay]), (qsc[:, 1:2], wd["l0f_ks"][lay]),
                              (rot64[:, :], wd["l0f_rot64"]), (tq[:, :], wd["l0f_tq"]),
                              (maskt[:, :], wd["l0f_mask"])])

    for h in range(16):
        tile = (h // 2) if h < 8 else (4 + (h - 8) // 2)
        hb = h % 2
        base = hb * 64
        isq = h < 8
        ctbl = 0 if isq else 2
        stbl = 1 if isq else 3
        scol = 0 if isq else 1

        def sq_s(s, tile=tile):
            b.w(s, "dve", "pe")
            ins = s.activation(sqscr[:, 0:T], qkvb[:, tile * T:(tile + 1) * T], AF.Square)
            b.inc(ins, "act")
        if hb == 0:
            b.s(sq_s)

        for tc in range(NTC):
            def mmsq(t, tc=tc, base=base, tile=tile):
                b.w(t, "act", "dve")
                ins = t.matmul(ps_sq[0:1, 0:TC], b.consts["ones_col_bf"][base:base + 64, 0:1],
                               sqscr[base:base + 64, tc * TC:(tc + 1) * TC], start=True, stop=True)
                b.inc(ins, "pe")
            b.t(mmsq)

            def rstd_v(v, tc=tc):
                b.w(v, "pe")
                ins = v.tensor_scalar(hrow[0:1, 0:TC], ps_sq[0:1, 0:TC], 1e-24, None, ALU.max)
                b.inc(ins, "dve")
            b.v(rstd_v)

            def rstd_s(s, tc=tc):
                b.w(s, "dve")
                ins = s.activation(hrow[0:1, 0:TC], hrow[0:1, 0:TC], AF.Ln)
                b.inc(ins, "act")
                s.wait_ge(b.sem["act"], b.cnt["act"])
                ins = s.activation(hrowb[0:1, tc * TC:(tc + 1) * TC], hrow[0:1, 0:TC], AF.Exp, scale=-0.5)
                b.inc(ins, "act")
            b.s(rstd_s)

        for tc in range(NTC):
            lo = tc * TC

            def bc(t, lo=lo, base=base):
                b.w(t, "act", "dve")
                ins = t.matmul(ps_bc[base:base + 64, 0:TC], b.consts["ones_row_bf"][0:1, 0:64],
                               hrowb[0:1, lo:lo + TC], start=True, stop=True)
                b.inc(ins, "pe")
            b.t(bc)

            def app1(v, lo=lo, base=base, tile=tile, scol=scol):
                b.w(v, "pe")
                v.tensor_mul(qnscr[base:base + 64, 0:TC],
                             qkvb[base:base + 64, tile * T + lo:tile * T + lo + TC],
                             ps_bc[base:base + 64, 0:TC])
                ins = v.tensor_scalar(qnscr[base:base + 64, 0:TC], qnscr[base:base + 64, 0:TC],
                                      qsc[base:base + 64, scol:scol + 1], None, ALU.mult)
                b.inc(ins, "dve")
            b.v(app1)

            def rotmm(t, base=base):
                b.w(t, "dve", "qw")
                ins = t.matmul(ps_o[base:base + 64, 0:TC], rot64[base:base + 64, 0:64],
                               qnscr[base:base + 64, 0:TC], start=True, stop=True)
                b.inc(ins, "pe")
            b.t(rotmm)

            def app2(v, lo=lo, base=base, tile=tile, ctbl=ctbl, stbl=stbl):
                b.w(v, "pe")
                v.tensor_mul(t1scr[base:base + 64, 0:TC], qnscr[base:base + 64, 0:TC],
                             tq[base:base + 64, ctbl * T + lo:ctbl * T + lo + TC])
                v.tensor_mul(t2scr[base:base + 64, 0:TC], ps_o[base:base + 64, 0:TC],
                             tq[base:base + 64, stbl * T + lo:stbl * T + lo + TC])
                ins = v.tensor_add(qkvb[base:base + 64, tile * T + lo:tile * T + lo + TC],
                                   t1scr[base:base + 64, 0:TC], t2scr[base:base + 64, 0:TC])
                b.inc(ins, "dve")
            b.v(app2)

    for h in range(8):
        tile = 8 + h // 2
        hb = h % 2
        base = hb * 64
        qtile = h // 2
        ktile = 4 + h // 2

        for tt in range(16):
            pspar = tt % 2

            def vtr(t, tt=tt, tile=tile, base=base, pspar=pspar):
                b.w(t, "dve")
                ins = t.transpose(ps_tr[:, pspar * 64:(pspar + 1) * 64],
                                  qkvb[base:base + 64, tile * T + tt * 128:tile * T + (tt + 1) * 128],
                                  b.consts["ident_bf"][base:base + 64, base:base + 64])
                b.inc(ins, "pe")
            b.t(vtr)

            def vev(v, tt=tt, pspar=pspar):
                b.w(v, "pe")
                ins = v.tensor_copy(vtbuf[:, tt * 64:(tt + 1) * 64], ps_tr[:, pspar * 64:(pspar + 1) * 64])
                b.inc(ins, "dve")
            b.v(vev)

        for wi in range(4):
            for ic in range(4):
                i0 = wi * TC + ic * 128

                def mmsim(t, wi=wi, ic=ic, i0=i0, base=base, qtile=qtile, ktile=ktile):
                    b.w(t, "dve")
                    if wi > 0:
                        t.matmul(ps_a[:, 0:TC], qkvb[base:base + 64, qtile * T + i0:qtile * T + i0 + 128],
                                 qkvb[base:base + 64, ktile * T + (wi - 1) * TC:ktile * T + wi * TC],
                                 start=True, stop=True)
                    ins = t.matmul(ps_b[:, 0:TC], qkvb[base:base + 64, qtile * T + i0:qtile * T + i0 + 128],
                                   qkvb[base:base + 64, ktile * T + wi * TC:ktile * T + (wi + 1) * TC],
                                   start=True, stop=True)
                    b.inc(ins, "pe")
                b.t(mmsim)

                def smax(v, wi=wi, ic=ic):
                    b.w(v, "pe", "qw")
                    if wi > 0:
                        v.tensor_add(simb[:, 0:TC], ps_a[:, 0:TC], maskt[:, ic * 1024:ic * 1024 + TC])
                    else:
                        v.memset(simb[:, 0:TC], -3.0e38)
                    v.tensor_add(simb[:, TC:1024], ps_b[:, 0:TC], maskt[:, ic * 1024 + TC:(ic + 1) * 1024])
                    ins = v.reduce_max(ncol[:, 0:1], simb[:, 0:1024], AX.X, negate=True)
                    b.inc(ins, "dve")
                    v.wait_ge(b.sem["dve"], b.cnt["dve"])
                b.v(smax)

                def sexp(s):
                    b.w(s, "dve")
                    ins = s.activation(attnb[:, 0:1024], simb[:, 0:1024], AF.Exp,
                                       bias=ncol[:, 0:1], accum_out=ncol[:, 1:2])
                    b.inc(ins, "act")
                b.s(sexp)

                def snorm(v):
                    b.w(v, "act")
                    ins = v.reciprocal(ncol[:, 2:3], ncol[:, 1:2])
                    b.inc(ins, "dve")
                    v.wait_ge(b.sem["dve"], b.cnt["dve"])
                    ins = v.tensor_scalar(attnb[:, 0:1024], attnb[:, 0:1024], ncol[:, 2:3],
                                          None, ALU.mult)
                    b.inc(ins, "dve")
                b.v(snorm)

                for js in range(8):
                    pspar = js % 2

                    def atr(t, js=js, pspar=pspar):
                        b.w(t, "dve")
                        ins = t.transpose(ps_tr[:, pspar * 128:(pspar + 1) * 128],
                                          attnb[:, js * 128:(js + 1) * 128],
                                          b.consts["ident_bf"][:, :])
                        b.inc(ins, "pe")
                    b.t(atr)

                    def aev(v, js=js, ic=ic, pspar=pspar):
                        b.w(v, "pe")
                        ins = v.tensor_copy(attnT[:, js * TC + ic * 128:js * TC + (ic + 1) * 128],
                                            ps_tr[:, pspar * 128:(pspar + 1) * 128])
                        b.inc(ins, "dve")
                    b.v(aev)

            def mmav(t, wi=wi, base=base):
                b.w(t, "dve")
                ins = None
                j0 = 4 if wi == 0 else 0
                for jt in range(j0, 8):
                    jtg = (wi - 1) * 4 + jt
                    ins = t.matmul(ps_o[base:base + 64, 0:TC], vtbuf[:, jtg * 64:(jtg + 1) * 64],
                                   attnT[:, jt * TC:(jt + 1) * TC],
                                   start=(jt == j0), stop=(jt == 7))
                b.inc(ins, "pe")
            b.t(mmav)

            def oev(v, wi=wi, base=base, qtile=qtile):
                b.w(v, "pe")
                ins = v.tensor_copy(obuf[base:base + 64, qtile * T + wi * TC:qtile * T + (wi + 1) * TC],
                                    ps_o[base:base + 64, 0:TC])
                b.inc(ins, "dve")
            b.v(oev)

    esS.close()
    b.fence()
    esW = b.scope()
    wob = b.sb(esW, "l0f_wob" + sfx, [128, 4 * 1024], BF16)
    emit_wload(b, [(wob[:, i * 1024:(i + 1) * 1024], wd["l0f_wo"][lay][i * 128:(i + 1) * 128, :])
                   for i in range(4)])

    for tc in range(NTC):
        lo = tc * TC
        for dc in range(8):
            ps = ps_a if dc % 2 == 0 else ps_b

            def mmwo(t, dc=dc, ps=ps, lo=lo):
                b.w(t, "dve", "pool")
                ins = None
                for ct in range(4):
                    ins = t.matmul(ps[:, 0:TC], wob[:, ct * 1024 + dc * 128:ct * 1024 + (dc + 1) * 128],
                                   obuf[:, ct * T + lo:ct * T + lo + TC], start=(ct == 0), stop=(ct == 3))
                b.inc(ins, "pe")
            b.t(mmwo)

            def evwo(v, dc=dc, ps=ps, lo=lo):
                b.w(v, "pe")
                ins = v.tensor_add(XF[:, dc * T + lo:dc * T + lo + TC],
                                   XF[:, dc * T + lo:dc * T + lo + TC], ps[:, 0:TC])
                b.inc(ins, "dve")
            b.v(evwo)
    esW.close()
    es.close()
    b.fence()


def _l0f_ffn(b, lay, XF, wd, es0_t, sfx):
    nc = b.nc
    (rows, ps_r, ps_m, ps_a, ps_b, f32scr) = es0_t
    es = b.scope()
    h2b = b.sb(es, "l0f_h2b" + sfx, [128, 8 * T], BF16)
    xln2 = b.sb(es, "l0f_xln2" + sfx, [128, 8 * T], BF16)
    gcols = b.sb(es, "l0f_gcols" + sfx, [128, 16], F32)
    b1a = b.sb(es, "l0f_b1a" + sfx, [128, 22], F32)
    b1g = b.sb(es, "l0f_b1g" + sfx, [128, 22], F32)
    b2c = b.sb(es, "l0f_b2c" + sfx, [128, 8], F32)

    emit_wload(b, [], direct=[(gcols[:, 0:8], wd["l0f_lng"][lay]),
                              (gcols[:, 8:16], wd["l0f_lnb"][lay]),
                              (b1a[:, :], wd["l0f_b1a"][lay]),
                              (b1g[:, :], wd["l0f_b1g"][lay]),
                              (b2c[:, :], wd["l0f_b2"][lay])])

    for tc in range(NTC):
        lo = tc * TC

        def mkh(v, lo=lo, tc=tc):
            b.w(v, "pe", "act")
            ins = None
            for i in range(8):
                if i < 4:
                    ins = v.tensor_copy(h2b[:, i * T + lo:i * T + lo + TC],
                                        XF[:, i * T + lo:i * T + lo + TC])
                else:
                    if tc == 0:
                        v.memset(h2b[:, i * T:i * T + 1], 0.0)
                        ins = v.tensor_copy(h2b[:, i * T + 1:i * T + TC],
                                            XF[:, i * T:i * T + TC - 1])
                    else:
                        ins = v.tensor_copy(h2b[:, i * T + lo:i * T + lo + TC],
                                            XF[:, i * T + lo - 1:i * T + lo + TC - 1])
            b.inc(ins, "dve")
        b.v(mkh)

    ln_emit(b, lambda i: h2b[:, i * T:(i + 1) * T], True, rows, "l0f_ln" + sfx)

    for tc in range(NTC):
        lo = tc * TC
        ln_bcast(b, rows, lo, TC, ps_r, ps_m)

        def appg(v, lo=lo):
            b.w(v, "pe", "qw")
            ins = None
            for i in range(8):
                v.tensor_mul(f32scr[:, 0:TC], h2b[:, i * T + lo:i * T + lo + TC], ps_r[:, 0:TC])
                v.tensor_add(f32scr[:, 0:TC], f32scr[:, 0:TC], ps_m[:, 0:TC])
                ins = v.tensor_scalar(xln2[:, i * T + lo:i * T + lo + TC], f32scr[:, 0:TC],
                                      gcols[:, i:i + 1], gcols[:, 8 + i:9 + i], ALU.mult, ALU.add)
            b.inc(ins, "dve")
        b.v(appg)

    esW = b.scope()
    hg = b.sb(esW, "l0f_hg" + sfx, [128, 22 * 1024], BF16)
    w1c = b.sb(esW, "l0f_w1c" + sfx, [128, 8 * 256], BF16)
    w2c = b.sb(esW, "l0f_w2c" + sfx, [128, 22 * 128], BF16)
    gel2 = [b.sb(esW, "l0f_gel0" + sfx, [128, TC], F32),
            b.sb(esW, "l0f_gel1" + sfx, [128, TC], F32)]
    b.l0f_snapH = [0, 0]
    b.l0f_snapO = [0, 0]

    for th in range(2):
        for pair in range(22):
            emit_wload(b, [(w1c[:, dc * 256:(dc + 1) * 256],
                            wd["l0f_w1p"][lay][dc * 128:(dc + 1) * 128, pair * 256:(pair + 1) * 256])
                           for dc in range(8)])
            for tcl in range(2):
                lo = th * 1024 + tcl * TC
                pp = (pair * 2 + tcl) % 2
                psA = ps_a if pp == 0 else ps_r
                psB = ps_b if pp == 0 else ps_m

                def mmh(t, pair=pair, lo=lo, pp=pp, psA=psA, psB=psB):
                    b.w(t, "pool")
                    if pair == 0:
                        b.w(t, "dve")
                    else:
                        b.wv(t, "dve", b.l0f_snapH[pp])
                    for dc in range(8):
                        t.matmul(psA[:, 0:TC], w1c[:, dc * 256:dc * 256 + 128],
                                 xln2[:, dc * T + lo:dc * T + lo + TC], start=(dc == 0), stop=(dc == 7))
                    ins = None
                    for dc in range(8):
                        ins = t.matmul(psB[:, 0:TC], w1c[:, dc * 256 + 128:dc * 256 + 256],
                                       xln2[:, dc * T + lo:dc * T + lo + TC], start=(dc == 0), stop=(dc == 7))
                    b.inc(ins, "pe")
                b.t(mmh)
                pes = b.cnt["pe"]

                def gg_s(s, pair=pair, pp=pp, psB=psB, pes=pes):
                    b.wv(s, "pe", pes)
                    b.wv(s, "dve", b.l0f_snapH[pp])
                    ins = s.activation(gel2[pp][:, 0:TC], psB[:, 0:TC], AF.Gelu, bias=b1g[:, pair:pair + 1])
                    b.inc(ins, "act")
                b.s(gg_s)
                acts = b.cnt["act"]

                def gg_v(v, pair=pair, tcl=tcl, pp=pp, psA=psA, acts=acts, pes=pes):
                    b.wv(v, "act", acts)
                    b.wv(v, "pe", pes)
                    ins = v.scalar_tensor_tensor(hg[:, pair * 1024 + tcl * TC:pair * 1024 + (tcl + 1) * TC],
                                                 psA[:, 0:TC], b1a[:, pair:pair + 1], gel2[pp][:, 0:TC],
                                                 ALU.add, ALU.mult)
                    b.inc(ins, "dve")
                b.v(gg_v)
                b.l0f_snapH[pp] = b.cnt["dve"]

        for dc in range(8):
            emit_wload(b, [(w2c[:, hc * 128:(hc + 1) * 128],
                            wd["l0f_w2p"][lay * 8 + dc][hc * 128:(hc + 1) * 128, :])
                           for hc in range(22)])
            for tcl in range(2):
                lo = th * 1024 + tcl * TC
                pp = (dc * 2 + tcl) % 2
                psA = ps_a if pp == 0 else ps_b

                def mmo(t, dc=dc, lo=lo, tcl=tcl, pp=pp, psA=psA):
                    b.w(t, "pool")
                    if dc == 0:
                        b.w(t, "dve")
                    else:
                        b.wv(t, "dve", b.l0f_snapO[pp])
                    ins = None
                    for hc in range(22):
                        ins = t.matmul(psA[:, 0:TC], w2c[:, hc * 128:(hc + 1) * 128],
                                       hg[:, hc * 1024 + tcl * TC:hc * 1024 + (tcl + 1) * TC],
                                       start=(hc == 0), stop=(hc == 21))
                    b.inc(ins, "pe")
                b.t(mmo)
                pes = b.cnt["pe"]

                def evo(v, dc=dc, lo=lo, pp=pp, psA=psA, pes=pes):
                    b.wv(v, "pe", pes)
                    ins = v.scalar_tensor_tensor(XF[:, dc * T + lo:dc * T + lo + TC],
                                                 psA[:, 0:TC], b2c[:, dc:dc + 1],
                                                 XF[:, dc * T + lo:dc * T + lo + TC],
                                                 ALU.add, ALU.add)
                    b.inc(ins, "dve")
                b.v(evo)
                b.l0f_snapO[pp] = b.cnt["dve"]
    esW.close()
    es.close()
    b.fence()


def emit_l0_ffn(b, X, wd, spill, n_layers=2):
    nc = b.nc
    x_sp = spill["x_park"]
    XF = X
    es0 = b.scope()
    ps_r = b.pt(es0, "l0f_ps_r", [128, TC], F32)
    ps_m = b.pt(es0, "l0f_ps_m", [128, TC], F32)
    ps_a = b.pt(es0, "l0f_ps_a", [128, TC], F32)
    ps_b = b.pt(es0, "l0f_ps_b", [128, TC], F32)
    f32scr = b.sb(es0, "l0f_f32scr", [128, TC], F32)
    rows = b.sb(es0, "l0f_rows", [1, 2 * T], BF16)
    es0_t = (rows, ps_r, ps_m, ps_a, ps_b, f32scr)

    ln_emit(b, lambda i: X[:, i * T:(i + 1) * T], False, rows, "l0f_in")

    def park(qq):
        b.w(qq, "dve", "act", "pe")
        for i in range(8):
            b.dma(qq, "qsp", x_sp[i * 128:(i + 1) * 128, :], X[:, i * T:(i + 1) * T])
    b.q(park)

    for tc in range(NTC):
        lo = tc * TC
        ln_bcast(b, rows, lo, TC, ps_r, ps_m)

        def apply(v, lo=lo):
            b.w(v, "pe", "qsp")
            ins = None
            for i in range(8):
                v.tensor_mul(f32scr[:, 0:TC], X[:, i * T + lo:i * T + lo + TC], ps_r[:, 0:TC])
                ins = v.tensor_add(XF[:, i * T + lo:i * T + lo + TC], f32scr[:, 0:TC], ps_m[:, 0:TC])
            b.inc(ins, "dve")
        b.v(apply)
    b.fence()

    for lay in range(n_layers):
        sfx = f"_{lay}"
        _l0f_attn(b, lay, XF, wd, es0_t, sfx)
        _l0f_ffn(b, lay, XF, wd, es0_t, sfx)

    esF = b.scope()
    ln_emit(b, lambda i: XF[:, i * T:(i + 1) * T], False, rows, "l0f_out")
    xold = b.sb(esF, "l0f_xold", [128, 2 * TC], F32)
    for tc in range(NTC):
        lo = tc * TC
        ln_bcast(b, rows, lo, TC, ps_r, ps_m)
        for i in range(8):
            par = i % 2

            def ldxo(qq, i=i, lo=lo, par=par):
                b.w(qq, "dve", "pe", "pool")
                b.dma(qq, "qsl", xold[:, par * TC:(par + 1) * TC],
                      x_sp[i * 128:(i + 1) * 128, lo:lo + TC])
            b.q(ldxo)

            def addres(v, i=i, lo=lo, par=par):
                b.w(v, "pe", "qsl")
                v.tensor_mul(f32scr[:, 0:TC], XF[:, i * T + lo:i * T + lo + TC], ps_r[:, 0:TC])
                v.tensor_add(f32scr[:, 0:TC], f32scr[:, 0:TC], ps_m[:, 0:TC])
                ins = v.tensor_add(X[:, i * T + lo:i * T + lo + TC],
                                   xold[:, par * TC:(par + 1) * TC], f32scr[:, 0:TC])
                b.inc(ins, "dve")
            b.v(addres)
    esF.close()
    es0.close()
    b.fence()


def l0f_dram(nc):
    wd = {}
    w = nc.dram_tensor("l0f_wqkv_d", [2, 1024, 1536], F32, kind="ExternalInput")
    wd["l0f_wqkv"] = [w.ap()[i] for i in range(2)]
    w = nc.dram_tensor("l0f_wo_d", [2, 512, 1024], F32, kind="ExternalInput")
    wd["l0f_wo"] = [w.ap()[i] for i in range(2)]
    w = nc.dram_tensor("l0f_qs_d", [2, 128, 1], F32, kind="ExternalInput")
    wd["l0f_qs"] = [w.ap()[i] for i in range(2)]
    w = nc.dram_tensor("l0f_ks_d", [2, 128, 1], F32, kind="ExternalInput")
    wd["l0f_ks"] = [w.ap()[i] for i in range(2)]
    wd["l0f_rot64"] = nc.dram_tensor("l0f_rot64_d", [128, 64], BF16, kind="ExternalInput").ap()
    wd["l0f_tq"] = nc.dram_tensor("l0f_tq_d", [128, 4 * T], BF16, kind="ExternalInput").ap()
    wd["l0f_mask"] = nc.dram_tensor("l0f_mask_d", [128, 4 * 1024], BF16, kind="ExternalInput").ap()
    w = nc.dram_tensor("l0f_lng_d", [2, 128, 8], F32, kind="ExternalInput")
    wd["l0f_lng"] = [w.ap()[i] for i in range(2)]
    w = nc.dram_tensor("l0f_lnb_d", [2, 128, 8], F32, kind="ExternalInput")
    wd["l0f_lnb"] = [w.ap()[i] for i in range(2)]
    w = nc.dram_tensor("l0f_w1p_d", [2, 1024, 5632], F32, kind="ExternalInput")
    wd["l0f_w1p"] = [w.ap()[i] for i in range(2)]
    w = nc.dram_tensor("l0f_b1a_d", [2, 128, 22], F32, kind="ExternalInput")
    wd["l0f_b1a"] = [w.ap()[i] for i in range(2)]
    w = nc.dram_tensor("l0f_b1g_d", [2, 128, 22], F32, kind="ExternalInput")
    wd["l0f_b1g"] = [w.ap()[i] for i in range(2)]
    w = nc.dram_tensor("l0f_w2p_d", [2 * 8, 2816, 128], F32, kind="ExternalInput")
    wd["l0f_w2p"] = [w.ap()[i] for i in range(16)]
    w = nc.dram_tensor("l0f_b2_d", [2, 128, 8], F32, kind="ExternalInput")
    wd["l0f_b2"] = [w.ap()[i] for i in range(2)]
    return wd


def prep_l0f(inputs):
    g = {}
    g["l0f_wqkv_d"] = _cc(inputs["L0_ffn_wqkv"].astype(np.float32))
    g["l0f_wo_d"] = _cc(inputs["L0_ffn_wo"].astype(np.float32))
    qs = inputs["L0_ffn_q_scale"]; ks = inputs["L0_ffn_k_scale"]
    g["l0f_qs_d"] = _cc(np.stack([np.tile(qs[l], 2).reshape(128, 1) for l in range(2)]))
    g["l0f_ks_d"] = _cc(np.stack([np.tile(ks[l], 2).reshape(128, 1) for l in range(2)]))
    R = np.zeros((64, 64), np.float32)
    for m in range(32):
        R[m + 32, m] = -1.0
        R[m, m + 32] = 1.0
    g["l0f_rot64_d"] = _cc(np.tile(R, (2, 1)).astype(bf16))
    half = np.arange(0, 64, 2, dtype=np.float32)
    inv_freq = 1.0 / 10000.0 ** (half / 64.0)
    t = np.arange(T, dtype=np.float32)
    freqs = t[:, None] * inv_freq[None, :]
    freqs = np.concatenate([freqs, freqs], -1)
    base_ = (half + 0.4 * 64) / (1.4 * 64)
    power = (t - T // 2) / 256.0
    scale = base_[None, :] ** power[:, None]
    scale = np.concatenate([scale, scale], -1)
    cos, sin = np.cos(freqs), np.sin(freqs)
    Tqc = (8.0 * cos * scale).T; Tqs = (8.0 * sin * scale).T
    Tkc = (cos / scale).T; Tks = (sin / scale).T
    tq = np.concatenate([np.tile(x, (2, 1)) for x in (Tqc, Tqs, Tkc, Tks)], 1)
    g["l0f_tq_d"] = _cc(tq.astype(bf16))
    ii = np.arange(512)[:, None]
    jj = np.arange(1024)[None, :]
    m = (jj < ii) | (jj > ii + 512)
    mask = np.where(m, np.float32(-3.0e38), np.float32(0.0))
    g["l0f_mask_d"] = _cc(mask.reshape(4, 128, 1024).transpose(1, 0, 2).reshape(128, 4096).astype(bf16))
    g["l0f_lng_d"] = _cc(inputs["L0_ffn_ln_g"].reshape(2, 8, 128).transpose(0, 2, 1))
    g["l0f_lnb_d"] = _cc(inputs["L0_ffn_ln_b"].reshape(2, 8, 128).transpose(0, 2, 1))
    w1 = inputs["L0_ffn_w1"]; b1 = inputs["L0_ffn_b1"]
    w1p = np.zeros((2, 1024, 5632), np.float32)
    b1a = np.zeros((2, 128, 22), np.float32); b1g = np.zeros((2, 128, 22), np.float32)
    for l in range(2):
        for p in range(22):
            n = min(128, 2730 - p * 128)
            w1p[l, :, p * 256:p * 256 + n] = w1[l, :, p * 128:p * 128 + n]
            w1p[l, :, p * 256 + 128:p * 256 + 128 + n] = w1[l, :, 2730 + p * 128:2730 + p * 128 + n]
            b1a[l, :n, p] = b1[l, p * 128:p * 128 + n]
            b1g[l, :n, p] = b1[l, 2730 + p * 128:2730 + p * 128 + n]
    g["l0f_w1p_d"] = _cc(w1p)
    g["l0f_b1a_d"] = _cc(b1a)
    g["l0f_b1g_d"] = _cc(b1g)
    w2 = inputs["L0_ffn_w2"]
    w2pad = np.zeros((2, 2816, 1024), np.float32)
    w2pad[:, :2730, :] = w2
    w2p = np.zeros((2 * 8, 2816, 128), np.float32)
    for l in range(2):
        for dc in range(8):
            w2p[l * 8 + dc] = w2pad[l, :, dc * 128:(dc + 1) * 128]
    g["l0f_w2p_d"] = _cc(w2p)
    g["l0f_b2_d"] = _cc(inputs["L0_ffn_b2"].reshape(2, 8, 128).transpose(0, 2, 1))
    return g



CONST_FLOATS = [1e-5, -0.5, 1.0 / 1024.0, 0.1, 1.0, 0.0, -1.0, 0.5, 2.0]


def alloc_consts(b, nc):
    t1 = nc.sbuf_tensor("c_ones_col_bf", [128, 1], BF16).__enter__()
    t2 = nc.sbuf_tensor("c_ones_row_f32", [1, 128], F32).__enter__()
    cp = nc.sbuf_tensor("c_pool", [128, len(CONST_FLOATS)], F32).__enter__()
    idf = nc.sbuf_tensor("c_ident_f32", [128, 128], F32).__enter__()
    t3 = nc.sbuf_tensor("c_ones_row_bf", [1, 128], BF16).__enter__()
    b.consts["ones_row_bf"] = t3
    idb = nc.sbuf_tensor("c_ident_bf", [128, 128], BF16).__enter__()
    b.consts["ident_bf"] = idb
    sel8 = nc.sbuf_tensor("c_sel8", [8, 8 * 128], BF16).__enter__()
    sel16 = nc.sbuf_tensor("c_sel16", [128, 16 * 128], BF16).__enter__()
    b.consts["ident_f32"] = idf
    b.consts["sel8"] = sel8
    b.consts["sel16"] = sel16
    b.consts["ones_col_bf"] = t1
    b.consts["ones_row_f32"] = t2
    b.consts["cp"] = cp
    b.wstg = nc.sbuf_tensor("c_wstg", [128, 2 * 512], F32).__enter__()
    for j, val in enumerate(CONST_FLOATS):
        nc.const_aps.register(F32, float(val), cp[:, j:j + 1])


def init_consts(b, const_dram=None):
    t1, t2, cp = b.consts["ones_col_bf"], b.consts["ones_row_f32"], b.consts["cp"]

    def ini(v):
        v.memset(t1[:, :], 1.0)
        v.memset(t2[0:1, :], 1.0)
        v.memset(b.consts["ones_row_bf"][0:1, :], 1.0)
        ins = None
        for j, val in enumerate(CONST_FLOATS):
            ins = v.memset(cp[:, j:j + 1], float(val))
        b.inc(ins, "dve")
    b.v(ini)
    if const_dram is not None:
        def ldc(qq):
            b.dma(qq, "qx", b.consts["ident_f32"][:, :], const_dram["ident_f32"])
            b.dma(qq, "qx", b.consts["ident_bf"][:, :], const_dram["ident_bf"])
            b.dma(qq, "qx", b.consts["sel8"][0:8, :], const_dram["sel8"])
            b.dma(qq, "qx", b.consts["sel16"][64:80, :], const_dram["sel16"])
        b.q(ldc)


def const_dram_decl(nc):
    return {
        "ident_f32": nc.dram_tensor("c_ident_f32_d", [128, 128], F32, kind="ExternalInput").ap(),
        "ident_bf": nc.dram_tensor("c_ident_bf_d", [128, 128], BF16, kind="ExternalInput").ap(),
        "sel8": nc.dram_tensor("c_sel8_d", [8, 8 * 128], BF16, kind="ExternalInput").ap(),
        "sel16": nc.dram_tensor("c_sel16_d", [16, 16 * 128], BF16, kind="ExternalInput").ap(),
    }


def const_host():
    sel8 = np.zeros((8, 8 * 128), np.float32)
    for e in range(8):
        sel8[e, e * 128:(e + 1) * 128] = 1.0
    sel16 = np.zeros((16, 16 * 128), np.float32)
    for n in range(16):
        sel16[n, n * 128:(n + 1) * 128] = 1.0
    return {"c_ident_f32_d": np.eye(128, dtype=np.float32), "c_ident_bf_d": np.eye(128, dtype=bf16),
            "c_sel8_d": sel8.astype(bf16), "c_sel16_d": sel16.astype(bf16)}


def l1a_dram(nc):
    wd = {
        "l1a_w_in": nc.dram_tensor("l1a_w_in", [1024, 4096], F32, kind="ExternalInput").ap(),
        "l1a_conv_w": nc.dram_tensor("l1a_conv_w", [128, 64], F32, kind="ExternalInput").ap(),
        "l1a_conv_b": nc.dram_tensor("l1a_conv_b", [128, 16], F32, kind="ExternalInput").ap(),
        "l1a_w_B": nc.dram_tensor("l1a_w_B", [128, 256], F32, kind="ExternalInput").ap(),
        "l1a_w_C": nc.dram_tensor("l1a_w_C", [16, 2048], F32, kind="ExternalInput").ap(),
        "l1a_w_out": nc.dram_tensor("l1a_w_out", [128, 16 * 1024], F32, kind="ExternalInput").ap(),
    }
    spill = {
        "l1a_xc": nc.dram_tensor("sp_l1a_xc", [2048, T], BF16, kind="Internal").ap(),
        "l1a_z": nc.dram_tensor("sp_l1a_z", [2048, T], BF16, kind="Internal").ap(),
    }
    return wd, spill


def prep_l1a(inputs):
    g = {}
    g["l1a_w_in"] = _cc(inputs["L1_attn_w_in"].astype(np.float32))
    cw = inputs["L1_attn_conv_w"].reshape(2048, 4)
    g["l1a_conv_w"] = _cc(cw.reshape(16, 128, 4).transpose(1, 0, 2).reshape(128, 64))
    g["l1a_conv_b"] = col_pack(inputs["L1_attn_conv_b"].reshape(2048))
    g["l1a_w_B"] = feat_tiles(inputs["L1_attn_w_B"])
    g["l1a_w_C"] = _cc(inputs["L1_attn_w_C"].astype(np.float32))
    g["l1a_w_out"] = feat_tiles(inputs["L1_attn_w_out"])
    return g


def build_l1_attn_standalone():
    nc = bass.Bass("TRN2", target_bir_lowering=False, debug=False)
    x_in = nc.dram_tensor("x_in", [D, T], F32, kind="ExternalInput").ap()
    x_out = nc.dram_tensor("x_out", [D, T], F32, kind="ExternalOutput").ap()
    wd, spill = l1a_dram(nc)
    b = B(nc)
    alloc_consts(b, nc)
    X = nc.sbuf_tensor("X", [128, 8 * T], F32).__enter__()
    with nc.Block() as block:
        b.block = block
        init_consts(b)

        def ldx(qq):
            for i in range(8):
                b.dma(qq, "qx", X[:, i * T:(i + 1) * T], x_in[i * 128:(i + 1) * 128, :])
        b.q(ldx)

        emit_l1_attn(b, X, wd, spill)

        def stx(qq):
            b.w(qq, "dve")
            for i in range(8):
                b.dma(qq, "qo", x_out[i * 128:(i + 1) * 128, :], X[:, i * T:(i + 1) * T])
            qq.wait_ge(b.sem["qo"], b.cnt["qo"])
        b.q(stx)
    return nc


def build_l0_attn_standalone(n_layers=8):
    nc = bass.Bass("TRN2", target_bir_lowering=False, debug=False)
    x_in = nc.dram_tensor("x_in", [D, T], F32, kind="ExternalInput").ap()
    x_out = nc.dram_tensor("x_out", [D, T], F32, kind="ExternalOutput").ap()
    xm_out = nc.dram_tensor("xm_out", [D, T], F32, kind="ExternalOutput").ap()
    cd = const_dram_decl(nc)
    wd, spill = l0a_dram(nc)
    spill["x_park"] = nc.dram_tensor("sp_x_park", [D, T], F32, kind="Internal").ap()
    b = B(nc)
    alloc_consts(b, nc)
    X = nc.sbuf_tensor("X", [128, 8 * T], F32).__enter__()
    with nc.Block() as block:
        b.block = block
        init_consts(b, cd)

        def ldx(qq):
            for i in range(8):
                b.dma(qq, "qx", X[:, i * T:(i + 1) * T], x_in[i * 128:(i + 1) * 128, :])
        b.q(ldx)

        emit_l0_attn(b, X, wd, spill, n_layers=n_layers)

        def stx(qq):
            b.w(qq, "dve")
            for i in range(8):
                b.dma(qq, "qo", x_out[i * 128:(i + 1) * 128, :], X[:, i * T:(i + 1) * T])
            qq.wait_ge(b.sem["qo"], b.cnt["qo"])
        b.q(stx)
    return nc


def build_l0_ffn_standalone(n_layers=2):
    nc = bass.Bass("TRN2", target_bir_lowering=False, debug=False)
    x_in = nc.dram_tensor("x_in", [D, T], F32, kind="ExternalInput").ap()
    x_out = nc.dram_tensor("x_out", [D, T], F32, kind="ExternalOutput").ap()
    cd = const_dram_decl(nc)
    wd = l0f_dram(nc)
    spill = {"x_park": nc.dram_tensor("sp_x_park", [D, T], F32, kind="Internal").ap()}
    b = B(nc)
    alloc_consts(b, nc)
    X = nc.sbuf_tensor("X", [128, 8 * T], F32).__enter__()
    with nc.Block() as block:
        b.block = block
        init_consts(b, cd)

        def ldx(qq):
            for i in range(8):
                b.dma(qq, "qx", X[:, i * T:(i + 1) * T], x_in[i * 128:(i + 1) * 128, :])
        b.q(ldx)

        emit_l0_ffn(b, X, wd, spill, n_layers=n_layers)

        def stx(qq):
            b.w(qq, "dve")
            for i in range(8):
                b.dma(qq, "qo", x_out[i * 128:(i + 1) * 128, :], X[:, i * T:(i + 1) * T])
            qq.wait_ge(b.sem["qo"], b.cnt["qo"])
        b.q(stx)
    return nc


def build_l1_ffn_standalone():
    nc = bass.Bass("TRN2", target_bir_lowering=False, debug=False)
    x_in = nc.dram_tensor("x_in", [D, T], F32, kind="ExternalInput").ap()
    x_out = nc.dram_tensor("x_out", [D, T], F32, kind="ExternalOutput").ap()
    cd = const_dram_decl(nc)
    wd = l1f_dram(nc)
    spill = {"x_park": nc.dram_tensor("sp_x_park", [D, T], F32, kind="Internal").ap()}
    b = B(nc)
    alloc_consts(b, nc)
    X = nc.sbuf_tensor("X", [128, 8 * T], F32).__enter__()
    with nc.Block() as block:
        b.block = block
        init_consts(b, cd)

        def ldx(qq):
            for i in range(8):
                b.dma(qq, "qx", X[:, i * T:(i + 1) * T], x_in[i * 128:(i + 1) * 128, :])
        b.q(ldx)

        emit_l1_ffn(b, X, wd, spill)

        def stx(qq):
            b.w(qq, "dve")
            for i in range(8):
                b.dma(qq, "qo", x_out[i * 128:(i + 1) * 128, :], X[:, i * T:(i + 1) * T])
            qq.wait_ge(b.sem["qo"], b.cnt["qo"])
        b.q(stx)
    return nc


def build_full():
    nc = bass.Bass("TRN2", target_bir_lowering=False, debug=False)
    x_in = nc.dram_tensor("x_in", [D, T], F32, kind="ExternalInput").ap()
    x_out = nc.dram_tensor("x_out", [D, T], F32, kind="ExternalOutput").ap()
    cd = const_dram_decl(nc)
    wd0a, sp0a = l0a_dram(nc)
    wd0f = l0f_dram(nc)
    wd1a, sp1a = l1a_dram(nc)
    wd1f = l1f_dram(nc)
    spill = dict(sp0a)
    spill.update(sp1a)
    spill["x_park"] = nc.dram_tensor("sp_x_park", [D, T], F32, kind="Internal").ap()
    b = B(nc)
    alloc_consts(b, nc)
    X = nc.sbuf_tensor("X", [128, 8 * T], F32).__enter__()
    with nc.Block() as block:
        b.block = block
        init_consts(b, cd)

        def ldx(qq):
            for i in range(8):
                b.dma(qq, "qx", X[:, i * T:(i + 1) * T], x_in[i * 128:(i + 1) * 128, :])
        b.q(ldx)

        emit_l0_attn(b, X, wd0a, spill)
        emit_l0_ffn(b, X, wd0f, spill)
        emit_l1_attn(b, X, wd1a, spill)
        emit_l1_ffn(b, X, wd1f, spill)

        def stx(qq):
            b.w(qq, "dve", "act", "pe")
            for i in range(8):
                b.dma(qq, "qo", x_out[i * 128:(i + 1) * 128, :], X[:, i * T:(i + 1) * T])
            qq.wait_ge(b.sem["qo"], b.cnt["qo"])
        b.q(stx)
    return nc


_BUILD_CACHE = {}


def _get_nc(name, builder):
    if name not in _BUILD_CACHE:
        _BUILD_CACHE[name] = builder()
    return _BUILD_CACHE[name]


def kernel(**inputs):
    wmap = {}
    wmap.update(const_host())
    wmap.update(prep_l0a(inputs))
    wmap.update(prep_l0f(inputs))
    wmap.update(prep_l1a(inputs))
    wmap.update(prep_l1f(inputs))
    x = np.ascontiguousarray(inputs["L0_attn_x"].astype(np.float32))
    xs = [np.ascontiguousarray(x[c].T) for c in range(NCORES)]
    nc = _get_nc("full", build_full)
    in_maps = [dict(wmap, x_in=xs[c]) for c in range(NCORES)]
    res = run_bass_kernel_spmd(nc, in_maps, core_ids=list(range(NCORES)))
    out = np.stack([res.results[c]["x_out"].T for c in range(NCORES)], axis=0)
    return np.ascontiguousarray(out.astype(np.float32))


# version 10 (final)
# speedup vs baseline: 1.1974x; 1.0094x over previous
import contextlib
import numpy as np
import ml_dtypes
import concourse.bass as bass
import concourse.mybir as mybir
from concourse.bass_utils import run_bass_kernel_spmd

F32 = mybir.dt.float32
BF16 = mybir.dt.bfloat16
AF = mybir.ActivationFunctionType
ALU = mybir.AluOpType
AX = mybir.AxisListType

T = 2048
D = 1024
NCORES = 8
TC = 512
NTC = T // TC
bf16 = ml_dtypes.bfloat16

SEMS = ("pe", "act", "dve", "pool", "qx", "qw", "qw2", "qsp", "qsl", "qz0", "qz1", "qo")


class B:
    def __init__(self, nc):
        self.nc = nc
        self.cnt = {s: 0 for s in SEMS}
        self.sem = {s: nc.semaphore(s).__enter__() for s in SEMS}
        self.block = None
        self.consts = {}

    def v(self, f):
        self.block.vector(f)

    def s(self, f):
        self.block.scalar(f)

    def t(self, f):
        self.block.tensor(f)

    def g(self, f):
        self.block.gpsimd(f)

    def q(self, f):
        self.block.sync(f)

    def inc(self, instr, name, k=1):
        instr.then_inc(self.sem[name], k)
        self.cnt[name] += k

    def dma(self, qeng, qname, out, in_):
        ins = qeng.dma_start(out=out, in_=in_)
        self.inc(ins, qname, 16)

    def w(self, eng, *names):
        for n in names:
            c = self.cnt[n]
            if c > 0:
                eng.wait_ge(self.sem[n], c)

    def wv(self, eng, name, val):
        if val > 0:
            eng.wait_ge(self.sem[name], val)

    def scope(self):
        return contextlib.ExitStack()

    def fence(self):
        def f(g):
            self.w(g, "pe", "act", "dve")
            ins = g.engine_nop()
            self.inc(ins, "pool")
        self.g(f)

    def sb(self, es, name, shape, dt):
        return es.enter_context(self.nc.sbuf_tensor(name, shape, dt))

    def pt(self, es, name, shape, dt):
        return es.enter_context(self.nc.psum_tensor(name, shape, dt))



def _cc(a):
    return np.ascontiguousarray(a)


def feat_tiles(w, P=128):
    F, M = w.shape
    assert F % P == 0
    return _cc(w.reshape(F // P, P, M).transpose(1, 0, 2).reshape(P, (F // P) * M))


def col_pack(vec, P=128):
    F = vec.shape[0]
    return _cc(vec.reshape(F // P, P).T)



def ln_emit(b, src_of, src_is_bf, rows, tag, ntiles=8, nfeat=None, eps=1e-5):
    nc = b.nc
    es = b.scope()
    scr8a = b.sb(es, tag + "_lncast", [128, ntiles * TC], BF16)
    scr8b = b.sb(es, tag + "_lnsq", [128, ntiles * TC], BF16)
    rowm = b.sb(es, tag + "_lnrowm", [1, T], F32)
    rowv = b.sb(es, tag + "_lnrowv", [1, T], F32)
    ps_sx = b.pt(es, tag + "_lnpsx", [1, TC], F32)
    ps_sq = b.pt(es, tag + "_lnpsq", [1, TC], F32)
    cast = [scr8a[:, i * TC:(i + 1) * TC] for i in range(ntiles)]
    sq = [scr8b[:, i * TC:(i + 1) * TC] for i in range(ntiles)]

    nfeat = nfeat or (128 * ntiles)
    inv = 1.0 / nfeat
    for tc in range(NTC):
        lo, hi = tc * TC, (tc + 1) * TC

        def casts(s, lo=lo, hi=hi):
            b.w(s, "pe", "dve", "qx", "pool")
            ins = None
            for i in range(ntiles):
                if not src_is_bf:
                    s.copy(cast[i][:, 0:TC], src_of(i)[:, lo:hi])
                ins = s.activation(sq[i][:, 0:TC], src_of(i)[:, lo:hi], AF.Square)
            b.inc(ins, "act")
        b.s(casts)

        def mms(t, lo=lo, hi=hi):
            b.w(t, "act")
            for i in range(ntiles):
                rhs = src_of(i)[:, lo:hi] if src_is_bf else cast[i][:, 0:TC]
                t.matmul(ps_sx[0:1, 0:TC], b.consts["ones_col_bf"][:, 0:1], rhs,
                         start=(i == 0), stop=(i == ntiles - 1))
            for i in range(ntiles):
                ins = t.matmul(ps_sq[0:1, 0:TC], b.consts["ones_col_bf"][:, 0:1],
                               sq[i][:, 0:TC], start=(i == 0), stop=(i == ntiles - 1))
            b.inc(ins, "pe")
        b.t(mms)

        def rowc(s, lo=lo, hi=hi):
            b.w(s, "pe")
            s.mul(rowm[0:1, lo:hi], ps_sx[0:1, 0:TC], inv)
            ins = s.mul(rowv[0:1, lo:hi], ps_sq[0:1, 0:TC], inv)
            b.inc(ins, "act")
        b.s(rowc)

    def rowfin_v(v):
        b.w(v, "act")
        v.tensor_mul(rows[0:1, T:2 * T], rowm[0:1, :], rowm[0:1, :])
        ins = v.tensor_sub(rowv[0:1, :], rowv[0:1, :], rows[0:1, T:2 * T])
        b.inc(ins, "dve")
    b.v(rowfin_v)

    def rowfin_s(s):
        b.w(s, "dve")
        s.activation(rows[0:1, T:2 * T], rowv[0:1, :], AF.Ln, bias=eps)
        ins = s.activation(rows[0:1, 0:T], rows[0:1, T:2 * T], AF.Exp, scale=-0.5)
        b.inc(ins, "act")
    b.s(rowfin_s)

    def nm2_v(v):
        b.w(v, "act")
        ins = v.scalar_tensor_tensor(rows[0:1, T:2 * T], rowm[0:1, :], -1.0, rows[0:1, 0:T],
                                     ALU.mult, ALU.mult)
        b.inc(ins, "dve")
    b.v(nm2_v)
    es.close()


def ln_bcast(b, rows, lo, n, ps_r, ps_m):
    def mm(t):
        b.w(t, "dve", "act")
        t.matmul(ps_r[:, 0:n], b.consts["ones_row_bf"][0:1, :], rows[0:1, lo:lo + n],
                 start=True, stop=True)
        ins = t.matmul(ps_m[:, 0:n], b.consts["ones_row_bf"][0:1, :], rows[0:1, T + lo:T + lo + n],
                       start=True, stop=True)
        b.inc(ins, "pe")
    b.t(mm)


def emit_wload(b, jobs, direct=(), war=None):
    stg = b.wstg
    if not hasattr(b, "_stgfree"):
        b._stgfree = [0, 0]
    jobs2 = []
    for dst, src in jobs:
        n = src.shape[1]
        o = 0
        while o < n:
            m = min(512, n - o)
            jobs2.append((dst[:, o:o + m], src[:, o:o + m]))
            o += m
    for idx, (dst, src) in enumerate(jobs2):
        par = idx % 2
        qsem = "qw" if par == 0 else "qw2"

        def ldq(qq, par=par, src=src, qsem=qsem):
            b.wv(qq, "pool", b._stgfree[par])
            ins = qq.dma_start(out=stg[0:src.shape[0], par * 512:par * 512 + src.shape[1]], in_=src)
            b.inc(ins, qsem, 16)
        b.q(ldq)

        def cst(g, par=par, dst=dst, p=src.shape[0], n=src.shape[1], qsem=qsem):
            b.w(g, qsem)
            if war is not None:
                b.wv(g, "pe", war)
            else:
                b.w(g, "pe", "act", "dve")
            ins = g.tensor_copy(dst, stg[0:p, par * 512:par * 512 + n])
            b.inc(ins, "pool")
        b.g(cst)
        b._stgfree[par] = b.cnt["pool"]

    if direct:
        def ldd(qq):
            b.w(qq, "pool")
            for dst, src in direct:
                b.dma(qq, "qw", dst, src)
        b.q(ldd)

    def qwbar(g):
        b.w(g, "qw", "qw2")
        ins = g.engine_nop()
        b.inc(ins, "pool")
    b.g(qwbar)



def _l1a_xln_apply(b, tc, X, rows, ps_r, ps_m, xln, f32scr):
    lo, hi = tc * TC, (tc + 1) * TC
    ln_bcast(b, rows, lo, TC, ps_r, ps_m)

    def apply(v):
        b.w(v, "pe")
        ins = None
        for i in range(8):
            v.tensor_mul(f32scr[:, 0:TC], X[:, i * T + lo:i * T + hi], ps_r[:, 0:TC])
            ins = v.tensor_add(xln[:, i * TC:(i + 1) * TC], f32scr[:, 0:TC], ps_m[:, 0:TC])
        b.inc(ins, "dve")
    b.v(apply)


def _l1a_phaseA_xp_chunk(b, tc, X, rows, ps_r, ps_m, ps_a, ps_b, wi, xpb, xln,
                         xst, cw, cb, acc, f32scr, xc_sp):
    lo = tc * TC
    _l1a_xln_apply(b, tc, X, rows, ps_r, ps_m, xln, f32scr)

    def halo(v):
        ins = None
        for ct in range(16):
            if tc == 0:
                ins = v.memset(xpb[:, ct * 520:ct * 520 + 3], 0.0)
            else:
                ins = v.tensor_copy(xpb[:, ct * 520:ct * 520 + 3],
                                    xpb[:, ct * 520 + 512:ct * 520 + 515])
        b.inc(ins, "dve")
    b.v(halo)

    for ec in range(16):
        ps = ps_a if ec % 2 == 0 else ps_b

        def mm(t, ec=ec, ps=ps):
            b.w(t, "pool", "dve")
            ins = None
            for dc in range(8):
                ins = t.matmul(ps[:, 0:TC],
                               wi[:, dc * 2048 + ec * 128:dc * 2048 + (ec + 1) * 128],
                               xln[:, dc * TC:(dc + 1) * TC],
                               start=(dc == 0), stop=(dc == 7))
            b.inc(ins, "pe")
        b.t(mm)

        def ev(v, ec=ec, ps=ps):
            b.w(v, "pe")
            ins = v.tensor_copy(xpb[:, ec * 520 + 3:ec * 520 + 515], ps[:, 0:TC])
            b.inc(ins, "dve")
        b.v(ev)

    for ct in range(16):
        def conv(v, ct=ct):
            b.w(v, "act", "qw")
            base = ct * 520
            v.tensor_scalar(acc[:, 0:TC], xpb[:, base + 3:base + 3 + TC],
                            cw[:, ct * 4 + 3:ct * 4 + 4], cb[:, ct:ct + 1],
                            ALU.mult, ALU.add)
            ins = None
            for k in (0, 1, 2):
                ins = v.scalar_tensor_tensor(acc[:, 0:TC], xpb[:, base + k:base + k + TC],
                                             cw[:, ct * 4 + k:ct * 4 + k + 1], acc[:, 0:TC],
                                             ALU.mult, ALU.add)
            b.inc(ins, "dve")
        b.v(conv)

        def sil(s, ct=ct):
            b.w(s, "dve", "qsp")
            ins = s.activation(xst[:, (ct % 2) * TC:(ct % 2 + 1) * TC], acc[:, 0:TC], AF.Silu)
            b.inc(ins, "act")
        b.s(sil)

        def spxc(qq, ct=ct):
            b.w(qq, "act")
            b.dma(qq, "qsp", xc_sp[ct * 128:(ct + 1) * 128, lo:lo + TC],
                  xst[:, (ct % 2) * TC:(ct % 2 + 1) * TC])
        b.q(spxc)


def _l1a_phaseA_z_chunk(b, tc, X, rows, ps_r, ps_m, ps_a, ps_b, wi, xln,
                        zst, f32scr, z_sp):
    lo = tc * TC
    _l1a_xln_apply(b, tc, X, rows, ps_r, ps_m, xln, f32scr)

    for ec in range(16):
        ps = ps_a if ec % 2 == 0 else ps_b
        par = ec % 2

        def mm(t, ec=ec, ps=ps):
            b.w(t, "pool", "dve")
            ins = None
            for dc in range(8):
                ins = t.matmul(ps[:, 0:TC],
                               wi[:, dc * 2048 + ec * 128:dc * 2048 + (ec + 1) * 128],
                               xln[:, dc * TC:(dc + 1) * TC],
                               start=(dc == 0), stop=(dc == 7))
            b.inc(ins, "pe")
        b.t(mm)

        def ev(v, ec=ec, ps=ps, par=par):
            b.w(v, "pe", "qsp")
            ins = v.tensor_copy(zst[:, par * TC:(par + 1) * TC], ps[:, 0:TC])
            b.inc(ins, "dve")
        b.v(ev)

        def spz(qq, ec=ec, par=par):
            b.w(qq, "dve")
            b.dma(qq, "qsp", z_sp[ec * 128:(ec + 1) * 128, lo:lo + TC],
                  zst[:, par * TC:(par + 1) * TC])
        b.q(spz)


def _l1a_phaseB_chunk(b, tc, ps_a, ps_b, ob, wo, wB, wC, xcc, zst, yg, nine, hsc,
                      hbf, xbs, zsil, xc_sp, z_sp):
    lo = tc * TC
    cur, prev = tc % 2, (tc + 1) % 2

    def ld(qq):
        b.w(qq, "pe", "act", "dve", "qsp")
        for ct in range(16):
            b.dma(qq, "qsl", xcc[:, ct * TC:(ct + 1) * TC],
                  xc_sp[ct * 128:(ct + 1) * 128, lo:lo + TC])
    b.q(ld)

    def mmxb(t):
        b.w(t, "qsl", "pool", "act")
        ins = None
        for ct in range(16):
            ins = t.matmul(ps_a[0:16, 0:TC], wB[:, ct * 16:(ct + 1) * 16],
                           xcc[:, ct * TC:(ct + 1) * TC], start=(ct == 0), stop=(ct == 15))
        b.inc(ins, "pe")
    b.t(mmxb)

    def evxb(s):
        b.w(s, "pe")
        ins = s.mul(xbs[0:16, 0:TC], ps_a[0:16, 0:TC], 0.1)
        b.inc(ins, "act")
    b.s(evxb)

    def scan(v):
        b.w(v, "act")
        init = 0.0 if tc == 0 else hsc[0:16, prev * TC + TC - 1:prev * TC + TC]
        v.tensor_tensor_scan(hsc[0:16, cur * TC:(cur + 1) * TC], nine[0:16, 0:TC],
                             xbs[0:16, 0:TC], init, ALU.mult, ALU.add)
        ins = v.tensor_copy(hbf[0:16, 0:TC], hsc[0:16, cur * TC:(cur + 1) * TC])
        b.inc(ins, "dve")
    b.v(scan)

    for ec in range(16):
        ps = ps_a if ec % 2 == 0 else ps_b
        par = ec % 2

        def ldz(qq, ec=ec, par=par):
            b.w(qq, "act", "pe")
            b.dma(qq, "qsl", zst[:, par * TC:(par + 1) * TC],
                  z_sp[ec * 128:(ec + 1) * 128, lo:lo + TC])
        b.q(ldz)

        def mmy(t, ec=ec, ps=ps):
            b.w(t, "dve")
            ins = t.matmul(ps[:, 0:TC], wC[0:16, ec * 128:(ec + 1) * 128],
                           hbf[0:16, 0:TC], start=True, stop=True)
            b.inc(ins, "pe")
        b.t(mmy)

        def silz(s, ec=ec, par=par):
            b.w(s, "qsl", "dve")
            ins = s.activation(zsil[:, 0:TC], zst[:, par * TC:(par + 1) * TC], AF.Silu)
            b.inc(ins, "act")
        b.s(silz)

        def gate(v, ec=ec, ps=ps):
            b.w(v, "pe", "act")
            ins = v.tensor_mul(yg[:, ec * TC:(ec + 1) * TC], ps[:, 0:TC], zsil[:, 0:TC])
            b.inc(ins, "dve")
        b.v(gate)

    for dc in range(8):
        ps = ps_a if dc % 2 == 0 else ps_b

        def mmo(t, dc=dc, ps=ps):
            b.w(t, "dve")
            ins = None
            for ct in range(16):
                ins = t.matmul(ps[:, 0:TC],
                               wo[:, ct * 1024 + dc * 128:ct * 1024 + (dc + 1) * 128],
                               yg[:, ct * TC:(ct + 1) * TC], start=(ct == 0), stop=(ct == 15))
            b.inc(ins, "pe")
        b.t(mmo)

        def evo(v, dc=dc, ps=ps):
            b.w(v, "pe")
            ins = v.tensor_copy(ob[:, dc * T + lo:dc * T + lo + TC], ps[:, 0:TC])
            b.inc(ins, "dve")
        b.v(evo)


def emit_residual_add(b, X, ob, rows, ps_r, ps_m, f32scr):
    for tc in range(NTC):
        lo = tc * TC
        ln_bcast(b, rows, lo, TC, ps_r, ps_m)

        def addres(v, lo=lo):
            b.w(v, "pe")
            ins = None
            for i in range(8):
                v.tensor_mul(f32scr[:, 0:TC], ob[:, i * T + lo:i * T + lo + TC], ps_r[:, 0:TC])
                v.tensor_add(f32scr[:, 0:TC], f32scr[:, 0:TC], ps_m[:, 0:TC])
                ins = v.tensor_add(X[:, i * T + lo:i * T + lo + TC],
                                   X[:, i * T + lo:i * T + lo + TC], f32scr[:, 0:TC])
            b.inc(ins, "dve")
        b.v(addres)


def emit_l1_attn(b, X, wd, spill, dbg=None, upto='full'):
    nc = b.nc
    xc_sp, z_sp = spill["l1a_xc"], spill["l1a_z"]
    es0 = b.scope()
    rows = b.sb(es0, "l1a_rows", [1, 2 * T], BF16)
    ps_r = b.pt(es0, "l1a_ps_r", [128, TC], F32)
    ps_m = b.pt(es0, "l1a_ps_m", [128, TC], F32)
    ps_a = b.pt(es0, "l1a_ps_a", [128, TC], F32)
    ps_b = b.pt(es0, "l1a_ps_b", [128, TC], F32)
    f32scr = b.sb(es0, "l1a_f32scr", [128, TC], F32)

    ln_emit(b, lambda i: X[:, i * T:(i + 1) * T], False, rows, "l1a_in")

    if dbg is not None and "rows" in dbg:
        def dmprows(qq):
            b.w(qq, "dve", "act")
            b.dma(qq, "qo", dbg["rows"], rows[0:1, :])
        b.q(dmprows)

    esA = b.scope()
    wi = b.sb(esA, "l1a_wi", [128, 8 * 2048], BF16)
    xpb = b.sb(esA, "l1a_xpb", [128, 16 * 520], BF16)
    xln = b.sb(esA, "l1a_xln", [128, 8 * TC], BF16)
    zst = b.sb(esA, "l1a_zst", [128, 2 * TC], BF16)
    xst = b.sb(esA, "l1a_xst", [128, 2 * TC], BF16)
    cw = b.sb(esA, "l1a_cw", [128, 64], F32)
    cb = b.sb(esA, "l1a_cb", [128, 16], F32)
    acc = b.sb(esA, "l1a_acc", [128, TC], F32)

    emit_wload(b, [(wi[:, i * 2048:(i + 1) * 2048],
                    wd["l1a_w_in"][i * 128:(i + 1) * 128, 0:2048]) for i in range(8)],
               direct=[(cw[:, :], wd["l1a_conv_w"]), (cb[:, :], wd["l1a_conv_b"])])

    for tc in range(NTC if upto != 'xp0' else 1):
        _l1a_phaseA_xp_chunk(b, tc, X, rows, ps_r, ps_m, ps_a, ps_b, wi, xpb,
                             xln, xst, cw, cb, acc, f32scr, xc_sp)
        if tc == 0 and dbg is not None and "xln" in dbg:
            def dmpxln(qq):
                b.w(qq, "dve", "act")
                b.dma(qq, "qo", dbg["xln"], xln[:, :])
                b.dma(qq, "qo", dbg["xpb"], xpb[:, :])
            b.q(dmpxln)

    if upto == 'xp0':
        esA.close()
        es0.close()
        return

    def fence_z(g):
        b.w(g, "pe", "act", "dve")
        ins = g.engine_nop()
        b.inc(ins, "pool")
    b.g(fence_z)
    emit_wload(b, [(wi[:, i * 2048:(i + 1) * 2048],
                    wd["l1a_w_in"][i * 128:(i + 1) * 128, 2048:4096]) for i in range(8)])

    for tc in range(NTC):
        _l1a_phaseA_z_chunk(b, tc, X, rows, ps_r, ps_m, ps_a, ps_b, wi, xln,
                            zst, f32scr, z_sp)
    esA.close()
    b.fence()

    esOB = b.scope()
    ob = b.sb(esOB, "l1a_ob", [128, 8 * T], BF16)

    esB = b.scope()
    wo = b.sb(esB, "l1a_wo", [128, 16 * 1024], BF16)
    wB = b.sb(esB, "l1a_wB", [128, 256], BF16)
    wC = b.sb(esB, "l1a_wC", [16, 2048], BF16)
    xcc = b.sb(esB, "l1a_xcc2", [128, 16 * TC], BF16)
    zst2 = b.sb(esB, "l1a_zst2", [128, 2 * TC], BF16)
    yg = b.sb(esB, "l1a_yg", [128, 16 * TC], BF16)
    nine = b.sb(esB, "l1a_nine", [16, TC], F32)
    hsc = b.sb(esB, "l1a_hsc", [16, 2 * TC], F32)
    hbf = b.sb(esB, "l1a_hbf", [16, TC], BF16)
    xbs = b.sb(esB, "l1a_xbs", [16, TC], F32)
    zsil = b.sb(esB, "l1a_zsil", [128, TC], F32)

    emit_wload(b, [(wo[:, i * 1024:(i + 1) * 1024],
                    wd["l1a_w_out"][:, i * 1024:(i + 1) * 1024]) for i in range(16)]
               + [(wB[:, :], wd["l1a_w_B"]), (wC[0:16, :], wd["l1a_w_C"])])

    def mset(v):
        ins = v.memset(nine[:, :], 0.9)
        b.inc(ins, "dve")
    b.v(mset)

    for tc in range(NTC):
        _l1a_phaseB_chunk(b, tc, ps_a, ps_b, ob, wo, wB, wC, xcc, zst2, yg,
                          nine, hsc, hbf, xbs, zsil, xc_sp, z_sp)
    esB.close()

    if dbg is not None and "ob" in dbg:
        def dmpob(qq):
            b.w(qq, "dve", "act")
            for i in range(8):
                b.dma(qq, "qo", dbg["ob"][i * 128:(i + 1) * 128, :], ob[:, i * T:(i + 1) * T])
        b.q(dmpob)
    ln_emit(b, lambda i: ob[:, i * T:(i + 1) * T], True, rows, "l1a_out")
    emit_residual_add(b, X, ob, rows, ps_r, ps_m, f32scr)
    esOB.close()
    es0.close()



def _l1f_router(b, xln, wd, combT):
    nc = b.nc
    es = b.scope()
    rwT = b.sb(es, "l1f_rwT_sb", [128, 8 * 8], BF16)
    ebias = b.sb(es, "l1f_eb_sb", [1, 8], F32)
    lgt = b.sb(es, "l1f_lgt", [128, 8], F32)
    eq1 = b.sb(es, "l1f_eq1", [128, 8], F32)
    eq2 = b.sb(es, "l1f_eq2", [128, 8], F32)
    l2t = b.sb(es, "l1f_l2t", [128, 8], F32)
    m1 = b.sb(es, "l1f_m1", [128, 4], F32)
    eqbf = b.sb(es, "l1f_eqbf", [128, 8], BF16)
    ps_s = b.pt(es, "l1f_ps_s", [128, 8], F32)
    ps_t = b.pt(es, "l1f_ps_t", [128, 128], F32)
    ps_tb = b.pt(es, "l1f_ps_tb", [8, 128], BF16)

    ebrep = b.sb(es, "l1f_ebrep", [128, 8], F32)
    emit_wload(b, [(rwT[:, i * 8:(i + 1) * 8], wd["l1f_rwT"][i * 128:(i + 1) * 128, :]) for i in range(8)],
               direct=[(ebias[0:1, :], wd["l1f_eb"])])

    def bias_bcast(t):
        b.w(t, "qw", "dve")
        ins = t.matmul(ps_t[:, 0:8], b.consts["ones_row_f32"][0:1, :], ebias[0:1, 0:8],
                       start=True, stop=True)
        b.inc(ins, "pe")
    b.t(bias_bcast)

    def evb(v):
        b.w(v, "pe")
        ins = v.tensor_copy(ebrep[:, 0:8], ps_t[:, 0:8])
        b.inc(ins, "dve")
    b.v(evb)

    for tt in range(16):
        lo = tt * 128

        def mml(t, lo=lo):
            b.w(t, "pool", "dve", "qw")
            ins = None
            for dc in range(8):
                ins = t.matmul(ps_s[:, 0:8], xln[:, dc * T + lo:dc * T + lo + 128],
                               rwT[:, dc * 8:(dc + 1) * 8], start=(dc == 0), stop=(dc == 7))
            b.inc(ins, "pe")
        b.t(mml)

        def top2(v):
            b.w(v, "pe")
            ops = [
                lambda: v.tensor_add(lgt[:, 0:8], ps_s[:, 0:8], ebrep[:, 0:8]),
                lambda: v.reduce_max(m1[:, 0:1], lgt[:, 0:8], AX.X),
                lambda: v.tensor_scalar(eq1[:, 0:8], lgt[:, 0:8], m1[:, 0:1], None, ALU.is_ge),
                lambda: v.scalar_tensor_tensor(l2t[:, 0:8], eq1[:, 0:8], -1e30, lgt[:, 0:8], ALU.mult, ALU.add),
                lambda: v.reduce_max(m1[:, 1:2], l2t[:, 0:8], AX.X),
                lambda: v.tensor_scalar(eq2[:, 0:8], l2t[:, 0:8], m1[:, 1:2], None, ALU.is_ge),
                lambda: v.tensor_sub(m1[:, 2:3], m1[:, 1:2], m1[:, 0:1]),
            ]
            for op in ops:
                ins = op()
                b.inc(ins, "dve")
                v.wait_ge(b.sem["dve"], b.cnt["dve"])
        b.v(top2)

        def sig(s):
            b.w(s, "dve")
            ins = s.activation(m1[:, 3:4], m1[:, 2:3], AF.Sigmoid)
            b.inc(ins, "act")
        b.s(sig)

        def comb(v):
            b.w(v, "act")
            ops = [
                lambda: v.tensor_scalar(m1[:, 2:3], m1[:, 3:4], -1.0, 1.0, ALU.mult, ALU.add),
                lambda: v.tensor_scalar(eq1[:, 0:8], eq1[:, 0:8], m1[:, 2:3], None, ALU.mult),
                lambda: v.scalar_tensor_tensor(eq1[:, 0:8], eq2[:, 0:8], m1[:, 3:4], eq1[:, 0:8],
                                               ALU.mult, ALU.add),
                lambda: v.tensor_copy(eqbf[:, 0:8], eq1[:, 0:8]),
            ]
            for op in ops:
                ins = op()
                b.inc(ins, "dve")
                v.wait_ge(b.sem["dve"], b.cnt["dve"])
        b.v(comb)

        def tr(t):
            b.w(t, "dve", "qx")
            ins = t.transpose(ps_tb[0:8, 0:128], eqbf[:, 0:8], b.consts["ident_bf"][:, :])
            b.inc(ins, "pe")
        b.t(tr)

        def evtr(v, lo=lo):
            b.w(v, "pe")
            ins = v.tensor_copy(combT[0:8, lo:lo + 128], ps_tb[0:8, 0:128])
            b.inc(ins, "dve")
        b.v(evtr)
    es.close()
    b.fence()


def emit_l1_ffn(b, X, wd, spill):
    nc = b.nc
    x_sp = spill["x_park"]
    es0 = b.scope()
    ps_r = b.pt(es0, "l1f_ps_r", [128, TC], F32)
    ps_m = b.pt(es0, "l1f_ps_m", [128, TC], F32)
    ps_a = b.pt(es0, "l1f_ps_a", [128, TC], F32)
    ps_b = b.pt(es0, "l1f_ps_b", [128, TC], F32)
    ps_c = b.pt(es0, "l1f_ps_c", [128, TC], F32)
    f32scr = b.sb(es0, "l1f_f32scr", [128, TC], F32)
    combT = b.sb(es0, "l1f_combT", [8, T], BF16)
    xln = b.sb(es0, "l1f_xln", [128, 8 * T], BF16)

    esR = b.scope()
    rows = b.sb(esR, "l1f_rows", [1, 2 * T], BF16)
    ln_emit(b, lambda i: X[:, i * T:(i + 1) * T], False, rows, "l1f_in")

    for tc in range(NTC):
        lo, hi = tc * TC, (tc + 1) * TC
        ln_bcast(b, rows, lo, TC, ps_r, ps_m)

        def apply(v, lo=lo, hi=hi):
            b.w(v, "pe")
            ins = None
            for i in range(8):
                v.tensor_mul(f32scr[:, 0:TC], X[:, i * T + lo:i * T + hi], ps_r[:, 0:TC])
                ins = v.tensor_add(xln[:, i * T + lo:i * T + hi], f32scr[:, 0:TC], ps_m[:, 0:TC])
            b.inc(ins, "dve")
        b.v(apply)

    _l1f_router(b, xln, wd, combT)
    esR.close()
    b.fence()

    def park(qq):
        b.w(qq, "dve", "act", "pe")
        for i in range(8):
            b.dma(qq, "qsp", x_sp[i * 128:(i + 1) * 128, :], X[:, i * T:(i + 1) * T])
    b.q(park)

    esE = b.scope()
    obacc = X
    hgbuf = b.sb(esE, "l1f_hg", [128, 16 * T], BF16)
    w13 = b.sb(esE, "l1f_w13", [128, 2 * 4096], BF16)
    w2s = b.sb(esE, "l1f_w2s", [128, 2 * 2048], BF16)
    cetile = b.sb(esE, "l1f_cetile", [128, T], BF16)
    silu2 = [f32scr, b.sb(esE, "l1f_silu1", [128, TC], F32)]
    ps_d = b.pt(esE, "l1f_ps_d", [128, TC], F32)
    ps_e = b.pt(esE, "l1f_ps_e", [128, TC], F32)

    b.l1f_war13 = [0, 0]
    b.l1f_war2 = [0, 0]
    b.l1f_snap1 = [0, 0]
    b.l1f_snap2 = [0, 0]

    def zero_ob(v):
        b.w(v, "qsp")
        ins = None
        for i in range(8):
            ins = v.memset(obacc[:, i * T:(i + 1) * T], 0.0)
        b.inc(ins, "dve")
    b.v(zero_ob)

    for ex in range(10):
        routed = ex >= 2
        w13src = wd["l1f_sw13"] if not routed else wd["l1f_w13"]
        w2src = wd["l1f_sw2r"] if not routed else wd["l1f_w2r"]
        eoff = ex if not routed else (ex - 2)

        for hc in range(16):
            par = hc % 2
            ap13 = w13src[eoff * 16 + hc]
            emit_wload(b, [(w13[:, par * 4096 + dc * 256:par * 4096 + (dc + 1) * 256],
                            ap13[dc * 128:(dc + 1) * 128, :]) for dc in range(8)],
                       war=b.l1f_war13[par])
            for tc in range(NTC):
                lo = tc * TC
                pp = (hc * NTC + tc) % 2
                psA = ps_a if pp == 0 else ps_d
                psB = ps_b if pp == 0 else ps_e

                def mm1(t, hc=hc, par=par, lo=lo, pp=pp, psA=psA, psB=psB, tc=tc):
                    b.w(t, "pool")
                    if hc == 0 and tc <= 1:
                        b.w(t, "dve")
                    else:
                        b.wv(t, "dve", b.l1f_snap1[pp])
                    for dc in range(8):
                        t.matmul(psA[:, 0:TC], w13[:, par * 4096 + dc * 256:par * 4096 + dc * 256 + 128],
                                 xln[:, dc * T + lo:dc * T + lo + TC], start=(dc == 0), stop=(dc == 7))
                    ins = None
                    for dc in range(8):
                        ins = t.matmul(psB[:, 0:TC], w13[:, par * 4096 + dc * 256 + 128:par * 4096 + dc * 256 + 256],
                                       xln[:, dc * T + lo:dc * T + lo + TC], start=(dc == 0), stop=(dc == 7))
                    b.inc(ins, "pe")
                b.t(mm1)
                pes = b.cnt["pe"]

                def gate_s(s, pp=pp, psA=psA, pes=pes):
                    b.wv(s, "pe", pes)
                    b.wv(s, "dve", b.l1f_snap1[pp])
                    ins = s.activation(silu2[pp][:, 0:TC], psA[:, 0:TC], AF.Silu)
                    b.inc(ins, "act")
                b.s(gate_s)
                acts = b.cnt["act"]

                def gate_v(v, hc=hc, lo=lo, pp=pp, psB=psB, acts=acts, pes=pes):
                    b.wv(v, "act", acts)
                    b.wv(v, "pe", pes)
                    ins = v.tensor_mul(hgbuf[:, hc * T + lo:hc * T + lo + TC],
                                       silu2[pp][:, 0:TC], psB[:, 0:TC])
                    b.inc(ins, "dve")
                b.v(gate_v)
                b.l1f_snap1[pp] = b.cnt["dve"]
            b.l1f_war13[par] = b.cnt["pe"]

        if routed:
            for tc in range(NTC):
                def cerep(t, e=eoff, tc=tc):
                    b.w(t, "dve", "qw", "qx")
                    ins = t.matmul(ps_c[:, 0:TC], b.consts["sel8"][0:8, e * 128:(e + 1) * 128],
                                   combT[0:8, tc * TC:(tc + 1) * TC], start=True, stop=True)
                    b.inc(ins, "pe")
                b.t(cerep)

                def evce(v, tc=tc):
                    b.w(v, "pe")
                    ins = v.tensor_copy(cetile[:, tc * TC:(tc + 1) * TC], ps_c[:, 0:TC])
                    b.inc(ins, "dve")
                b.v(evce)

        for dc in range(8):
            par = dc % 2
            ap2 = w2src[eoff * 8 + dc]
            emit_wload(b, [(w2s[:, par * 2048 + hc * 128:par * 2048 + (hc + 1) * 128],
                            ap2[hc * 128:(hc + 1) * 128, :]) for hc in range(16)],
                       war=b.l1f_war2[par])
            for tc in range(NTC):
                lo = tc * TC
                pp = (dc * NTC + tc) % 2
                psA = ps_a if pp == 0 else ps_b

                def mm2(t, dc=dc, par=par, lo=lo, pp=pp, psA=psA, tc=tc):
                    b.w(t, "pool")
                    if dc == 0 and tc <= 1:
                        b.w(t, "dve")
                    else:
                        b.wv(t, "dve", b.l1f_snap2[pp])
                    ins = None
                    for hc in range(16):
                        ins = t.matmul(psA[:, 0:TC], w2s[:, par * 2048 + hc * 128:par * 2048 + (hc + 1) * 128],
                                       hgbuf[:, hc * T + lo:hc * T + lo + TC],
                                       start=(hc == 0), stop=(hc == 15))
                    b.inc(ins, "pe")
                b.t(mm2)
                pes = b.cnt["pe"]

                if routed:
                    def acc(v, dc=dc, lo=lo, psA=psA, pes=pes):
                        b.wv(v, "pe", pes)
                        v.tensor_mul(f32scr[:, 0:TC], psA[:, 0:TC], cetile[:, lo:lo + TC])
                        ins = v.tensor_add(obacc[:, dc * T + lo:dc * T + lo + TC],
                                           obacc[:, dc * T + lo:dc * T + lo + TC], f32scr[:, 0:TC])
                        b.inc(ins, "dve")
                    b.v(acc)
                else:
                    def acc(v, dc=dc, lo=lo, psA=psA, pes=pes):
                        b.wv(v, "pe", pes)
                        ins = v.tensor_add(obacc[:, dc * T + lo:dc * T + lo + TC],
                                           obacc[:, dc * T + lo:dc * T + lo + TC], psA[:, 0:TC])
                        b.inc(ins, "dve")
                    b.v(acc)
                b.l1f_snap2[pp] = b.cnt["dve"]
            b.l1f_war2[par] = b.cnt["pe"]

    esE.close()
    b.fence()
    esE = b.scope()
    rows = b.sb(esE, "l1f_rows2", [1, 2 * T], BF16)
    xold = b.sb(esE, "l1f_xold", [128, 2 * TC], F32)

    ln_emit(b, lambda i: obacc[:, i * T:(i + 1) * T], False, rows, "l1f_out")

    for tc in range(NTC):
        lo = tc * TC
        ln_bcast(b, rows, lo, TC, ps_r, ps_m)
        for i in range(8):
            par = i % 2

            def ldxo(qq, i=i, lo=lo, par=par):
                b.w(qq, "dve", "pe", "pool")
                b.dma(qq, "qsl", xold[:, par * TC:(par + 1) * TC],
                      x_sp[i * 128:(i + 1) * 128, lo:lo + TC])
            b.q(ldxo)

            def addres(v, i=i, lo=lo, par=par):
                b.w(v, "pe", "qsl")
                v.tensor_mul(f32scr[:, 0:TC], obacc[:, i * T + lo:i * T + lo + TC], ps_r[:, 0:TC])
                v.tensor_add(f32scr[:, 0:TC], f32scr[:, 0:TC], ps_m[:, 0:TC])
                ins = v.tensor_add(X[:, i * T + lo:i * T + lo + TC],
                                   xold[:, par * TC:(par + 1) * TC], f32scr[:, 0:TC])
                b.inc(ins, "dve")
            b.v(addres)
    esE.close()
    es0.close()
    b.fence()


def l1f_dram(nc):
    wd = {
        "l1f_rwT": nc.dram_tensor("l1f_rwT", [1024, 8], F32, kind="ExternalInput").ap(),
        "l1f_eb": nc.dram_tensor("l1f_eb", [1, 8], F32, kind="ExternalInput").ap(),
    }
    w13 = nc.dram_tensor("l1f_w13p", [8 * 16, 1024, 256], F32, kind="ExternalInput")
    wd["l1f_w13"] = [w13.ap()[i] for i in range(128)]
    sw13 = nc.dram_tensor("l1f_sw13p", [2 * 16, 1024, 256], F32, kind="ExternalInput")
    wd["l1f_sw13"] = [sw13.ap()[i] for i in range(32)]
    w2r = nc.dram_tensor("l1f_w2rp", [8 * 8, 2048, 128], F32, kind="ExternalInput")
    wd["l1f_w2r"] = [w2r.ap()[i] for i in range(64)]
    sw2 = nc.dram_tensor("l1f_sw2rp", [2 * 8, 2048, 128], F32, kind="ExternalInput")
    wd["l1f_sw2r"] = [sw2.ap()[i] for i in range(16)]
    return wd


def prep_l1f(inputs):
    g = {}
    g["l1f_rwT"] = _cc(inputs["L1_ffn_router_w"].T.astype(np.float32))
    g["l1f_eb"] = _cc(inputs["L1_ffn_expert_bias"].reshape(1, 8).astype(np.float32))

    def pack13(w1, w3):
        E = w1.shape[0]
        out = np.empty((E * 16, 1024, 256), np.float32)
        for e in range(E):
            for hc in range(16):
                out[e * 16 + hc, :, 0:128] = w1[e, hc * 128:(hc + 1) * 128, :].T
                out[e * 16 + hc, :, 128:256] = w3[e, hc * 128:(hc + 1) * 128, :].T
        return _cc(out)

    def pack2(w2):
        E = w2.shape[0]
        out = np.empty((E * 8, 2048, 128), np.float32)
        for e in range(E):
            for dc in range(8):
                out[e * 8 + dc] = w2[e, dc * 128:(dc + 1) * 128, :].T
        return _cc(out)

    g["l1f_w13p"] = pack13(inputs["L1_ffn_w1"], inputs["L1_ffn_w3"])
    g["l1f_sw13p"] = pack13(inputs["L1_ffn_sw1"], inputs["L1_ffn_sw3"])
    g["l1f_w2rp"] = pack2(inputs["L1_ffn_w2"])
    g["l1f_sw2rp"] = pack2(inputs["L1_ffn_sw2"])
    return g



def _l0a_phaseA(b, lay, XM, wd, wi, xpb, xmbf, xst, zst, cw, cb, acc, xin_sp, z_sp):
    for tc in range(NTC):
        lo = tc * TC

        def cast(v, lo=lo):
            b.w(v, "pe")
            ins = None
            for i in range(8):
                ins = v.tensor_copy(xmbf[:, i * TC:(i + 1) * TC], XM[:, i * T + lo:i * T + lo + TC])
            b.inc(ins, "dve")
        b.v(cast)

        def halo(v, tc=tc):
            ins = None
            for ct in range(16):
                if tc == 0:
                    ins = v.memset(xpb[:, ct * 520:ct * 520 + 3], 0.0)
                else:
                    ins = v.tensor_copy(xpb[:, ct * 520:ct * 520 + 3],
                                        xpb[:, ct * 520 + 512:ct * 520 + 515])
            b.inc(ins, "dve")
        b.v(halo)

        snapA = [b.cnt["dve"], b.cnt["dve"]]
        for ec in range(16):
            ps = b.l0a_ps_a if ec % 2 == 0 else b.l0a_ps_b

            def mm(t, ec=ec, ps=ps, w0=snapA[ec % 2]):
                b.w(t, "pool", "qw")
                b.wv(t, "dve", w0)
                ins = None
                for dc in range(8):
                    ins = t.matmul(ps[:, 0:TC],
                                   wi[:, dc * 2048 + ec * 128:dc * 2048 + (ec + 1) * 128],
                                   xmbf[:, dc * TC:(dc + 1) * TC],
                                   start=(dc == 0), stop=(dc == 7))
                b.inc(ins, "pe")
            b.t(mm)
            pes = b.cnt["pe"]

            def ev(v, ec=ec, ps=ps, pes=pes):
                b.wv(v, "pe", pes)
                ins = v.tensor_copy(xpb[:, ec * 520 + 3:ec * 520 + 515], ps[:, 0:TC])
                b.inc(ins, "dve")
            b.v(ev)
            snapA[ec % 2] = b.cnt["dve"]

        snapS = [0, 0]
        snapC = [0, 0]
        for ct in range(16):
            ac = (ct % 2) * TC

            def conv(v, ct=ct, ac=ac, w0=snapS[ct % 2]):
                b.w(v, "qw")
                b.wv(v, "act", w0)
                base = ct * 520
                v.tensor_scalar(acc[:, ac:ac + TC], xpb[:, base + 3:base + 3 + TC],
                                cw[:, ct * 4 + 3:ct * 4 + 4], cb[:, ct:ct + 1],
                                ALU.mult, ALU.add)
                ins = None
                for k in (0, 1, 2):
                    ins = v.scalar_tensor_tensor(acc[:, ac:ac + TC], xpb[:, base + k:base + k + TC],
                                                 cw[:, ct * 4 + k:ct * 4 + k + 1], acc[:, ac:ac + TC],
                                                 ALU.mult, ALU.add)
                b.inc(ins, "dve")
            b.v(conv)
            dves = b.cnt["dve"]

            def sil(s, ct=ct, ac=ac, dves=dves):
                b.wv(s, "dve", dves)
                b.w(s, "qsp")
                ins = s.activation(xst[:, (ct % 2) * TC:(ct % 2 + 1) * TC], acc[:, ac:ac + TC], AF.Silu)
                b.inc(ins, "act")
            b.s(sil)
            snapS[ct % 2] = b.cnt["act"]

            def spxc(qq, ct=ct, lo=lo):
                b.w(qq, "act")
                b.dma(qq, "qsp", xin_sp[ct * 128:(ct + 1) * 128, lo:lo + TC],
                      xst[:, (ct % 2) * TC:(ct % 2 + 1) * TC])
            b.q(spxc)

    def fence_z(g):
        b.w(g, "pe", "act", "dve")
        ins = g.engine_nop()
        b.inc(ins, "pool")
    b.g(fence_z)
    emit_wload(b, [(wi[:, i * 2048:(i + 1) * 2048],
                    wd["l0a_w_inT"][lay][i * 128:(i + 1) * 128, 2048:4096]) for i in range(8)])
    for tc in range(NTC):
        lo = tc * TC

        def cast(v, lo=lo):
            b.w(v, "pe")
            ins = None
            for i in range(8):
                ins = v.tensor_copy(xmbf[:, i * TC:(i + 1) * TC], XM[:, i * T + lo:i * T + lo + TC])
            b.inc(ins, "dve")
        b.v(cast)

        snapZ = [b.cnt["dve"], b.cnt["dve"]]
        for ec in range(16):
            ps = b.l0a_ps_a if ec % 2 == 0 else b.l0a_ps_b
            par = ec % 2

            def mm(t, ec=ec, ps=ps, w0=snapZ[ec % 2]):
                b.w(t, "pool", "qw")
                b.wv(t, "dve", w0)
                ins = None
                for dc in range(8):
                    ins = t.matmul(ps[:, 0:TC],
                                   wi[:, dc * 2048 + ec * 128:dc * 2048 + (ec + 1) * 128],
                                   xmbf[:, dc * TC:(dc + 1) * TC],
                                   start=(dc == 0), stop=(dc == 7))
                b.inc(ins, "pe")
            b.t(mm)
            pes = b.cnt["pe"]

            def ev(v, ec=ec, ps=ps, par=par, pes=pes):
                b.wv(v, "pe", pes)
                b.w(v, "qsp")
                ins = v.tensor_copy(zst[:, par * TC:(par + 1) * TC], ps[:, 0:TC])
                b.inc(ins, "dve")
            b.v(ev)
            snapZ[ec % 2] = b.cnt["dve"]

            def spz(qq, ec=ec, par=par, lo=lo):
                b.w(qq, "dve")
                b.dma(qq, "qsp", z_sp[ec * 128:(ec + 1) * 128, lo:lo + TC],
                      zst[:, par * TC:(par + 1) * TC])
            b.q(spz)


def _l0a_phaseB(b, lay, XM, wd, bufs, xin_sp, z_sp):
    nc = b.nc
    (wout, wx, wdt, acoef, bdt, dpar, xcc, zstg, bcache, ccache, bnpack, apack,
     delta, du, dubf, yred, zsil, ygbuf, state, dtrbf, bbrow, ccrow) = bufs
    ps_a, ps_b = b.l0a_ps_a, b.l0a_ps_b

    def mkA_s(s):
        b.w(s, "qw", "dve", "pe")
        ins = s.activation(acoef[:, :], b.l0a_alog[:, :], AF.Exp)
        b.inc(ins, "act")
    b.s(mkA_s)

    def mkA_v(v):
        b.w(v, "act")
        ins = v.tensor_scalar(acoef[:, :], acoef[:, :], -1.0, None, ALU.mult)
        b.inc(ins, "dve")
    b.v(mkA_v)

    for tc in range(NTC):
        lo = tc * TC

        def ld(qq, lo=lo):
            b.w(qq, "pe", "act", "dve", "qsp")
            for ct in range(16):
                b.dma(qq, "qsl", xcc[:, ct * TC:(ct + 1) * TC],
                      xin_sp[ct * 128:(ct + 1) * 128, lo:lo + TC])
        b.q(ld)

        def mmxd(t):
            b.w(t, "qsl", "pool", "act")
            for ct in range(16):
                t.matmul(ps_a[0:64, 0:TC], wx[:, ct * 96:ct * 96 + 64],
                         xcc[:, ct * TC:(ct + 1) * TC], start=(ct == 0), stop=(ct == 15))
            for ct in range(16):
                t.matmul(ps_b[0:16, 0:TC], wx[:, ct * 96 + 64:ct * 96 + 80],
                         xcc[:, ct * TC:(ct + 1) * TC], start=(ct == 0), stop=(ct == 15))
            ins = None
            for ct in range(16):
                ins = t.matmul(ps_b[32:48, 0:TC], wx[:, ct * 96 + 80:ct * 96 + 96],
                               xcc[:, ct * TC:(ct + 1) * TC], start=(ct == 0), stop=(ct == 15))
            b.inc(ins, "pe")
        b.t(mmxd)

        def evxd(v):
            b.w(v, "pe")
            v.tensor_copy(dtrbf, ps_a[0:64, 0:TC])
            v.tensor_copy(bbrow, ps_b[0:16, 0:TC])
            ins = v.tensor_copy(ccrow, ps_b[32:48, 0:TC])
            b.inc(ins, "dve")
        b.v(evxd)
        snapBC = [b.cnt["dve"], b.cnt["dve"]]

        for n in range(16):
            ps = ps_a if n % 2 == 0 else ps_b

            def mmbc(t, n=n, ps=ps, w0=snapBC[n % 2]):
                b.w(t, "qx")
                b.wv(t, "dve", w0)
                t.matmul(ps[:, 0:TC], b.consts["sel16"][64:80, n * 128:(n + 1) * 128],
                         bbrow, start=True, stop=True)
                ins = t.matmul(ps[:, TC:2 * TC],
                               b.consts["sel16"][64:80, n * 128:(n + 1) * 128],
                               ccrow, start=True, stop=True)
                b.inc(ins, "pe")
            b.t(mmbc)
            pes = b.cnt["pe"]

            def evbc(v, n=n, ps=ps, pes=pes):
                b.wv(v, "pe", pes)
                v.tensor_copy(bcache[:, n * TC:(n + 1) * TC], ps[:, 0:TC])
                ins = v.tensor_copy(ccache[:, n * TC:(n + 1) * TC], ps[:, TC:2 * TC])
                b.inc(ins, "dve")
            b.v(evbc)
            snapBC[n % 2] = b.cnt["dve"]

        snap = {"softpv": [0] * 17, "scans": [0] * 17, "gpred": [0] * 17, "mmdt": [0] * 17}

        def tail(ct, lo):
            def silz(s, ct=ct):
                b.w(s, "qz0" if ct % 2 == 0 else "qz1", "dve")
                ins = s.activation(zsil[:, 0:TC], zstg[:, (ct % 2) * TC:(ct % 2 + 1) * TC], AF.Silu)
                b.inc(ins, "act")
            b.s(silz)
            snap_silz[ct % 2] = b.cnt["act"]

            def yg(v, ct=ct):
                b.w(v, "act")
                v.scalar_tensor_tensor(yred[:, 0:TC], xcc[:, ct * TC:(ct + 1) * TC],
                                       dpar[:, ct:ct + 1], yred[:, 0:TC], ALU.mult, ALU.add)
                ins = v.tensor_mul(ygbuf[:, ct * TC:(ct + 1) * TC], yred[:, 0:TC], zsil[:, 0:TC])
                b.inc(ins, "dve")
            b.v(yg)

        snap_silz = [0, 0]
        for ct in range(16):
            def ldz(qq, ct=ct, lo=lo, w0=snap_silz[ct % 2]):
                b.wv(qq, "act", w0)
                b.dma(qq, "qz0" if ct % 2 == 0 else "qz1",
                      zstg[:, (ct % 2) * TC:(ct % 2 + 1) * TC],
                      z_sp[ct * 128:(ct + 1) * 128, lo:lo + TC])
            b.q(ldz)

            def mmdt(t, ct=ct):
                b.wv(t, "dve", snap["softpv"][ct])
                if ct == 0:
                    b.w(t, "dve")
                ins = t.matmul(ps_a[:, 0:TC], wdt[0:64, ct * 128:(ct + 1) * 128],
                               dtrbf, start=True, stop=True)
                b.inc(ins, "pe")
            b.t(mmdt)
            snap["mmdt"][ct] = b.cnt["pe"]

            def softp_s(s, ct=ct):
                b.wv(s, "pe", snap["mmdt"][ct])
                b.wv(s, "dve", snap["softpv"][ct])
                ins = s.activation(delta[:, 0:TC], ps_a[:, 0:TC], AF.Abs, bias=bdt[:, ct:ct + 1])
                b.inc(ins, "act")
                s.wait_ge(b.sem["act"], b.cnt["act"])
                ins = s.activation(delta[:, 0:TC], delta[:, 0:TC], AF.Exp, scale=-1.0)
                b.inc(ins, "act")
                s.wait_ge(b.sem["act"], b.cnt["act"])
                ins = s.activation(delta[:, 0:TC], delta[:, 0:TC], AF.Ln, bias=1.0)
                b.inc(ins, "act")
            b.s(softp_s)

            def softp_v(v, ct=ct):
                b.w(v, "act")
                v.tensor_scalar(du[:, 0:TC], ps_a[:, 0:TC], bdt[:, ct:ct + 1], 0.0,
                                ALU.add, ALU.max)
                v.tensor_add(delta[:, 0:TC], delta[:, 0:TC], du[:, 0:TC])
                ins = v.tensor_mul(dubf[:, 0:TC], delta[:, 0:TC], xcc[:, ct * TC:(ct + 1) * TC])
                b.inc(ins, "dve")
                v.wait_ge(b.sem["dve"], b.cnt["dve"])
                dub = dubf[:, 0:TC].rearrange("p (a t) -> p a t", a=1).broadcast_to([128, 16, TC])
                ins = v.tensor_mul(bnpack[:, :].rearrange("p (n t) -> p n t", n=16),
                                   dub,
                                   bcache[:, :].rearrange("p (n t) -> p n t", n=16))
                b.inc(ins, "dve")
            b.v(softp_v)
            snap["softpv"][ct + 1] = b.cnt["dve"]

            for g in range(8):
                slot = g % 2

                def exps(s, ct=ct, g=g, slot=slot):
                    if g >= 2:
                        b.wv(s, "dve", snap["scans_g"][g - 2])
                    else:
                        b.wv(s, "dve", snap["softpv"][ct + 1])
                    ins = None
                    for j in range(2):
                        n = g * 2 + j
                        ins = s.activation(apack[:, (slot * 2 + j) * TC:(slot * 2 + j + 1) * TC],
                                           delta[:, 0:TC],
                                           AF.Exp, scale=acoef[:, ct * 16 + n:ct * 16 + n + 1])
                    b.inc(ins, "act")
                b.s(exps)
                if g == 0:
                    snap["scans_g"] = [0] * 8

                def scans(v, ct=ct, g=g, tc=tc, slot=slot):
                    b.w(v, "act")
                    ins = None
                    for j in range(2):
                        n = g * 2 + j
                        init = 0.0 if tc == 0 else state[:, ct * 16 + n:ct * 16 + n + 1]
                        ins = v.tensor_tensor_scan(bnpack[:, n * TC:(n + 1) * TC],
                                                   apack[:, (slot * 2 + j) * TC:(slot * 2 + j + 1) * TC],
                                                   bnpack[:, n * TC:(n + 1) * TC],
                                                   init, ALU.mult, ALU.add)
                    b.inc(ins, "dve")
                b.v(scans)
                snap["scans_g"][g] = b.cnt["dve"]

            def post(v, ct=ct):
                sv = bnpack[:, :].rearrange("p (n t) -> p n t", n=16)[:, :, TC - 1:TC]
                v.tensor_copy(state[:, ct * 16:(ct + 1) * 16].rearrange("p (n o) -> p n o", o=1), sv)
                ins = v.tensor_mul(bnpack[:, :], bnpack[:, :], ccache[:, :])
                b.inc(ins, "dve")
            b.v(post)
            snap["scans"][ct] = b.cnt["dve"]

            def gred(v, ct=ct):
                ins = v.tensor_reduce(yred[:, 0:TC], bnpack[:, :].rearrange("p (n t) -> p t n", n=16),
                                      AX.X, ALU.add)
                b.inc(ins, "dve")
            b.v(gred)
            snap["gpred"][ct + 1] = b.cnt["dve"]

            if ct > 0:
                tail(ct - 1, lo)
        tail(15, lo)
        snapO = [0, 0]
        for dc in range(8):
            ps = ps_a if dc % 2 == 0 else ps_b

            def mmo(t, dc=dc, ps=ps, w0=snapO[dc % 2]):
                b.w(t, "pool")
                if dc < 2:
                    b.w(t, "dve")
                else:
                    b.wv(t, "dve", w0)
                ins = None
                for ct in range(16):
                    ins = t.matmul(ps[:, 0:TC],
                                   wout[:, ct * 1024 + dc * 128:ct * 1024 + (dc + 1) * 128],
                                   ygbuf[:, ct * TC:(ct + 1) * TC], start=(ct == 0), stop=(ct == 15))
                b.inc(ins, "pe")
            b.t(mmo)
            pes = b.cnt["pe"]

            def evo(v, dc=dc, ps=ps, lo=lo, pes=pes):
                b.wv(v, "pe", pes)
                ins = v.tensor_add(XM[:, dc * T + lo:dc * T + lo + TC],
                                   XM[:, dc * T + lo:dc * T + lo + TC], ps[:, 0:TC])
                b.inc(ins, "dve")
            b.v(evo)
            snapO[dc % 2] = b.cnt["dve"]


def emit_l0_attn(b, X, wd, spill, n_layers=8):
    nc = b.nc
    x_sp = spill["x_park"]
    xin_sp, z_sp = spill["l0a_xin"], spill["l0a_z"]
    XM = X

    es0 = b.scope()
    ps_r = b.pt(es0, "l0a_ps_r", [128, TC], F32)
    ps_m = b.pt(es0, "l0a_ps_m", [128, TC], F32)
    b.l0a_ps_a = b.pt(es0, "l0a_ps_a", [128, 2 * TC], F32)
    b.l0a_ps_b = b.pt(es0, "l0a_ps_b", [128, 2 * TC], F32)
    f32scr = b.sb(es0, "l0a_f32scr", [128, TC], F32)

    esR = b.scope()
    rows = b.sb(esR, "l0a_rows", [1, 2 * T], BF16)
    ln_emit(b, lambda i: X[:, i * T:(i + 1) * T], False, rows, "l0a_in")

    def park(qq):
        b.w(qq, "dve", "act", "pe")
        for i in range(8):
            b.dma(qq, "qsp", x_sp[i * 128:(i + 1) * 128, :], X[:, i * T:(i + 1) * T])
    b.q(park)

    for tc in range(NTC):
        lo = tc * TC
        ln_bcast(b, rows, lo, TC, ps_r, ps_m)

        def apply(v, lo=lo):
            b.w(v, "pe", "qsp")
            ins = None
            for i in range(8):
                v.tensor_mul(f32scr[:, 0:TC], X[:, i * T + lo:i * T + lo + TC], ps_r[:, 0:TC])
                ins = v.tensor_add(XM[:, i * T + lo:i * T + lo + TC], f32scr[:, 0:TC], ps_m[:, 0:TC])
            b.inc(ins, "dve")
        b.v(apply)
    esR.close()
    b.fence()

    for lay in range(n_layers):
        sfx = f"_{lay}"
        esW = b.scope()
        wout = b.sb(esW, "l0a_wout" + sfx, [128, 16 * 1024], BF16)
        wx = b.sb(esW, "l0a_wx" + sfx, [128, 16 * 96], BF16)
        wdt = b.sb(esW, "l0a_wdt" + sfx, [64, 2048], BF16)
        acoef = b.sb(esW, "l0a_acoef" + sfx, [128, 256], F32)
        b.l0a_alog = acoef
        bdt = b.sb(esW, "l0a_bdt" + sfx, [128, 16], F32)
        dpar = b.sb(esW, "l0a_dpar" + sfx, [128, 16], F32)

        esA = b.scope()
        wi = b.sb(esA, "l0a_wi" + sfx, [128, 8 * 2048], BF16)
        xpb = b.sb(esA, "l0a_xpb" + sfx, [128, 16 * 520], BF16)
        xmbf = b.sb(esA, "l0a_xmbf" + sfx, [128, 8 * TC], BF16)
        xst = b.sb(esA, "l0a_xst" + sfx, [128, 2 * TC], BF16)
        zst = b.sb(esA, "l0a_zst" + sfx, [128, 2 * TC], BF16)
        cw = b.sb(esA, "l0a_cw" + sfx, [128, 64], F32)
        cb = b.sb(esA, "l0a_cb" + sfx, [128, 16], F32)
        acc = b.sb(esA, "l0a_acc" + sfx, [128, 2 * TC], F32)
        emit_wload(b, [(wi[:, i * 2048:(i + 1) * 2048],
                        wd["l0a_w_inT"][lay][i * 128:(i + 1) * 128, 0:2048]) for i in range(8)],
                   direct=[(cw[:, :], wd["l0a_conv_w"][lay]), (cb[:, :], wd["l0a_conv_b"][lay])])
        emit_wload(b, [(wout[:, i * 1024:(i + 1) * 1024],
                        wd["l0a_w_outT"][lay][i * 128:(i + 1) * 128, :]) for i in range(16)]
                   + [(wx[:, i * 96:(i + 1) * 96], wd["l0a_w_xT"][lay][i * 128:(i + 1) * 128, :]) for i in range(16)]
                   + [(wdt[0:64, :], wd["l0a_w_dtT"][lay])],
                   direct=[(b.l0a_alog[:, :], wd["l0a_A_log"][lay]),
                           (bdt[:, :], wd["l0a_b_dt"][lay]),
                           (dpar[:, :], wd["l0a_D_par"][lay])])
        _l0a_phaseA(b, lay, XM, wd, wi, xpb, xmbf, xst, zst, cw, cb, acc, xin_sp, z_sp)
        esA.close()
        b.fence()

        esB = b.scope()
        xcc = b.sb(esB, "l0a_xcc" + sfx, [128, 16 * TC], BF16)
        zstg = b.sb(esB, "l0a_zstg" + sfx, [128, 2 * TC], BF16)
        bcache = b.sb(esB, "l0a_bcache" + sfx, [128, 16 * TC], BF16)
        ccache = b.sb(esB, "l0a_ccache" + sfx, [128, 16 * TC], BF16)
        bnpack = b.sb(esB, "l0a_bnpack" + sfx, [128, 16 * TC], BF16)
        dubf = b.sb(esB, "l0a_dubf" + sfx, [128, TC], BF16)
        apack = b.sb(esB, "l0a_apack" + sfx, [128, 4 * TC], F32)
        du = apack[:, 3 * TC:4 * TC]
        delta = b.sb(esB, "l0a_delta" + sfx, [128, TC], F32)
        yred = b.sb(esB, "l0a_yred" + sfx, [128, TC], F32)
        zsil = b.sb(esB, "l0a_zsil" + sfx, [128, TC], BF16)
        ygbuf = xcc
        state = b.sb(esB, "l0a_state" + sfx, [128, 256], BF16)
        xdblb = b.sb(esB, "l0a_xdblb" + sfx, [128, 2 * TC], BF16)
        dtrbf = xdblb[0:64, 0:TC]
        bbrow = xdblb[64:80, 0:TC]
        ccrow = xdblb[64:80, TC:2 * TC]
        bufs = (wout, wx, wdt, acoef, bdt, dpar, xcc, zstg, bcache, ccache, bnpack,
                apack, delta, du, dubf, yred, zsil, ygbuf, state, dtrbf, bbrow, ccrow)
        _l0a_phaseB(b, lay, XM, wd, bufs, xin_sp, z_sp)
        esB.close()
        esW.close()
        b.fence()

    esF = b.scope()
    rows = b.sb(esF, "l0a_rows2", [1, 2 * T], BF16)
    ln_emit(b, lambda i: XM[:, i * T:(i + 1) * T], False, rows, "l0a_out")
    xold = b.sb(esF, "l0a_xold", [128, 2 * TC], F32)
    for tc in range(NTC):
        lo = tc * TC
        ln_bcast(b, rows, lo, TC, ps_r, ps_m)
        for i in range(8):
            par = i % 2

            def ldxo(qq, i=i, lo=lo, par=par):
                b.w(qq, "dve", "pe", "pool")
                b.dma(qq, "qsl", xold[:, par * TC:(par + 1) * TC],
                      x_sp[i * 128:(i + 1) * 128, lo:lo + TC])
            b.q(ldxo)

            def addres(v, i=i, lo=lo, par=par):
                b.w(v, "pe", "qsl")
                v.tensor_mul(f32scr[:, 0:TC], XM[:, i * T + lo:i * T + lo + TC], ps_r[:, 0:TC])
                v.tensor_add(f32scr[:, 0:TC], f32scr[:, 0:TC], ps_m[:, 0:TC])
                ins = v.tensor_add(X[:, i * T + lo:i * T + lo + TC],
                                   xold[:, par * TC:(par + 1) * TC], f32scr[:, 0:TC])
                b.inc(ins, "dve")
            b.v(addres)
    esF.close()
    es0.close()
    b.fence()


def l0a_dram(nc):
    wd = {}
    w = nc.dram_tensor("l0a_w_inT", [8, 1024, 4096], F32, kind="ExternalInput")
    wd["l0a_w_inT"] = [w.ap()[i] for i in range(8)]
    w = nc.dram_tensor("l0a_conv_w_p", [8, 128, 64], F32, kind="ExternalInput")
    wd["l0a_conv_w"] = [w.ap()[i] for i in range(8)]
    w = nc.dram_tensor("l0a_conv_b_p", [8, 128, 16], F32, kind="ExternalInput")
    wd["l0a_conv_b"] = [w.ap()[i] for i in range(8)]
    w = nc.dram_tensor("l0a_w_xT", [8, 2048, 96], F32, kind="ExternalInput")
    wd["l0a_w_xT"] = [w.ap()[i] for i in range(8)]
    w = nc.dram_tensor("l0a_w_dtT", [8, 64, 2048], F32, kind="ExternalInput")
    wd["l0a_w_dtT"] = [w.ap()[i] for i in range(8)]
    w = nc.dram_tensor("l0a_A_log_p", [8, 128, 256], F32, kind="ExternalInput")
    wd["l0a_A_log"] = [w.ap()[i] for i in range(8)]
    w = nc.dram_tensor("l0a_b_dt_p", [8, 128, 16], F32, kind="ExternalInput")
    wd["l0a_b_dt"] = [w.ap()[i] for i in range(8)]
    w = nc.dram_tensor("l0a_D_par_p", [8, 128, 16], F32, kind="ExternalInput")
    wd["l0a_D_par"] = [w.ap()[i] for i in range(8)]
    w = nc.dram_tensor("l0a_w_outT", [8, 2048, 1024], F32, kind="ExternalInput")
    wd["l0a_w_outT"] = [w.ap()[i] for i in range(8)]
    spill = {
        "l0a_xin": nc.dram_tensor("sp_l0a_xin", [2048, T], BF16, kind="Internal").ap(),
        "l0a_z": nc.dram_tensor("sp_l0a_z", [2048, T], BF16, kind="Internal").ap(),
    }
    return wd, spill


def prep_l0a(inputs):
    g = {}
    g["l0a_w_inT"] = _cc(inputs["L0_attn_W_in"].transpose(0, 2, 1).astype(np.float32))
    cw = inputs["L0_attn_conv_w"]
    g["l0a_conv_w_p"] = _cc(cw.reshape(8, 16, 128, 4).transpose(0, 2, 1, 3).reshape(8, 128, 64))
    g["l0a_conv_b_p"] = _cc(inputs["L0_attn_conv_b"].reshape(8, 16, 128).transpose(0, 2, 1))
    g["l0a_w_xT"] = _cc(inputs["L0_attn_W_x"].transpose(0, 2, 1).astype(np.float32))
    g["l0a_w_dtT"] = _cc(inputs["L0_attn_W_dt"].transpose(0, 2, 1).astype(np.float32))
    al = inputs["L0_attn_A_log"]
    g["l0a_A_log_p"] = _cc(al.reshape(8, 16, 128, 16).transpose(0, 2, 1, 3).reshape(8, 128, 256))
    g["l0a_b_dt_p"] = _cc(inputs["L0_attn_b_dt"].reshape(8, 16, 128).transpose(0, 2, 1))
    g["l0a_D_par_p"] = _cc(inputs["L0_attn_D_par"].reshape(8, 16, 128).transpose(0, 2, 1))
    g["l0a_w_outT"] = _cc(inputs["L0_attn_W_out"].transpose(0, 2, 1).astype(np.float32))
    return g



def _l0f_attn(b, lay, XF, wd, es0_t, sfx):
    nc = b.nc
    (rows, ps_r, ps_m, ps_a, ps_b, f32scr) = es0_t
    es = b.scope()
    qkvb = b.sb(es, "l0f_qkvb" + sfx, [128, 12 * T], BF16)
    obuf = b.sb(es, "l0f_obuf" + sfx, [128, 4 * T], BF16)
    esQ = b.scope()
    wqkv = b.sb(esQ, "l0f_wqkv" + sfx, [128, 8 * 1536], BF16)
    hbuf = b.sb(esQ, "l0f_hbuf" + sfx, [128, 8 * TC], BF16)
    ps_o = b.pt(es, "l0f_ps_o" + sfx, [128, TC], F32)
    ps_tr = b.pt(es, "l0f_ps_tr" + sfx, [128, 256], BF16)
    ps_bc = b.pt(es, "l0f_ps_bc" + sfx, [128, TC], F32)
    ps_sq = b.pt(es, "l0f_ps_sq" + sfx, [1, TC], F32)

    emit_wload(b, [(wqkv[:, i * 1536:(i + 1) * 1536], wd["l0f_wqkv"][lay][i * 128:(i + 1) * 128, :])
                   for i in range(8)])
    for tc in range(NTC):
        lo = tc * TC

        def mkh(v, lo=lo, tc=tc):
            b.w(v, "pe")
            ins = None
            for i in range(8):
                if i < 4:
                    ins = v.tensor_copy(hbuf[:, i * TC:(i + 1) * TC], XF[:, i * T + lo:i * T + lo + TC])
                else:
                    if tc == 0:
                        v.memset(hbuf[:, i * TC:i * TC + 1], 0.0)
                        ins = v.tensor_copy(hbuf[:, i * TC + 1:(i + 1) * TC],
                                            XF[:, i * T:i * T + TC - 1])
                    else:
                        ins = v.tensor_copy(hbuf[:, i * TC:(i + 1) * TC],
                                            XF[:, i * T + lo - 1:i * T + lo + TC - 1])
            b.inc(ins, "dve")
        b.v(mkh)

        snapQ = [b.cnt["dve"], b.cnt["dve"]]
        for ec in range(12):
            ps = ps_a if ec % 2 == 0 else ps_b

            def mm(t, ec=ec, ps=ps, w0=snapQ[ec % 2]):
                b.w(t, "pool", "qw")
                b.wv(t, "dve", w0)
                ins = None
                for dc in range(8):
                    ins = t.matmul(ps[:, 0:TC], wqkv[:, dc * 1536 + ec * 128:dc * 1536 + (ec + 1) * 128],
                                   hbuf[:, dc * TC:(dc + 1) * TC], start=(dc == 0), stop=(dc == 7))
                b.inc(ins, "pe")
            b.t(mm)
            pes = b.cnt["pe"]

            def ev(v, ec=ec, ps=ps, lo=lo, pes=pes):
                b.wv(v, "pe", pes)
                ins = v.tensor_copy(qkvb[:, ec * T + lo:ec * T + lo + TC], ps[:, 0:TC])
                b.inc(ins, "dve")
            b.v(ev)
            snapQ[ec % 2] = b.cnt["dve"]

    esQ.close()
    b.fence()
    esS = b.scope()
    sqscr = b.sb(esS, "l0f_sqscr" + sfx, [128, T], BF16)
    hrow = b.sb(esS, "l0f_hrow" + sfx, [1, TC], F32)
    hrowb = b.sb(esS, "l0f_hrowb" + sfx, [1, T], BF16)
    qsc = b.sb(esS, "l0f_qsc" + sfx, [128, 2], F32)
    rot64 = b.sb(esS, "l0f_rot64" + sfx, [128, 64], BF16)
    tq = b.sb(esS, "l0f_tq" + sfx, [128, 4 * T], BF16)
    maskt = b.sb(esS, "l0f_mask" + sfx, [128, 4 * 1024], BF16)
    vtbuf = b.sb(esS, "l0f_vt" + sfx, [128, 16 * 64], BF16)
    attnT = b.sb(esS, "l0f_attnT" + sfx, [128, 8 * TC], BF16)
    simb = b.sb(esS, "l0f_simb" + sfx, [128, 1024], F32)
    attnb = b.sb(esS, "l0f_attnb" + sfx, [128, 1024], BF16)
    ncol = b.sb(esS, "l0f_ncol" + sfx, [128, 4], F32)
    qnscr = b.sb(esS, "l0f_qn" + sfx, [128, TC], BF16)
    t1scr = b.sb(esS, "l0f_t1" + sfx, [128, TC], F32)
    t2scr = b.sb(esS, "l0f_t2" + sfx, [128, TC], F32)
    emit_wload(b, [], direct=[(qsc[:, 0:1], wd["l0f_qs"][lay]), (qsc[:, 1:2], wd["l0f_ks"][lay]),
                              (rot64[:, :], wd["l0f_rot64"]), (tq[:, :], wd["l0f_tq"]),
                              (maskt[:, :], wd["l0f_mask"])])

    for h in range(16):
        tile = (h // 2) if h < 8 else (4 + (h - 8) // 2)
        hb = h % 2
        base = hb * 64
        isq = h < 8
        ctbl = 0 if isq else 2
        stbl = 1 if isq else 3
        scol = 0 if isq else 1

        def sq_s(s, tile=tile):
            b.w(s, "dve", "pe")
            ins = s.activation(sqscr[:, 0:T], qkvb[:, tile * T:(tile + 1) * T], AF.Square)
            b.inc(ins, "act")
        if hb == 0:
            b.s(sq_s)

        for tc in range(NTC):
            def mmsq(t, tc=tc, base=base, tile=tile):
                b.w(t, "act", "dve")
                ins = t.matmul(ps_sq[0:1, 0:TC], b.consts["ones_col_bf"][base:base + 64, 0:1],
                               sqscr[base:base + 64, tc * TC:(tc + 1) * TC], start=True, stop=True)
                b.inc(ins, "pe")
            b.t(mmsq)

            def rstd_v(v, tc=tc):
                b.w(v, "pe")
                ins = v.tensor_scalar(hrow[0:1, 0:TC], ps_sq[0:1, 0:TC], 1e-24, None, ALU.max)
                b.inc(ins, "dve")
            b.v(rstd_v)

            def rstd_s(s, tc=tc):
                b.w(s, "dve")
                ins = s.activation(hrow[0:1, 0:TC], hrow[0:1, 0:TC], AF.Ln)
                b.inc(ins, "act")
                s.wait_ge(b.sem["act"], b.cnt["act"])
                ins = s.activation(hrowb[0:1, tc * TC:(tc + 1) * TC], hrow[0:1, 0:TC], AF.Exp, scale=-0.5)
                b.inc(ins, "act")
            b.s(rstd_s)

        for tc in range(NTC):
            lo = tc * TC

            def bc(t, lo=lo, base=base):
                b.w(t, "act", "dve")
                ins = t.matmul(ps_bc[base:base + 64, 0:TC], b.consts["ones_row_bf"][0:1, 0:64],
                               hrowb[0:1, lo:lo + TC], start=True, stop=True)
                b.inc(ins, "pe")
            b.t(bc)

            def app1(v, lo=lo, base=base, tile=tile, scol=scol):
                b.w(v, "pe")
                v.tensor_mul(qnscr[base:base + 64, 0:TC],
                             qkvb[base:base + 64, tile * T + lo:tile * T + lo + TC],
                             ps_bc[base:base + 64, 0:TC])
                ins = v.tensor_scalar(qnscr[base:base + 64, 0:TC], qnscr[base:base + 64, 0:TC],
                                      qsc[base:base + 64, scol:scol + 1], None, ALU.mult)
                b.inc(ins, "dve")
            b.v(app1)

            def rotmm(t, base=base):
                b.w(t, "dve", "qw")
                ins = t.matmul(ps_o[base:base + 64, 0:TC], rot64[base:base + 64, 0:64],
                               qnscr[base:base + 64, 0:TC], start=True, stop=True)
                b.inc(ins, "pe")
            b.t(rotmm)

            def app2(v, lo=lo, base=base, tile=tile, ctbl=ctbl, stbl=stbl):
                b.w(v, "pe")
                v.tensor_mul(t1scr[base:base + 64, 0:TC], qnscr[base:base + 64, 0:TC],
                             tq[base:base + 64, ctbl * T + lo:ctbl * T + lo + TC])
                v.tensor_mul(t2scr[base:base + 64, 0:TC], ps_o[base:base + 64, 0:TC],
                             tq[base:base + 64, stbl * T + lo:stbl * T + lo + TC])
                ins = v.tensor_add(qkvb[base:base + 64, tile * T + lo:tile * T + lo + TC],
                                   t1scr[base:base + 64, 0:TC], t2scr[base:base + 64, 0:TC])
                b.inc(ins, "dve")
            b.v(app2)

    for h in range(8):
        tile = 8 + h // 2
        hb = h % 2
        base = hb * 64
        qtile = h // 2
        ktile = 4 + h // 2

        for tt in range(16):
            pspar = tt % 2

            def vtr(t, tt=tt, tile=tile, base=base, pspar=pspar):
                b.w(t, "dve")
                ins = t.transpose(ps_tr[:, pspar * 64:(pspar + 1) * 64],
                                  qkvb[base:base + 64, tile * T + tt * 128:tile * T + (tt + 1) * 128],
                                  b.consts["ident_bf"][base:base + 64, base:base + 64])
                b.inc(ins, "pe")
            b.t(vtr)

            def vev(v, tt=tt, pspar=pspar):
                b.w(v, "pe")
                ins = v.tensor_copy(vtbuf[:, tt * 64:(tt + 1) * 64], ps_tr[:, pspar * 64:(pspar + 1) * 64])
                b.inc(ins, "dve")
            b.v(vev)

        for wi in range(4):
            for ic in range(4):
                i0 = wi * TC + ic * 128

                def mmsim(t, wi=wi, ic=ic, i0=i0, base=base, qtile=qtile, ktile=ktile):
                    b.w(t, "dve")
                    if wi > 0:
                        t.matmul(ps_a[:, 0:TC], qkvb[base:base + 64, qtile * T + i0:qtile * T + i0 + 128],
                                 qkvb[base:base + 64, ktile * T + (wi - 1) * TC:ktile * T + wi * TC],
                                 start=True, stop=True)
                    ins = t.matmul(ps_b[:, 0:TC], qkvb[base:base + 64, qtile * T + i0:qtile * T + i0 + 128],
                                   qkvb[base:base + 64, ktile * T + wi * TC:ktile * T + (wi + 1) * TC],
                                   start=True, stop=True)
                    b.inc(ins, "pe")
                b.t(mmsim)

                def smax(v, wi=wi, ic=ic):
                    b.w(v, "pe", "qw")
                    if wi > 0:
                        v.tensor_add(simb[:, 0:TC], ps_a[:, 0:TC], maskt[:, ic * 1024:ic * 1024 + TC])
                    else:
                        v.memset(simb[:, 0:TC], -3.0e38)
                    v.tensor_add(simb[:, TC:1024], ps_b[:, 0:TC], maskt[:, ic * 1024 + TC:(ic + 1) * 1024])
                    ins = v.reduce_max(ncol[:, 0:1], simb[:, 0:1024], AX.X, negate=True)
                    b.inc(ins, "dve")
                    v.wait_ge(b.sem["dve"], b.cnt["dve"])
                b.v(smax)

                def sexp(s):
                    b.w(s, "dve")
                    ins = s.activation(attnb[:, 0:1024], simb[:, 0:1024], AF.Exp,
                                       bias=ncol[:, 0:1], accum_out=ncol[:, 1:2])
                    b.inc(ins, "act")
                b.s(sexp)

                def snorm(v):
                    b.w(v, "act")
                    ins = v.reciprocal(ncol[:, 2:3], ncol[:, 1:2])
                    b.inc(ins, "dve")
                    v.wait_ge(b.sem["dve"], b.cnt["dve"])
                    ins = v.tensor_scalar(attnb[:, 0:1024], attnb[:, 0:1024], ncol[:, 2:3],
                                          None, ALU.mult)
                    b.inc(ins, "dve")
                b.v(snorm)

                for js in range(8):
                    pspar = js % 2

                    def atr(t, js=js, pspar=pspar):
                        b.w(t, "dve")
                        ins = t.transpose(ps_tr[:, pspar * 128:(pspar + 1) * 128],
                                          attnb[:, js * 128:(js + 1) * 128],
                                          b.consts["ident_bf"][:, :])
                        b.inc(ins, "pe")
                    b.t(atr)

                    def aev(v, js=js, ic=ic, pspar=pspar):
                        b.w(v, "pe")
                        ins = v.tensor_copy(attnT[:, js * TC + ic * 128:js * TC + (ic + 1) * 128],
                                            ps_tr[:, pspar * 128:(pspar + 1) * 128])
                        b.inc(ins, "dve")
                    b.v(aev)

            def mmav(t, wi=wi, base=base):
                b.w(t, "dve")
                ins = None
                j0 = 4 if wi == 0 else 0
                for jt in range(j0, 8):
                    jtg = (wi - 1) * 4 + jt
                    ins = t.matmul(ps_o[base:base + 64, 0:TC], vtbuf[:, jtg * 64:(jtg + 1) * 64],
                                   attnT[:, jt * TC:(jt + 1) * TC],
                                   start=(jt == j0), stop=(jt == 7))
                b.inc(ins, "pe")
            b.t(mmav)

            def oev(v, wi=wi, base=base, qtile=qtile):
                b.w(v, "pe")
                ins = v.tensor_copy(obuf[base:base + 64, qtile * T + wi * TC:qtile * T + (wi + 1) * TC],
                                    ps_o[base:base + 64, 0:TC])
                b.inc(ins, "dve")
            b.v(oev)

    esS.close()
    b.fence()
    esW = b.scope()
    wob = b.sb(esW, "l0f_wob" + sfx, [128, 4 * 1024], BF16)
    emit_wload(b, [(wob[:, i * 1024:(i + 1) * 1024], wd["l0f_wo"][lay][i * 128:(i + 1) * 128, :])
                   for i in range(4)])

    for tc in range(NTC):
        lo = tc * TC
        for dc in range(8):
            ps = ps_a if dc % 2 == 0 else ps_b

            def mmwo(t, dc=dc, ps=ps, lo=lo):
                b.w(t, "dve", "pool")
                ins = None
                for ct in range(4):
                    ins = t.matmul(ps[:, 0:TC], wob[:, ct * 1024 + dc * 128:ct * 1024 + (dc + 1) * 128],
                                   obuf[:, ct * T + lo:ct * T + lo + TC], start=(ct == 0), stop=(ct == 3))
                b.inc(ins, "pe")
            b.t(mmwo)

            def evwo(v, dc=dc, ps=ps, lo=lo):
                b.w(v, "pe")
                ins = v.tensor_add(XF[:, dc * T + lo:dc * T + lo + TC],
                                   XF[:, dc * T + lo:dc * T + lo + TC], ps[:, 0:TC])
                b.inc(ins, "dve")
            b.v(evwo)
    esW.close()
    es.close()
    b.fence()


def _l0f_ffn(b, lay, XF, wd, es0_t, sfx):
    nc = b.nc
    (rows, ps_r, ps_m, ps_a, ps_b, f32scr) = es0_t
    es = b.scope()
    h2b = b.sb(es, "l0f_h2b" + sfx, [128, 8 * T], BF16)
    xln2 = b.sb(es, "l0f_xln2" + sfx, [128, 8 * T], BF16)
    gcols = b.sb(es, "l0f_gcols" + sfx, [128, 16], F32)
    b1a = b.sb(es, "l0f_b1a" + sfx, [128, 22], F32)
    b1g = b.sb(es, "l0f_b1g" + sfx, [128, 22], F32)
    b2c = b.sb(es, "l0f_b2c" + sfx, [128, 8], F32)

    emit_wload(b, [], direct=[(gcols[:, 0:8], wd["l0f_lng"][lay]),
                              (gcols[:, 8:16], wd["l0f_lnb"][lay]),
                              (b1a[:, :], wd["l0f_b1a"][lay]),
                              (b1g[:, :], wd["l0f_b1g"][lay]),
                              (b2c[:, :], wd["l0f_b2"][lay])])

    for tc in range(NTC):
        lo = tc * TC

        def mkh(v, lo=lo, tc=tc):
            b.w(v, "pe", "act")
            ins = None
            for i in range(8):
                if i < 4:
                    ins = v.tensor_copy(h2b[:, i * T + lo:i * T + lo + TC],
                                        XF[:, i * T + lo:i * T + lo + TC])
                else:
                    if tc == 0:
                        v.memset(h2b[:, i * T:i * T + 1], 0.0)
                        ins = v.tensor_copy(h2b[:, i * T + 1:i * T + TC],
                                            XF[:, i * T:i * T + TC - 1])
                    else:
                        ins = v.tensor_copy(h2b[:, i * T + lo:i * T + lo + TC],
                                            XF[:, i * T + lo - 1:i * T + lo + TC - 1])
            b.inc(ins, "dve")
        b.v(mkh)

    ln_emit(b, lambda i: h2b[:, i * T:(i + 1) * T], True, rows, "l0f_ln" + sfx)

    for tc in range(NTC):
        lo = tc * TC
        ln_bcast(b, rows, lo, TC, ps_r, ps_m)

        def appg(v, lo=lo):
            b.w(v, "pe", "qw")
            ins = None
            for i in range(8):
                v.tensor_mul(f32scr[:, 0:TC], h2b[:, i * T + lo:i * T + lo + TC], ps_r[:, 0:TC])
                v.tensor_add(f32scr[:, 0:TC], f32scr[:, 0:TC], ps_m[:, 0:TC])
                ins = v.tensor_scalar(xln2[:, i * T + lo:i * T + lo + TC], f32scr[:, 0:TC],
                                      gcols[:, i:i + 1], gcols[:, 8 + i:9 + i], ALU.mult, ALU.add)
            b.inc(ins, "dve")
        b.v(appg)

    esW = b.scope()
    hg = b.sb(esW, "l0f_hg" + sfx, [128, 22 * 1024], BF16)
    w1c = b.sb(esW, "l0f_w1c" + sfx, [128, 8 * 256], BF16)
    w2c = b.sb(esW, "l0f_w2c" + sfx, [128, 22 * 128], BF16)
    gel2 = [b.sb(esW, "l0f_gel0" + sfx, [128, TC], F32),
            b.sb(esW, "l0f_gel1" + sfx, [128, TC], F32)]
    b.l0f_snapH = [0, 0]
    b.l0f_snapO = [0, 0]

    for th in range(2):
        for pair in range(22):
            emit_wload(b, [(w1c[:, dc * 256:(dc + 1) * 256],
                            wd["l0f_w1p"][lay][dc * 128:(dc + 1) * 128, pair * 256:(pair + 1) * 256])
                           for dc in range(8)])
            for tcl in range(2):
                lo = th * 1024 + tcl * TC
                pp = (pair * 2 + tcl) % 2
                psA = ps_a if pp == 0 else ps_r
                psB = ps_b if pp == 0 else ps_m

                def mmh(t, pair=pair, lo=lo, pp=pp, psA=psA, psB=psB):
                    b.w(t, "pool")
                    if pair == 0:
                        b.w(t, "dve")
                    else:
                        b.wv(t, "dve", b.l0f_snapH[pp])
                    for dc in range(8):
                        t.matmul(psA[:, 0:TC], w1c[:, dc * 256:dc * 256 + 128],
                                 xln2[:, dc * T + lo:dc * T + lo + TC], start=(dc == 0), stop=(dc == 7))
                    ins = None
                    for dc in range(8):
                        ins = t.matmul(psB[:, 0:TC], w1c[:, dc * 256 + 128:dc * 256 + 256],
                                       xln2[:, dc * T + lo:dc * T + lo + TC], start=(dc == 0), stop=(dc == 7))
                    b.inc(ins, "pe")
                b.t(mmh)
                pes = b.cnt["pe"]

                def gg_s(s, pair=pair, pp=pp, psB=psB, pes=pes):
                    b.wv(s, "pe", pes)
                    b.wv(s, "dve", b.l0f_snapH[pp])
                    ins = s.activation(gel2[pp][:, 0:TC], psB[:, 0:TC], AF.Gelu, bias=b1g[:, pair:pair + 1])
                    b.inc(ins, "act")
                b.s(gg_s)
                acts = b.cnt["act"]

                def gg_v(v, pair=pair, tcl=tcl, pp=pp, psA=psA, acts=acts, pes=pes):
                    b.wv(v, "act", acts)
                    b.wv(v, "pe", pes)
                    ins = v.scalar_tensor_tensor(hg[:, pair * 1024 + tcl * TC:pair * 1024 + (tcl + 1) * TC],
                                                 psA[:, 0:TC], b1a[:, pair:pair + 1], gel2[pp][:, 0:TC],
                                                 ALU.add, ALU.mult)
                    b.inc(ins, "dve")
                b.v(gg_v)
                b.l0f_snapH[pp] = b.cnt["dve"]

        for dc in range(8):
            emit_wload(b, [(w2c[:, hc * 128:(hc + 1) * 128],
                            wd["l0f_w2p"][lay * 8 + dc][hc * 128:(hc + 1) * 128, :])
                           for hc in range(22)])
            for tcl in range(2):
                lo = th * 1024 + tcl * TC
                pp = (dc * 2 + tcl) % 2
                psA = ps_a if pp == 0 else ps_b

                def mmo(t, dc=dc, lo=lo, tcl=tcl, pp=pp, psA=psA):
                    b.w(t, "pool")
                    if dc == 0:
                        b.w(t, "dve")
                    else:
                        b.wv(t, "dve", b.l0f_snapO[pp])
                    ins = None
                    for hc in range(22):
                        ins = t.matmul(psA[:, 0:TC], w2c[:, hc * 128:(hc + 1) * 128],
                                       hg[:, hc * 1024 + tcl * TC:hc * 1024 + (tcl + 1) * TC],
                                       start=(hc == 0), stop=(hc == 21))
                    b.inc(ins, "pe")
                b.t(mmo)
                pes = b.cnt["pe"]

                def evo(v, dc=dc, lo=lo, pp=pp, psA=psA, pes=pes):
                    b.wv(v, "pe", pes)
                    ins = v.scalar_tensor_tensor(XF[:, dc * T + lo:dc * T + lo + TC],
                                                 psA[:, 0:TC], b2c[:, dc:dc + 1],
                                                 XF[:, dc * T + lo:dc * T + lo + TC],
                                                 ALU.add, ALU.add)
                    b.inc(ins, "dve")
                b.v(evo)
                b.l0f_snapO[pp] = b.cnt["dve"]
    esW.close()
    es.close()
    b.fence()


def emit_l0_ffn(b, X, wd, spill, n_layers=2):
    nc = b.nc
    x_sp = spill["x_park"]
    XF = X
    es0 = b.scope()
    ps_r = b.pt(es0, "l0f_ps_r", [128, TC], F32)
    ps_m = b.pt(es0, "l0f_ps_m", [128, TC], F32)
    ps_a = b.pt(es0, "l0f_ps_a", [128, TC], F32)
    ps_b = b.pt(es0, "l0f_ps_b", [128, TC], F32)
    f32scr = b.sb(es0, "l0f_f32scr", [128, TC], F32)
    rows = b.sb(es0, "l0f_rows", [1, 2 * T], BF16)
    es0_t = (rows, ps_r, ps_m, ps_a, ps_b, f32scr)

    ln_emit(b, lambda i: X[:, i * T:(i + 1) * T], False, rows, "l0f_in")

    def park(qq):
        b.w(qq, "dve", "act", "pe")
        for i in range(8):
            b.dma(qq, "qsp", x_sp[i * 128:(i + 1) * 128, :], X[:, i * T:(i + 1) * T])
    b.q(park)

    for tc in range(NTC):
        lo = tc * TC
        ln_bcast(b, rows, lo, TC, ps_r, ps_m)

        def apply(v, lo=lo):
            b.w(v, "pe", "qsp")
            ins = None
            for i in range(8):
                v.tensor_mul(f32scr[:, 0:TC], X[:, i * T + lo:i * T + lo + TC], ps_r[:, 0:TC])
                ins = v.tensor_add(XF[:, i * T + lo:i * T + lo + TC], f32scr[:, 0:TC], ps_m[:, 0:TC])
            b.inc(ins, "dve")
        b.v(apply)
    b.fence()

    for lay in range(n_layers):
        sfx = f"_{lay}"
        _l0f_attn(b, lay, XF, wd, es0_t, sfx)
        _l0f_ffn(b, lay, XF, wd, es0_t, sfx)

    esF = b.scope()
    ln_emit(b, lambda i: XF[:, i * T:(i + 1) * T], False, rows, "l0f_out")
    xold = b.sb(esF, "l0f_xold", [128, 2 * TC], F32)
    for tc in range(NTC):
        lo = tc * TC
        ln_bcast(b, rows, lo, TC, ps_r, ps_m)
        for i in range(8):
            par = i % 2

            def ldxo(qq, i=i, lo=lo, par=par):
                b.w(qq, "dve", "pe", "pool")
                b.dma(qq, "qsl", xold[:, par * TC:(par + 1) * TC],
                      x_sp[i * 128:(i + 1) * 128, lo:lo + TC])
            b.q(ldxo)

            def addres(v, i=i, lo=lo, par=par):
                b.w(v, "pe", "qsl")
                v.tensor_mul(f32scr[:, 0:TC], XF[:, i * T + lo:i * T + lo + TC], ps_r[:, 0:TC])
                v.tensor_add(f32scr[:, 0:TC], f32scr[:, 0:TC], ps_m[:, 0:TC])
                ins = v.tensor_add(X[:, i * T + lo:i * T + lo + TC],
                                   xold[:, par * TC:(par + 1) * TC], f32scr[:, 0:TC])
                b.inc(ins, "dve")
            b.v(addres)
    esF.close()
    es0.close()
    b.fence()


def l0f_dram(nc):
    wd = {}
    w = nc.dram_tensor("l0f_wqkv_d", [2, 1024, 1536], F32, kind="ExternalInput")
    wd["l0f_wqkv"] = [w.ap()[i] for i in range(2)]
    w = nc.dram_tensor("l0f_wo_d", [2, 512, 1024], F32, kind="ExternalInput")
    wd["l0f_wo"] = [w.ap()[i] for i in range(2)]
    w = nc.dram_tensor("l0f_qs_d", [2, 128, 1], F32, kind="ExternalInput")
    wd["l0f_qs"] = [w.ap()[i] for i in range(2)]
    w = nc.dram_tensor("l0f_ks_d", [2, 128, 1], F32, kind="ExternalInput")
    wd["l0f_ks"] = [w.ap()[i] for i in range(2)]
    wd["l0f_rot64"] = nc.dram_tensor("l0f_rot64_d", [128, 64], BF16, kind="ExternalInput").ap()
    wd["l0f_tq"] = nc.dram_tensor("l0f_tq_d", [128, 4 * T], BF16, kind="ExternalInput").ap()
    wd["l0f_mask"] = nc.dram_tensor("l0f_mask_d", [128, 4 * 1024], BF16, kind="ExternalInput").ap()
    w = nc.dram_tensor("l0f_lng_d", [2, 128, 8], F32, kind="ExternalInput")
    wd["l0f_lng"] = [w.ap()[i] for i in range(2)]
    w = nc.dram_tensor("l0f_lnb_d", [2, 128, 8], F32, kind="ExternalInput")
    wd["l0f_lnb"] = [w.ap()[i] for i in range(2)]
    w = nc.dram_tensor("l0f_w1p_d", [2, 1024, 5632], F32, kind="ExternalInput")
    wd["l0f_w1p"] = [w.ap()[i] for i in range(2)]
    w = nc.dram_tensor("l0f_b1a_d", [2, 128, 22], F32, kind="ExternalInput")
    wd["l0f_b1a"] = [w.ap()[i] for i in range(2)]
    w = nc.dram_tensor("l0f_b1g_d", [2, 128, 22], F32, kind="ExternalInput")
    wd["l0f_b1g"] = [w.ap()[i] for i in range(2)]
    w = nc.dram_tensor("l0f_w2p_d", [2 * 8, 2816, 128], F32, kind="ExternalInput")
    wd["l0f_w2p"] = [w.ap()[i] for i in range(16)]
    w = nc.dram_tensor("l0f_b2_d", [2, 128, 8], F32, kind="ExternalInput")
    wd["l0f_b2"] = [w.ap()[i] for i in range(2)]
    return wd


def prep_l0f(inputs):
    g = {}
    g["l0f_wqkv_d"] = _cc(inputs["L0_ffn_wqkv"].astype(np.float32))
    g["l0f_wo_d"] = _cc(inputs["L0_ffn_wo"].astype(np.float32))
    qs = inputs["L0_ffn_q_scale"]; ks = inputs["L0_ffn_k_scale"]
    g["l0f_qs_d"] = _cc(np.stack([np.tile(qs[l], 2).reshape(128, 1) for l in range(2)]))
    g["l0f_ks_d"] = _cc(np.stack([np.tile(ks[l], 2).reshape(128, 1) for l in range(2)]))
    R = np.zeros((64, 64), np.float32)
    for m in range(32):
        R[m + 32, m] = -1.0
        R[m, m + 32] = 1.0
    g["l0f_rot64_d"] = _cc(np.tile(R, (2, 1)).astype(bf16))
    half = np.arange(0, 64, 2, dtype=np.float32)
    inv_freq = 1.0 / 10000.0 ** (half / 64.0)
    t = np.arange(T, dtype=np.float32)
    freqs = t[:, None] * inv_freq[None, :]
    freqs = np.concatenate([freqs, freqs], -1)
    base_ = (half + 0.4 * 64) / (1.4 * 64)
    power = (t - T // 2) / 256.0
    scale = base_[None, :] ** power[:, None]
    scale = np.concatenate([scale, scale], -1)
    cos, sin = np.cos(freqs), np.sin(freqs)
    Tqc = (8.0 * cos * scale).T; Tqs = (8.0 * sin * scale).T
    Tkc = (cos / scale).T; Tks = (sin / scale).T
    tq = np.concatenate([np.tile(x, (2, 1)) for x in (Tqc, Tqs, Tkc, Tks)], 1)
    g["l0f_tq_d"] = _cc(tq.astype(bf16))
    ii = np.arange(512)[:, None]
    jj = np.arange(1024)[None, :]
    m = (jj < ii) | (jj > ii + 512)
    mask = np.where(m, np.float32(-3.0e38), np.float32(0.0))
    g["l0f_mask_d"] = _cc(mask.reshape(4, 128, 1024).transpose(1, 0, 2).reshape(128, 4096).astype(bf16))
    g["l0f_lng_d"] = _cc(inputs["L0_ffn_ln_g"].reshape(2, 8, 128).transpose(0, 2, 1))
    g["l0f_lnb_d"] = _cc(inputs["L0_ffn_ln_b"].reshape(2, 8, 128).transpose(0, 2, 1))
    w1 = inputs["L0_ffn_w1"]; b1 = inputs["L0_ffn_b1"]
    w1p = np.zeros((2, 1024, 5632), np.float32)
    b1a = np.zeros((2, 128, 22), np.float32); b1g = np.zeros((2, 128, 22), np.float32)
    for l in range(2):
        for p in range(22):
            n = min(128, 2730 - p * 128)
            w1p[l, :, p * 256:p * 256 + n] = w1[l, :, p * 128:p * 128 + n]
            w1p[l, :, p * 256 + 128:p * 256 + 128 + n] = w1[l, :, 2730 + p * 128:2730 + p * 128 + n]
            b1a[l, :n, p] = b1[l, p * 128:p * 128 + n]
            b1g[l, :n, p] = b1[l, 2730 + p * 128:2730 + p * 128 + n]
    g["l0f_w1p_d"] = _cc(w1p)
    g["l0f_b1a_d"] = _cc(b1a)
    g["l0f_b1g_d"] = _cc(b1g)
    w2 = inputs["L0_ffn_w2"]
    w2pad = np.zeros((2, 2816, 1024), np.float32)
    w2pad[:, :2730, :] = w2
    w2p = np.zeros((2 * 8, 2816, 128), np.float32)
    for l in range(2):
        for dc in range(8):
            w2p[l * 8 + dc] = w2pad[l, :, dc * 128:(dc + 1) * 128]
    g["l0f_w2p_d"] = _cc(w2p)
    g["l0f_b2_d"] = _cc(inputs["L0_ffn_b2"].reshape(2, 8, 128).transpose(0, 2, 1))
    return g



CONST_FLOATS = [1e-5, -0.5, 1.0 / 1024.0, 0.1, 1.0, 0.0, -1.0, 0.5, 2.0]


def alloc_consts(b, nc):
    t1 = nc.sbuf_tensor("c_ones_col_bf", [128, 1], BF16).__enter__()
    t2 = nc.sbuf_tensor("c_ones_row_f32", [1, 128], F32).__enter__()
    cp = nc.sbuf_tensor("c_pool", [128, len(CONST_FLOATS)], F32).__enter__()
    idf = nc.sbuf_tensor("c_ident_f32", [128, 128], F32).__enter__()
    t3 = nc.sbuf_tensor("c_ones_row_bf", [1, 128], BF16).__enter__()
    b.consts["ones_row_bf"] = t3
    idb = nc.sbuf_tensor("c_ident_bf", [128, 128], BF16).__enter__()
    b.consts["ident_bf"] = idb
    sel8 = nc.sbuf_tensor("c_sel8", [8, 8 * 128], BF16).__enter__()
    sel16 = nc.sbuf_tensor("c_sel16", [128, 16 * 128], BF16).__enter__()
    b.consts["ident_f32"] = idf
    b.consts["sel8"] = sel8
    b.consts["sel16"] = sel16
    b.consts["ones_col_bf"] = t1
    b.consts["ones_row_f32"] = t2
    b.consts["cp"] = cp
    b.wstg = nc.sbuf_tensor("c_wstg", [128, 2 * 512], F32).__enter__()
    for j, val in enumerate(CONST_FLOATS):
        nc.const_aps.register(F32, float(val), cp[:, j:j + 1])


def init_consts(b, const_dram=None):
    t1, t2, cp = b.consts["ones_col_bf"], b.consts["ones_row_f32"], b.consts["cp"]

    def ini(v):
        v.memset(t1[:, :], 1.0)
        v.memset(t2[0:1, :], 1.0)
        v.memset(b.consts["ones_row_bf"][0:1, :], 1.0)
        ins = None
        for j, val in enumerate(CONST_FLOATS):
            ins = v.memset(cp[:, j:j + 1], float(val))
        b.inc(ins, "dve")
    b.v(ini)
    if const_dram is not None:
        def ldc(qq):
            b.dma(qq, "qx", b.consts["ident_f32"][:, :], const_dram["ident_f32"])
            b.dma(qq, "qx", b.consts["ident_bf"][:, :], const_dram["ident_bf"])
            b.dma(qq, "qx", b.consts["sel8"][0:8, :], const_dram["sel8"])
            b.dma(qq, "qx", b.consts["sel16"][64:80, :], const_dram["sel16"])
        b.q(ldc)


def const_dram_decl(nc):
    return {
        "ident_f32": nc.dram_tensor("c_ident_f32_d", [128, 128], F32, kind="ExternalInput").ap(),
        "ident_bf": nc.dram_tensor("c_ident_bf_d", [128, 128], BF16, kind="ExternalInput").ap(),
        "sel8": nc.dram_tensor("c_sel8_d", [8, 8 * 128], BF16, kind="ExternalInput").ap(),
        "sel16": nc.dram_tensor("c_sel16_d", [16, 16 * 128], BF16, kind="ExternalInput").ap(),
    }


def const_host():
    sel8 = np.zeros((8, 8 * 128), np.float32)
    for e in range(8):
        sel8[e, e * 128:(e + 1) * 128] = 1.0
    sel16 = np.zeros((16, 16 * 128), np.float32)
    for n in range(16):
        sel16[n, n * 128:(n + 1) * 128] = 1.0
    return {"c_ident_f32_d": np.eye(128, dtype=np.float32), "c_ident_bf_d": np.eye(128, dtype=bf16),
            "c_sel8_d": sel8.astype(bf16), "c_sel16_d": sel16.astype(bf16)}


def l1a_dram(nc):
    wd = {
        "l1a_w_in": nc.dram_tensor("l1a_w_in", [1024, 4096], F32, kind="ExternalInput").ap(),
        "l1a_conv_w": nc.dram_tensor("l1a_conv_w", [128, 64], F32, kind="ExternalInput").ap(),
        "l1a_conv_b": nc.dram_tensor("l1a_conv_b", [128, 16], F32, kind="ExternalInput").ap(),
        "l1a_w_B": nc.dram_tensor("l1a_w_B", [128, 256], F32, kind="ExternalInput").ap(),
        "l1a_w_C": nc.dram_tensor("l1a_w_C", [16, 2048], F32, kind="ExternalInput").ap(),
        "l1a_w_out": nc.dram_tensor("l1a_w_out", [128, 16 * 1024], F32, kind="ExternalInput").ap(),
    }
    spill = {
        "l1a_xc": nc.dram_tensor("sp_l1a_xc", [2048, T], BF16, kind="Internal").ap(),
        "l1a_z": nc.dram_tensor("sp_l1a_z", [2048, T], BF16, kind="Internal").ap(),
    }
    return wd, spill


def prep_l1a(inputs):
    g = {}
    g["l1a_w_in"] = _cc(inputs["L1_attn_w_in"].astype(np.float32))
    cw = inputs["L1_attn_conv_w"].reshape(2048, 4)
    g["l1a_conv_w"] = _cc(cw.reshape(16, 128, 4).transpose(1, 0, 2).reshape(128, 64))
    g["l1a_conv_b"] = col_pack(inputs["L1_attn_conv_b"].reshape(2048))
    g["l1a_w_B"] = feat_tiles(inputs["L1_attn_w_B"])
    g["l1a_w_C"] = _cc(inputs["L1_attn_w_C"].astype(np.float32))
    g["l1a_w_out"] = feat_tiles(inputs["L1_attn_w_out"])
    return g


def build_l1_attn_standalone():
    nc = bass.Bass("TRN2", target_bir_lowering=False, debug=False)
    x_in = nc.dram_tensor("x_in", [D, T], F32, kind="ExternalInput").ap()
    x_out = nc.dram_tensor("x_out", [D, T], F32, kind="ExternalOutput").ap()
    wd, spill = l1a_dram(nc)
    b = B(nc)
    alloc_consts(b, nc)
    X = nc.sbuf_tensor("X", [128, 8 * T], F32).__enter__()
    with nc.Block() as block:
        b.block = block
        init_consts(b)

        def ldx(qq):
            for i in range(8):
                b.dma(qq, "qx", X[:, i * T:(i + 1) * T], x_in[i * 128:(i + 1) * 128, :])
        b.q(ldx)

        emit_l1_attn(b, X, wd, spill)

        def stx(qq):
            b.w(qq, "dve")
            for i in range(8):
                b.dma(qq, "qo", x_out[i * 128:(i + 1) * 128, :], X[:, i * T:(i + 1) * T])
            qq.wait_ge(b.sem["qo"], b.cnt["qo"])
        b.q(stx)
    return nc


def build_l0_attn_standalone(n_layers=8):
    nc = bass.Bass("TRN2", target_bir_lowering=False, debug=False)
    x_in = nc.dram_tensor("x_in", [D, T], F32, kind="ExternalInput").ap()
    x_out = nc.dram_tensor("x_out", [D, T], F32, kind="ExternalOutput").ap()
    xm_out = nc.dram_tensor("xm_out", [D, T], F32, kind="ExternalOutput").ap()
    cd = const_dram_decl(nc)
    wd, spill = l0a_dram(nc)
    spill["x_park"] = nc.dram_tensor("sp_x_park", [D, T], F32, kind="Internal").ap()
    b = B(nc)
    alloc_consts(b, nc)
    X = nc.sbuf_tensor("X", [128, 8 * T], F32).__enter__()
    with nc.Block() as block:
        b.block = block
        init_consts(b, cd)

        def ldx(qq):
            for i in range(8):
                b.dma(qq, "qx", X[:, i * T:(i + 1) * T], x_in[i * 128:(i + 1) * 128, :])
        b.q(ldx)

        emit_l0_attn(b, X, wd, spill, n_layers=n_layers)

        def stx(qq):
            b.w(qq, "dve")
            for i in range(8):
                b.dma(qq, "qo", x_out[i * 128:(i + 1) * 128, :], X[:, i * T:(i + 1) * T])
            qq.wait_ge(b.sem["qo"], b.cnt["qo"])
        b.q(stx)
    return nc


def build_l0_ffn_standalone(n_layers=2):
    nc = bass.Bass("TRN2", target_bir_lowering=False, debug=False)
    x_in = nc.dram_tensor("x_in", [D, T], F32, kind="ExternalInput").ap()
    x_out = nc.dram_tensor("x_out", [D, T], F32, kind="ExternalOutput").ap()
    cd = const_dram_decl(nc)
    wd = l0f_dram(nc)
    spill = {"x_park": nc.dram_tensor("sp_x_park", [D, T], F32, kind="Internal").ap()}
    b = B(nc)
    alloc_consts(b, nc)
    X = nc.sbuf_tensor("X", [128, 8 * T], F32).__enter__()
    with nc.Block() as block:
        b.block = block
        init_consts(b, cd)

        def ldx(qq):
            for i in range(8):
                b.dma(qq, "qx", X[:, i * T:(i + 1) * T], x_in[i * 128:(i + 1) * 128, :])
        b.q(ldx)

        emit_l0_ffn(b, X, wd, spill, n_layers=n_layers)

        def stx(qq):
            b.w(qq, "dve")
            for i in range(8):
                b.dma(qq, "qo", x_out[i * 128:(i + 1) * 128, :], X[:, i * T:(i + 1) * T])
            qq.wait_ge(b.sem["qo"], b.cnt["qo"])
        b.q(stx)
    return nc


def build_l1_ffn_standalone():
    nc = bass.Bass("TRN2", target_bir_lowering=False, debug=False)
    x_in = nc.dram_tensor("x_in", [D, T], F32, kind="ExternalInput").ap()
    x_out = nc.dram_tensor("x_out", [D, T], F32, kind="ExternalOutput").ap()
    cd = const_dram_decl(nc)
    wd = l1f_dram(nc)
    spill = {"x_park": nc.dram_tensor("sp_x_park", [D, T], F32, kind="Internal").ap()}
    b = B(nc)
    alloc_consts(b, nc)
    X = nc.sbuf_tensor("X", [128, 8 * T], F32).__enter__()
    with nc.Block() as block:
        b.block = block
        init_consts(b, cd)

        def ldx(qq):
            for i in range(8):
                b.dma(qq, "qx", X[:, i * T:(i + 1) * T], x_in[i * 128:(i + 1) * 128, :])
        b.q(ldx)

        emit_l1_ffn(b, X, wd, spill)

        def stx(qq):
            b.w(qq, "dve")
            for i in range(8):
                b.dma(qq, "qo", x_out[i * 128:(i + 1) * 128, :], X[:, i * T:(i + 1) * T])
            qq.wait_ge(b.sem["qo"], b.cnt["qo"])
        b.q(stx)
    return nc


def build_full():
    nc = bass.Bass("TRN2", target_bir_lowering=False, debug=False)
    x_in = nc.dram_tensor("x_in", [D, T], F32, kind="ExternalInput").ap()
    x_out = nc.dram_tensor("x_out", [D, T], F32, kind="ExternalOutput").ap()
    cd = const_dram_decl(nc)
    wd0a, sp0a = l0a_dram(nc)
    wd0f = l0f_dram(nc)
    wd1a, sp1a = l1a_dram(nc)
    wd1f = l1f_dram(nc)
    spill = dict(sp0a)
    spill.update(sp1a)
    spill["x_park"] = nc.dram_tensor("sp_x_park", [D, T], F32, kind="Internal").ap()
    b = B(nc)
    alloc_consts(b, nc)
    X = nc.sbuf_tensor("X", [128, 8 * T], F32).__enter__()
    with nc.Block() as block:
        b.block = block
        init_consts(b, cd)

        def ldx(qq):
            for i in range(8):
                b.dma(qq, "qx", X[:, i * T:(i + 1) * T], x_in[i * 128:(i + 1) * 128, :])
        b.q(ldx)

        emit_l0_attn(b, X, wd0a, spill)
        emit_l0_ffn(b, X, wd0f, spill)
        emit_l1_attn(b, X, wd1a, spill)
        emit_l1_ffn(b, X, wd1f, spill)

        def stx(qq):
            b.w(qq, "dve", "act", "pe")
            for i in range(8):
                b.dma(qq, "qo", x_out[i * 128:(i + 1) * 128, :], X[:, i * T:(i + 1) * T])
            qq.wait_ge(b.sem["qo"], b.cnt["qo"])
        b.q(stx)
    return nc


_BUILD_CACHE = {}


def _get_nc(name, builder):
    if name not in _BUILD_CACHE:
        _BUILD_CACHE[name] = builder()
    return _BUILD_CACHE[name]


def kernel(**inputs):
    wmap = {}
    wmap.update(const_host())
    wmap.update(prep_l0a(inputs))
    wmap.update(prep_l0f(inputs))
    wmap.update(prep_l1a(inputs))
    wmap.update(prep_l1f(inputs))
    x = np.ascontiguousarray(inputs["L0_attn_x"].astype(np.float32))
    xs = [np.ascontiguousarray(x[c].T) for c in range(NCORES)]
    nc = _get_nc("full", build_full)
    in_maps = [dict(wmap, x_in=xs[c]) for c in range(NCORES)]
    res = run_bass_kernel_spmd(nc, in_maps, core_ids=list(range(NCORES)))
    out = np.stack([res.results[c]["x_out"].T for c in range(NCORES)], axis=0)
    return np.ascontiguousarray(out.astype(np.float32))
